# version 10
# baseline (speedup 1.0000x reference)
import os

import numpy as np
import ml_dtypes

import concourse.bass as bass
import concourse.mybir as mybir
from concourse.bass_utils import run_bass_kernel_spmd
from concourse.aot_env import allow_ici_all_to_all

F32 = mybir.dt.float32
BF16 = mybir.dt.bfloat16
OP = mybir.AluOpType
AF = mybir.ActivationFunctionType
AX = mybir.AxisListType.X

D = 2048
QL = 1536
KVL = 512
H = 16
DR = 64
DV = 128
EPS = 1e-5
ATT_SCALE = float(1.0 / np.sqrt(128 + 64))
NC = 8
HG = 4

C_RNE = float(np.float32(1.5 * 2 ** 23))
VELT = float(np.float32(2 ** 20 + 1))
M_SUB = float(np.float32(1.5 * 2 ** 14))
E4_MIN = float(np.float32(2 ** -6))


class Sem:
    __slots__ = ("h", "n", "name")

    def __init__(self, h, name):
        self.h, self.n, self.name = h, 0, name

    def inc(self, k=1):
        self.n += k
        return self.n


def build_program(S=4096, dbg=False):
    B = 2
    NT = B * S // NC
    TC = NT // 128
    W1 = NT
    KQ = S // 4
    QC = S // 128
    KC = S // 128
    QB = min(512, S)
    QCB = QB // 128
    NTW = max(1, S // 512)
    TWW = min(512, S)
    AGW = max(W1, 1024)
    assert KQ <= 1024 and NT % 128 == 0

    nc = bass.Bass("TRN2", target_bir_lowering=False, debug=False,
                   detect_race_conditions=bool(os.environ.get("KRACE")))
    st = nc._state

    def XI(name, shape, dtype=F32):
        return nc.dram_tensor(name, list(shape), dtype, kind="ExternalInput")

    x_in = XI("x", [NT, D])
    qaT_b = XI("qaT_b", [12, 128, 16, 128])
    kvaT_b = XI("kvaT_b", [4, 128, 16, 128])
    kvrT_b = XI("kvrT_b", [128, 16, DR])
    qrwT_b = XI("qrwT_b", [8, 128, 12, 128])
    qbT_b = XI("qbT_b", [HG, 128, 12, 128])
    kvbknT_b = XI("kvbknT_b", [HG, 128, 4, 128])
    kvbvT_b = XI("kvbvT_b", [HG, 128, 4, 128])
    owT_b = XI("owT_b", [2, 16, 128, 1024])
    ffnw_in = XI("ffnw", [2, 128, D])
    qalnw_in = XI("qalnw", [12, 128, 1])
    qalnb_in = XI("qalnb", [12, 128, 1])
    kvalnw_in = XI("kvalnw", [4, 128, 1])
    kvalnb_in = XI("kvalnb", [4, 128, 1])
    bias_in = XI("bias_rep", [128, D])
    xts_in = XI("xts_rep", [128, 1])
    ident_in = XI("ident", [128, 128], BF16)
    identf_in = XI("identf", [128, 128], F32)
    ktcos_in = XI("ktcos", [NT, DR])
    ktsf_in = XI("ktsf", [NT, DR])
    fcos_in = XI("fcos2", [128, W1])
    fsf_in = XI("fsf2", [128, W1])

    out_t = nc.dram_tensor("out", [NT, D], F32, kind="ExternalOutput")

    NAGC = 25
    ag_in = nc.dram_tensor("ag_in", [NAGC, 128, W1], BF16)
    ag_out_a = nc.dram_tensor("ag_out_a", [4, 12, 128, W1], BF16)
    ag_out_b = nc.dram_tensor("ag_out_b", [4, 5, 128, W1], BF16)
    ag_out_c = nc.dram_tensor("ag_out_c", [4, 8, 128, W1], BF16)
    a2a_in = nc.dram_tensor("a2a_in", [HG, 4, TC, 128, 128], BF16)
    a2a_out = nc.dram_tensor("a2a_out", [HG, 4, 4, TC, 128, 128], BF16)
    amax_in = nc.dram_tensor("amax_in", [1, 16], F32)
    amax_out = nc.dram_tensor("amax_out", [NC, 1, 16], F32,
                              addr_space="Shared")
    wdq_in = nc.dram_tensor("wdq_in", [16, 128, 256], BF16)
    a2a_mine = nc.dram_tensor("a2a_mine", [16, TC, 128, 128], BF16)
    wdq_out = nc.dram_tensor("wdq_out", [NC, 16, 128, 256], BF16,
                             addr_space="Shared")
    xdq_scr = nc.dram_tensor("xdq_scr", [TC, 128, D], BF16,
                             kind="ExternalOutput" if dbg else "Internal")
    dbg_wsc = nc.dram_tensor("dbg_wsc", [128, 16], F32, kind="ExternalOutput") \
        if dbg else None
    dbg_att = nc.dram_tensor("dbg_att", [128, S], BF16, kind="ExternalOutput") \
        if dbg else None
    dbg_p = nc.dram_tensor("dbg_p", [128, S], BF16, kind="ExternalOutput") \
        if dbg else None
    dbg_st = nc.dram_tensor("dbg_st", [128, 64], F32, kind="ExternalOutput") \
        if dbg else None
    dbg_agin = nc.dram_tensor("dbg_agin", [NAGC, 128, W1], BF16,
                              kind="ExternalOutput") if dbg else None
    dbg_a2a = nc.dram_tensor("dbg_a2a", [HG, 4, TC, 128, 128], BF16,
                             kind="ExternalOutput") if dbg else None
    dbg_ptb = nc.dram_tensor("dbg_ptb", [128, (S // 128) * QB], BF16,
                             kind="ExternalOutput") if dbg else None
    dbg_pb2 = nc.dram_tensor("dbg_pb2", [128, 2 * S], BF16,
                             kind="ExternalOutput") if dbg else None
    dbg_st2 = nc.dram_tensor("dbg_st2", [128, 64], F32,
                             kind="ExternalOutput") if dbg else None
    dbg_agout = nc.dram_tensor("dbg_agout", [4, NAGC, 128, W1], BF16,
                               kind="ExternalOutput") if dbg else None
    dbg_kn = nc.dram_tensor("dbg_kn", [128, 2 * S], BF16,
                            kind="ExternalOutput") if dbg else None
    dbg_xnT = nc.dram_tensor("dbg_xnT", [128, 16 * W1], BF16,
                             kind="ExternalOutput") if dbg else None
    dbg_qn = nc.dram_tensor("dbg_qn", [128, 2 * S], BF16,
                            kind="ExternalOutput") if dbg else None

    SB_BASE = 16512
    SB_TOP = 229344
    offs = {}

    def _alloc(name, shape, dtype, off):
        itb = 2 if dtype == BF16 else 4
        per = int(np.prod(shape[1:])) * itb
        assert off + per <= SB_TOP, (name, off, per)
        offs[name] = (off, per)
        return st.alloc_sbuf_tensor_at(name, list(shape), dtype, offset=off), off + per

    cur = SB_BASE

    def FX(name, shape, dtype):
        nonlocal cur
        off = (cur + 31) // 32 * 32
        t, cur = _alloc(name, shape, dtype, off)
        return t

    ident = FX("identW", [128, 128], BF16)
    identf = FX("identF", [128, 128], F32)
    onesb = FX("onesb", [128, 128], BF16)
    ones128f = FX("ones128f", [128, 128], F32)
    xtsb = FX("xtsb", [128, 1], F32)
    wsc = FX("wsc", [128, 16], F32)
    stats = FX("stats", [128, 64], F32)
    sconst = FX("sconst", [128, 8], F32)
    biasb = FX("biasb", [128, D], F32)
    fws = FX("fws", [128, 256], F32)
    lnwq = FX("lnwq", [128, 12], F32)
    lnbq = FX("lnbq", [128, 12], F32)
    lnwk = FX("lnwk", [128, 4], F32)
    lnbk = FX("lnbk", [128, 4], F32)
    warm = FX("warm", [128, 2], F32)
    AB = (cur + 255) // 256 * 256

    def mkarena():
        pos = [AB]

        def A(name, shape, dtype):
            off = (pos[0] + 31) // 32 * 32
            t, pos[0] = _alloc(name, shape, dtype, off)
            return t
        return A, pos

    A1, a1pos = mkarena()
    xtile = A1("xtile", [128, 2 * D], F32)
    xnbf = A1("xnbf", [128, D], BF16)
    xcs = A1("xcs", [128, D], F32)
    sqf = A1("sqf", [128, D], F32)
    xnT = A1("xnT", [128, 16 * W1], BF16)
    wcb = A1("wcb", [128, 4 * 2048], BF16)
    qlat = A1("qlat", [128, 12 * W1], BF16)
    sqb = A1("sqb", [128, 2 * W1], BF16)
    mub = A1("mub", [128, W1], F32)
    rstdb = A1("rstdb", [128, W1], F32)
    scr1 = A1("scr1", [128, max(W1, 128)], F32)
    qrev = A1("qrev", [128, W1], F32)
    qrrot = A1("qrrot", [128, W1], F32)
    kvlT = A1("kvlT", [128, 4 * W1], BF16)
    krT = A1("krT", [64, W1], BF16)
    krop = A1("krop", [128, DR], F32)
    agst = A1("agst", [128, AGW], BF16)
    kcosb = A1("kcosb", [128, TC * DR], F32)
    ksfb = A1("ksfb", [128, TC * DR], F32)
    fcosb = A1("fcosb", [128, W1], F32)
    fsfb = A1("fsfb", [128, W1], F32)
    fwn = A1("fwn", [128, D], F32)
    fwq = A1("fwq", [128, D], BF16)

    A2, a2pos = mkarena()
    knT = A2("knT", [128, 2 * S], BF16)
    vTb = A2("vTb", [128, 2 * S], BF16)
    qnT = A2("qnT", [128, 2 * S], BF16)
    qrTh = A2("qrTh", [64, 2 * S], BF16)
    krTg = A2("krTg", [64, S], BF16)
    latq = A2("latq", [128, 4 * TWW], BF16)
    pbuf = A2("pbuf", [128, 2 * S], BF16)
    ptb = A2("ptb", [128, KC * QB], BF16)
    attnT = A2("attnT", [128, S], BF16)
    qbTb = A2("qbTb", [128, 12 * 128], BF16)
    kvknb = A2("kvknb", [128, 4 * 128], BF16)
    kvvb = A2("kvvb", [128, 4 * 128], BF16)

    A3, a3pos = mkarena()
    p3x = A3("p3x", [128, 2 * D], F32)
    p3xn = A3("p3xn", [128, D], F32)
    p3c = A3("p3c", [128, D], F32)
    p3rA = A3("p3rA", [128, D], F32)
    p3rB = A3("p3rB", [128, D], F32)
    p3rC = A3("p3rC", [128, D], F32)
    p3rD = A3("p3rD", [128, D], F32)
    p3bs = A3("p3bs", [128, 128], F32)
    p3sA = A3("p3sA", [128, 128], F32)
    p3sB = A3("p3sB", [128, 128], F32)
    p3xdq = A3("p3xdq", [128, D], BF16)
    p3xdqb = A3("p3xdqb", [128, 2 * D], BF16)
    p3xdqT = A3("p3xdqT", [128, 2 * D], BF16)
    p3at = A3("p3at", [128, 2 * D], BF16)
    p3wdq = A3("p3wdq", [128, 16 * 1024], BF16)
    p3ow = A3("p3ow", [128, 16 * 1024], BF16)
    p3xb = A3("p3xb", [128, 2 * 1024], F32)

    TAIL = (max(a1pos[0], a2pos[0], a3pos[0]) + 255) // 256 * 256
    tl = [TAIL]

    def A4(name, shape, dtype):
        off = (tl[0] + 31) // 32 * 32
        t, tl[0] = _alloc(name, shape, dtype, off)
        return t

    q_c = A4("q_c", [128, D], F32)
    q_rA = A4("q_rA", [128, 1024], F32)
    q_rB = A4("q_rB", [128, 1024], F32)
    q_rC = A4("q_rC", [128, 1024], F32)
    q_rD = A4("q_rD", [128, 1024], F32)
    q_bs = A4("q_bs", [128, 64], F32)
    q_sA = A4("q_sA", [128, 64], F32)
    q_sB = A4("q_sB", [128, 64], F32)

    psA = st.alloc_psum_tensor("psA", [128, 2048], F32)
    psB = st.alloc_psum_tensor("psB", [128, 512], F32)
    psC = st.alloc_psum_tensor("psC", [128, 2048], BF16)
    psD = st.alloc_psum_tensor("psD", [128, 512], F32)

    EM = {k: [] for k in ("sync", "tensor", "vector", "scalar", "gpsimd")}

    class Q:
        def __init__(self, name):
            self.name = name

        def __call__(self, fn, inc=None, k=1):
            drain = self.name in ("vector", "scalar")
            if inc is None:
                EM[self.name].append(fn)
                if drain:
                    EM[self.name].append(lambda e: e.drain())
                return None
            v = inc.inc(k)
            if drain:
                EM[self.name].append(fn)
                EM[self.name].append(
                    lambda e, s=inc, kk=k: e.drain().then_inc(s.h, kk))
            else:
                EM[self.name].append(lambda e, f=fn, s=inc, kk=k: f(e).then_inc(s.h, kk))
            return v

        def w(self, sem, val):
            if val and val > 0:
                EM[self.name].append(lambda e, s=sem, v=val: e.wait_ge(s.h, v))

        def dma(self, out, in_, sem, slow=False, dyn=None):
            v = sem.inc(16)
            if dyn is None:
                EM[self.name].append(
                    lambda e, o=out, i=in_, s=sem, sl=slow: e.dma_start(
                        out=o, in_=i, allow_slow_non_contiguous=sl).then_inc(s.h, 16))
            else:
                ten, idx = dyn

                def _f(e, o=out, tn=ten, ix=idx, s=sem):
                    si, hl, tt = ix
                    src = tn.ap()[si][bass.ds(_rank_base(e), 1)].squeeze(0)[hl, tt]
                    return e.dma_start(out=o, in_=src).then_inc(s.h, 16)
                EM[self.name].append(_f)
            return v

    _rank_cache = {}

    def _rank_base(e):
        key = id(e)
        if key not in _rank_cache:
            _rank_cache[key] = e.compute_val(e.partition_id() % 4)
        return _rank_cache[key]

    _rank2_cache = {}

    def _rank2c(e, c0):
        key = (id(e), c0)
        if key not in _rank2_cache:
            _rank2_cache[key] = e.compute_val((e.partition_id() % 4) * 2 + c0)
        return _rank2_cache[key]

    T, V, Sc, G, Y = Q("tensor"), Q("vector"), Q("scalar"), Q("gpsimd"), Q("sync")

    def T_mm(ps_ap, lhs_ap, rhs_ap, start, stop, inc=None, nmax=512):
        n = rhs_ap.shape[-1]
        vals = None
        for o in range(0, n, nmax):
            w = min(nmax, n - o)
            last = o + w >= n
            vals = T(lambda e, p=ps_ap[:, o:o + w] if ps_ap.ndim == 2 else None,
                     l=lhs_ap, r=rhs_ap[:, o:o + w], st_=start, sp=stop:
                     e.matmul(p, l, r, start=st_, stop=sp),
                     inc=inc if last else None)
        return vals

    def mks(name):
        return Sem(st.alloc_semaphore(name), name)

    s_ld = mks("ld")
    s_x = mks("x")
    s_wg = mks("wg")
    s_tc = mks("tc")
    s_vc = mks("vc")
    s_sc = mks("sc")
    s_tp = mks("tp")
    s_pe = mks("pe")
    s_ag = mks("ag")
    s_cc = mks("cc")
    s_qk = mks("qk")
    s_mx = mks("mx")
    s_ex = mks("ex")
    s_ar = mks("ar")
    s_fe = mks("fe")
    s_pn = mks("pn")
    s_pv = mks("pv")
    s_pq = mks("pq")
    s_a2 = mks("a2")
    s_lat = mks("lat")
    s_latq = [mks(f"latq{i}") for i in range(4)]
    s_xs = [mks("xs0"), mks("xs1")]
    s_wgs = [mks("wgs0"), mks("wgs1"), mks("wgs2"), mks("wgs3")]
    s_fw = mks("fw")
    s_hw = mks("hw")
    s_qr = [mks("qr0"), mks("qr1")]
    s_krg = mks("krg")
    s_p3t = [mks("p3t0"), mks("p3t1")]
    s_p3xl = [mks("p3xl0"), mks("p3xl1")]
    s_ost = [mks("ost0"), mks("ost1")]
    s_wv = mks("wv")
    s_am = mks("am")
    s_rot = mks("rot")
    s_q2 = mks("q2")
    s_p3 = mks("p3")
    s_p3g = mks("p3g")
    s_xq = mks("xq")
    s_out = mks("out")
    s_fin = mks("fin")

    Y.dma(ident.ap(), ident_in.ap(), s_ld)
    Y.dma(identf.ap(), identf_in.ap(), s_ld)
    Y.dma(biasb.ap(), bias_in.ap(), s_ld)
    Y.dma(xtsb.ap(), xts_in.ap(), s_ld)
    for j in range(12):
        Y.dma(lnwq[:, j:j + 1], qalnw_in[j], s_ld)
        Y.dma(lnbq[:, j:j + 1], qalnb_in[j], s_ld)
    for j in range(4):
        Y.dma(lnwk[:, j:j + 1], kvalnw_in[j], s_ld)
        Y.dma(lnbk[:, j:j + 1], kvalnb_in[j], s_ld)
    for t in range(TC):
        Y.dma(kcosb[:, t * DR:(t + 1) * DR], ktcos_in[t * 128:(t + 1) * 128, :], s_ld)
        Y.dma(ksfb[:, t * DR:(t + 1) * DR], ktsf_in[t * 128:(t + 1) * 128, :], s_ld)
    Y.dma(fcosb.ap(), fcos_in.ap(), s_ld)
    Y.dma(fsfb.ap(), fsf_in.ap(), s_ld)
    ld_all = s_ld.n
    for t in range(min(2, TC)):
        Y.dma(xtile[:, (t % 2) * D:(t % 2) * D + D], x_in[t * 128:(t + 1) * 128, :],
              s_xs[t % 2])
    fw0_ld = Y.dma(fwn.ap(), ffnw_in[0], s_fw)

    V.w(s_ld, ld_all)
    Sc.w(s_ld, ld_all)
    Sc(lambda e: e.activation(warm[:, 0:1], xtsb.ap(), AF.Exp,
                              accum_out=warm[:, 1:2]))
    Sc(lambda e: e.activation(warm[:, 0:1], xtsb.ap(), AF.Square))
    Sc(lambda e: e.activation(warm[:, 0:1], warm[:, 0:1], AF.Ln))
    V(lambda e: e.memset(onesb.ap(), 1.0))
    V(lambda e: e.memset(ones128f.ap(), 1.0))
    V(lambda e: e.memset(sconst.ap(), 0.0))
    V(lambda e: e.memset(wsc.ap(), 0.0))
    V(lambda e: e.memset(stats.ap(), 0.0))

    psc_last_evac = [0, 0]
    psc_used = [0, 0]
    psc_next = [0]

    def transpose_group(srcs, dst_cb):
        hidx = psc_next[0] % 2
        psc_next[0] += 1
        if psc_used[hidx] > 0:
            T.w(s_pe, psc_last_evac[hidx])
        n = len(srcs)
        for i, ap_ in enumerate(srcs):
            wdt = ap_.shape[-1]
            dst = psC[0:wdt, hidx * 1024 + i * 128: hidx * 1024 + i * 128 + 128]
            T(lambda e, d=dst, s=ap_: e.transpose(d, s, ident.ap()),
              inc=s_tp if i == n - 1 else None)
        psc_used[hidx] += 1
        psc_last_evac[hidx] = dst_cb(hidx)
        return hidx

    negmu = sconst[:, 0:1]
    varl = sconst[:, 1:2]
    rstd1 = sconst[:, 2:3]

    for t in range(TC):
        xt = xtile[:, (t % 2) * D:(t % 2) * D + D]
        V.w(s_xs[t % 2], 16 * (t // 2 + 1))
        V.w(s_tp, s_tp.n)
        V(lambda e, xt=xt: e.reduce_sum(negmu, xt, axis=AX, negate=True))
        V(lambda e: e.tensor_scalar_mul(negmu, negmu, 1.0 / D))
        vv = V(lambda e, xt=xt: e.tensor_scalar_add(xcs.ap(), xt, negmu), inc=s_vc)
        xt_free = vv
        Sc.w(s_vc, vv)
        sv = Sc(lambda e: e.activation(sqf.ap(), xcs.ap(), AF.Square, accum_out=varl),
                inc=s_sc)
        V.w(s_sc, sv)
        vv = V(lambda e: e.tensor_scalar(varl, varl, 1.0 / D, float(EPS),
                                         OP.mult, OP.add), inc=s_vc)
        Sc.w(s_vc, vv)
        Sc(lambda e: e.activation(rstd1, varl, AF.Ln))
        sv = Sc(lambda e: e.activation(rstd1, rstd1, AF.Exp, scale=-0.5), inc=s_sc)
        V.w(s_sc, sv)
        v_xn = V(lambda e: e.tensor_scalar_mul(xnbf.ap(), xcs.ap(), rstd1), inc=s_vc)
        for g in range(2):
            T.w(s_vc, v_xn)
            if t == 0 and g == 0:
                T.w(s_ld, ld_all)
            tiles = [xnbf[:, (g * 8 + i) * 128:(g * 8 + i + 1) * 128] for i in range(8)]

            def ev(hidx, t=t, g=g):
                Sc.w(s_tp, s_tp.n)
                dst = xnT.ap().rearrange("p (k w) -> p k w", w=W1)[
                    :, g * 8:g * 8 + 8, t * 128:(t + 1) * 128]
                src = psC.ap().rearrange("p (h k w) -> p h k w", h=2, w=128)[:, hidx]
                return Sc(lambda e, d=dst, s=src: e.copy(out=d, in_=s), inc=s_pe)
            transpose_group(tiles, ev)
        if t + 2 < TC:
            Y.w(s_vc, xt_free)
            Y.dma(xtile[:, ((t + 2) % 2) * D:((t + 2) % 2) * D + D],
                  x_in[(t + 2) * 128:(t + 3) * 128, :], s_xs[t % 2])
    xnT_ready = s_pe.n

    wslot_seq = [0]
    wslot_consumed = {}

    def load_wblock(dram_ap_3d, kn, wn):
        sl = wslot_seq[0] % 4
        wslot_seq[0] += 1
        if sl in wslot_consumed:
            G.w(s_tc, wslot_consumed[sl])
        flat = wcb[:, sl * 2048: sl * 2048 + kn * wn]
        dst = flat.rearrange("p (k w) -> p k w", w=wn)
        v = G.dma(dst, dram_ap_3d, s_wgs[sl])
        return flat, v, sl

    def p1_proj(n_chunks, w_blk, out_cb):
        for ncx in range(n_chunks):
            blk, dv, sl = load_wblock(w_blk[ncx], 16, 128)
            T.w(s_wgs[sl], dv)
            mmv = 0
            for kcx in range(16):
                lhs = blk[:, kcx * 128:(kcx + 1) * 128]
                rhs = xnT.ap().rearrange("p (k w) -> p k w", w=W1)[:, kcx]
                mmv = T_mm(psA[:, 0:W1], lhs, rhs, kcx == 0, kcx == 15, inc=s_tc)
            wslot_consumed[sl] = mmv
            out_cb(ncx, mmv)

    T.w(s_pe, xnT_ready)
    if dbg:
        G.w(s_pe, xnT_ready)
        G.dma(dbg_xnT.ap(), xnT.ap(), s_out)

    def qa_out(ncx, mmv):
        Sc.w(s_tc, mmv)
        ev = Sc(lambda e, n=ncx: e.copy(out=qlat[:, n * W1:(n + 1) * W1],
                                        in_=psA[:, 0:W1]), inc=s_sc)
        T.w(s_sc, ev)

    p1_proj(12, qaT_b, qa_out)

    def ln_feature_major(nch, buf, lnw, lnb, dim):
        T.w(s_sc, s_sc.n)
        for ncx in range(nch):
            T_mm(psA[:, 1024:1024 + W1], onesb.ap(), buf[:, ncx * W1:(ncx + 1) * W1],
                 ncx == 0, ncx == nch - 1, inc=s_tc)
        V.w(s_tc, s_tc.n)
        vv = V(lambda e: e.tensor_scalar_mul(mub.ap(), psA[:, 1024:1024 + W1],
                                             1.0 / dim), inc=s_vc)
        T.w(s_vc, vv)
        sq_mm = {}
        for ncx in range(nch):
            if ncx >= 2:
                Sc.w(s_tc, sq_mm[ncx - 2])
            sv = Sc(lambda e, n=ncx: e.activation(
                sqb[:, (n % 2) * W1:(n % 2) * W1 + W1],
                buf[:, n * W1:(n + 1) * W1], AF.Square), inc=s_sc)
            T.w(s_sc, sv)
            sq_mm[ncx] = T_mm(psA[:, 1024:1024 + W1], onesb.ap(),
                              sqb[:, (ncx % 2) * W1:(ncx % 2) * W1 + W1],
                              ncx == 0, ncx == nch - 1, inc=s_tc)
        V.w(s_tc, s_tc.n)
        V(lambda e: e.tensor_scalar_mul(rstdb.ap(), psA[:, 1024:1024 + W1], 1.0 / dim))
        V(lambda e: e.tensor_tensor(out=mub.ap(), in0=mub.ap(), in1=mub.ap(),
                                    op=OP.mult) if False else
          e.tensor_tensor(out=scr1[:, 0:W1], in0=mub.ap(), in1=mub.ap(), op=OP.mult))
        V(lambda e: e.tensor_tensor(out=rstdb.ap(), in0=rstdb.ap(), in1=scr1[:, 0:W1],
                                    op=OP.subtract))
        vv = V(lambda e: e.tensor_scalar_add(rstdb.ap(), rstdb.ap(), float(EPS)),
               inc=s_vc)
        Sc.w(s_vc, vv)
        Sc(lambda e: e.activation(rstdb.ap(), rstdb.ap(), AF.Ln))
        sv = Sc(lambda e: e.activation(rstdb.ap(), rstdb.ap(), AF.Exp, scale=-0.5),
                inc=s_sc)
        V.w(s_sc, sv)
        out_vals = []
        for ncx in range(nch):
            V(lambda e, n=ncx: e.tensor_tensor(
                out=scr1[:, 0:W1], in0=buf[:, n * W1:(n + 1) * W1], in1=mub.ap(),
                op=OP.subtract))
            V(lambda e: e.tensor_tensor(out=scr1[:, 0:W1], in0=scr1[:, 0:W1],
                                        in1=rstdb.ap(), op=OP.mult))
            out_vals.append(V(lambda e, n=ncx: e.tensor_scalar(
                buf[:, n * W1:(n + 1) * W1], scr1[:, 0:W1],
                lnw[:, n:n + 1], lnb[:, n:n + 1], OP.mult, OP.add), inc=s_vc))
        return out_vals

    V.w(s_ld, ld_all)
    qa_vals = ln_feature_major(12, qlat, lnwq, lnbq, QL)
    for ncx in range(12):
        G.w(s_vc, qa_vals[ncx])
        G.dma(ag_in[ncx], qlat[:, ncx * W1:(ncx + 1) * W1], s_ag)

    G.w(s_ag, s_ag.n)
    cc_a = G(lambda e: e.collective_compute(
        "AllGather", OP.bypass, replica_groups=[[0, 1, 2, 3], [4, 5, 6, 7]],
        ins=[ag_in.ap()[0:12].opt()], outs=[ag_out_a.ap().opt()]), inc=s_cc)

    def kv_out(ncx, mmv):
        Sc.w(s_tc, mmv)
        ev = Sc(lambda e, n=ncx: e.copy(out=kvlT[:, n * W1:(n + 1) * W1],
                                        in_=psA[:, 0:W1]), inc=s_sc)
        T.w(s_sc, ev)

    p1_proj(4, kvaT_b, kv_out)
    kv_vals = ln_feature_major(4, kvlT, lnwk, lnbk, KVL)
    for ncx in range(4):
        G.w(s_vc, kv_vals[ncx])
        G.dma(ag_in[20 + ncx], kvlT[:, ncx * W1:(ncx + 1) * W1], s_ag)

    kr_blk, kr_dv, kr_sl = load_wblock(kvrT_b.ap(), 16, DR)
    T.w(s_wgs[kr_sl], kr_dv)
    kr_mm = 0
    for t in range(TC):
        for kcx in range(16):
            lhs = xnT.ap().rearrange("p (k w) -> p k w", w=W1)[
                :, kcx, t * 128:(t + 1) * 128]
            rhs = kr_blk[:, kcx * DR:(kcx + 1) * DR]
            kr_mm = T(lambda e, l=lhs, r=rhs, st_=(kcx == 0), sp=(kcx == 15):
                      e.matmul(psB[:, 0:DR], l, r, start=st_, stop=sp), inc=s_tc)
        V.w(s_tc, kr_mm)
        V.w(s_tp, s_tp.n)
        cosr = kcosb[:, t * DR:(t + 1) * DR]
        sfr = ksfb[:, t * DR:(t + 1) * DR]
        vv = V(lambda e: e.tensor_copy(out=krop.ap(), in_=psB[:, 0:DR]), inc=s_vc)
        T.w(s_vc, vv)

        def _rope_a(e, cosr=cosr, sfr=sfr):
            e.tensor_tensor(out=scr1[:, 0:DR], in0=krop.ap(), in1=cosr, op=OP.mult)
            e.tensor_tensor(out=scr1[:, DR:DR + 32], in0=krop[:, 32:64],
                            in1=sfr[:, 0:32], op=OP.mult)
            return e.tensor_tensor(out=scr1[:, DR + 32:DR + 64], in0=krop[:, 0:32],
                                   in1=sfr[:, 32:64], op=OP.mult)
        V(_rope_a)
        vv = V(lambda e: e.tensor_tensor(out=xnbf[:, 0:DR], in0=scr1[:, 0:DR],
                                         in1=scr1[:, DR:2 * DR], op=OP.add),
               inc=s_vc)
        T.w(s_vc, vv)

        def evk(hidx, t=t):
            Sc.w(s_tp, s_tp.n)
            return Sc(lambda e, t_=t, h=hidx: e.copy(
                out=krT[0:64, t_ * 128:(t_ + 1) * 128],
                in_=psC[0:64, h * 1024:h * 1024 + 128]), inc=s_pe)
        transpose_group([xnbf[:, 0:DR]], evk)
    wslot_consumed[kr_sl] = kr_mm
    G.w(s_pe, s_pe.n)
    G.dma(ag_in[24, 0:64, :], krT.ap(), s_ag)
    G.dma(ag_in[24, 64:128, :], krT.ap(), s_ag)
    G.w(s_ag, s_ag.n)
    cc_b = G(lambda e: e.collective_compute(
        "AllGather", OP.bypass, replica_groups=[[0, 1, 2, 3], [4, 5, 6, 7]],
        ins=[ag_in.ap()[20:25].opt()], outs=[ag_out_b.ap().opt()]), inc=s_cc)

    qr_mm_last = 0
    for ncx in range(8):
        blk, dv, sl = load_wblock(qrwT_b[ncx], 12, 128)
        T.w(s_wgs[sl], dv)
        mmv = 0
        for kcx in range(12):
            lhs = blk[:, kcx * 128:(kcx + 1) * 128]
            rhs = qlat.ap().rearrange("p (k w) -> p k w", w=W1)[:, kcx]
            mmv = T_mm(psA[:, 0:W1], lhs, rhs, kcx == 0, kcx == 11, inc=s_tc)
        wslot_consumed[sl] = mmv
        qr_mm_last = mmv
        Sc.w(s_tc, mmv)
        Sc.w(s_rot, s_rot.n)
        Sc.w(s_vc, s_vc.n)
        ev = Sc(lambda e: e.copy(out=qrev.ap(), in_=psA[:, 0:W1]), inc=s_sc)
        T.w(s_sc, ev)
        G.w(s_sc, ev)
        G.dma(qrrot[0:32, :], qrev[32:64, :], s_rot)
        G.dma(qrrot[32:64, :], qrev[0:32, :], s_rot)
        G.dma(qrrot[64:96, :], qrev[96:128, :], s_rot)
        rv = G.dma(qrrot[96:128, :], qrev[64:96, :], s_rot)
        V.w(s_rot, rv)
        V.w(s_ld, ld_all)
        V.w(s_ag, s_ag.n)
        V(lambda e: e.tensor_tensor(out=qrrot.ap(), in0=qrrot.ap(), in1=fsfb.ap(),
                                    op=OP.mult))
        V(lambda e: e.tensor_tensor(out=qrev.ap(), in0=qrev.ap(), in1=fcosb.ap(),
                                    op=OP.mult))
        vv = V(lambda e: e.tensor_tensor(out=agst[:, 0:W1], in0=qrev.ap(),
                                         in1=qrrot.ap(), op=OP.add), inc=s_vc)
        G.w(s_vc, vv)
        G.dma(ag_in[12 + ncx], agst[:, 0:W1], s_ag)

    V.w(s_fw, fw0_ld)
    fwA_vals = []
    for ch in range(2):
        if ch == 1:
            lv = Y.dma(fwn.ap(), ffnw_in[1], s_fw)
            V.w(s_fw, lv)
        wv_ = fwn.ap().rearrange("p (b k) -> p b k", k=16)
        wn_ = fwq.ap().bitcast(F32).rearrange("p (b k) -> p b k", k=8)
        V(lambda e, wv=wv_, wn=wn_: e.tensor_tensor(
            out=wn, in0=wv[:, :, 0:16:2], in1=wv[:, :, 1:16:2], op=OP.min))
        V(lambda e, wv=wv_: e.tensor_tensor(out=wv[:, :, 0:8], in0=wv[:, :, 0:16:2],
                                            in1=wv[:, :, 1:16:2], op=OP.max))
        V(lambda e, wv=wv_: e.tensor_tensor(out=wv[:, :, 0:4], in0=wv[:, :, 0:8:2],
                                            in1=wv[:, :, 1:8:2], op=OP.max))
        V(lambda e, wn=wn_: e.tensor_tensor(out=wn[:, :, 0:4], in0=wn[:, :, 0:8:2],
                                            in1=wn[:, :, 1:8:2], op=OP.min))
        V(lambda e, wv=wv_: e.tensor_tensor(out=wv[:, :, 0:2], in0=wv[:, :, 0:4:2],
                                            in1=wv[:, :, 1:4:2], op=OP.max))
        V(lambda e, wn=wn_: e.tensor_tensor(out=wn[:, :, 0:2], in0=wn[:, :, 0:4:2],
                                            in1=wn[:, :, 1:4:2], op=OP.min))
        V(lambda e, wv=wv_, c=ch: e.tensor_tensor(
            out=fws[:, c * 128:(c + 1) * 128], in0=wv[:, :, 0:1].squeeze(2),
            in1=wv[:, :, 1:2].squeeze(2), op=OP.max))
        V(lambda e, wn=wn_, c=ch: e.tensor_tensor(
            out=scr1[:, 0:128], in0=wn[:, :, 0:1].squeeze(2),
            in1=wn[:, :, 1:2].squeeze(2), op=OP.min))
        V(lambda e: e.tensor_scalar_mul(scr1[:, 0:128], scr1[:, 0:128], -1.0))
        fwA_vals.append(V(lambda e, c=ch: e.tensor_tensor(
            out=fws[:, c * 128:(c + 1) * 128], in0=fws[:, c * 128:(c + 1) * 128],
            in1=scr1[:, 0:128], op=OP.max), inc=s_vc))
        if ch == 0:
            Y.w(s_vc, fwA_vals[0])
    vv = V(lambda e: e.reduce_max(wsc[:, 0:1], fws.ap(), axis=AX), inc=s_vc)
    T.w(s_vc, vv)
    T.w(s_ld, ld_all)
    tv = T(lambda e: e.transpose(psB[0:1, 128:256], wsc[:, 0:1], identf.ap()), inc=s_tc)
    V.w(s_tc, tv)
    vv = V(lambda e: e.reduce_max(wsc[0:1, 4:5], psB[0:1, 128:256], axis=AX),
           inc=s_vc)
    T.w(s_vc, vv)
    G.w(s_vc, vv)
    qv = G.dma(amax_in.ap(), wsc[0:1, 0:16], s_q2)
    G.w(s_q2, qv)
    cc_amax = G(lambda e: e.collective_compute(
        "AllGather", OP.bypass, replica_groups=[list(range(NC))],
        ins=[amax_in.ap().opt()], outs=[amax_out.ap().opt()]), inc=s_cc)
    G.w(s_ag, s_ag.n)
    cc_c = G(lambda e: e.collective_compute(
        "AllGather", OP.bypass, replica_groups=[[0, 1, 2, 3], [4, 5, 6, 7]],
        ins=[ag_in.ap()[12:20].opt()], outs=[ag_out_c.ap().opt()]), inc=s_cc)

    G.w(s_cc, cc_amax)
    qv = G.dma(wsc[0:1, 8:16],
               amax_out.ap().rearrange("c a k -> a (c k)")[0:1, 4:125:16], s_q2,
               slow=True)
    V.w(s_q2, qv)
    V(lambda e: e.reduce_max(wsc[0:1, 1:2], wsc[0:1, 8:16], axis=AX))
    V(lambda e: e.tensor_scalar_mul(wsc[0:1, 2:3], wsc[0:1, 1:2], 1.0 / 2688.0))
    vv = V(lambda e: e.tensor_scalar_mul(wsc[0:1, 3:4], wsc[0:1, 2:3], 6.0), inc=s_vc)
    G.w(s_vc, vv)
    qv = G.dma(amax_in.ap(), wsc[0:1, 0:16], s_q2)
    G.w(s_q2, qv)
    qv = G.dma(wsc.ap(), amax_in.ap()[0:1, 0:16].broadcast_to([128, 16]), s_q2)
    V.w(s_q2, qv)
    V(lambda e: e.reciprocal(wsc[:, 5:6], wsc[:, 3:4]))
    V(lambda e: e.tensor_scalar_mul(wsc[:, 6:7], wsc[:, 2:3], xtsb.ap()))
    V(lambda e: e.tensor_scalar_mul(wsc[:, 7:8], xtsb.ap(), 6.0))
    rfx_done = V(lambda e: e.reciprocal(wsc[:, 7:8], wsc[:, 7:8]), inc=s_vc)

    def nvfp4(src, width, bs, sA, sB, cc, rA, rB, rC, rD, rfac, ts_lane, dst,
              post=None):
        nb = width // 16
        sv = src.rearrange("p (b k) -> p b k", k=16)
        t8 = rA[:, 0:width // 2].rearrange("p (b k) -> p b k", k=8)
        t8n = rB[:, 0:width // 2].rearrange("p (b k) -> p b k", k=8)
        V(lambda e: e.tensor_tensor(out=t8, in0=sv[:, :, 0:16:2],
                                    in1=sv[:, :, 1:16:2], op=OP.max))
        V(lambda e: e.tensor_tensor(out=t8n, in0=sv[:, :, 0:16:2],
                                    in1=sv[:, :, 1:16:2], op=OP.min))
        V(lambda e: e.tensor_tensor(out=t8[:, :, 0:4], in0=t8[:, :, 0:8:2],
                                    in1=t8[:, :, 1:8:2], op=OP.max))
        V(lambda e: e.tensor_tensor(out=t8n[:, :, 0:4], in0=t8n[:, :, 0:8:2],
                                    in1=t8n[:, :, 1:8:2], op=OP.min))
        V(lambda e: e.tensor_tensor(out=t8[:, :, 0:2], in0=t8[:, :, 0:4:2],
                                    in1=t8[:, :, 1:4:2], op=OP.max))
        V(lambda e: e.tensor_tensor(out=t8n[:, :, 0:2], in0=t8n[:, :, 0:4:2],
                                    in1=t8n[:, :, 1:4:2], op=OP.min))
        V(lambda e: e.tensor_tensor(out=bs[:, 0:nb], in0=t8[:, :, 0:1].squeeze(2),
                                    in1=t8[:, :, 1:2].squeeze(2), op=OP.max))
        V(lambda e: e.tensor_tensor(out=sA[:, 0:nb], in0=t8n[:, :, 0:1].squeeze(2),
                                    in1=t8n[:, :, 1:2].squeeze(2), op=OP.min))
        V(lambda e: e.tensor_scalar_mul(sA[:, 0:nb], sA[:, 0:nb], -1.0))
        V(lambda e: e.tensor_tensor(out=bs[:, 0:nb], in0=bs[:, 0:nb],
                                    in1=sA[:, 0:nb], op=OP.max))
        V(lambda e: e.tensor_scalar_mul(bs[:, 0:nb], bs[:, 0:nb], rfac))
        V(lambda e: e.tensor_scalar(bs[:, 0:nb], bs[:, 0:nb], 0.0, 448.0,
                                    OP.max, OP.min))
        V(lambda e: e.tensor_scalar_mul(sA[:, 0:nb], bs[:, 0:nb], VELT))
        V(lambda e: e.tensor_tensor(out=sB[:, 0:nb], in0=sA[:, 0:nb],
                                    in1=bs[:, 0:nb], op=OP.subtract))
        V(lambda e: e.tensor_tensor(out=sA[:, 0:nb], in0=sA[:, 0:nb],
                                    in1=sB[:, 0:nb], op=OP.subtract))
        V(lambda e: e.tensor_scalar(sB[:, 0:nb], bs[:, 0:nb], M_SUB, M_SUB,
                                    OP.add, OP.subtract))
        V(lambda e: e.tensor_scalar(cc[:, 0:nb], bs[:, 0:nb], E4_MIN, None, OP.is_lt))
        V(lambda e: e.tensor_tensor(out=sB[:, 0:nb], in0=sB[:, 0:nb],
                                    in1=sA[:, 0:nb], op=OP.subtract))
        V(lambda e: e.tensor_tensor(out=sB[:, 0:nb], in0=sB[:, 0:nb],
                                    in1=cc[:, 0:nb], op=OP.mult))
        V(lambda e: e.tensor_tensor(out=bs[:, 0:nb], in0=sA[:, 0:nb],
                                    in1=sB[:, 0:nb], op=OP.add))
        V(lambda e: e.tensor_scalar(sA[:, 0:nb], bs[:, 0:nb], 0.0, None, OP.is_equal))
        V(lambda e: e.tensor_tensor(out=sB[:, 0:nb], in0=bs[:, 0:nb],
                                    in1=sA[:, 0:nb], op=OP.add))
        V(lambda e: e.tensor_scalar_mul(sB[:, 0:nb], sB[:, 0:nb], ts_lane))
        V(lambda e: e.reciprocal(sB[:, 0:nb], sB[:, 0:nb]))
        V(lambda e: e.tensor_tensor(
            out=cc.rearrange("p (b k) -> p b k", k=16), in0=sv,
            in1=sB[:, 0:nb].unsqueeze(2).broadcast_to([128, nb, 16]), op=OP.mult))
        V(lambda e: e.tensor_scalar(cc, cc, -6.0, 6.0, OP.max, OP.min))
        V(lambda e: e.tensor_scalar(rB, cc, 2.0, C_RNE, OP.mult, OP.add))
        V(lambda e: e.tensor_scalar(rB, rB, C_RNE, 0.5, OP.subtract, OP.mult))
        V(lambda e: e.tensor_scalar(rC, cc, C_RNE, C_RNE, OP.add, OP.subtract))
        V(lambda e: e.tensor_scalar(rD, cc, 0.5, C_RNE, OP.mult, OP.add))
        V(lambda e: e.tensor_scalar(rD, rD, C_RNE, 2.0, OP.subtract, OP.mult))
        V(lambda e: e.tensor_tensor(out=rA, in0=cc, in1=cc, op=OP.mult))
        V(lambda e: e.tensor_scalar(cc, rA, 16.0, None, OP.is_lt))
        V(lambda e: e.tensor_scalar(rA, rA, 4.0, None, OP.is_lt))
        V(lambda e: e.tensor_tensor(out=rC, in0=rC, in1=rD, op=OP.subtract))
        V(lambda e: e.tensor_tensor(out=rC, in0=rC, in1=cc, op=OP.mult))
        V(lambda e: e.tensor_tensor(out=rD, in0=rD, in1=rC, op=OP.add))
        V(lambda e: e.tensor_tensor(out=rB, in0=rB, in1=rD, op=OP.subtract))
        V(lambda e: e.tensor_tensor(out=rB, in0=rB, in1=rA, op=OP.mult))
        V(lambda e: e.tensor_tensor(out=rD, in0=rD, in1=rB, op=OP.add))
        if post is not None:
            V(lambda e: e.tensor_scalar_mul(bs[:, 0:nb], bs[:, 0:nb], post))
        return V(lambda e: e.tensor_tensor(
            out=dst.rearrange("p (b k) -> p b k", k=16),
            in0=rD.rearrange("p (b k) -> p b k", k=16),
            in1=bs[:, 0:nb].unsqueeze(2).broadcast_to([128, nb, 16]), op=OP.mult),
            inc=s_vc)

    qv_prev = 0
    for ch in range(2):
        Y.w(s_tc, max(kr_mm, qr_mm_last))
        Y.w(s_vc, qv_prev)
        if ch == 0:
            Y.w(s_vc, fwA_vals[1])
        lv = Y.dma(q_c.ap(), ffnw_in[ch], s_fw)
        V.w(s_fw, lv)
        V.w(s_vc, rfx_done)
        V.w(s_tc, qr_mm_last)
        V.w(s_tp, s_tp.n)
        for hf in range(2):
            qv = nvfp4(q_c[:, hf * 1024:(hf + 1) * 1024], 1024,
                       q_bs.ap(), q_sA.ap(), q_sB.ap(),
                       cc=q_rB.ap(), rA=q_rA.ap(), rB=q_rC.ap(), rC=q_rD.ap(),
                       rD=fwn[:, 0:1024], rfac=wsc[:, 5:6], ts_lane=wsc[:, 2:3],
                       dst=fwq[:, hf * 1024:(hf + 1) * 1024])
        qv_prev = qv
        for g in range(2):
            T.w(s_vc, qv)
            tiles = [fwq[:, (g * 8 + i) * 128:(g * 8 + i + 1) * 128] for i in range(8)]

            def evw(hidx, ch=ch, g=g):
                Sc.w(s_tp, s_tp.n)
                Sc.w(s_q2, s_q2.n)
                return Sc(lambda e, h=hidx: e.copy(
                    out=agst[:, 0:1024], in_=psC[:, h * 1024:h * 1024 + 1024]),
                    inc=s_pe)
            hid = transpose_group(tiles, evw)
            G.w(s_pe, psc_last_evac[hid])
            for i in range(8):
                G.dma(wdq_in[g * 8 + i, :, ch * 128:(ch + 1) * 128],
                      agst[:, i * 128:(i + 1) * 128], s_q2)
    G.w(s_q2, s_q2.n)
    cc_wdq = G(lambda e: e.collective_compute(
        "AllGather", OP.bypass, replica_groups=[list(range(NC))],
        ins=[wdq_in.ap().opt()], outs=[wdq_out.ap().opt()]), inc=s_cc)

    Y.w(s_cc, cc_b)
    for src in range(4):
        Y.dma(krTg[0:64, src * W1:(src + 1) * W1], ag_out_b[src, 4, 0:64, :], s_krg)
    krg_all = s_krg.n

    lat_slot = [0]
    lat_slot_consumed = {}

    def dma_lat(chunk, tw):
        sl = lat_slot[0] % 4
        lat_slot[0] += 1
        if sl in lat_slot_consumed:
            Y.w(s_tc, lat_slot_consumed[sl])
        if chunk < 12:
            gat, gch = ag_out_a, chunk
            Y.w(s_cc, cc_a)
        else:
            gat, gch = ag_out_b, chunk - 20
            Y.w(s_cc, cc_b)
        dv = 0
        npc = max(1, TWW // W1)
        pw = min(TWW, W1)
        for p in range(npc):
            tok0 = tw * TWW + p * pw
            src = tok0 // W1
            coff = tok0 % W1
            dv = Y.dma(latq[:, sl * TWW + p * pw: sl * TWW + (p + 1) * pw],
                       gat[src, gch, :, coff:coff + pw], s_latq[sl])
        return sl, dv

    psD_last_evac = [0]
    attnT_last_store = [0]
    head_qk_done = {-2: 0, -1: 0}

    proj_last_mm = [0]

    def proj_steps(h):
        hp = h % 2
        G.w(s_tc, proj_last_mm[0])
        G.dma(qbTb.ap().rearrange("p (k w) -> p k w", w=128), qbT_b[h], s_hw)
        G.dma(kvknb.ap().rearrange("p (k w) -> p k w", w=128), kvbknT_b[h], s_hw)
        gv3 = G.dma(kvvb.ap().rearrange("p (k w) -> p k w", w=128), kvbvT_b[h], s_hw)
        chq0 = h // 2
        roff = (h % 2) * 64
        if h >= 2:
            Y.w(s_qk, head_qk_done[h - 2])
        sqr = s_qr[h % 2]
        qrv = sqr.inc(16)
        out_ap = qrTh[0:64, hp * S:(hp + 1) * S].rearrange("p (s w) -> p s w", w=W1)
        Y.w(s_cc, cc_c)

        def _fq(e, o=out_ap, c0=chq0, ro=roff, sh=sqr.h):
            src_ap = ag_out_c.ap()[:, bass.ds(_rank2c(e, c0), 1), ro:ro + 64, :]
            src_ap = src_ap.squeeze(1).transpose([1, 0, 2])
            return e.dma_start(out=o, in_=src_ap).then_inc(sh, 16)
        EM["sync"].append(_fq)
        T.w(s_hw, gv3)
        yield qrv
        for tw in range(NTW):
            mmv = 0
            for halfq in range(2):
                for kcx in range(6 * halfq, 6 * halfq + 6):
                    sl, dv = dma_lat(kcx, tw)
                    T.w(s_latq[sl], dv)
                    mmv = T(lambda e, k=kcx, sl=sl, st_=(kcx == 0), sp=(kcx == 11):
                            e.matmul(psB.ap(), qbTb[:, k * 128:(k + 1) * 128],
                                     latq[:, sl * TWW:(sl + 1) * TWW],
                                     start=st_, stop=sp), inc=s_tc)
                    lat_slot_consumed[sl] = mmv
                yield
            Sc.w(s_tc, mmv)
            ev = Sc(lambda e, hp=hp, tw=tw: e.copy(
                out=qnT[:, hp * S + tw * TWW: hp * S + (tw + 1) * TWW],
                in_=psB.ap()), inc=s_sc)
            T.w(s_sc, ev)
            kv_sl = []
            mmv = 0
            for kcx in range(4):
                sl, dv = dma_lat(20 + kcx, tw)
                kv_sl.append(sl)
                T.w(s_latq[sl], dv)
                mmv = T(lambda e, k=kcx, sl=sl, st_=(kcx == 0), sp=(kcx == 3):
                        e.matmul(psB.ap(), kvknb[:, k * 128:(k + 1) * 128],
                                 latq[:, sl * TWW:(sl + 1) * TWW],
                                 start=st_, stop=sp), inc=s_tc)
            yield
            Sc.w(s_tc, mmv)
            ev = Sc(lambda e, hp=hp, tw=tw: e.copy(
                out=knT[:, hp * S + tw * TWW: hp * S + (tw + 1) * TWW],
                in_=psB.ap()), inc=s_sc)
            T.w(s_sc, ev)
            mmv = 0
            for tsub in range(TWW // 128):
                for kcx in range(4):
                    sl = kv_sl[kcx]
                    mmv = T(lambda e, k=kcx, sl=sl, ts=tsub,
                            st_=(kcx == 0), sp=(kcx == 3): e.matmul(
                                psB[:, ts * 128:(ts + 1) * 128],
                                latq[:, sl * TWW + ts * 128: sl * TWW + (ts + 1) * 128],
                                kvvb[:, k * 128:(k + 1) * 128],
                                start=st_, stop=sp), inc=s_tc)
            for sl in kv_sl:
                lat_slot_consumed[sl] = mmv
            Sc.w(s_tc, mmv)
            ev = Sc(lambda e, hp=hp, tw=tw: e.copy(
                out=vTb[:, hp * S + tw * TWW: hp * S + (tw + 1) * TWW],
                in_=psB.ap()), inc=s_sc)
            T.w(s_sc, ev)
            proj_last_mm[0] = mmv
            yield

    psA_q_last_exp = [0, 0]
    pbuf_last_tp = [0, 0]
    ptb_last_pv = [0]
    qc_glob = [0]

    def attn_head(h, qr_ld, next_gen):
        hp = h % 2
        kn = knT[:, hp * S:(hp + 1) * S]
        vv_ = vTb[:, hp * S:(hp + 1) * S]
        qn = qnT[:, hp * S:(hp + 1) * S]
        qr = qrTh[0:64, hp * S:(hp + 1) * S]
        T.w(s_qr[h % 2], qr_ld)
        T.w(s_krg, krg_all)

        def emit_transp(info):
            g_, pb_, qoff_, pnv_ = info
            for gg in range((KC + 7) // 8):
                ntl = min(8, KC - gg * 8)
                T.w(s_pn, pnv_)
                if ptb_last_pv[0] > 0:
                    T.w(s_pv, ptb_last_pv[0])
                tiles = [pb_[:, (gg * 8 + i) * 128:(gg * 8 + i + 1) * 128]
                         for i in range(ntl)]

                def evp(hidx, gg=gg, qoff=qoff_, ntl=ntl):
                    V.w(s_tp, s_tp.n)
                    if ptb_last_pv[0] > 0:
                        V.w(s_pv, ptb_last_pv[0])
                    dst = ptb.ap().rearrange("p (k w) -> p k w", w=QB)[
                        :, gg * 8: gg * 8 + ntl, qoff:qoff + 128]
                    src = psC.ap().rearrange("p (h k w) -> p h k w", h=2, w=128)[
                        :, hidx, 0:ntl, :]
                    return V(lambda e, d=dst, s=src: e.tensor_copy(out=d, in_=s),
                             inc=s_pe)
                transpose_group(tiles, evp)
            pbuf_last_tp[g_ % 2] = s_tp.n

        def emit_pv(qb):
            T.w(s_pe, s_pe.n)
            T.w(s_pq, psD_last_evac[0])
            pvv = 0
            for kcx in range(KC):
                pvv = T(lambda e, k=kcx, v=vv_, st_=(kcx == 0), sp=(kcx == KC - 1):
                        e.matmul(psD.ap(), v[:, k * 128:(k + 1) * 128],
                                 ptb[:, k * QB:(k + 1) * QB],
                                 start=st_, stop=sp), inc=s_pv)
            ptb_last_pv[0] = pvv
            V.w(s_pv, pvv)
            if qb == 0:
                V.w(s_a2, attnT_last_store[0])
            pqv = V(lambda e, qb=qb: e.tensor_copy(
                out=attnT[:, qb * QB:(qb + 1) * QB], in_=psD.ap()), inc=s_pq)
            psD_last_evac[0] = pqv

        pending = None
        for qc in range(QC):
            g = qc_glob[0]
            qc_glob[0] += 1
            sb_ = (g % 2) * 32
            pb = pbuf[:, (g % 2) * S:(g % 2) * S + S]
            nmq = stats[:, sb_ + 0: sb_ + 1]
            nms = stats[:, sb_ + 1: sb_ + 2]
            for j in range(4):
                reg = j % 2
                T.w(s_ex, psA_q_last_exp[reg])
                T_mm(psA[:, reg * 1024: reg * 1024 + KQ],
                     qn[:, qc * 128:(qc + 1) * 128],
                     kn[:, j * KQ:(j + 1) * KQ], True, False)
                qkv = T_mm(psA[:, reg * 1024: reg * 1024 + KQ],
                           qr[:, qc * 128:(qc + 1) * 128],
                           krTg[0:64, j * KQ:(j + 1) * KQ], False, True, inc=s_qk)
                if j == 0:
                    V.w(s_qk, qkv)
                    V(lambda e, nmq=nmq: e.reduce_max(
                        nmq, psA[:, 0:KQ], axis=AX, negate=True))
                    mxv = V(lambda e, nms=nms, nmq=nmq: e.tensor_scalar_mul(
                        nms, nmq, ATT_SCALE), inc=s_mx)
                    Sc.w(s_mx, mxv)
                    if g >= 2:
                        Sc.w(s_tp, pbuf_last_tp[g % 2])
                Sc.w(s_qk, qkv)
                ssum = stats[:, sb_ + 20 + j: sb_ + 21 + j]
                exv = Sc(lambda e, j=j, reg=reg, pb=pb, nms=nms, ssum=ssum:
                         e.activation(pb[:, j * KQ:(j + 1) * KQ],
                                      psA[:, reg * 1024: reg * 1024 + KQ],
                                      AF.Exp, bias=nms, scale=ATT_SCALE,
                                      accum_out=ssum), inc=s_ex)
                psA_q_last_exp[reg] = exv
            V.w(s_ex, s_ex.n)
            tot = stats[:, sb_ + 24: sb_ + 25]
            rcp = stats[:, sb_ + 25: sb_ + 26]
            V(lambda e, tot=tot, sb_=sb_: e.reduce_sum(
                tot, stats[:, sb_ + 20: sb_ + 24], axis=AX))
            V(lambda e, rcp=rcp, tot=tot: e.reciprocal(rcp, tot))
            pnv = V(lambda e, pb=pb, rcp=rcp: e.tensor_scalar_mul(
                pb, pb, rcp), inc=s_pn)
            if pending is not None:
                emit_transp(pending)
                if qc % QCB == 0 and qc > 0:
                    emit_pv(qc // QCB - 1)
            pending = (g, pb, (qc % QCB) * 128, pnv)
            if next_gen is not None:
                next(next_gen, None)
        emit_transp(pending)
        emit_pv(QC // QCB - 1)
        G.w(s_pq, s_pq.n)
        av = 0
        for j in range(4):
            for tcl in range(TC):
                av = G.dma(a2a_in[h, j, tcl],
                           attnT[:, j * W1 + tcl * 128: j * W1 + (tcl + 1) * 128],
                           s_a2)
        attnT_last_store[0] = av
        G.w(s_a2, s_a2.n)
        cc_a2a_h[h] = G(lambda e, h=h: e.collective_compute(
            "AllGather", OP.bypass, replica_groups=[[0, 1, 2, 3], [4, 5, 6, 7]],
            ins=[a2a_in.ap()[h].opt()], outs=[a2a_out.ap()[h].opt()]), inc=s_cc)

    cc_a2a_h = {}
    cur_gen = proj_steps(0)
    cur_qr = next(cur_gen)
    for h in range(HG):
        for _ in cur_gen:
            pass
        if h + 1 < HG:
            nxt_gen = proj_steps(h + 1)
            nxt_qr = next(nxt_gen)
        else:
            nxt_gen, nxt_qr = None, 0
        attn_head(h, cur_qr, nxt_gen)
        head_qk_done[h] = s_qk.n
        cur_gen, cur_qr = nxt_gen, nxt_qr
    cc_a2a = cc_a2a_h[HG - 1]

    T.w(s_ex, s_ex.n)
    Y.w(s_qk, s_qk.n)
    Y.w(s_pv, s_pv.n)
    Y.w(s_tp, s_tp.n)
    Y.w(s_a2, s_a2.n)
    for _e in (V, Sc):
        _e.w(s_qk, s_qk.n)
        _e.w(s_pv, s_pv.n)
        _e.w(s_tp, s_tp.n)
        _e.w(s_a2, s_a2.n)
    p3x_cnt = [s_xs[0].n, s_xs[1].n]
    for t in range(min(2, TC)):
        Y.dma(p3x[:, (t % 2) * D:(t % 2) * D + D], x_in[t * 128:(t + 1) * 128, :],
              s_xs[t % 2])
    qv = 0
    for t in range(TC):
        V.w(s_xs[t % 2], p3x_cnt[t % 2] + 16 * (t // 2 + 1))
        xt = p3x[:, (t % 2) * D:(t % 2) * D + D]
        V(lambda e, xt=xt: e.reduce_sum(negmu, xt, axis=AX, negate=True))
        V(lambda e: e.tensor_scalar_mul(negmu, negmu, 1.0 / D))
        vv = V(lambda e, xt=xt: e.tensor_scalar_add(p3xn.ap(), xt, negmu), inc=s_vc)
        Sc.w(s_vc, vv)
        sv = Sc(lambda e: e.activation(p3c.ap(), p3xn.ap(), AF.Square, accum_out=varl),
                inc=s_sc)
        V.w(s_sc, sv)
        vv = V(lambda e: e.tensor_scalar(varl, varl, 1.0 / D, float(EPS),
                                         OP.mult, OP.add), inc=s_vc)
        Sc.w(s_vc, vv)
        Sc(lambda e: e.activation(rstd1, varl, AF.Ln))
        sv = Sc(lambda e: e.activation(rstd1, rstd1, AF.Exp, scale=-0.5), inc=s_sc)
        V.w(s_sc, sv)
        V(lambda e: e.tensor_scalar_mul(p3xn.ap(), p3xn.ap(), rstd1))
        V.w(s_xq, s_xq.n)
        qv = nvfp4(p3xn.ap(), D, p3bs.ap(), p3sA.ap(), p3sB.ap(),
                   cc=p3c.ap(), rA=p3rA.ap(), rB=p3rB.ap(), rC=p3rC.ap(),
                   rD=p3rD.ap(), rfac=wsc[:, 7:8], ts_lane=xtsb.ap(),
                   dst=p3xdq.ap(), post=wsc[:, 6:7])
        G.w(s_vc, qv)
        G.dma(xdq_scr[t], p3xdq.ap(), s_xq)
        if t + 2 < TC:
            Y.w(s_vc, qv)
            Y.dma(p3x[:, ((t + 2) % 2) * D:((t + 2) % 2) * D + D],
                  x_in[(t + 2) * 128:(t + 3) * 128, :], s_xs[t % 2])
    quant_done = qv

    Y.w(s_cc, cc_a2a)
    amv = 0
    for src_i in range(4):
        amv = s_am.inc(16)

        def _fa(e, si=src_i):
            sap = a2a_out.ap()[:, si][:, bass.ds(_rank_base(e), 1)].squeeze(1)
            return e.dma_start(out=a2a_mine[si * 4:(si + 1) * 4],
                               in_=sap).then_inc(s_am.h, 16)
        EM["sync"].append(_fa)
    a2a_mine_done = amv
    tile_mm = {}
    tile_cb = {}
    prev_mm = {0: None, 1: None}
    prev_cb = {0: None, 1: None}
    prev_ost = {0: 0, 1: 0}
    half_last_mm = 0
    for half in range(2):
        Y.w(s_cc, cc_wdq)
        Y.w(s_cc, cc_a2a)
        Y.w(s_vc, quant_done)
        G.w(s_cc, cc_a2a)
        if half == 1:
            Y.w(s_tc, half_last_mm)
            G.w(s_tc, half_last_mm)
        Y.w(s_am, s_am.n)
        Y.w(s_xq, s_xq.n)
        wv = 0
        for kcx in range(16):
            for sp in range(4):
                src = half * 4 + sp
                wv = Y.dma(p3wdq[:, kcx * 1024 + sp * 256: kcx * 1024 + (sp + 1) * 256],
                           wdq_out[src, kcx], s_wv)
        ov = 0
        for dvc in range(16):
            ov = G.dma(p3ow[:, dvc * 1024:(dvc + 1) * 1024], owT_b[half, dvc], s_p3g)
        for t in range(TC):
            tp = t % 2
            idx = half * TC + t
            if prev_mm[tp] is not None:
                Y.w(s_tc, prev_mm[tp])
                Y.w(s_vc, prev_cb[tp])
            av = 0
            for src in range(4):
                for hl in range(4):
                    av = Y.dma(p3at[:, tp * D + (src * 4 + hl) * 128:
                                    tp * D + (src * 4 + hl + 1) * 128],
                               a2a_mine[src * 4 + hl, t], s_p3t[tp])
            xv = Y.dma(p3xdqb[:, tp * D: tp * D + D], xdq_scr[t], s_p3t[tp])
            x3v = Y.dma(p3x[:, tp * D: tp * D + D], x_in[t * 128:(t + 1) * 128, :],
                        s_p3xl[tp])
            T.w(s_p3t[tp], xv)
            for gg in range(2):
                tiles = [p3xdqb[:, tp * D + (gg * 8 + i) * 128:
                                tp * D + (gg * 8 + i + 1) * 128] for i in range(8)]

                def evx(hidx, gg=gg, tp=tp):
                    Sc.w(s_tp, s_tp.n)
                    dst = p3xdqT[:, tp * D + gg * 1024: tp * D + (gg + 1) * 1024]
                    return Sc(lambda e, d=dst, h=hidx: e.copy(
                        out=d, in_=psC[:, h * 1024:(h + 1) * 1024]), inc=s_pe)
                transpose_group(tiles, evx)
            T.w(s_pe, s_pe.n)
            if t == 0:
                T.w(s_wv, wv)
                T.w(s_p3g, ov)
            side = t % 2
            for kcx in range(16):
                T_mm(psA[:, side * 1024:(side + 1) * 1024],
                     p3xdqT[:, tp * D + kcx * 128: tp * D + (kcx + 1) * 128],
                     p3wdq[:, kcx * 1024:(kcx + 1) * 1024], kcx == 0, False)
            mmv = 0
            for dvc in range(16):
                mmv = T_mm(psA[:, side * 1024:(side + 1) * 1024],
                           p3at[:, tp * D + dvc * 128: tp * D + (dvc + 1) * 128],
                           p3ow[:, dvc * 1024:(dvc + 1) * 1024],
                           False, dvc == 15, inc=s_tc)
            tile_mm[idx] = mmv
            if t == TC - 1:
                half_last_mm = mmv
            V.w(s_p3xl[tp], x3v)
            V.w(s_ost[tp], prev_ost[tp])
            V(lambda e, tp=tp, half=half: e.tensor_tensor(
                out=p3xb[:, tp * 1024:(tp + 1) * 1024],
                in0=p3x[:, tp * D + half * 1024: tp * D + half * 1024 + 1024],
                in1=biasb[:, half * 1024:(half + 1) * 1024], op=OP.add))
            V.w(s_tc, mmv)
            cbv = V(lambda e, tp=tp, side=side: e.tensor_tensor(
                out=p3xb[:, tp * 1024:(tp + 1) * 1024],
                in0=p3xb[:, tp * 1024:(tp + 1) * 1024],
                in1=psA[:, side * 1024:(side + 1) * 1024], op=OP.add), inc=s_vc)
            tile_cb[idx] = cbv
            T.w(s_vc, cbv)
            G.w(s_vc, cbv)
            prev_ost[tp] = G.dma(
                out_t[t * 128:(t + 1) * 128, half * 1024:half * 1024 + 1024],
                p3xb[:, tp * 1024:(tp + 1) * 1024], s_ost[tp])
            prev_mm[tp] = mmv
            prev_cb[tp] = cbv

    if dbg:
        G.dma(dbg_a2a.ap(), a2a_in.ap(), s_out)
        G.dma(dbg_agin.ap(), ag_in.ap(), s_out)
        G.dma(dbg_agout[0:4, 0:12], ag_out_a.ap(), s_out)
        G.dma(dbg_kn.ap(), knT.ap(), s_out)
        G.dma(dbg_qn.ap(), qnT.ap(), s_out)
        G.dma(dbg_wsc.ap(), wsc.ap(), s_out)
        G.dma(dbg_att.ap(), attnT.ap(), s_out)
        G.dma(dbg_p.ap(), pbuf[:, 0:S], s_out)
        G.dma(dbg_st.ap(), stats.ap(), s_out)
    G.w(s_out, s_out.n)
    G.w(s_ost[0], s_ost[0].n)
    G.w(s_ost[1], s_ost[1].n)
    G(lambda e: e.engine_nop(), inc=s_fin)
    G.w(s_fin, s_fin.n)

    with nc.Block() as block:
        @block.sync
        def _(eng):
            for f in EM["sync"]:
                f(eng)

        @block.tensor
        def _(eng):
            for f in EM["tensor"]:
                f(eng)

        @block.vector
        def _(eng):
            for f in EM["vector"]:
                f(eng)

        @block.scalar
        def _(eng):
            for f in EM["scalar"]:
                f(eng)

        @block.gpsimd
        def _(eng):
            for f in EM["gpsimd"]:
                f(eng)

    return nc


def _rope_tables(S):
    inv = 1.0 / 10000.0 ** (np.arange(0, DR, 2, dtype=np.float32) / DR)
    t = np.arange(S, dtype=np.float32)
    fr = np.outer(t, inv).astype(np.float32)
    emb = np.concatenate([fr, fr], -1)
    return np.cos(emb).astype(np.float32), np.sin(emb).astype(np.float32)


def stage_inputs(hs, q_a_w, q_a_ln_w, q_a_ln_b, q_b_w, q_rope_w, kv_a_w,
                 kv_a_ln_w, kv_a_ln_b, kv_b_w, o_w, ffn_w, ffn_b, x_ts, S):
    B = 2
    NT = B * S // NC
    xf = np.ascontiguousarray(hs.reshape(B * S, D))
    cos, sin = _rope_tables(S)
    sfold = np.concatenate([-sin[:, 0:32], sin[:, 32:64]], axis=1)

    def colblk(mat, nkc, nw=128):
        K, N = mat.shape
        return np.ascontiguousarray(
            mat.reshape(nkc, 128, N // nw, nw).transpose(2, 1, 0, 3))

    qaT = np.ascontiguousarray(q_a_w.T)
    kvaT = np.ascontiguousarray(kv_a_w.T)
    qrwT = np.ascontiguousarray(q_rope_w.T)
    owT = np.ascontiguousarray(o_w.T)

    qaT_b = colblk(qaT, 16)
    kvaT_b = colblk(kvaT[:, 0:KVL], 16)
    kvrT_b = np.ascontiguousarray(
        kvaT[:, KVL:KVL + DR].reshape(16, 128, DR).transpose(1, 0, 2))
    qrwT_b = colblk(qrwT, 12)
    owT_hb = np.ascontiguousarray(
        owT.reshape(16, 128, 2, 1024).transpose(2, 0, 1, 3))

    shared = dict(
        qaT_b=qaT_b, kvaT_b=kvaT_b, kvrT_b=kvrT_b, qrwT_b=qrwT_b, owT_b=owT_hb,
        qalnw=np.ascontiguousarray(q_a_ln_w.reshape(12, 128, 1)),
        qalnb=np.ascontiguousarray(q_a_ln_b.reshape(12, 128, 1)),
        kvalnw=np.ascontiguousarray(kv_a_ln_w.reshape(4, 128, 1)),
        kvalnb=np.ascontiguousarray(kv_a_ln_b.reshape(4, 128, 1)),
        bias_rep=np.ascontiguousarray(np.broadcast_to(ffn_b.reshape(1, D), (128, D))),
        xts_rep=np.full((128, 1), np.float32(x_ts), np.float32),
        ident=np.eye(128, dtype=ml_dtypes.bfloat16),
        identf=np.eye(128, dtype=np.float32),
    )
    ins = []
    for c in range(NC):
        hg0 = 4 * (c % 4)
        pos = (c % 4) * NT + np.arange(NT)
        qb_rows = np.concatenate(
            [q_b_w[128 * (hg0 + i):128 * (hg0 + i + 1)] for i in range(HG)], 0)
        kn_rows = np.concatenate(
            [kv_b_w[256 * (hg0 + i):256 * (hg0 + i) + 128] for i in range(HG)], 0)
        v_rows = np.concatenate(
            [kv_b_w[256 * (hg0 + i) + 128:256 * (hg0 + i + 1)] for i in range(HG)], 0)
        qbT = np.ascontiguousarray(qb_rows.T)
        knT_ = np.ascontiguousarray(kn_rows.T)
        vT_ = np.ascontiguousarray(v_rows.T)
        d = dict(shared)
        d.update(
            x=np.ascontiguousarray(xf[c * NT:(c + 1) * NT]),
            qbT_b=np.ascontiguousarray(
                qbT.reshape(12, 128, HG, 128).transpose(2, 1, 0, 3)),
            kvbknT_b=np.ascontiguousarray(
                knT_.reshape(4, 128, HG, 128).transpose(2, 1, 0, 3)),
            kvbvT_b=np.ascontiguousarray(
                vT_.reshape(4, 128, HG, 128).transpose(2, 1, 0, 3)),
            ffnw=np.ascontiguousarray(
                ffn_w[256 * c:256 * (c + 1)].reshape(2, 128, D)),
            ktcos=np.ascontiguousarray(cos[pos]),
            ktsf=np.ascontiguousarray(sfold[pos]),
            fcos2=np.ascontiguousarray(np.concatenate([cos[pos].T, cos[pos].T], 0)),
            fsf2=np.ascontiguousarray(np.concatenate([sfold[pos].T, sfold[pos].T], 0)),
        )
        ins.append(d)
    return ins


def kernel(L0_attn_hidden_states, L0_attn_q_a_w, L0_attn_q_a_ln_w,
           L0_attn_q_a_ln_b, L0_attn_q_b_w, L0_attn_q_rope_w, L0_attn_kv_a_w,
           L0_attn_kv_a_ln_w, L0_attn_kv_a_ln_b, L0_attn_kv_b_w, L0_attn_o_w,
           L0_ffn_weight, L0_ffn_bias, L0_ffn_x_tensor_scale):
    hs = np.asarray(L0_attn_hidden_states, dtype=np.float32)
    B, S, _ = hs.shape
    in_maps = stage_inputs(
        hs, np.asarray(L0_attn_q_a_w, np.float32),
        np.asarray(L0_attn_q_a_ln_w, np.float32),
        np.asarray(L0_attn_q_a_ln_b, np.float32),
        np.asarray(L0_attn_q_b_w, np.float32),
        np.asarray(L0_attn_q_rope_w, np.float32),
        np.asarray(L0_attn_kv_a_w, np.float32),
        np.asarray(L0_attn_kv_a_ln_w, np.float32),
        np.asarray(L0_attn_kv_a_ln_b, np.float32),
        np.asarray(L0_attn_kv_b_w, np.float32),
        np.asarray(L0_attn_o_w, np.float32),
        np.asarray(L0_ffn_weight, np.float32),
        np.asarray(L0_ffn_bias, np.float32),
        np.float32(L0_ffn_x_tensor_scale), S)
    nc = build_program(S)
    res = run_bass_kernel_spmd(nc, in_maps, core_ids=list(range(NC)))
    NT = B * S // NC
    out = np.concatenate([np.asarray(res.results[c]["out"]) for c in range(NC)],
                         axis=0)
    return out.reshape(B, S, D).astype(np.float32)


# version 11
# speedup vs baseline: 1.0267x; 1.0267x over previous
import os

import numpy as np
import ml_dtypes

import concourse.bass as bass
import concourse.mybir as mybir
from concourse.bass_utils import run_bass_kernel_spmd
from concourse.aot_env import allow_ici_all_to_all

F32 = mybir.dt.float32
BF16 = mybir.dt.bfloat16
OP = mybir.AluOpType
AF = mybir.ActivationFunctionType
AX = mybir.AxisListType.X

D = 2048
QL = 1536
KVL = 512
H = 16
DR = 64
DV = 128
EPS = 1e-5
ATT_SCALE = float(1.0 / np.sqrt(128 + 64))
NC = 8
HG = 4

C_RNE = float(np.float32(1.5 * 2 ** 23))
VELT = float(np.float32(2 ** 20 + 1))
M_SUB = float(np.float32(1.5 * 2 ** 14))
E4_MIN = float(np.float32(2 ** -6))


class Sem:
    __slots__ = ("h", "n", "name")

    def __init__(self, h, name):
        self.h, self.n, self.name = h, 0, name

    def inc(self, k=1):
        self.n += k
        return self.n


def build_program(S=4096, dbg=False):
    B = 2
    NT = B * S // NC
    TC = NT // 128
    W1 = NT
    KQ = S // 4
    QC = S // 128
    KC = S // 128
    QB = min(512, S)
    QCB = QB // 128
    NTW = max(1, S // 512)
    TWW = min(512, S)
    AGW = max(W1, 1024)
    assert KQ <= 1024 and NT % 128 == 0

    nc = bass.Bass("TRN2", target_bir_lowering=False, debug=False,
                   detect_race_conditions=bool(os.environ.get("KRACE")))
    st = nc._state

    def XI(name, shape, dtype=F32):
        return nc.dram_tensor(name, list(shape), dtype, kind="ExternalInput")

    x_in = XI("x", [NT, D])
    qaT_b = XI("qaT_b", [12, 128, 16, 128])
    kvaT_b = XI("kvaT_b", [4, 128, 16, 128])
    kvrT_b = XI("kvrT_b", [128, 16, DR])
    qrwT_b = XI("qrwT_b", [8, 128, 12, 128])
    qbT_b = XI("qbT_b", [HG, 128, 12, 128])
    kvbknT_b = XI("kvbknT_b", [HG, 128, 4, 128])
    kvbvT_b = XI("kvbvT_b", [HG, 128, 4, 128])
    owT_b = XI("owT_b", [2, 16, 128, 1024])
    ffnw_in = XI("ffnw", [2, 128, D])
    qalnw_in = XI("qalnw", [12, 128, 1])
    qalnb_in = XI("qalnb", [12, 128, 1])
    kvalnw_in = XI("kvalnw", [4, 128, 1])
    kvalnb_in = XI("kvalnb", [4, 128, 1])
    bias_in = XI("bias_rep", [128, D])
    xts_in = XI("xts_rep", [128, 1])
    ident_in = XI("ident", [128, 128], BF16)
    identf_in = XI("identf", [128, 128], F32)
    ktcos_in = XI("ktcos", [NT, DR])
    ktsf_in = XI("ktsf", [NT, DR])
    fcos_in = XI("fcos2", [128, W1])
    fsf_in = XI("fsf2", [128, W1])

    out_t = nc.dram_tensor("out", [NT, D], F32, kind="ExternalOutput")

    NAGC = 25
    ag_in = nc.dram_tensor("ag_in", [NAGC, 128, W1], BF16)
    ag_out_a = nc.dram_tensor("ag_out_a", [4, 12, 128, W1], BF16)
    ag_out_b = nc.dram_tensor("ag_out_b", [4, 5, 128, W1], BF16)
    ag_out_c = nc.dram_tensor("ag_out_c", [4, 8, 128, W1], BF16)
    a2a_in = nc.dram_tensor("a2a_in", [HG, 4, TC, 128, 128], BF16)
    a2a_out = nc.dram_tensor("a2a_out", [HG, 4, 4, TC, 128, 128], BF16)
    amax_in = nc.dram_tensor("amax_in", [1, 16], F32)
    amax_out = nc.dram_tensor("amax_out", [NC, 1, 16], F32,
                              addr_space="Shared")
    wdq_in = nc.dram_tensor("wdq_in", [16, 128, 256], BF16)
    a2a_mine = nc.dram_tensor("a2a_mine", [16, TC, 128, 128], BF16)
    wdq_out = nc.dram_tensor("wdq_out", [NC, 16, 128, 256], BF16,
                             addr_space="Shared")
    xdq_scr = nc.dram_tensor("xdq_scr", [TC, 128, D], BF16,
                             kind="ExternalOutput" if dbg else "Internal")
    dbg_wsc = nc.dram_tensor("dbg_wsc", [128, 16], F32, kind="ExternalOutput") \
        if dbg else None
    dbg_att = nc.dram_tensor("dbg_att", [128, S], BF16, kind="ExternalOutput") \
        if dbg else None
    dbg_p = nc.dram_tensor("dbg_p", [128, S], BF16, kind="ExternalOutput") \
        if dbg else None
    dbg_st = nc.dram_tensor("dbg_st", [128, 64], F32, kind="ExternalOutput") \
        if dbg else None
    dbg_agin = nc.dram_tensor("dbg_agin", [NAGC, 128, W1], BF16,
                              kind="ExternalOutput") if dbg else None
    dbg_a2a = nc.dram_tensor("dbg_a2a", [HG, 4, TC, 128, 128], BF16,
                             kind="ExternalOutput") if dbg else None
    dbg_ptb = nc.dram_tensor("dbg_ptb", [128, (S // 128) * QB], BF16,
                             kind="ExternalOutput") if dbg else None
    dbg_pb2 = nc.dram_tensor("dbg_pb2", [128, 2 * S], BF16,
                             kind="ExternalOutput") if dbg else None
    dbg_st2 = nc.dram_tensor("dbg_st2", [128, 64], F32,
                             kind="ExternalOutput") if dbg else None
    dbg_agout = nc.dram_tensor("dbg_agout", [4, NAGC, 128, W1], BF16,
                               kind="ExternalOutput") if dbg else None
    dbg_kn = nc.dram_tensor("dbg_kn", [128, 2 * S], BF16,
                            kind="ExternalOutput") if dbg else None
    dbg_xnT = nc.dram_tensor("dbg_xnT", [128, 16 * W1], BF16,
                             kind="ExternalOutput") if dbg else None
    dbg_qn = nc.dram_tensor("dbg_qn", [128, 2 * S], BF16,
                            kind="ExternalOutput") if dbg else None

    SB_BASE = 16512
    SB_TOP = 229344
    offs = {}

    def _alloc(name, shape, dtype, off):
        itb = 2 if dtype == BF16 else 4
        per = int(np.prod(shape[1:])) * itb
        assert off + per <= SB_TOP, (name, off, per)
        offs[name] = (off, per)
        return st.alloc_sbuf_tensor_at(name, list(shape), dtype, offset=off), off + per

    cur = SB_BASE

    def FX(name, shape, dtype):
        nonlocal cur
        off = (cur + 31) // 32 * 32
        t, cur = _alloc(name, shape, dtype, off)
        return t

    ident = FX("identW", [128, 128], BF16)
    identf = FX("identF", [128, 128], F32)
    onesb = FX("onesb", [128, 128], BF16)
    ones128f = FX("ones128f", [128, 128], F32)
    xtsb = FX("xtsb", [128, 1], F32)
    wsc = FX("wsc", [128, 16], F32)
    stats = FX("stats", [128, 64], F32)
    sconst = FX("sconst", [128, 8], F32)
    biasb = FX("biasb", [128, D], F32)
    fws = FX("fws", [128, 256], F32)
    lnwq = FX("lnwq", [128, 12], F32)
    lnbq = FX("lnbq", [128, 12], F32)
    lnwk = FX("lnwk", [128, 4], F32)
    lnbk = FX("lnbk", [128, 4], F32)
    warm = FX("warm", [128, 2], F32)
    AB = (cur + 255) // 256 * 256

    def mkarena():
        pos = [AB]

        def A(name, shape, dtype):
            off = (pos[0] + 31) // 32 * 32
            t, pos[0] = _alloc(name, shape, dtype, off)
            return t
        return A, pos

    A1, a1pos = mkarena()
    xtile = A1("xtile", [128, 2 * D], F32)
    xnbf = A1("xnbf", [128, D], BF16)
    xcs = A1("xcs", [128, D], F32)
    sqf = A1("sqf", [128, D], F32)
    xnT = A1("xnT", [128, 16 * W1], BF16)
    wcb = A1("wcb", [128, 4 * 2048], BF16)
    qlat = A1("qlat", [128, 12 * W1], BF16)
    sqb = A1("sqb", [128, 2 * W1], BF16)
    mub = A1("mub", [128, W1], F32)
    rstdb = A1("rstdb", [128, W1], F32)
    scr1 = A1("scr1", [128, max(W1, 128)], F32)
    qrev = A1("qrev", [128, W1], F32)
    qrrot = A1("qrrot", [128, W1], F32)
    kvlT = A1("kvlT", [128, 4 * W1], BF16)
    krT = A1("krT", [64, W1], BF16)
    krop = A1("krop", [128, DR], F32)
    agst = A1("agst", [128, AGW], BF16)
    kcosb = A1("kcosb", [128, TC * DR], F32)
    ksfb = A1("ksfb", [128, TC * DR], F32)
    fcosb = A1("fcosb", [128, W1], F32)
    fsfb = A1("fsfb", [128, W1], F32)
    fwn = A1("fwn", [128, D], F32)
    fwq = A1("fwq", [128, D], BF16)

    A2, a2pos = mkarena()
    knT = A2("knT", [128, 2 * S], BF16)
    vTb = A2("vTb", [128, 2 * S], BF16)
    qnT = A2("qnT", [128, 2 * S], BF16)
    qrTh = A2("qrTh", [64, 2 * S], BF16)
    krTg = A2("krTg", [64, S], BF16)
    latq = A2("latq", [128, 4 * TWW], BF16)
    pbuf = A2("pbuf", [128, 2 * S], BF16)
    ptb = A2("ptb", [128, KC * QB], BF16)
    attnT = A2("attnT", [128, S], BF16)
    qbTb = A2("qbTb", [128, 12 * 128], BF16)
    kvknb = A2("kvknb", [128, 4 * 128], BF16)
    kvvb = A2("kvvb", [128, 4 * 128], BF16)

    A3, a3pos = mkarena()
    p3x = A3("p3x", [128, 2 * D], F32)
    p3xdqb = A3("p3xdqb", [128, 2 * D], BF16)
    p3xdqT = A3("p3xdqT", [128, 2 * D], BF16)
    p3at = A3("p3at", [128, 2 * D], BF16)
    p3wdq = A3("p3wdq", [128, 16 * 1024], BF16)
    p3ow = A3("p3ow", [128, 16 * 1024], BF16)
    p3xb = A3("p3xb", [128, 2 * 1024], F32)

    a5pos = [(a2pos[0] + 255) // 256 * 256]

    def A5(name, shape, dtype):
        off = (a5pos[0] + 31) // 32 * 32
        t, a5pos[0] = _alloc(name, shape, dtype, off)
        return t

    p3xA = A5("p3xA", [128, D], F32)
    p3xdq = A5("p3xdq", [128, D], BF16)

    TAIL = (max(a1pos[0], a2pos[0], a3pos[0], a5pos[0]) + 255) // 256 * 256
    tl = [TAIL]

    def A4(name, shape, dtype):
        off = (tl[0] + 31) // 32 * 32
        t, tl[0] = _alloc(name, shape, dtype, off)
        return t

    q_c = A4("q_c", [128, D], F32)
    q_rA = A4("q_rA", [128, 1024], F32)
    q_rB = A4("q_rB", [128, 1024], F32)
    q_rC = A4("q_rC", [128, 1024], F32)
    q_rD = A4("q_rD", [128, 1024], F32)
    q_bs = A4("q_bs", [128, 64], F32)
    q_sA = A4("q_sA", [128, 64], F32)
    q_sB = A4("q_sB", [128, 64], F32)

    psA = st.alloc_psum_tensor("psA", [128, 2048], F32)
    psB = st.alloc_psum_tensor("psB", [128, 512], F32)
    psC = st.alloc_psum_tensor("psC", [128, 2048], BF16)
    psD = st.alloc_psum_tensor("psD", [128, 512], F32)

    EM = {k: [] for k in ("sync", "tensor", "vector", "scalar", "gpsimd")}

    class Q:
        def __init__(self, name):
            self.name = name

        def __call__(self, fn, inc=None, k=1):
            drain = self.name in ("vector", "scalar")
            if inc is None:
                EM[self.name].append(fn)
                if drain:
                    EM[self.name].append(lambda e: e.drain())
                return None
            v = inc.inc(k)
            if drain:
                EM[self.name].append(fn)
                EM[self.name].append(
                    lambda e, s=inc, kk=k: e.drain().then_inc(s.h, kk))
            else:
                EM[self.name].append(lambda e, f=fn, s=inc, kk=k: f(e).then_inc(s.h, kk))
            return v

        def w(self, sem, val):
            if val and val > 0:
                EM[self.name].append(lambda e, s=sem, v=val: e.wait_ge(s.h, v))

        def dma(self, out, in_, sem, slow=False, dyn=None):
            v = sem.inc(16)
            if dyn is None:
                EM[self.name].append(
                    lambda e, o=out, i=in_, s=sem, sl=slow: e.dma_start(
                        out=o, in_=i, allow_slow_non_contiguous=sl).then_inc(s.h, 16))
            else:
                ten, idx = dyn

                def _f(e, o=out, tn=ten, ix=idx, s=sem):
                    si, hl, tt = ix
                    src = tn.ap()[si][bass.ds(_rank_base(e), 1)].squeeze(0)[hl, tt]
                    return e.dma_start(out=o, in_=src).then_inc(s.h, 16)
                EM[self.name].append(_f)
            return v

    _rank_cache = {}

    def _rank_base(e):
        key = id(e)
        if key not in _rank_cache:
            _rank_cache[key] = e.compute_val(e.partition_id() % 4)
        return _rank_cache[key]

    _rank2_cache = {}

    def _rank2c(e, c0):
        key = (id(e), c0)
        if key not in _rank2_cache:
            _rank2_cache[key] = e.compute_val((e.partition_id() % 4) * 2 + c0)
        return _rank2_cache[key]

    T, V, Sc, G, Y = Q("tensor"), Q("vector"), Q("scalar"), Q("gpsimd"), Q("sync")

    def T_mm(ps_ap, lhs_ap, rhs_ap, start, stop, inc=None, nmax=512):
        n = rhs_ap.shape[-1]
        vals = None
        for o in range(0, n, nmax):
            w = min(nmax, n - o)
            last = o + w >= n
            vals = T(lambda e, p=ps_ap[:, o:o + w] if ps_ap.ndim == 2 else None,
                     l=lhs_ap, r=rhs_ap[:, o:o + w], st_=start, sp=stop:
                     e.matmul(p, l, r, start=st_, stop=sp),
                     inc=inc if last else None)
        return vals

    def mks(name):
        return Sem(st.alloc_semaphore(name), name)

    s_ld = mks("ld")
    s_x = mks("x")
    s_wg = mks("wg")
    s_tc = mks("tc")
    s_vc = mks("vc")
    s_sc = mks("sc")
    s_tp = mks("tp")
    s_pe = mks("pe")
    s_ag = mks("ag")
    s_cc = mks("cc")
    s_qk = mks("qk")
    s_mx = mks("mx")
    s_ex = mks("ex")
    s_ar = mks("ar")
    s_fe = mks("fe")
    s_pn = mks("pn")
    s_pv = mks("pv")
    s_pq = mks("pq")
    s_a2 = mks("a2")
    s_lat = mks("lat")
    s_latq = [mks(f"latq{i}") for i in range(4)]
    s_xs = [mks("xs0"), mks("xs1")]
    s_wgs = [mks("wgs0"), mks("wgs1"), mks("wgs2"), mks("wgs3")]
    s_fw = mks("fw")
    s_hw = mks("hw")
    s_qr = [mks("qr0"), mks("qr1")]
    s_krg = mks("krg")
    s_p3t = [mks("p3t0"), mks("p3t1")]
    s_p3xl = [mks("p3xl0"), mks("p3xl1")]
    s_ost = [mks("ost0"), mks("ost1")]
    s_wv = mks("wv")
    s_am = mks("am")
    s_rot = mks("rot")
    s_q2 = mks("q2")
    s_p3 = mks("p3")
    s_p3g = mks("p3g")
    s_xq = mks("xq")
    s_out = mks("out")
    s_fin = mks("fin")

    Y.dma(ident.ap(), ident_in.ap(), s_ld)
    Y.dma(identf.ap(), identf_in.ap(), s_ld)
    Y.dma(biasb.ap(), bias_in.ap(), s_ld)
    Y.dma(xtsb.ap(), xts_in.ap(), s_ld)
    for j in range(12):
        Y.dma(lnwq[:, j:j + 1], qalnw_in[j], s_ld)
        Y.dma(lnbq[:, j:j + 1], qalnb_in[j], s_ld)
    for j in range(4):
        Y.dma(lnwk[:, j:j + 1], kvalnw_in[j], s_ld)
        Y.dma(lnbk[:, j:j + 1], kvalnb_in[j], s_ld)
    for t in range(TC):
        Y.dma(kcosb[:, t * DR:(t + 1) * DR], ktcos_in[t * 128:(t + 1) * 128, :], s_ld)
        Y.dma(ksfb[:, t * DR:(t + 1) * DR], ktsf_in[t * 128:(t + 1) * 128, :], s_ld)
    Y.dma(fcosb.ap(), fcos_in.ap(), s_ld)
    Y.dma(fsfb.ap(), fsf_in.ap(), s_ld)
    ld_all = s_ld.n
    for t in range(min(2, TC)):
        Y.dma(xtile[:, (t % 2) * D:(t % 2) * D + D], x_in[t * 128:(t + 1) * 128, :],
              s_xs[t % 2])
    fw0_ld = Y.dma(fwn.ap(), ffnw_in[0], s_fw)

    V.w(s_ld, ld_all)
    Sc.w(s_ld, ld_all)
    Sc(lambda e: e.activation(warm[:, 0:1], xtsb.ap(), AF.Exp,
                              accum_out=warm[:, 1:2]))
    Sc(lambda e: e.activation(warm[:, 0:1], xtsb.ap(), AF.Square))
    Sc(lambda e: e.activation(warm[:, 0:1], warm[:, 0:1], AF.Ln))
    V(lambda e: e.memset(onesb.ap(), 1.0))
    V(lambda e: e.memset(ones128f.ap(), 1.0))
    V(lambda e: e.memset(sconst.ap(), 0.0))
    V(lambda e: e.memset(wsc.ap(), 0.0))
    V(lambda e: e.memset(stats.ap(), 0.0))

    psc_last_evac = [0, 0]
    psc_used = [0, 0]
    psc_next = [0]

    def transpose_group(srcs, dst_cb):
        hidx = psc_next[0] % 2
        psc_next[0] += 1
        if psc_used[hidx] > 0:
            T.w(s_pe, psc_last_evac[hidx])
        n = len(srcs)
        for i, ap_ in enumerate(srcs):
            wdt = ap_.shape[-1]
            dst = psC[0:wdt, hidx * 1024 + i * 128: hidx * 1024 + i * 128 + 128]
            T(lambda e, d=dst, s=ap_: e.transpose(d, s, ident.ap()),
              inc=s_tp if i == n - 1 else None)
        psc_used[hidx] += 1
        psc_last_evac[hidx] = dst_cb(hidx)
        return hidx

    negmu = sconst[:, 0:1]
    varl = sconst[:, 1:2]
    rstd1 = sconst[:, 2:3]

    for t in range(TC):
        xt = xtile[:, (t % 2) * D:(t % 2) * D + D]
        V.w(s_xs[t % 2], 16 * (t // 2 + 1))
        V.w(s_tp, s_tp.n)
        V(lambda e, xt=xt: e.reduce_sum(negmu, xt, axis=AX, negate=True))
        V(lambda e: e.tensor_scalar_mul(negmu, negmu, 1.0 / D))
        vv = V(lambda e, xt=xt: e.tensor_scalar_add(xcs.ap(), xt, negmu), inc=s_vc)
        xt_free = vv
        Sc.w(s_vc, vv)
        sv = Sc(lambda e: e.activation(sqf.ap(), xcs.ap(), AF.Square, accum_out=varl),
                inc=s_sc)
        V.w(s_sc, sv)
        vv = V(lambda e: e.tensor_scalar(varl, varl, 1.0 / D, float(EPS),
                                         OP.mult, OP.add), inc=s_vc)
        Sc.w(s_vc, vv)
        Sc(lambda e: e.activation(rstd1, varl, AF.Ln))
        sv = Sc(lambda e: e.activation(rstd1, rstd1, AF.Exp, scale=-0.5), inc=s_sc)
        V.w(s_sc, sv)
        v_xn = V(lambda e: e.tensor_scalar_mul(xnbf.ap(), xcs.ap(), rstd1), inc=s_vc)
        for g in range(2):
            T.w(s_vc, v_xn)
            if t == 0 and g == 0:
                T.w(s_ld, ld_all)
            tiles = [xnbf[:, (g * 8 + i) * 128:(g * 8 + i + 1) * 128] for i in range(8)]

            def ev(hidx, t=t, g=g):
                Sc.w(s_tp, s_tp.n)
                dst = xnT.ap().rearrange("p (k w) -> p k w", w=W1)[
                    :, g * 8:g * 8 + 8, t * 128:(t + 1) * 128]
                src = psC.ap().rearrange("p (h k w) -> p h k w", h=2, w=128)[:, hidx]
                return Sc(lambda e, d=dst, s=src: e.copy(out=d, in_=s), inc=s_pe)
            transpose_group(tiles, ev)
        if t + 2 < TC:
            Y.w(s_vc, xt_free)
            Y.dma(xtile[:, ((t + 2) % 2) * D:((t + 2) % 2) * D + D],
                  x_in[(t + 2) * 128:(t + 3) * 128, :], s_xs[t % 2])
    xnT_ready = s_pe.n

    wslot_seq = [0]
    wslot_consumed = {}

    def load_wblock(dram_ap_3d, kn, wn):
        sl = wslot_seq[0] % 4
        wslot_seq[0] += 1
        if sl in wslot_consumed:
            G.w(s_tc, wslot_consumed[sl])
        flat = wcb[:, sl * 2048: sl * 2048 + kn * wn]
        dst = flat.rearrange("p (k w) -> p k w", w=wn)
        v = G.dma(dst, dram_ap_3d, s_wgs[sl])
        return flat, v, sl

    def p1_proj(n_chunks, w_blk, out_cb):
        for ncx in range(n_chunks):
            blk, dv, sl = load_wblock(w_blk[ncx], 16, 128)
            T.w(s_wgs[sl], dv)
            mmv = 0
            for kcx in range(16):
                lhs = blk[:, kcx * 128:(kcx + 1) * 128]
                rhs = xnT.ap().rearrange("p (k w) -> p k w", w=W1)[:, kcx]
                mmv = T_mm(psA[:, 0:W1], lhs, rhs, kcx == 0, kcx == 15, inc=s_tc)
            wslot_consumed[sl] = mmv
            out_cb(ncx, mmv)

    T.w(s_pe, xnT_ready)
    if dbg:
        G.w(s_pe, xnT_ready)
        G.dma(dbg_xnT.ap(), xnT.ap(), s_out)

    def qa_out(ncx, mmv):
        Sc.w(s_tc, mmv)
        ev = Sc(lambda e, n=ncx: e.copy(out=qlat[:, n * W1:(n + 1) * W1],
                                        in_=psA[:, 0:W1]), inc=s_sc)
        T.w(s_sc, ev)

    p1_proj(12, qaT_b, qa_out)

    def ln_feature_major(nch, buf, lnw, lnb, dim):
        T.w(s_sc, s_sc.n)
        for ncx in range(nch):
            T_mm(psA[:, 1024:1024 + W1], onesb.ap(), buf[:, ncx * W1:(ncx + 1) * W1],
                 ncx == 0, ncx == nch - 1, inc=s_tc)
        V.w(s_tc, s_tc.n)
        vv = V(lambda e: e.tensor_scalar_mul(mub.ap(), psA[:, 1024:1024 + W1],
                                             1.0 / dim), inc=s_vc)
        T.w(s_vc, vv)
        sq_mm = {}
        for ncx in range(nch):
            if ncx >= 2:
                Sc.w(s_tc, sq_mm[ncx - 2])
            sv = Sc(lambda e, n=ncx: e.activation(
                sqb[:, (n % 2) * W1:(n % 2) * W1 + W1],
                buf[:, n * W1:(n + 1) * W1], AF.Square), inc=s_sc)
            T.w(s_sc, sv)
            sq_mm[ncx] = T_mm(psA[:, 1024:1024 + W1], onesb.ap(),
                              sqb[:, (ncx % 2) * W1:(ncx % 2) * W1 + W1],
                              ncx == 0, ncx == nch - 1, inc=s_tc)
        V.w(s_tc, s_tc.n)
        V(lambda e: e.tensor_scalar_mul(rstdb.ap(), psA[:, 1024:1024 + W1], 1.0 / dim))
        V(lambda e: e.tensor_tensor(out=mub.ap(), in0=mub.ap(), in1=mub.ap(),
                                    op=OP.mult) if False else
          e.tensor_tensor(out=scr1[:, 0:W1], in0=mub.ap(), in1=mub.ap(), op=OP.mult))
        V(lambda e: e.tensor_tensor(out=rstdb.ap(), in0=rstdb.ap(), in1=scr1[:, 0:W1],
                                    op=OP.subtract))
        vv = V(lambda e: e.tensor_scalar_add(rstdb.ap(), rstdb.ap(), float(EPS)),
               inc=s_vc)
        Sc.w(s_vc, vv)
        Sc(lambda e: e.activation(rstdb.ap(), rstdb.ap(), AF.Ln))
        sv = Sc(lambda e: e.activation(rstdb.ap(), rstdb.ap(), AF.Exp, scale=-0.5),
                inc=s_sc)
        V.w(s_sc, sv)
        out_vals = []
        for ncx in range(nch):
            V(lambda e, n=ncx: e.tensor_tensor(
                out=scr1[:, 0:W1], in0=buf[:, n * W1:(n + 1) * W1], in1=mub.ap(),
                op=OP.subtract))
            V(lambda e: e.tensor_tensor(out=scr1[:, 0:W1], in0=scr1[:, 0:W1],
                                        in1=rstdb.ap(), op=OP.mult))
            out_vals.append(V(lambda e, n=ncx: e.tensor_scalar(
                buf[:, n * W1:(n + 1) * W1], scr1[:, 0:W1],
                lnw[:, n:n + 1], lnb[:, n:n + 1], OP.mult, OP.add), inc=s_vc))
        return out_vals

    V.w(s_ld, ld_all)
    qa_vals = ln_feature_major(12, qlat, lnwq, lnbq, QL)
    for ncx in range(12):
        G.w(s_vc, qa_vals[ncx])
        G.dma(ag_in[ncx], qlat[:, ncx * W1:(ncx + 1) * W1], s_ag)

    G.w(s_ag, s_ag.n)
    cc_a = G(lambda e: e.collective_compute(
        "AllGather", OP.bypass, replica_groups=[[0, 1, 2, 3], [4, 5, 6, 7]],
        ins=[ag_in.ap()[0:12].opt()], outs=[ag_out_a.ap().opt()]), inc=s_cc)

    def kv_out(ncx, mmv):
        Sc.w(s_tc, mmv)
        ev = Sc(lambda e, n=ncx: e.copy(out=kvlT[:, n * W1:(n + 1) * W1],
                                        in_=psA[:, 0:W1]), inc=s_sc)
        T.w(s_sc, ev)

    p1_proj(4, kvaT_b, kv_out)
    kv_vals = ln_feature_major(4, kvlT, lnwk, lnbk, KVL)
    for ncx in range(4):
        G.w(s_vc, kv_vals[ncx])
        G.dma(ag_in[20 + ncx], kvlT[:, ncx * W1:(ncx + 1) * W1], s_ag)

    kr_blk, kr_dv, kr_sl = load_wblock(kvrT_b.ap(), 16, DR)
    T.w(s_wgs[kr_sl], kr_dv)
    kr_mm = 0
    for t in range(TC):
        for kcx in range(16):
            lhs = xnT.ap().rearrange("p (k w) -> p k w", w=W1)[
                :, kcx, t * 128:(t + 1) * 128]
            rhs = kr_blk[:, kcx * DR:(kcx + 1) * DR]
            kr_mm = T(lambda e, l=lhs, r=rhs, st_=(kcx == 0), sp=(kcx == 15):
                      e.matmul(psB[:, 0:DR], l, r, start=st_, stop=sp), inc=s_tc)
        V.w(s_tc, kr_mm)
        V.w(s_tp, s_tp.n)
        cosr = kcosb[:, t * DR:(t + 1) * DR]
        sfr = ksfb[:, t * DR:(t + 1) * DR]
        vv = V(lambda e: e.tensor_copy(out=krop.ap(), in_=psB[:, 0:DR]), inc=s_vc)
        T.w(s_vc, vv)

        def _rope_a(e, cosr=cosr, sfr=sfr):
            e.tensor_tensor(out=scr1[:, 0:DR], in0=krop.ap(), in1=cosr, op=OP.mult)
            e.tensor_tensor(out=scr1[:, DR:DR + 32], in0=krop[:, 32:64],
                            in1=sfr[:, 0:32], op=OP.mult)
            return e.tensor_tensor(out=scr1[:, DR + 32:DR + 64], in0=krop[:, 0:32],
                                   in1=sfr[:, 32:64], op=OP.mult)
        V(_rope_a)
        vv = V(lambda e: e.tensor_tensor(out=xnbf[:, 0:DR], in0=scr1[:, 0:DR],
                                         in1=scr1[:, DR:2 * DR], op=OP.add),
               inc=s_vc)
        T.w(s_vc, vv)

        def evk(hidx, t=t):
            Sc.w(s_tp, s_tp.n)
            return Sc(lambda e, t_=t, h=hidx: e.copy(
                out=krT[0:64, t_ * 128:(t_ + 1) * 128],
                in_=psC[0:64, h * 1024:h * 1024 + 128]), inc=s_pe)
        transpose_group([xnbf[:, 0:DR]], evk)
    wslot_consumed[kr_sl] = kr_mm
    G.w(s_pe, s_pe.n)
    G.dma(ag_in[24, 0:64, :], krT.ap(), s_ag)
    G.dma(ag_in[24, 64:128, :], krT.ap(), s_ag)
    G.w(s_ag, s_ag.n)
    cc_b = G(lambda e: e.collective_compute(
        "AllGather", OP.bypass, replica_groups=[[0, 1, 2, 3], [4, 5, 6, 7]],
        ins=[ag_in.ap()[20:25].opt()], outs=[ag_out_b.ap().opt()]), inc=s_cc)

    qr_mm_last = 0
    for ncx in range(8):
        blk, dv, sl = load_wblock(qrwT_b[ncx], 12, 128)
        T.w(s_wgs[sl], dv)
        mmv = 0
        for kcx in range(12):
            lhs = blk[:, kcx * 128:(kcx + 1) * 128]
            rhs = qlat.ap().rearrange("p (k w) -> p k w", w=W1)[:, kcx]
            mmv = T_mm(psA[:, 0:W1], lhs, rhs, kcx == 0, kcx == 11, inc=s_tc)
        wslot_consumed[sl] = mmv
        qr_mm_last = mmv
        Sc.w(s_tc, mmv)
        Sc.w(s_rot, s_rot.n)
        Sc.w(s_vc, s_vc.n)
        ev = Sc(lambda e: e.copy(out=qrev.ap(), in_=psA[:, 0:W1]), inc=s_sc)
        T.w(s_sc, ev)
        G.w(s_sc, ev)
        G.dma(qrrot[0:32, :], qrev[32:64, :], s_rot)
        G.dma(qrrot[32:64, :], qrev[0:32, :], s_rot)
        G.dma(qrrot[64:96, :], qrev[96:128, :], s_rot)
        rv = G.dma(qrrot[96:128, :], qrev[64:96, :], s_rot)
        V.w(s_rot, rv)
        V.w(s_ld, ld_all)
        V.w(s_ag, s_ag.n)
        V(lambda e: e.tensor_tensor(out=qrrot.ap(), in0=qrrot.ap(), in1=fsfb.ap(),
                                    op=OP.mult))
        V(lambda e: e.tensor_tensor(out=qrev.ap(), in0=qrev.ap(), in1=fcosb.ap(),
                                    op=OP.mult))
        vv = V(lambda e: e.tensor_tensor(out=agst[:, 0:W1], in0=qrev.ap(),
                                         in1=qrrot.ap(), op=OP.add), inc=s_vc)
        G.w(s_vc, vv)
        G.dma(ag_in[12 + ncx], agst[:, 0:W1], s_ag)

    V.w(s_fw, fw0_ld)
    fwA_vals = []
    for ch in range(2):
        if ch == 1:
            lv = Y.dma(fwn.ap(), ffnw_in[1], s_fw)
            V.w(s_fw, lv)
        wv_ = fwn.ap().rearrange("p (b k) -> p b k", k=16)
        wn_ = fwq.ap().bitcast(F32).rearrange("p (b k) -> p b k", k=8)
        V(lambda e, wv=wv_, wn=wn_: e.tensor_tensor(
            out=wn, in0=wv[:, :, 0:16:2], in1=wv[:, :, 1:16:2], op=OP.min))
        V(lambda e, wv=wv_: e.tensor_tensor(out=wv[:, :, 0:8], in0=wv[:, :, 0:16:2],
                                            in1=wv[:, :, 1:16:2], op=OP.max))
        V(lambda e, wv=wv_: e.tensor_tensor(out=wv[:, :, 0:4], in0=wv[:, :, 0:8:2],
                                            in1=wv[:, :, 1:8:2], op=OP.max))
        V(lambda e, wn=wn_: e.tensor_tensor(out=wn[:, :, 0:4], in0=wn[:, :, 0:8:2],
                                            in1=wn[:, :, 1:8:2], op=OP.min))
        V(lambda e, wv=wv_: e.tensor_tensor(out=wv[:, :, 0:2], in0=wv[:, :, 0:4:2],
                                            in1=wv[:, :, 1:4:2], op=OP.max))
        V(lambda e, wn=wn_: e.tensor_tensor(out=wn[:, :, 0:2], in0=wn[:, :, 0:4:2],
                                            in1=wn[:, :, 1:4:2], op=OP.min))
        V(lambda e, wv=wv_, c=ch: e.tensor_tensor(
            out=fws[:, c * 128:(c + 1) * 128], in0=wv[:, :, 0:1].squeeze(2),
            in1=wv[:, :, 1:2].squeeze(2), op=OP.max))
        V(lambda e, wn=wn_, c=ch: e.tensor_tensor(
            out=scr1[:, 0:128], in0=wn[:, :, 0:1].squeeze(2),
            in1=wn[:, :, 1:2].squeeze(2), op=OP.min))
        V(lambda e: e.tensor_scalar_mul(scr1[:, 0:128], scr1[:, 0:128], -1.0))
        fwA_vals.append(V(lambda e, c=ch: e.tensor_tensor(
            out=fws[:, c * 128:(c + 1) * 128], in0=fws[:, c * 128:(c + 1) * 128],
            in1=scr1[:, 0:128], op=OP.max), inc=s_vc))
        if ch == 0:
            Y.w(s_vc, fwA_vals[0])
    vv = V(lambda e: e.reduce_max(wsc[:, 0:1], fws.ap(), axis=AX), inc=s_vc)
    T.w(s_vc, vv)
    T.w(s_ld, ld_all)
    tv = T(lambda e: e.transpose(psB[0:1, 128:256], wsc[:, 0:1], identf.ap()), inc=s_tc)
    V.w(s_tc, tv)
    vv = V(lambda e: e.reduce_max(wsc[0:1, 4:5], psB[0:1, 128:256], axis=AX),
           inc=s_vc)
    T.w(s_vc, vv)
    G.w(s_vc, vv)
    qv = G.dma(amax_in.ap(), wsc[0:1, 0:16], s_q2)
    G.w(s_q2, qv)
    cc_amax = G(lambda e: e.collective_compute(
        "AllGather", OP.bypass, replica_groups=[list(range(NC))],
        ins=[amax_in.ap().opt()], outs=[amax_out.ap().opt()]), inc=s_cc)
    G.w(s_ag, s_ag.n)
    cc_c = G(lambda e: e.collective_compute(
        "AllGather", OP.bypass, replica_groups=[[0, 1, 2, 3], [4, 5, 6, 7]],
        ins=[ag_in.ap()[12:20].opt()], outs=[ag_out_c.ap().opt()]), inc=s_cc)

    G.w(s_cc, cc_amax)
    qv = G.dma(wsc[0:1, 8:16],
               amax_out.ap().rearrange("c a k -> a (c k)")[0:1, 4:125:16], s_q2,
               slow=True)
    V.w(s_q2, qv)
    V(lambda e: e.reduce_max(wsc[0:1, 1:2], wsc[0:1, 8:16], axis=AX))
    V(lambda e: e.tensor_scalar_mul(wsc[0:1, 2:3], wsc[0:1, 1:2], 1.0 / 2688.0))
    vv = V(lambda e: e.tensor_scalar_mul(wsc[0:1, 3:4], wsc[0:1, 2:3], 6.0), inc=s_vc)
    G.w(s_vc, vv)
    qv = G.dma(amax_in.ap(), wsc[0:1, 0:16], s_q2)
    G.w(s_q2, qv)
    qv = G.dma(wsc.ap(), amax_in.ap()[0:1, 0:16].broadcast_to([128, 16]), s_q2)
    V.w(s_q2, qv)
    V(lambda e: e.reciprocal(wsc[:, 5:6], wsc[:, 3:4]))
    V(lambda e: e.tensor_scalar_mul(wsc[:, 6:7], wsc[:, 2:3], xtsb.ap()))
    V(lambda e: e.tensor_scalar_mul(wsc[:, 7:8], xtsb.ap(), 6.0))
    rfx_done = V(lambda e: e.reciprocal(wsc[:, 7:8], wsc[:, 7:8]), inc=s_vc)

    def nvfp4(src, width, bs, sA, sB, cc, rA, rB, rC, rD, rfac, ts_lane, dst,
              post=None):
        last = [None]
        for r in nvfp4_gen(src, width, bs, sA, sB, cc, rA, rB, rC, rD, rfac,
                           ts_lane, dst, post):
            if r is not None:
                last[0] = r
        return last[0]

    def nvfp4_gen(src, width, bs, sA, sB, cc, rA, rB, rC, rD, rfac, ts_lane,
                  dst, post=None):
        nb = width // 16
        sv = src.rearrange("p (b k) -> p b k", k=16)
        t8 = rA[:, 0:width // 2].rearrange("p (b k) -> p b k", k=8)
        t8n = rB[:, 0:width // 2].rearrange("p (b k) -> p b k", k=8)
        V(lambda e: e.tensor_tensor(out=t8, in0=sv[:, :, 0:16:2],
                                    in1=sv[:, :, 1:16:2], op=OP.max))
        yield None
        V(lambda e: e.tensor_tensor(out=t8n, in0=sv[:, :, 0:16:2],
                                    in1=sv[:, :, 1:16:2], op=OP.min))
        yield None
        V(lambda e: e.tensor_tensor(out=t8[:, :, 0:4], in0=t8[:, :, 0:8:2],
                                    in1=t8[:, :, 1:8:2], op=OP.max))
        yield None
        V(lambda e: e.tensor_tensor(out=t8n[:, :, 0:4], in0=t8n[:, :, 0:8:2],
                                    in1=t8n[:, :, 1:8:2], op=OP.min))
        yield None
        V(lambda e: e.tensor_tensor(out=t8[:, :, 0:2], in0=t8[:, :, 0:4:2],
                                    in1=t8[:, :, 1:4:2], op=OP.max))
        yield None
        V(lambda e: e.tensor_tensor(out=t8n[:, :, 0:2], in0=t8n[:, :, 0:4:2],
                                    in1=t8n[:, :, 1:4:2], op=OP.min))
        yield None
        V(lambda e: e.tensor_tensor(out=bs[:, 0:nb], in0=t8[:, :, 0:1].squeeze(2),
                                    in1=t8[:, :, 1:2].squeeze(2), op=OP.max))
        yield None
        V(lambda e: e.tensor_tensor(out=sA[:, 0:nb], in0=t8n[:, :, 0:1].squeeze(2),
                                    in1=t8n[:, :, 1:2].squeeze(2), op=OP.min))
        yield None
        V(lambda e: e.tensor_scalar_mul(sA[:, 0:nb], sA[:, 0:nb], -1.0))
        yield None
        V(lambda e: e.tensor_tensor(out=bs[:, 0:nb], in0=bs[:, 0:nb],
                                    in1=sA[:, 0:nb], op=OP.max))
        yield None
        V(lambda e: e.tensor_scalar_mul(bs[:, 0:nb], bs[:, 0:nb], rfac))
        yield None
        V(lambda e: e.tensor_scalar(bs[:, 0:nb], bs[:, 0:nb], 0.0, 448.0,
                                    OP.max, OP.min))
        yield None
        V(lambda e: e.tensor_scalar_mul(sA[:, 0:nb], bs[:, 0:nb], VELT))
        yield None
        V(lambda e: e.tensor_tensor(out=sB[:, 0:nb], in0=sA[:, 0:nb],
                                    in1=bs[:, 0:nb], op=OP.subtract))
        yield None
        V(lambda e: e.tensor_tensor(out=sA[:, 0:nb], in0=sA[:, 0:nb],
                                    in1=sB[:, 0:nb], op=OP.subtract))
        yield None
        V(lambda e: e.tensor_scalar(sB[:, 0:nb], bs[:, 0:nb], M_SUB, M_SUB,
                                    OP.add, OP.subtract))
        yield None
        V(lambda e: e.tensor_scalar(cc[:, 0:nb], bs[:, 0:nb], E4_MIN, None, OP.is_lt))
        yield None
        V(lambda e: e.tensor_tensor(out=sB[:, 0:nb], in0=sB[:, 0:nb],
                                    in1=sA[:, 0:nb], op=OP.subtract))
        yield None
        V(lambda e: e.tensor_tensor(out=sB[:, 0:nb], in0=sB[:, 0:nb],
                                    in1=cc[:, 0:nb], op=OP.mult))
        yield None
        V(lambda e: e.tensor_tensor(out=bs[:, 0:nb], in0=sA[:, 0:nb],
                                    in1=sB[:, 0:nb], op=OP.add))
        yield None
        V(lambda e: e.tensor_scalar(sA[:, 0:nb], bs[:, 0:nb], 0.0, None, OP.is_equal))
        yield None
        V(lambda e: e.tensor_tensor(out=sB[:, 0:nb], in0=bs[:, 0:nb],
                                    in1=sA[:, 0:nb], op=OP.add))
        yield None
        V(lambda e: e.tensor_scalar_mul(sB[:, 0:nb], sB[:, 0:nb], ts_lane))
        yield None
        V(lambda e: e.reciprocal(sB[:, 0:nb], sB[:, 0:nb]))
        yield None
        V(lambda e: e.tensor_tensor(
            out=cc.rearrange("p (b k) -> p b k", k=16), in0=sv,
            in1=sB[:, 0:nb].unsqueeze(2).broadcast_to([128, nb, 16]), op=OP.mult))
        yield None
        V(lambda e: e.tensor_scalar(cc, cc, -6.0, 6.0, OP.max, OP.min))
        yield None
        V(lambda e: e.tensor_scalar(rB, cc, 2.0, C_RNE, OP.mult, OP.add))
        yield None
        V(lambda e: e.tensor_scalar(rB, rB, C_RNE, 0.5, OP.subtract, OP.mult))
        yield None
        V(lambda e: e.tensor_scalar(rC, cc, C_RNE, C_RNE, OP.add, OP.subtract))
        yield None
        V(lambda e: e.tensor_scalar(rD, cc, 0.5, C_RNE, OP.mult, OP.add))
        yield None
        V(lambda e: e.tensor_scalar(rD, rD, C_RNE, 2.0, OP.subtract, OP.mult))
        yield None
        V(lambda e: e.tensor_tensor(out=rA, in0=cc, in1=cc, op=OP.mult))
        yield None
        V(lambda e: e.tensor_scalar(cc, rA, 16.0, None, OP.is_lt))
        yield None
        V(lambda e: e.tensor_scalar(rA, rA, 4.0, None, OP.is_lt))
        yield None
        V(lambda e: e.tensor_tensor(out=rC, in0=rC, in1=rD, op=OP.subtract))
        yield None
        V(lambda e: e.tensor_tensor(out=rC, in0=rC, in1=cc, op=OP.mult))
        yield None
        V(lambda e: e.tensor_tensor(out=rD, in0=rD, in1=rC, op=OP.add))
        yield None
        V(lambda e: e.tensor_tensor(out=rB, in0=rB, in1=rD, op=OP.subtract))
        yield None
        V(lambda e: e.tensor_tensor(out=rB, in0=rB, in1=rA, op=OP.mult))
        yield None
        V(lambda e: e.tensor_tensor(out=rD, in0=rD, in1=rB, op=OP.add))
        yield None
        if post is not None:
            V(lambda e: e.tensor_scalar_mul(bs[:, 0:nb], bs[:, 0:nb], post))
            yield None
        yield V(lambda e: e.tensor_tensor(
            out=dst.rearrange("p (b k) -> p b k", k=16),
            in0=rD.rearrange("p (b k) -> p b k", k=16),
            in1=bs[:, 0:nb].unsqueeze(2).broadcast_to([128, nb, 16]), op=OP.mult),
            inc=s_vc)

    qv_prev = 0
    for ch in range(2):
        Y.w(s_tc, max(kr_mm, qr_mm_last))
        Y.w(s_vc, qv_prev)
        if ch == 0:
            Y.w(s_vc, fwA_vals[1])
        lv = Y.dma(q_c.ap(), ffnw_in[ch], s_fw)
        V.w(s_fw, lv)
        V.w(s_vc, rfx_done)
        V.w(s_tc, qr_mm_last)
        V.w(s_tp, s_tp.n)
        for hf in range(2):
            qv = nvfp4(q_c[:, hf * 1024:(hf + 1) * 1024], 1024,
                       q_bs.ap(), q_sA.ap(), q_sB.ap(),
                       cc=q_rB.ap(), rA=q_rA.ap(), rB=q_rC.ap(), rC=q_rD.ap(),
                       rD=fwn[:, 0:1024], rfac=wsc[:, 5:6], ts_lane=wsc[:, 2:3],
                       dst=fwq[:, hf * 1024:(hf + 1) * 1024])
        qv_prev = qv
        for g in range(2):
            T.w(s_vc, qv)
            tiles = [fwq[:, (g * 8 + i) * 128:(g * 8 + i + 1) * 128] for i in range(8)]

            def evw(hidx, ch=ch, g=g):
                Sc.w(s_tp, s_tp.n)
                Sc.w(s_q2, s_q2.n)
                return Sc(lambda e, h=hidx: e.copy(
                    out=agst[:, 0:1024], in_=psC[:, h * 1024:h * 1024 + 1024]),
                    inc=s_pe)
            hid = transpose_group(tiles, evw)
            G.w(s_pe, psc_last_evac[hid])
            for i in range(8):
                G.dma(wdq_in[g * 8 + i, :, ch * 128:(ch + 1) * 128],
                      agst[:, i * 128:(i + 1) * 128], s_q2)
    partB_done = qv_prev
    G.w(s_q2, s_q2.n)
    cc_wdq = G(lambda e: e.collective_compute(
        "AllGather", OP.bypass, replica_groups=[list(range(NC))],
        ins=[wdq_in.ap().opt()], outs=[wdq_out.ap().opt()]), inc=s_cc)

    Y.w(s_cc, cc_b)
    for src in range(4):
        Y.dma(krTg[0:64, src * W1:(src + 1) * W1], ag_out_b[src, 4, 0:64, :], s_krg)
    krg_all = s_krg.n

    lat_slot = [0]
    lat_slot_consumed = {}

    def dma_lat(chunk, tw):
        sl = lat_slot[0] % 4
        lat_slot[0] += 1
        if sl in lat_slot_consumed:
            Y.w(s_tc, lat_slot_consumed[sl])
        if chunk < 12:
            gat, gch = ag_out_a, chunk
            Y.w(s_cc, cc_a)
        else:
            gat, gch = ag_out_b, chunk - 20
            Y.w(s_cc, cc_b)
        dv = 0
        npc = max(1, TWW // W1)
        pw = min(TWW, W1)
        for p in range(npc):
            tok0 = tw * TWW + p * pw
            src = tok0 // W1
            coff = tok0 % W1
            dv = Y.dma(latq[:, sl * TWW + p * pw: sl * TWW + (p + 1) * pw],
                       gat[src, gch, :, coff:coff + pw], s_latq[sl])
        return sl, dv

    psD_last_evac = [0]
    attnT_last_store = [0]
    head_qk_done = {-2: 0, -1: 0}

    proj_last_mm = [0]

    def proj_steps(h):
        hp = h % 2
        G.w(s_tc, proj_last_mm[0])
        G.dma(qbTb.ap().rearrange("p (k w) -> p k w", w=128), qbT_b[h], s_hw)
        G.dma(kvknb.ap().rearrange("p (k w) -> p k w", w=128), kvbknT_b[h], s_hw)
        gv3 = G.dma(kvvb.ap().rearrange("p (k w) -> p k w", w=128), kvbvT_b[h], s_hw)
        chq0 = h // 2
        roff = (h % 2) * 64
        if h >= 2:
            Y.w(s_qk, head_qk_done[h - 2])
        sqr = s_qr[h % 2]
        qrv = sqr.inc(16)
        out_ap = qrTh[0:64, hp * S:(hp + 1) * S].rearrange("p (s w) -> p s w", w=W1)
        Y.w(s_cc, cc_c)

        def _fq(e, o=out_ap, c0=chq0, ro=roff, sh=sqr.h):
            src_ap = ag_out_c.ap()[:, bass.ds(_rank2c(e, c0), 1), ro:ro + 64, :]
            src_ap = src_ap.squeeze(1).transpose([1, 0, 2])
            return e.dma_start(out=o, in_=src_ap).then_inc(sh, 16)
        EM["sync"].append(_fq)
        T.w(s_hw, gv3)
        yield qrv
        for tw in range(NTW):
            mmv = 0
            for halfq in range(2):
                for kcx in range(6 * halfq, 6 * halfq + 6):
                    sl, dv = dma_lat(kcx, tw)
                    T.w(s_latq[sl], dv)
                    mmv = T(lambda e, k=kcx, sl=sl, st_=(kcx == 0), sp=(kcx == 11):
                            e.matmul(psB.ap(), qbTb[:, k * 128:(k + 1) * 128],
                                     latq[:, sl * TWW:(sl + 1) * TWW],
                                     start=st_, stop=sp), inc=s_tc)
                    lat_slot_consumed[sl] = mmv
                yield
            Sc.w(s_tc, mmv)
            ev = Sc(lambda e, hp=hp, tw=tw: e.copy(
                out=qnT[:, hp * S + tw * TWW: hp * S + (tw + 1) * TWW],
                in_=psB.ap()), inc=s_sc)
            T.w(s_sc, ev)
            kv_sl = []
            mmv = 0
            for kcx in range(4):
                sl, dv = dma_lat(20 + kcx, tw)
                kv_sl.append(sl)
                T.w(s_latq[sl], dv)
                mmv = T(lambda e, k=kcx, sl=sl, st_=(kcx == 0), sp=(kcx == 3):
                        e.matmul(psB.ap(), kvknb[:, k * 128:(k + 1) * 128],
                                 latq[:, sl * TWW:(sl + 1) * TWW],
                                 start=st_, stop=sp), inc=s_tc)
            yield
            Sc.w(s_tc, mmv)
            ev = Sc(lambda e, hp=hp, tw=tw: e.copy(
                out=knT[:, hp * S + tw * TWW: hp * S + (tw + 1) * TWW],
                in_=psB.ap()), inc=s_sc)
            T.w(s_sc, ev)
            mmv = 0
            for tsub in range(TWW // 128):
                for kcx in range(4):
                    sl = kv_sl[kcx]
                    mmv = T(lambda e, k=kcx, sl=sl, ts=tsub,
                            st_=(kcx == 0), sp=(kcx == 3): e.matmul(
                                psB[:, ts * 128:(ts + 1) * 128],
                                latq[:, sl * TWW + ts * 128: sl * TWW + (ts + 1) * 128],
                                kvvb[:, k * 128:(k + 1) * 128],
                                start=st_, stop=sp), inc=s_tc)
            for sl in kv_sl:
                lat_slot_consumed[sl] = mmv
            Sc.w(s_tc, mmv)
            ev = Sc(lambda e, hp=hp, tw=tw: e.copy(
                out=vTb[:, hp * S + tw * TWW: hp * S + (tw + 1) * TWW],
                in_=psB.ap()), inc=s_sc)
            T.w(s_sc, ev)
            proj_last_mm[0] = mmv
            yield

    psA_q_last_exp = [0, 0]
    pbuf_last_tp = [0, 0]
    ptb_last_pv = [0]
    qc_glob = [0]

    def attn_head(h, qr_ld, next_gen):
        hp = h % 2
        kn = knT[:, hp * S:(hp + 1) * S]
        vv_ = vTb[:, hp * S:(hp + 1) * S]
        qn = qnT[:, hp * S:(hp + 1) * S]
        qr = qrTh[0:64, hp * S:(hp + 1) * S]
        T.w(s_qr[h % 2], qr_ld)
        T.w(s_krg, krg_all)

        def emit_transp(info):
            g_, pb_, qoff_, pnv_ = info
            for gg in range((KC + 7) // 8):
                ntl = min(8, KC - gg * 8)
                T.w(s_pn, pnv_)
                if ptb_last_pv[0] > 0:
                    T.w(s_pv, ptb_last_pv[0])
                tiles = [pb_[:, (gg * 8 + i) * 128:(gg * 8 + i + 1) * 128]
                         for i in range(ntl)]

                def evp(hidx, gg=gg, qoff=qoff_, ntl=ntl):
                    V.w(s_tp, s_tp.n)
                    if ptb_last_pv[0] > 0:
                        V.w(s_pv, ptb_last_pv[0])
                    dst = ptb.ap().rearrange("p (k w) -> p k w", w=QB)[
                        :, gg * 8: gg * 8 + ntl, qoff:qoff + 128]
                    src = psC.ap().rearrange("p (h k w) -> p h k w", h=2, w=128)[
                        :, hidx, 0:ntl, :]
                    return V(lambda e, d=dst, s=src: e.tensor_copy(out=d, in_=s),
                             inc=s_pe)
                transpose_group(tiles, evp)
            pbuf_last_tp[g_ % 2] = s_tp.n

        def emit_pv(qb):
            T.w(s_pe, s_pe.n)
            T.w(s_pq, psD_last_evac[0])
            pvv = 0
            for kcx in range(KC):
                pvv = T(lambda e, k=kcx, v=vv_, st_=(kcx == 0), sp=(kcx == KC - 1):
                        e.matmul(psD.ap(), v[:, k * 128:(k + 1) * 128],
                                 ptb[:, k * QB:(k + 1) * QB],
                                 start=st_, stop=sp), inc=s_pv)
            ptb_last_pv[0] = pvv
            V.w(s_pv, pvv)
            if qb == 0:
                V.w(s_a2, attnT_last_store[0])
            pqv = V(lambda e, qb=qb: e.tensor_copy(
                out=attnT[:, qb * QB:(qb + 1) * QB], in_=psD.ap()), inc=s_pq)
            psD_last_evac[0] = pqv

        pending = None
        for qc in range(QC):
            g = qc_glob[0]
            qc_glob[0] += 1
            sb_ = (g % 2) * 32
            pb = pbuf[:, (g % 2) * S:(g % 2) * S + S]
            nmq = stats[:, sb_ + 0: sb_ + 1]
            nms = stats[:, sb_ + 1: sb_ + 2]
            for j in range(4):
                reg = j % 2
                T.w(s_ex, psA_q_last_exp[reg])
                T_mm(psA[:, reg * 1024: reg * 1024 + KQ],
                     qn[:, qc * 128:(qc + 1) * 128],
                     kn[:, j * KQ:(j + 1) * KQ], True, False)
                qkv = T_mm(psA[:, reg * 1024: reg * 1024 + KQ],
                           qr[:, qc * 128:(qc + 1) * 128],
                           krTg[0:64, j * KQ:(j + 1) * KQ], False, True, inc=s_qk)
                if j == 0:
                    V.w(s_qk, qkv)
                    V(lambda e, nmq=nmq: e.reduce_max(
                        nmq, psA[:, 0:KQ], axis=AX, negate=True))
                    mxv = V(lambda e, nms=nms, nmq=nmq: e.tensor_scalar_mul(
                        nms, nmq, ATT_SCALE), inc=s_mx)
                    Sc.w(s_mx, mxv)
                    if g >= 2:
                        Sc.w(s_tp, pbuf_last_tp[g % 2])
                Sc.w(s_qk, qkv)
                ssum = stats[:, sb_ + 20 + j: sb_ + 21 + j]
                exv = Sc(lambda e, j=j, reg=reg, pb=pb, nms=nms, ssum=ssum:
                         e.activation(pb[:, j * KQ:(j + 1) * KQ],
                                      psA[:, reg * 1024: reg * 1024 + KQ],
                                      AF.Exp, bias=nms, scale=ATT_SCALE,
                                      accum_out=ssum), inc=s_ex)
                psA_q_last_exp[reg] = exv
            V.w(s_ex, s_ex.n)
            tot = stats[:, sb_ + 24: sb_ + 25]
            rcp = stats[:, sb_ + 25: sb_ + 26]
            V(lambda e, tot=tot, sb_=sb_: e.reduce_sum(
                tot, stats[:, sb_ + 20: sb_ + 24], axis=AX))
            V(lambda e, rcp=rcp, tot=tot: e.reciprocal(rcp, tot))
            pnv = V(lambda e, pb=pb, rcp=rcp: e.tensor_scalar_mul(
                pb, pb, rcp), inc=s_pn)
            if pending is not None:
                emit_transp(pending)
                if qc % QCB == 0 and qc > 0:
                    emit_pv(qc // QCB - 1)
            pending = (g, pb, (qc % QCB) * 128, pnv)
            if next_gen is not None:
                next(next_gen, None)
            if q3_box[0] is not None:
                for _ in range(3):
                    next(q3_box[0], None)
        emit_transp(pending)
        emit_pv(QC // QCB - 1)
        G.w(s_pq, s_pq.n)
        av = 0
        for j in range(4):
            for tcl in range(TC):
                av = G.dma(a2a_in[h, j, tcl],
                           attnT[:, j * W1 + tcl * 128: j * W1 + (tcl + 1) * 128],
                           s_a2)
        attnT_last_store[0] = av
        G.w(s_a2, s_a2.n)
        cc_a2a_h[h] = G(lambda e, h=h: e.collective_compute(
            "AllGather", OP.bypass, replica_groups=[[0, 1, 2, 3], [4, 5, 6, 7]],
            ins=[a2a_in.ap()[h].opt()], outs=[a2a_out.ap()[h].opt()]), inc=s_cc)

    q3_done = [0]

    def quant3a_gen():
        Y.w(s_tc, max(kr_mm, qr_mm_last))
        Y.w(s_vc, partB_done)
        Y.w(s_sc, s_sc.n)
        Y.w(s_q2, s_q2.n)
        Y.w(s_ag, s_ag.n)
        x_next = Y.dma(p3xA.ap(), x_in[0:128, :], s_xs[0])
        qv = 0
        for t in range(TC):
            V.w(s_xs[0], x_next)
            V(lambda e: e.reduce_sum(negmu, p3xA.ap(), axis=AX, negate=True))
            V(lambda e: e.tensor_scalar_mul(negmu, negmu, 1.0 / D))
            vv = V(lambda e: e.tensor_scalar_add(q_c.ap(), p3xA.ap(), negmu),
                   inc=s_vc)
            yield
            Sc.w(s_vc, vv)
            Sc.w(s_q2, s_q2.n)
            sv = Sc(lambda e: e.activation(p3xA.ap(), q_c.ap(), AF.Square,
                                           accum_out=varl), inc=s_sc)
            V.w(s_sc, sv)
            vv = V(lambda e: e.tensor_scalar(varl, varl, 1.0 / D, float(EPS),
                                             OP.mult, OP.add), inc=s_vc)
            Sc.w(s_vc, vv)
            Sc(lambda e: e.activation(rstd1, varl, AF.Ln))
            sv = Sc(lambda e: e.activation(rstd1, rstd1, AF.Exp, scale=-0.5),
                    inc=s_sc)
            V.w(s_sc, sv)
            V(lambda e: e.tensor_scalar_mul(q_c.ap(), q_c.ap(), rstd1))
            V.w(s_xq, s_xq.n)
            yield
            for hf in range(2):
                for r in nvfp4_gen(q_c[:, hf * 1024:(hf + 1) * 1024], 1024,
                                   q_bs.ap(), q_sA.ap(), q_sB.ap(),
                                   cc=q_rB.ap(), rA=q_rA.ap(), rB=q_rC.ap(),
                                   rC=q_rD.ap(), rD=p3xA[:, 0:1024],
                                   rfac=wsc[:, 7:8], ts_lane=xtsb.ap(),
                                   dst=p3xdq[:, hf * 1024:(hf + 1) * 1024],
                                   post=wsc[:, 6:7]):
                    if r is not None:
                        qv = r
                    yield
            G.w(s_vc, qv)
            G.dma(xdq_scr[t], p3xdq.ap(), s_xq)
            if t + 1 < TC:
                Y.w(s_vc, qv)
                x_next = Y.dma(p3xA.ap(), x_in[(t + 1) * 128:(t + 2) * 128, :],
                               s_xs[0])
            yield
        q3_done[0] = qv

    cc_a2a_h = {}
    q3_box = [quant3a_gen()]
    cur_gen = proj_steps(0)
    cur_qr = next(cur_gen)
    for h in range(HG):
        for _ in cur_gen:
            pass
        if h + 1 < HG:
            nxt_gen = proj_steps(h + 1)
            nxt_qr = next(nxt_gen)
        else:
            nxt_gen, nxt_qr = None, 0
        attn_head(h, cur_qr, nxt_gen)
        head_qk_done[h] = s_qk.n
        cur_gen, cur_qr = nxt_gen, nxt_qr
    cc_a2a = cc_a2a_h[HG - 1]

    for _ in q3_box[0]:
        pass
    q3_box[0] = None
    quant_done = q3_done[0]
    T.w(s_ex, s_ex.n)

    Y.w(s_cc, cc_a2a)
    amv = 0
    for src_i in range(4):
        amv = s_am.inc(16)

        def _fa(e, si=src_i):
            sap = a2a_out.ap()[:, si][:, bass.ds(_rank_base(e), 1)].squeeze(1)
            return e.dma_start(out=a2a_mine[si * 4:(si + 1) * 4],
                               in_=sap).then_inc(s_am.h, 16)
        EM["sync"].append(_fa)
    a2a_mine_done = amv
    tile_mm = {}
    tile_cb = {}
    prev_mm = {0: None, 1: None}
    prev_cb = {0: None, 1: None}
    prev_ost = {0: 0, 1: 0}
    half_last_mm = 0
    for half in range(2):
        Y.w(s_cc, cc_wdq)
        Y.w(s_cc, cc_a2a)
        Y.w(s_vc, quant_done)
        G.w(s_cc, cc_a2a)
        if half == 1:
            Y.w(s_tc, half_last_mm)
            G.w(s_tc, half_last_mm)
        Y.w(s_am, s_am.n)
        Y.w(s_xq, s_xq.n)
        wv = 0
        for kcx in range(16):
            for sp in range(4):
                src = half * 4 + sp
                wv = Y.dma(p3wdq[:, kcx * 1024 + sp * 256: kcx * 1024 + (sp + 1) * 256],
                           wdq_out[src, kcx], s_wv)
        ov = 0
        for dvc in range(16):
            ov = G.dma(p3ow[:, dvc * 1024:(dvc + 1) * 1024], owT_b[half, dvc], s_p3g)
        for t in range(TC):
            tp = t % 2
            idx = half * TC + t
            if prev_mm[tp] is not None:
                Y.w(s_tc, prev_mm[tp])
                Y.w(s_vc, prev_cb[tp])
            av = 0
            for src in range(4):
                for hl in range(4):
                    av = Y.dma(p3at[:, tp * D + (src * 4 + hl) * 128:
                                    tp * D + (src * 4 + hl + 1) * 128],
                               a2a_mine[src * 4 + hl, t], s_p3t[tp])
            xv = Y.dma(p3xdqb[:, tp * D: tp * D + D], xdq_scr[t], s_p3t[tp])
            x3v = Y.dma(p3x[:, tp * D: tp * D + D], x_in[t * 128:(t + 1) * 128, :],
                        s_p3xl[tp])
            T.w(s_p3t[tp], xv)
            for gg in range(2):
                tiles = [p3xdqb[:, tp * D + (gg * 8 + i) * 128:
                                tp * D + (gg * 8 + i + 1) * 128] for i in range(8)]

                def evx(hidx, gg=gg, tp=tp):
                    Sc.w(s_tp, s_tp.n)
                    dst = p3xdqT[:, tp * D + gg * 1024: tp * D + (gg + 1) * 1024]
                    return Sc(lambda e, d=dst, h=hidx: e.copy(
                        out=d, in_=psC[:, h * 1024:(h + 1) * 1024]), inc=s_pe)
                transpose_group(tiles, evx)
            T.w(s_pe, s_pe.n)
            if t == 0:
                T.w(s_wv, wv)
                T.w(s_p3g, ov)
            side = t % 2
            for kcx in range(16):
                T_mm(psA[:, side * 1024:(side + 1) * 1024],
                     p3xdqT[:, tp * D + kcx * 128: tp * D + (kcx + 1) * 128],
                     p3wdq[:, kcx * 1024:(kcx + 1) * 1024], kcx == 0, False)
            mmv = 0
            for dvc in range(16):
                mmv = T_mm(psA[:, side * 1024:(side + 1) * 1024],
                           p3at[:, tp * D + dvc * 128: tp * D + (dvc + 1) * 128],
                           p3ow[:, dvc * 1024:(dvc + 1) * 1024],
                           False, dvc == 15, inc=s_tc)
            tile_mm[idx] = mmv
            if t == TC - 1:
                half_last_mm = mmv
            V.w(s_p3xl[tp], x3v)
            V.w(s_ost[tp], prev_ost[tp])
            V(lambda e, tp=tp, half=half: e.tensor_tensor(
                out=p3xb[:, tp * 1024:(tp + 1) * 1024],
                in0=p3x[:, tp * D + half * 1024: tp * D + half * 1024 + 1024],
                in1=biasb[:, half * 1024:(half + 1) * 1024], op=OP.add))
            V.w(s_tc, mmv)
            cbv = V(lambda e, tp=tp, side=side: e.tensor_tensor(
                out=p3xb[:, tp * 1024:(tp + 1) * 1024],
                in0=p3xb[:, tp * 1024:(tp + 1) * 1024],
                in1=psA[:, side * 1024:(side + 1) * 1024], op=OP.add), inc=s_vc)
            tile_cb[idx] = cbv
            T.w(s_vc, cbv)
            G.w(s_vc, cbv)
            prev_ost[tp] = G.dma(
                out_t[t * 128:(t + 1) * 128, half * 1024:half * 1024 + 1024],
                p3xb[:, tp * 1024:(tp + 1) * 1024], s_ost[tp])
            prev_mm[tp] = mmv
            prev_cb[tp] = cbv

    if dbg:
        G.dma(dbg_a2a.ap(), a2a_in.ap(), s_out)
        G.dma(dbg_agin.ap(), ag_in.ap(), s_out)
        G.dma(dbg_agout[0:4, 0:12], ag_out_a.ap(), s_out)
        G.dma(dbg_kn.ap(), knT.ap(), s_out)
        G.dma(dbg_qn.ap(), qnT.ap(), s_out)
        G.dma(dbg_wsc.ap(), wsc.ap(), s_out)
        G.dma(dbg_att.ap(), attnT.ap(), s_out)
        G.dma(dbg_p.ap(), pbuf[:, 0:S], s_out)
        G.dma(dbg_st.ap(), stats.ap(), s_out)
    G.w(s_out, s_out.n)
    G.w(s_ost[0], s_ost[0].n)
    G.w(s_ost[1], s_ost[1].n)
    G(lambda e: e.engine_nop(), inc=s_fin)
    G.w(s_fin, s_fin.n)

    with nc.Block() as block:
        @block.sync
        def _(eng):
            for f in EM["sync"]:
                f(eng)

        @block.tensor
        def _(eng):
            for f in EM["tensor"]:
                f(eng)

        @block.vector
        def _(eng):
            for f in EM["vector"]:
                f(eng)

        @block.scalar
        def _(eng):
            for f in EM["scalar"]:
                f(eng)

        @block.gpsimd
        def _(eng):
            for f in EM["gpsimd"]:
                f(eng)

    return nc


def _rope_tables(S):
    inv = 1.0 / 10000.0 ** (np.arange(0, DR, 2, dtype=np.float32) / DR)
    t = np.arange(S, dtype=np.float32)
    fr = np.outer(t, inv).astype(np.float32)
    emb = np.concatenate([fr, fr], -1)
    return np.cos(emb).astype(np.float32), np.sin(emb).astype(np.float32)


def stage_inputs(hs, q_a_w, q_a_ln_w, q_a_ln_b, q_b_w, q_rope_w, kv_a_w,
                 kv_a_ln_w, kv_a_ln_b, kv_b_w, o_w, ffn_w, ffn_b, x_ts, S):
    B = 2
    NT = B * S // NC
    xf = np.ascontiguousarray(hs.reshape(B * S, D))
    cos, sin = _rope_tables(S)
    sfold = np.concatenate([-sin[:, 0:32], sin[:, 32:64]], axis=1)

    def colblk(mat, nkc, nw=128):
        K, N = mat.shape
        return np.ascontiguousarray(
            mat.reshape(nkc, 128, N // nw, nw).transpose(2, 1, 0, 3))

    qaT = np.ascontiguousarray(q_a_w.T)
    kvaT = np.ascontiguousarray(kv_a_w.T)
    qrwT = np.ascontiguousarray(q_rope_w.T)
    owT = np.ascontiguousarray(o_w.T)

    qaT_b = colblk(qaT, 16)
    kvaT_b = colblk(kvaT[:, 0:KVL], 16)
    kvrT_b = np.ascontiguousarray(
        kvaT[:, KVL:KVL + DR].reshape(16, 128, DR).transpose(1, 0, 2))
    qrwT_b = colblk(qrwT, 12)
    owT_hb = np.ascontiguousarray(
        owT.reshape(16, 128, 2, 1024).transpose(2, 0, 1, 3))

    shared = dict(
        qaT_b=qaT_b, kvaT_b=kvaT_b, kvrT_b=kvrT_b, qrwT_b=qrwT_b, owT_b=owT_hb,
        qalnw=np.ascontiguousarray(q_a_ln_w.reshape(12, 128, 1)),
        qalnb=np.ascontiguousarray(q_a_ln_b.reshape(12, 128, 1)),
        kvalnw=np.ascontiguousarray(kv_a_ln_w.reshape(4, 128, 1)),
        kvalnb=np.ascontiguousarray(kv_a_ln_b.reshape(4, 128, 1)),
        bias_rep=np.ascontiguousarray(np.broadcast_to(ffn_b.reshape(1, D), (128, D))),
        xts_rep=np.full((128, 1), np.float32(x_ts), np.float32),
        ident=np.eye(128, dtype=ml_dtypes.bfloat16),
        identf=np.eye(128, dtype=np.float32),
    )
    ins = []
    for c in range(NC):
        hg0 = 4 * (c % 4)
        pos = (c % 4) * NT + np.arange(NT)
        qb_rows = np.concatenate(
            [q_b_w[128 * (hg0 + i):128 * (hg0 + i + 1)] for i in range(HG)], 0)
        kn_rows = np.concatenate(
            [kv_b_w[256 * (hg0 + i):256 * (hg0 + i) + 128] for i in range(HG)], 0)
        v_rows = np.concatenate(
            [kv_b_w[256 * (hg0 + i) + 128:256 * (hg0 + i + 1)] for i in range(HG)], 0)
        qbT = np.ascontiguousarray(qb_rows.T)
        knT_ = np.ascontiguousarray(kn_rows.T)
        vT_ = np.ascontiguousarray(v_rows.T)
        d = dict(shared)
        d.update(
            x=np.ascontiguousarray(xf[c * NT:(c + 1) * NT]),
            qbT_b=np.ascontiguousarray(
                qbT.reshape(12, 128, HG, 128).transpose(2, 1, 0, 3)),
            kvbknT_b=np.ascontiguousarray(
                knT_.reshape(4, 128, HG, 128).transpose(2, 1, 0, 3)),
            kvbvT_b=np.ascontiguousarray(
                vT_.reshape(4, 128, HG, 128).transpose(2, 1, 0, 3)),
            ffnw=np.ascontiguousarray(
                ffn_w[256 * c:256 * (c + 1)].reshape(2, 128, D)),
            ktcos=np.ascontiguousarray(cos[pos]),
            ktsf=np.ascontiguousarray(sfold[pos]),
            fcos2=np.ascontiguousarray(np.concatenate([cos[pos].T, cos[pos].T], 0)),
            fsf2=np.ascontiguousarray(np.concatenate([sfold[pos].T, sfold[pos].T], 0)),
        )
        ins.append(d)
    return ins


def kernel(L0_attn_hidden_states, L0_attn_q_a_w, L0_attn_q_a_ln_w,
           L0_attn_q_a_ln_b, L0_attn_q_b_w, L0_attn_q_rope_w, L0_attn_kv_a_w,
           L0_attn_kv_a_ln_w, L0_attn_kv_a_ln_b, L0_attn_kv_b_w, L0_attn_o_w,
           L0_ffn_weight, L0_ffn_bias, L0_ffn_x_tensor_scale):
    hs = np.asarray(L0_attn_hidden_states, dtype=np.float32)
    B, S, _ = hs.shape
    in_maps = stage_inputs(
        hs, np.asarray(L0_attn_q_a_w, np.float32),
        np.asarray(L0_attn_q_a_ln_w, np.float32),
        np.asarray(L0_attn_q_a_ln_b, np.float32),
        np.asarray(L0_attn_q_b_w, np.float32),
        np.asarray(L0_attn_q_rope_w, np.float32),
        np.asarray(L0_attn_kv_a_w, np.float32),
        np.asarray(L0_attn_kv_a_ln_w, np.float32),
        np.asarray(L0_attn_kv_a_ln_b, np.float32),
        np.asarray(L0_attn_kv_b_w, np.float32),
        np.asarray(L0_attn_o_w, np.float32),
        np.asarray(L0_ffn_weight, np.float32),
        np.asarray(L0_ffn_bias, np.float32),
        np.float32(L0_ffn_x_tensor_scale), S)
    nc = build_program(S)
    res = run_bass_kernel_spmd(nc, in_maps, core_ids=list(range(NC)))
    NT = B * S // NC
    out = np.concatenate([np.asarray(res.results[c]["out"]) for c in range(NC)],
                         axis=0)
    return out.reshape(B, S, D).astype(np.float32)


# version 12
# speedup vs baseline: 1.0537x; 1.0263x over previous
import os

import numpy as np
import ml_dtypes

import concourse.bass as bass
import concourse.mybir as mybir
from concourse.bass_utils import run_bass_kernel_spmd
from concourse.aot_env import allow_ici_all_to_all

F32 = mybir.dt.float32
BF16 = mybir.dt.bfloat16
OP = mybir.AluOpType
AF = mybir.ActivationFunctionType
AX = mybir.AxisListType.X

D = 2048
QL = 1536
KVL = 512
H = 16
DR = 64
DV = 128
EPS = 1e-5
ATT_SCALE = float(1.0 / np.sqrt(128 + 64))
NC = 8
HG = 4

C_RNE = float(np.float32(1.5 * 2 ** 23))
VELT = float(np.float32(2 ** 20 + 1))
M_SUB = float(np.float32(1.5 * 2 ** 14))
E4_MIN = float(np.float32(2 ** -6))


class Sem:
    __slots__ = ("h", "n", "name")

    def __init__(self, h, name):
        self.h, self.n, self.name = h, 0, name

    def inc(self, k=1):
        self.n += k
        return self.n


def build_program(S=4096, dbg=False):
    B = 2
    NT = B * S // NC
    TC = NT // 128
    W1 = NT
    KQ = S // 4
    QC = S // 128
    KC = S // 128
    QB = min(512, S)
    QCB = QB // 128
    NTW = max(1, S // 512)
    TWW = min(512, S)
    AGW = max(W1, 1024)
    assert KQ <= 1024 and NT % 128 == 0

    nc = bass.Bass("TRN2", target_bir_lowering=False, debug=False,
                   detect_race_conditions=bool(os.environ.get("KRACE")))
    st = nc._state

    def XI(name, shape, dtype=F32):
        return nc.dram_tensor(name, list(shape), dtype, kind="ExternalInput")

    x_in = XI("x", [NT, D])
    qaT_b = XI("qaT_b", [12, 128, 16, 128])
    kvaT_b = XI("kvaT_b", [4, 128, 16, 128])
    kvrT_b = XI("kvrT_b", [128, 16, DR])
    qrwT_b = XI("qrwT_b", [8, 128, 12, 128])
    qbT_b = XI("qbT_b", [HG, 128, 12, 128])
    kvbknT_b = XI("kvbknT_b", [HG, 128, 4, 128])
    kvbvT_b = XI("kvbvT_b", [HG, 128, 4, 128])
    owT_b = XI("owT_b", [2, 16, 128, 1024])
    ffnw_in = XI("ffnw", [2, 128, D])
    qalnw_in = XI("qalnw", [12, 128, 1])
    qalnb_in = XI("qalnb", [12, 128, 1])
    kvalnw_in = XI("kvalnw", [4, 128, 1])
    kvalnb_in = XI("kvalnb", [4, 128, 1])
    bias_in = XI("bias_rep", [128, D])
    xts_in = XI("xts_rep", [128, 1])
    ident_in = XI("ident", [128, 128], BF16)
    identf_in = XI("identf", [128, 128], F32)
    ktcos_in = XI("ktcos", [NT, DR])
    ktsf_in = XI("ktsf", [NT, DR])
    fcos_in = XI("fcos2", [128, W1])
    fsf_in = XI("fsf2", [128, W1])

    out_t = nc.dram_tensor("out", [NT, D], F32, kind="ExternalOutput")

    NAGC = 25
    ag_in = nc.dram_tensor("ag_in", [NAGC, 128, W1], BF16)
    ag_out_a = nc.dram_tensor("ag_out_a", [4, 12, 128, W1], BF16)
    ag_out_b = nc.dram_tensor("ag_out_b", [4, 5, 128, W1], BF16)
    ag_out_c = nc.dram_tensor("ag_out_c", [4, 8, 128, W1], BF16)
    a2a_in = nc.dram_tensor("a2a_in", [HG, 4, TC, 128, 128], BF16)
    a2a_out = nc.dram_tensor("a2a_out", [HG, 4, 4, TC, 128, 128], BF16)
    amax_in = nc.dram_tensor("amax_in", [1, 16], F32)
    amax_out = nc.dram_tensor("amax_out", [NC, 1, 16], F32,
                              addr_space="Shared")
    wdq_in = nc.dram_tensor("wdq_in", [16, 128, 256], BF16)
    a2a_mine = nc.dram_tensor("a2a_mine", [16, TC, 128, 128], BF16)
    wdq_out = nc.dram_tensor("wdq_out", [NC, 16, 128, 256], BF16,
                             addr_space="Shared")
    xdq_scr = nc.dram_tensor("xdq_scr", [TC, 128, D], BF16,
                             kind="ExternalOutput" if dbg else "Internal")
    dbg_wsc = nc.dram_tensor("dbg_wsc", [128, 16], F32, kind="ExternalOutput") \
        if dbg else None
    dbg_att = nc.dram_tensor("dbg_att", [128, S], BF16, kind="ExternalOutput") \
        if dbg else None
    dbg_p = nc.dram_tensor("dbg_p", [128, S], BF16, kind="ExternalOutput") \
        if dbg else None
    dbg_st = nc.dram_tensor("dbg_st", [128, 64], F32, kind="ExternalOutput") \
        if dbg else None
    dbg_agin = nc.dram_tensor("dbg_agin", [NAGC, 128, W1], BF16,
                              kind="ExternalOutput") if dbg else None
    dbg_a2a = nc.dram_tensor("dbg_a2a", [HG, 4, TC, 128, 128], BF16,
                             kind="ExternalOutput") if dbg else None
    dbg_ptb = nc.dram_tensor("dbg_ptb", [128, (S // 128) * QB], BF16,
                             kind="ExternalOutput") if dbg else None
    dbg_pb2 = nc.dram_tensor("dbg_pb2", [128, 2 * S], BF16,
                             kind="ExternalOutput") if dbg else None
    dbg_st2 = nc.dram_tensor("dbg_st2", [128, 64], F32,
                             kind="ExternalOutput") if dbg else None
    dbg_agout = nc.dram_tensor("dbg_agout", [4, NAGC, 128, W1], BF16,
                               kind="ExternalOutput") if dbg else None
    dbg_kn = nc.dram_tensor("dbg_kn", [128, 2 * S], BF16,
                            kind="ExternalOutput") if dbg else None
    dbg_xnT = nc.dram_tensor("dbg_xnT", [128, 16 * W1], BF16,
                             kind="ExternalOutput") if dbg else None
    dbg_qn = nc.dram_tensor("dbg_qn", [128, 2 * S], BF16,
                            kind="ExternalOutput") if dbg else None

    SB_BASE = 16512
    SB_TOP = 229344
    offs = {}

    def _alloc(name, shape, dtype, off):
        itb = 2 if dtype == BF16 else 4
        per = int(np.prod(shape[1:])) * itb
        assert off + per <= SB_TOP, (name, off, per)
        offs[name] = (off, per)
        return st.alloc_sbuf_tensor_at(name, list(shape), dtype, offset=off), off + per

    cur = SB_BASE

    def FX(name, shape, dtype):
        nonlocal cur
        off = (cur + 31) // 32 * 32
        t, cur = _alloc(name, shape, dtype, off)
        return t

    ident = FX("identW", [128, 128], BF16)
    identf = FX("identF", [128, 128], F32)
    onesb = FX("onesb", [128, 128], BF16)
    ones128f = FX("ones128f", [128, 128], F32)
    xtsb = FX("xtsb", [128, 1], F32)
    wsc = FX("wsc", [128, 16], F32)
    stats = FX("stats", [128, 64], F32)
    sconst = FX("sconst", [128, 8], F32)
    biasb = FX("biasb", [128, D], F32)
    fws = FX("fws", [128, 256], F32)
    lnwq = FX("lnwq", [128, 12], F32)
    lnbq = FX("lnbq", [128, 12], F32)
    lnwk = FX("lnwk", [128, 4], F32)
    lnbk = FX("lnbk", [128, 4], F32)
    warm = FX("warm", [128, 2], F32)
    AB = (cur + 255) // 256 * 256

    def mkarena():
        pos = [AB]

        def A(name, shape, dtype):
            off = (pos[0] + 31) // 32 * 32
            t, pos[0] = _alloc(name, shape, dtype, off)
            return t
        return A, pos

    A1, a1pos = mkarena()
    xtile = A1("xtile", [128, 2 * D], F32)
    xnbf = A1("xnbf", [128, D], BF16)
    xcs = A1("xcs", [128, D], F32)
    sqf = A1("sqf", [128, D], F32)
    xnT = A1("xnT", [128, 16 * W1], BF16)
    wcb = A1("wcb", [128, 4 * 2048], BF16)
    qlat = A1("qlat", [128, 12 * W1], BF16)
    sqb = A1("sqb", [128, 2 * W1], BF16)
    mub = A1("mub", [128, W1], F32)
    rstdb = A1("rstdb", [128, W1], F32)
    scr1 = A1("scr1", [128, max(W1, 128)], F32)
    qrev = A1("qrev", [128, W1], F32)
    qrrot = A1("qrrot", [128, W1], F32)
    kvlT = A1("kvlT", [128, 4 * W1], BF16)
    krT = A1("krT", [64, W1], BF16)
    krop = A1("krop", [128, DR], F32)
    agst = A1("agst", [128, AGW], BF16)
    kcosb = A1("kcosb", [128, TC * DR], F32)
    ksfb = A1("ksfb", [128, TC * DR], F32)
    fcosb = A1("fcosb", [128, W1], F32)
    fsfb = A1("fsfb", [128, W1], F32)
    fwn = A1("fwn", [128, D], F32)
    fwq = A1("fwq", [128, D], BF16)

    A2, a2pos = mkarena()
    knT = A2("knT", [128, 2 * S], BF16)
    vTb = A2("vTb", [128, 2 * S], BF16)
    qnT = A2("qnT", [128, 2 * S], BF16)
    qrTh = A2("qrTh", [64, 2 * S], BF16)
    krTg = A2("krTg", [64, S], BF16)
    latq = A2("latq", [128, 4 * TWW], BF16)
    pbuf = A2("pbuf", [128, 2 * S], BF16)
    ptb = A2("ptb", [128, KC * QB], BF16)
    attnT = A2("attnT", [128, S], BF16)
    qbTb = A2("qbTb", [128, 12 * 128], BF16)
    kvknb = A2("kvknb", [128, 4 * 128], BF16)
    kvvb = A2("kvvb", [128, 4 * 128], BF16)

    A3, a3pos = mkarena()
    p3x = A3("p3x", [128, 2 * D], F32)
    p3xdqb = A3("p3xdqb", [128, 2 * D], BF16)
    p3xdqT = A3("p3xdqT", [128, 2 * D], BF16)
    p3at = A3("p3at", [128, 2 * D], BF16)
    p3wdq = A3("p3wdq", [128, 16 * 1024], BF16)
    p3ow = A3("p3ow", [128, 16 * 1024], BF16)
    p3xb = A3("p3xb", [128, 2 * 1024], F32)

    a5pos = [(a2pos[0] + 255) // 256 * 256]

    def A5(name, shape, dtype):
        off = (a5pos[0] + 31) // 32 * 32
        t, a5pos[0] = _alloc(name, shape, dtype, off)
        return t

    p3xA = A5("p3xA", [128, D], F32)
    p3xdq = A5("p3xdq", [128, D], BF16)

    TAIL = (max(a1pos[0], a2pos[0], a3pos[0], a5pos[0]) + 255) // 256 * 256
    tl = [TAIL]

    def A4(name, shape, dtype):
        off = (tl[0] + 31) // 32 * 32
        t, tl[0] = _alloc(name, shape, dtype, off)
        return t

    q_c = A4("q_c", [128, D], F32)
    q_rA = A4("q_rA", [128, 1024], F32)
    q_rB = A4("q_rB", [128, 1024], F32)
    q_rC = A4("q_rC", [128, 1024], F32)
    q_rD = A4("q_rD", [128, 1024], F32)
    q_bs = A4("q_bs", [128, 64], F32)
    q_sA = A4("q_sA", [128, 64], F32)
    q_sB = A4("q_sB", [128, 64], F32)

    psA = st.alloc_psum_tensor("psA", [128, 2048], F32)
    psB = st.alloc_psum_tensor("psB", [128, 512], F32)
    psC = st.alloc_psum_tensor("psC", [128, 2048], BF16)
    psD = st.alloc_psum_tensor("psD", [128, 512], F32)

    EM = {k: [] for k in ("sync", "tensor", "vector", "scalar", "gpsimd")}

    class Q:
        def __init__(self, name):
            self.name = name

        def __call__(self, fn, inc=None, k=1):
            drain = self.name in ("vector", "scalar")
            if inc is None:
                EM[self.name].append(fn)
                if drain:
                    EM[self.name].append(lambda e: e.drain())
                return None
            v = inc.inc(k)
            if drain:
                EM[self.name].append(fn)
                EM[self.name].append(
                    lambda e, s=inc, kk=k: e.drain().then_inc(s.h, kk))
            else:
                EM[self.name].append(lambda e, f=fn, s=inc, kk=k: f(e).then_inc(s.h, kk))
            return v

        def w(self, sem, val):
            if val and val > 0:
                EM[self.name].append(lambda e, s=sem, v=val: e.wait_ge(s.h, v))

        def dma(self, out, in_, sem, slow=False, dyn=None):
            v = sem.inc(16)
            if dyn is None:
                EM[self.name].append(
                    lambda e, o=out, i=in_, s=sem, sl=slow: e.dma_start(
                        out=o, in_=i, allow_slow_non_contiguous=sl).then_inc(s.h, 16))
            else:
                ten, idx = dyn

                def _f(e, o=out, tn=ten, ix=idx, s=sem):
                    si, hl, tt = ix
                    src = tn.ap()[si][bass.ds(_rank_base(e), 1)].squeeze(0)[hl, tt]
                    return e.dma_start(out=o, in_=src).then_inc(s.h, 16)
                EM[self.name].append(_f)
            return v

    _rank_cache = {}

    def _rank_base(e):
        key = id(e)
        if key not in _rank_cache:
            _rank_cache[key] = e.compute_val(e.partition_id() % 4)
        return _rank_cache[key]

    _rank2_cache = {}

    def _rank2c(e, c0):
        key = (id(e), c0)
        if key not in _rank2_cache:
            _rank2_cache[key] = e.compute_val((e.partition_id() % 4) * 2 + c0)
        return _rank2_cache[key]

    T, V, Sc, G, Y = Q("tensor"), Q("vector"), Q("scalar"), Q("gpsimd"), Q("sync")

    def T_mm(ps_ap, lhs_ap, rhs_ap, start, stop, inc=None, nmax=512):
        n = rhs_ap.shape[-1]
        vals = None
        for o in range(0, n, nmax):
            w = min(nmax, n - o)
            last = o + w >= n
            vals = T(lambda e, p=ps_ap[:, o:o + w] if ps_ap.ndim == 2 else None,
                     l=lhs_ap, r=rhs_ap[:, o:o + w], st_=start, sp=stop:
                     e.matmul(p, l, r, start=st_, stop=sp),
                     inc=inc if last else None)
        return vals

    def mks(name):
        return Sem(st.alloc_semaphore(name), name)

    s_ld = mks("ld")
    s_x = mks("x")
    s_wg = mks("wg")
    s_tc = mks("tc")
    s_vc = mks("vc")
    s_sc = mks("sc")
    s_tp = mks("tp")
    s_pe = mks("pe")
    s_ag = mks("ag")
    s_cc = mks("cc")
    s_qk = mks("qk")
    s_mx = mks("mx")
    s_ex = mks("ex")
    s_ar = mks("ar")
    s_fe = mks("fe")
    s_pn = mks("pn")
    s_pv = mks("pv")
    s_pq = mks("pq")
    s_a2 = mks("a2")
    s_lat = mks("lat")
    s_latq = [mks(f"latq{i}") for i in range(4)]
    s_xs = [mks("xs0"), mks("xs1")]
    s_wgs = [mks("wgs0"), mks("wgs1"), mks("wgs2"), mks("wgs3")]
    s_fw = mks("fw")
    s_hw = mks("hw")
    s_qr = [mks("qr0"), mks("qr1")]
    s_krg = mks("krg")
    s_p3t = [mks("p3t0"), mks("p3t1")]
    s_p3xl = [mks("p3xl0"), mks("p3xl1")]
    s_ost = [mks("ost0"), mks("ost1")]
    s_wv = mks("wv")
    s_am = mks("am")
    s_rot = mks("rot")
    s_q2 = mks("q2")
    s_p3 = mks("p3")
    s_p3g = mks("p3g")
    s_xq = mks("xq")
    s_out = mks("out")
    s_fin = mks("fin")

    Y.dma(ident.ap(), ident_in.ap(), s_ld)
    Y.dma(identf.ap(), identf_in.ap(), s_ld)
    Y.dma(biasb.ap(), bias_in.ap(), s_ld)
    Y.dma(xtsb.ap(), xts_in.ap(), s_ld)
    for j in range(12):
        Y.dma(lnwq[:, j:j + 1], qalnw_in[j], s_ld)
        Y.dma(lnbq[:, j:j + 1], qalnb_in[j], s_ld)
    for j in range(4):
        Y.dma(lnwk[:, j:j + 1], kvalnw_in[j], s_ld)
        Y.dma(lnbk[:, j:j + 1], kvalnb_in[j], s_ld)
    for t in range(TC):
        Y.dma(kcosb[:, t * DR:(t + 1) * DR], ktcos_in[t * 128:(t + 1) * 128, :], s_ld)
        Y.dma(ksfb[:, t * DR:(t + 1) * DR], ktsf_in[t * 128:(t + 1) * 128, :], s_ld)
    Y.dma(fcosb.ap(), fcos_in.ap(), s_ld)
    Y.dma(fsfb.ap(), fsf_in.ap(), s_ld)
    ld_all = s_ld.n
    for t in range(min(2, TC)):
        Y.dma(xtile[:, (t % 2) * D:(t % 2) * D + D], x_in[t * 128:(t + 1) * 128, :],
              s_xs[t % 2])
    fw0_ld = Y.dma(fwn.ap(), ffnw_in[0], s_fw)

    V.w(s_ld, ld_all)
    Sc.w(s_ld, ld_all)
    Sc(lambda e: e.activation(warm[:, 0:1], xtsb.ap(), AF.Exp,
                              accum_out=warm[:, 1:2]))
    Sc(lambda e: e.activation(warm[:, 0:1], xtsb.ap(), AF.Square))
    Sc(lambda e: e.activation(warm[:, 0:1], warm[:, 0:1], AF.Ln))
    V(lambda e: e.memset(onesb.ap(), 1.0))
    V(lambda e: e.memset(ones128f.ap(), 1.0))
    V(lambda e: e.memset(sconst.ap(), 0.0))
    V(lambda e: e.memset(wsc.ap(), 0.0))
    V(lambda e: e.memset(stats.ap(), 0.0))

    psc_last_evac = [0, 0]
    psc_used = [0, 0]
    psc_next = [0]

    def transpose_group(srcs, dst_cb):
        hidx = psc_next[0] % 2
        psc_next[0] += 1
        if psc_used[hidx] > 0:
            T.w(s_pe, psc_last_evac[hidx])
        n = len(srcs)
        for i, ap_ in enumerate(srcs):
            wdt = ap_.shape[-1]
            dst = psC[0:wdt, hidx * 1024 + i * 128: hidx * 1024 + i * 128 + 128]
            T(lambda e, d=dst, s=ap_: e.transpose(d, s, ident.ap()),
              inc=s_tp if i == n - 1 else None)
        psc_used[hidx] += 1
        psc_last_evac[hidx] = dst_cb(hidx)
        return hidx

    negmu = sconst[:, 0:1]
    varl = sconst[:, 1:2]
    rstd1 = sconst[:, 2:3]

    for t in range(TC):
        xt = xtile[:, (t % 2) * D:(t % 2) * D + D]
        V.w(s_xs[t % 2], 16 * (t // 2 + 1))
        V.w(s_tp, s_tp.n)
        V(lambda e, xt=xt: e.reduce_sum(negmu, xt, axis=AX, negate=True))
        V(lambda e: e.tensor_scalar_mul(negmu, negmu, 1.0 / D))
        vv = V(lambda e, xt=xt: e.tensor_scalar_add(xcs.ap(), xt, negmu), inc=s_vc)
        xt_free = vv
        Sc.w(s_vc, vv)
        sv = Sc(lambda e: e.activation(sqf.ap(), xcs.ap(), AF.Square, accum_out=varl),
                inc=s_sc)
        V.w(s_sc, sv)
        vv = V(lambda e: e.tensor_scalar(varl, varl, 1.0 / D, float(EPS),
                                         OP.mult, OP.add), inc=s_vc)
        Sc.w(s_vc, vv)
        Sc(lambda e: e.activation(rstd1, varl, AF.Ln))
        sv = Sc(lambda e: e.activation(rstd1, rstd1, AF.Exp, scale=-0.5), inc=s_sc)
        V.w(s_sc, sv)
        v_xn = V(lambda e: e.tensor_scalar_mul(xnbf.ap(), xcs.ap(), rstd1), inc=s_vc)
        for g in range(2):
            T.w(s_vc, v_xn)
            if t == 0 and g == 0:
                T.w(s_ld, ld_all)
            tiles = [xnbf[:, (g * 8 + i) * 128:(g * 8 + i + 1) * 128] for i in range(8)]

            def ev(hidx, t=t, g=g):
                Sc.w(s_tp, s_tp.n)
                dst = xnT.ap().rearrange("p (k w) -> p k w", w=W1)[
                    :, g * 8:g * 8 + 8, t * 128:(t + 1) * 128]
                src = psC.ap().rearrange("p (h k w) -> p h k w", h=2, w=128)[:, hidx]
                return Sc(lambda e, d=dst, s=src: e.copy(out=d, in_=s), inc=s_pe)
            transpose_group(tiles, ev)
        if t + 2 < TC:
            Y.w(s_vc, xt_free)
            Y.dma(xtile[:, ((t + 2) % 2) * D:((t + 2) % 2) * D + D],
                  x_in[(t + 2) * 128:(t + 3) * 128, :], s_xs[t % 2])
    xnT_ready = s_pe.n

    wslot_seq = [0]
    wslot_consumed = {}

    def load_wblock(dram_ap_3d, kn, wn):
        sl = wslot_seq[0] % 4
        wslot_seq[0] += 1
        if sl in wslot_consumed:
            G.w(s_tc, wslot_consumed[sl])
        flat = wcb[:, sl * 2048: sl * 2048 + kn * wn]
        dst = flat.rearrange("p (k w) -> p k w", w=wn)
        v = G.dma(dst, dram_ap_3d, s_wgs[sl])
        return flat, v, sl

    def p1_proj(n_chunks, w_blk, out_cb):
        for ncx in range(n_chunks):
            blk, dv, sl = load_wblock(w_blk[ncx], 16, 128)
            T.w(s_wgs[sl], dv)
            mmv = 0
            for kcx in range(16):
                lhs = blk[:, kcx * 128:(kcx + 1) * 128]
                rhs = xnT.ap().rearrange("p (k w) -> p k w", w=W1)[:, kcx]
                mmv = T_mm(psA[:, 0:W1], lhs, rhs, kcx == 0, kcx == 15, inc=s_tc)
            wslot_consumed[sl] = mmv
            out_cb(ncx, mmv)

    T.w(s_pe, xnT_ready)
    if dbg:
        G.w(s_pe, xnT_ready)
        G.dma(dbg_xnT.ap(), xnT.ap(), s_out)

    def qa_out(ncx, mmv):
        Sc.w(s_tc, mmv)
        ev = Sc(lambda e, n=ncx: e.copy(out=qlat[:, n * W1:(n + 1) * W1],
                                        in_=psA[:, 0:W1]), inc=s_sc)
        T.w(s_sc, ev)

    p1_proj(12, qaT_b, qa_out)

    def ln_feature_major(nch, buf, lnw, lnb, dim):
        T.w(s_sc, s_sc.n)
        for ncx in range(nch):
            T_mm(psA[:, 1024:1024 + W1], onesb.ap(), buf[:, ncx * W1:(ncx + 1) * W1],
                 ncx == 0, ncx == nch - 1, inc=s_tc)
        V.w(s_tc, s_tc.n)
        vv = V(lambda e: e.tensor_scalar_mul(mub.ap(), psA[:, 1024:1024 + W1],
                                             1.0 / dim), inc=s_vc)
        T.w(s_vc, vv)
        sq_mm = {}
        for ncx in range(nch):
            if ncx >= 2:
                Sc.w(s_tc, sq_mm[ncx - 2])
            sv = Sc(lambda e, n=ncx: e.activation(
                sqb[:, (n % 2) * W1:(n % 2) * W1 + W1],
                buf[:, n * W1:(n + 1) * W1], AF.Square), inc=s_sc)
            T.w(s_sc, sv)
            sq_mm[ncx] = T_mm(psA[:, 1024:1024 + W1], onesb.ap(),
                              sqb[:, (ncx % 2) * W1:(ncx % 2) * W1 + W1],
                              ncx == 0, ncx == nch - 1, inc=s_tc)
        V.w(s_tc, s_tc.n)
        V(lambda e: e.tensor_scalar_mul(rstdb.ap(), psA[:, 1024:1024 + W1], 1.0 / dim))
        V(lambda e: e.tensor_tensor(out=mub.ap(), in0=mub.ap(), in1=mub.ap(),
                                    op=OP.mult) if False else
          e.tensor_tensor(out=scr1[:, 0:W1], in0=mub.ap(), in1=mub.ap(), op=OP.mult))
        V(lambda e: e.tensor_tensor(out=rstdb.ap(), in0=rstdb.ap(), in1=scr1[:, 0:W1],
                                    op=OP.subtract))
        vv = V(lambda e: e.tensor_scalar_add(rstdb.ap(), rstdb.ap(), float(EPS)),
               inc=s_vc)
        Sc.w(s_vc, vv)
        Sc(lambda e: e.activation(rstdb.ap(), rstdb.ap(), AF.Ln))
        sv = Sc(lambda e: e.activation(rstdb.ap(), rstdb.ap(), AF.Exp, scale=-0.5),
                inc=s_sc)
        V.w(s_sc, sv)
        out_vals = []
        for ncx in range(nch):
            V(lambda e, n=ncx: e.tensor_tensor(
                out=scr1[:, 0:W1], in0=buf[:, n * W1:(n + 1) * W1], in1=mub.ap(),
                op=OP.subtract))
            V(lambda e: e.tensor_tensor(out=scr1[:, 0:W1], in0=scr1[:, 0:W1],
                                        in1=rstdb.ap(), op=OP.mult))
            out_vals.append(V(lambda e, n=ncx: e.tensor_scalar(
                buf[:, n * W1:(n + 1) * W1], scr1[:, 0:W1],
                lnw[:, n:n + 1], lnb[:, n:n + 1], OP.mult, OP.add), inc=s_vc))
        return out_vals

    V.w(s_ld, ld_all)
    qa_vals = ln_feature_major(12, qlat, lnwq, lnbq, QL)
    for ncx in range(12):
        G.w(s_vc, qa_vals[ncx])
        G.dma(ag_in[ncx], qlat[:, ncx * W1:(ncx + 1) * W1], s_ag)

    G.w(s_ag, s_ag.n)
    cc_a = G(lambda e: e.collective_compute(
        "AllGather", OP.bypass, replica_groups=[[0, 1, 2, 3], [4, 5, 6, 7]],
        ins=[ag_in.ap()[0:12].opt()], outs=[ag_out_a.ap().opt()]), inc=s_cc)

    def kv_out(ncx, mmv):
        Sc.w(s_tc, mmv)
        ev = Sc(lambda e, n=ncx: e.copy(out=kvlT[:, n * W1:(n + 1) * W1],
                                        in_=psA[:, 0:W1]), inc=s_sc)
        T.w(s_sc, ev)

    p1_proj(4, kvaT_b, kv_out)
    kv_vals = ln_feature_major(4, kvlT, lnwk, lnbk, KVL)
    for ncx in range(4):
        G.w(s_vc, kv_vals[ncx])
        G.dma(ag_in[20 + ncx], kvlT[:, ncx * W1:(ncx + 1) * W1], s_ag)

    kr_blk, kr_dv, kr_sl = load_wblock(kvrT_b.ap(), 16, DR)
    T.w(s_wgs[kr_sl], kr_dv)
    kr_mm = 0
    for t in range(TC):
        for kcx in range(16):
            lhs = xnT.ap().rearrange("p (k w) -> p k w", w=W1)[
                :, kcx, t * 128:(t + 1) * 128]
            rhs = kr_blk[:, kcx * DR:(kcx + 1) * DR]
            kr_mm = T(lambda e, l=lhs, r=rhs, st_=(kcx == 0), sp=(kcx == 15):
                      e.matmul(psB[:, 0:DR], l, r, start=st_, stop=sp), inc=s_tc)
        V.w(s_tc, kr_mm)
        V.w(s_tp, s_tp.n)
        cosr = kcosb[:, t * DR:(t + 1) * DR]
        sfr = ksfb[:, t * DR:(t + 1) * DR]
        vv = V(lambda e: e.tensor_copy(out=krop.ap(), in_=psB[:, 0:DR]), inc=s_vc)
        T.w(s_vc, vv)

        def _rope_a(e, cosr=cosr, sfr=sfr):
            e.tensor_tensor(out=scr1[:, 0:DR], in0=krop.ap(), in1=cosr, op=OP.mult)
            e.tensor_tensor(out=scr1[:, DR:DR + 32], in0=krop[:, 32:64],
                            in1=sfr[:, 0:32], op=OP.mult)
            return e.tensor_tensor(out=scr1[:, DR + 32:DR + 64], in0=krop[:, 0:32],
                                   in1=sfr[:, 32:64], op=OP.mult)
        V(_rope_a)
        vv = V(lambda e: e.tensor_tensor(out=xnbf[:, 0:DR], in0=scr1[:, 0:DR],
                                         in1=scr1[:, DR:2 * DR], op=OP.add),
               inc=s_vc)
        T.w(s_vc, vv)

        def evk(hidx, t=t):
            Sc.w(s_tp, s_tp.n)
            return Sc(lambda e, t_=t, h=hidx: e.copy(
                out=krT[0:64, t_ * 128:(t_ + 1) * 128],
                in_=psC[0:64, h * 1024:h * 1024 + 128]), inc=s_pe)
        transpose_group([xnbf[:, 0:DR]], evk)
    wslot_consumed[kr_sl] = kr_mm
    G.w(s_pe, s_pe.n)
    G.dma(ag_in[24, 0:64, :], krT.ap(), s_ag)
    G.dma(ag_in[24, 64:128, :], krT.ap(), s_ag)
    G.w(s_ag, s_ag.n)
    cc_b = G(lambda e: e.collective_compute(
        "AllGather", OP.bypass, replica_groups=[[0, 1, 2, 3], [4, 5, 6, 7]],
        ins=[ag_in.ap()[20:25].opt()], outs=[ag_out_b.ap().opt()]), inc=s_cc)

    qr_mm_last = 0
    for ncx in range(8):
        blk, dv, sl = load_wblock(qrwT_b[ncx], 12, 128)
        T.w(s_wgs[sl], dv)
        mmv = 0
        for kcx in range(12):
            lhs = blk[:, kcx * 128:(kcx + 1) * 128]
            rhs = qlat.ap().rearrange("p (k w) -> p k w", w=W1)[:, kcx]
            mmv = T_mm(psA[:, 0:W1], lhs, rhs, kcx == 0, kcx == 11, inc=s_tc)
        wslot_consumed[sl] = mmv
        qr_mm_last = mmv
        Sc.w(s_tc, mmv)
        Sc.w(s_rot, s_rot.n)
        Sc.w(s_vc, s_vc.n)
        ev = Sc(lambda e: e.copy(out=qrev.ap(), in_=psA[:, 0:W1]), inc=s_sc)
        T.w(s_sc, ev)
        G.w(s_sc, ev)
        G.dma(qrrot[0:32, :], qrev[32:64, :], s_rot)
        G.dma(qrrot[32:64, :], qrev[0:32, :], s_rot)
        G.dma(qrrot[64:96, :], qrev[96:128, :], s_rot)
        rv = G.dma(qrrot[96:128, :], qrev[64:96, :], s_rot)
        V.w(s_rot, rv)
        V.w(s_ld, ld_all)
        V.w(s_ag, s_ag.n)
        V(lambda e: e.tensor_tensor(out=qrrot.ap(), in0=qrrot.ap(), in1=fsfb.ap(),
                                    op=OP.mult))
        V(lambda e: e.tensor_tensor(out=qrev.ap(), in0=qrev.ap(), in1=fcosb.ap(),
                                    op=OP.mult))
        vv = V(lambda e: e.tensor_tensor(out=agst[:, 0:W1], in0=qrev.ap(),
                                         in1=qrrot.ap(), op=OP.add), inc=s_vc)
        G.w(s_vc, vv)
        G.dma(ag_in[12 + ncx], agst[:, 0:W1], s_ag)

    V.w(s_fw, fw0_ld)
    fwA_vals = []
    for ch in range(2):
        if ch == 1:
            lv = Y.dma(fwn.ap(), ffnw_in[1], s_fw)
            V.w(s_fw, lv)
        wv_ = fwn.ap().rearrange("p (b k) -> p b k", k=16)
        wn_ = fwq.ap().bitcast(F32).rearrange("p (b k) -> p b k", k=8)
        V(lambda e, wv=wv_, wn=wn_: e.tensor_tensor(
            out=wn, in0=wv[:, :, 0:16:2], in1=wv[:, :, 1:16:2], op=OP.min))
        V(lambda e, wv=wv_: e.tensor_tensor(out=wv[:, :, 0:8], in0=wv[:, :, 0:16:2],
                                            in1=wv[:, :, 1:16:2], op=OP.max))
        V(lambda e, wv=wv_: e.tensor_tensor(out=wv[:, :, 0:4], in0=wv[:, :, 0:8:2],
                                            in1=wv[:, :, 1:8:2], op=OP.max))
        V(lambda e, wn=wn_: e.tensor_tensor(out=wn[:, :, 0:4], in0=wn[:, :, 0:8:2],
                                            in1=wn[:, :, 1:8:2], op=OP.min))
        V(lambda e, wv=wv_: e.tensor_tensor(out=wv[:, :, 0:2], in0=wv[:, :, 0:4:2],
                                            in1=wv[:, :, 1:4:2], op=OP.max))
        V(lambda e, wn=wn_: e.tensor_tensor(out=wn[:, :, 0:2], in0=wn[:, :, 0:4:2],
                                            in1=wn[:, :, 1:4:2], op=OP.min))
        V(lambda e, wv=wv_, c=ch: e.tensor_tensor(
            out=fws[:, c * 128:(c + 1) * 128], in0=wv[:, :, 0:1].squeeze(2),
            in1=wv[:, :, 1:2].squeeze(2), op=OP.max))
        V(lambda e, wn=wn_, c=ch: e.tensor_tensor(
            out=scr1[:, 0:128], in0=wn[:, :, 0:1].squeeze(2),
            in1=wn[:, :, 1:2].squeeze(2), op=OP.min))
        V(lambda e: e.tensor_scalar_mul(scr1[:, 0:128], scr1[:, 0:128], -1.0))
        fwA_vals.append(V(lambda e, c=ch: e.tensor_tensor(
            out=fws[:, c * 128:(c + 1) * 128], in0=fws[:, c * 128:(c + 1) * 128],
            in1=scr1[:, 0:128], op=OP.max), inc=s_vc))
        if ch == 0:
            Y.w(s_vc, fwA_vals[0])
    vv = V(lambda e: e.reduce_max(wsc[:, 0:1], fws.ap(), axis=AX), inc=s_vc)
    T.w(s_vc, vv)
    T.w(s_ld, ld_all)
    tv = T(lambda e: e.transpose(psB[0:1, 128:256], wsc[:, 0:1], identf.ap()), inc=s_tc)
    V.w(s_tc, tv)
    vv = V(lambda e: e.reduce_max(wsc[0:1, 4:5], psB[0:1, 128:256], axis=AX),
           inc=s_vc)
    T.w(s_vc, vv)
    G.w(s_vc, vv)
    qv = G.dma(amax_in.ap(), wsc[0:1, 0:16], s_q2)
    G.w(s_q2, qv)
    cc_amax = G(lambda e: e.collective_compute(
        "AllGather", OP.bypass, replica_groups=[list(range(NC))],
        ins=[amax_in.ap().opt()], outs=[amax_out.ap().opt()]), inc=s_cc)
    G.w(s_ag, s_ag.n)
    cc_c = G(lambda e: e.collective_compute(
        "AllGather", OP.bypass, replica_groups=[[0, 1, 2, 3], [4, 5, 6, 7]],
        ins=[ag_in.ap()[12:20].opt()], outs=[ag_out_c.ap().opt()]), inc=s_cc)

    G.w(s_cc, cc_amax)
    qv = G.dma(wsc[0:1, 8:16],
               amax_out.ap().rearrange("c a k -> a (c k)")[0:1, 4:125:16], s_q2,
               slow=True)
    V.w(s_q2, qv)
    V(lambda e: e.reduce_max(wsc[0:1, 1:2], wsc[0:1, 8:16], axis=AX))
    V(lambda e: e.tensor_scalar_mul(wsc[0:1, 2:3], wsc[0:1, 1:2], 1.0 / 2688.0))
    vv = V(lambda e: e.tensor_scalar_mul(wsc[0:1, 3:4], wsc[0:1, 2:3], 6.0), inc=s_vc)
    G.w(s_vc, vv)
    qv = G.dma(amax_in.ap(), wsc[0:1, 0:16], s_q2)
    G.w(s_q2, qv)
    qv = G.dma(wsc.ap(), amax_in.ap()[0:1, 0:16].broadcast_to([128, 16]), s_q2)
    V.w(s_q2, qv)
    V(lambda e: e.reciprocal(wsc[:, 5:6], wsc[:, 3:4]))
    V(lambda e: e.tensor_scalar_mul(wsc[:, 6:7], wsc[:, 2:3], xtsb.ap()))
    V(lambda e: e.tensor_scalar_mul(wsc[:, 7:8], xtsb.ap(), 6.0))
    rfx_done = V(lambda e: e.reciprocal(wsc[:, 7:8], wsc[:, 7:8]), inc=s_vc)

    def nvfp4(src, width, bs, sA, sB, cc, rA, rB, rC, rD, rfac, ts_lane, dst,
              post=None):
        last = [None]
        for r in nvfp4_gen(src, width, bs, sA, sB, cc, rA, rB, rC, rD, rfac,
                           ts_lane, dst, post):
            if r is not None:
                last[0] = r
        return last[0]

    def nvfp4_gen(src, width, bs, sA, sB, cc, rA, rB, rC, rD, rfac, ts_lane,
                  dst, post=None):
        nb = width // 16
        sv = src.rearrange("p (b k) -> p b k", k=16)
        t8 = rA[:, 0:width // 2].rearrange("p (b k) -> p b k", k=8)
        t8n = rB[:, 0:width // 2].rearrange("p (b k) -> p b k", k=8)
        V(lambda e: e.tensor_tensor(out=t8, in0=sv[:, :, 0:16:2],
                                    in1=sv[:, :, 1:16:2], op=OP.max))
        yield None
        V(lambda e: e.tensor_tensor(out=t8n, in0=sv[:, :, 0:16:2],
                                    in1=sv[:, :, 1:16:2], op=OP.min))
        yield None
        V(lambda e: e.tensor_tensor(out=t8[:, :, 0:4], in0=t8[:, :, 0:8:2],
                                    in1=t8[:, :, 1:8:2], op=OP.max))
        yield None
        V(lambda e: e.tensor_tensor(out=t8n[:, :, 0:4], in0=t8n[:, :, 0:8:2],
                                    in1=t8n[:, :, 1:8:2], op=OP.min))
        yield None
        V(lambda e: e.tensor_tensor(out=t8[:, :, 0:2], in0=t8[:, :, 0:4:2],
                                    in1=t8[:, :, 1:4:2], op=OP.max))
        yield None
        V(lambda e: e.tensor_tensor(out=t8n[:, :, 0:2], in0=t8n[:, :, 0:4:2],
                                    in1=t8n[:, :, 1:4:2], op=OP.min))
        yield None
        V(lambda e: e.tensor_tensor(out=bs[:, 0:nb], in0=t8[:, :, 0:1].squeeze(2),
                                    in1=t8[:, :, 1:2].squeeze(2), op=OP.max))
        yield None
        V(lambda e: e.tensor_tensor(out=sA[:, 0:nb], in0=t8n[:, :, 0:1].squeeze(2),
                                    in1=t8n[:, :, 1:2].squeeze(2), op=OP.min))
        yield None
        V(lambda e: e.tensor_scalar_mul(sA[:, 0:nb], sA[:, 0:nb], -1.0))
        yield None
        V(lambda e: e.tensor_tensor(out=bs[:, 0:nb], in0=bs[:, 0:nb],
                                    in1=sA[:, 0:nb], op=OP.max))
        yield None
        V(lambda e: e.tensor_scalar_mul(bs[:, 0:nb], bs[:, 0:nb], rfac))
        yield None
        V(lambda e: e.tensor_scalar(bs[:, 0:nb], bs[:, 0:nb], 0.0, 448.0,
                                    OP.max, OP.min))
        yield None
        V(lambda e: e.tensor_scalar_mul(sA[:, 0:nb], bs[:, 0:nb], VELT))
        yield None
        V(lambda e: e.tensor_tensor(out=sB[:, 0:nb], in0=sA[:, 0:nb],
                                    in1=bs[:, 0:nb], op=OP.subtract))
        yield None
        V(lambda e: e.tensor_tensor(out=sA[:, 0:nb], in0=sA[:, 0:nb],
                                    in1=sB[:, 0:nb], op=OP.subtract))
        yield None
        V(lambda e: e.tensor_scalar(sB[:, 0:nb], bs[:, 0:nb], M_SUB, M_SUB,
                                    OP.add, OP.subtract))
        yield None
        V(lambda e: e.tensor_scalar(cc[:, 0:nb], bs[:, 0:nb], E4_MIN, None, OP.is_lt))
        yield None
        V(lambda e: e.tensor_tensor(out=sB[:, 0:nb], in0=sB[:, 0:nb],
                                    in1=sA[:, 0:nb], op=OP.subtract))
        yield None
        V(lambda e: e.tensor_tensor(out=sB[:, 0:nb], in0=sB[:, 0:nb],
                                    in1=cc[:, 0:nb], op=OP.mult))
        yield None
        V(lambda e: e.tensor_tensor(out=bs[:, 0:nb], in0=sA[:, 0:nb],
                                    in1=sB[:, 0:nb], op=OP.add))
        yield None
        V(lambda e: e.tensor_scalar(sA[:, 0:nb], bs[:, 0:nb], 0.0, None, OP.is_equal))
        yield None
        V(lambda e: e.tensor_tensor(out=sB[:, 0:nb], in0=bs[:, 0:nb],
                                    in1=sA[:, 0:nb], op=OP.add))
        yield None
        V(lambda e: e.tensor_scalar_mul(sB[:, 0:nb], sB[:, 0:nb], ts_lane))
        yield None
        V(lambda e: e.reciprocal(sB[:, 0:nb], sB[:, 0:nb]))
        yield None
        V(lambda e: e.tensor_tensor(
            out=cc.rearrange("p (b k) -> p b k", k=16), in0=sv,
            in1=sB[:, 0:nb].unsqueeze(2).broadcast_to([128, nb, 16]), op=OP.mult))
        yield None
        V(lambda e: e.tensor_scalar(cc, cc, -6.0, 6.0, OP.max, OP.min))
        yield None
        V(lambda e: e.tensor_scalar(rB, cc, 2.0, C_RNE, OP.mult, OP.add))
        yield None
        V(lambda e: e.tensor_scalar(rB, rB, C_RNE, 0.5, OP.subtract, OP.mult))
        yield None
        V(lambda e: e.tensor_scalar(rC, cc, C_RNE, C_RNE, OP.add, OP.subtract))
        yield None
        V(lambda e: e.tensor_scalar(rD, cc, 0.5, C_RNE, OP.mult, OP.add))
        yield None
        V(lambda e: e.tensor_scalar(rD, rD, C_RNE, 2.0, OP.subtract, OP.mult))
        yield None
        V(lambda e: e.tensor_tensor(out=rA, in0=cc, in1=cc, op=OP.mult))
        yield None
        V(lambda e: e.tensor_scalar(cc, rA, 16.0, None, OP.is_lt))
        yield None
        V(lambda e: e.tensor_scalar(rA, rA, 4.0, None, OP.is_lt))
        yield None
        V(lambda e: e.tensor_tensor(out=rC, in0=rC, in1=rD, op=OP.subtract))
        yield None
        V(lambda e: e.tensor_tensor(out=rC, in0=rC, in1=cc, op=OP.mult))
        yield None
        V(lambda e: e.tensor_tensor(out=rD, in0=rD, in1=rC, op=OP.add))
        yield None
        V(lambda e: e.tensor_tensor(out=rB, in0=rB, in1=rD, op=OP.subtract))
        yield None
        V(lambda e: e.tensor_tensor(out=rB, in0=rB, in1=rA, op=OP.mult))
        yield None
        V(lambda e: e.tensor_tensor(out=rD, in0=rD, in1=rB, op=OP.add))
        yield None
        if post is not None:
            V(lambda e: e.tensor_scalar_mul(bs[:, 0:nb], bs[:, 0:nb], post))
            yield None
        yield V(lambda e: e.tensor_tensor(
            out=dst.rearrange("p (b k) -> p b k", k=16),
            in0=rD.rearrange("p (b k) -> p b k", k=16),
            in1=bs[:, 0:nb].unsqueeze(2).broadcast_to([128, nb, 16]), op=OP.mult),
            inc=s_vc)

    qv_prev = 0
    for ch in range(2):
        Y.w(s_tc, max(kr_mm, qr_mm_last))
        Y.w(s_vc, qv_prev)
        if ch == 0:
            Y.w(s_vc, fwA_vals[1])
        lv = Y.dma(q_c.ap(), ffnw_in[ch], s_fw)
        V.w(s_fw, lv)
        V.w(s_vc, rfx_done)
        V.w(s_tc, qr_mm_last)
        V.w(s_tp, s_tp.n)
        for hf in range(2):
            qv = nvfp4(q_c[:, hf * 1024:(hf + 1) * 1024], 1024,
                       q_bs.ap(), q_sA.ap(), q_sB.ap(),
                       cc=q_rB.ap(), rA=q_rA.ap(), rB=q_rC.ap(), rC=q_rD.ap(),
                       rD=fwn[:, 0:1024], rfac=wsc[:, 5:6], ts_lane=wsc[:, 2:3],
                       dst=fwq[:, hf * 1024:(hf + 1) * 1024])
        qv_prev = qv
        for g in range(2):
            T.w(s_vc, qv)
            tiles = [fwq[:, (g * 8 + i) * 128:(g * 8 + i + 1) * 128] for i in range(8)]

            def evw(hidx, ch=ch, g=g):
                Sc.w(s_tp, s_tp.n)
                Sc.w(s_q2, s_q2.n)
                return Sc(lambda e, h=hidx: e.copy(
                    out=agst[:, 0:1024], in_=psC[:, h * 1024:h * 1024 + 1024]),
                    inc=s_pe)
            hid = transpose_group(tiles, evw)
            G.w(s_pe, psc_last_evac[hid])
            for i in range(8):
                G.dma(wdq_in[g * 8 + i, :, ch * 128:(ch + 1) * 128],
                      agst[:, i * 128:(i + 1) * 128], s_q2)
    partB_done = qv_prev
    G.w(s_q2, s_q2.n)
    cc_wdq = G(lambda e: e.collective_compute(
        "AllGather", OP.bypass, replica_groups=[list(range(NC))],
        ins=[wdq_in.ap().opt()], outs=[wdq_out.ap().opt()]), inc=s_cc)

    Y.w(s_cc, cc_b)
    for src in range(4):
        Y.dma(krTg[0:64, src * W1:(src + 1) * W1], ag_out_b[src, 4, 0:64, :], s_krg)
    krg_all = s_krg.n

    lat_slot = [0]
    lat_slot_consumed = {}

    def dma_lat(chunk, tw):
        sl = lat_slot[0] % 4
        lat_slot[0] += 1
        if sl in lat_slot_consumed:
            Y.w(s_tc, lat_slot_consumed[sl])
        if chunk < 12:
            gat, gch = ag_out_a, chunk
            Y.w(s_cc, cc_a)
        else:
            gat, gch = ag_out_b, chunk - 20
            Y.w(s_cc, cc_b)
        dv = 0
        npc = max(1, TWW // W1)
        pw = min(TWW, W1)
        for p in range(npc):
            tok0 = tw * TWW + p * pw
            src = tok0 // W1
            coff = tok0 % W1
            dv = Y.dma(latq[:, sl * TWW + p * pw: sl * TWW + (p + 1) * pw],
                       gat[src, gch, :, coff:coff + pw], s_latq[sl])
        return sl, dv

    psD_last_evac = [0]
    attnT_last_store = [0]
    head_qk_done = {-2: 0, -1: 0}

    proj_last_mm = [0]

    def proj_steps(h):
        hp = h % 2
        G.w(s_tc, proj_last_mm[0])
        G.dma(qbTb.ap().rearrange("p (k w) -> p k w", w=128), qbT_b[h], s_hw)
        G.dma(kvknb.ap().rearrange("p (k w) -> p k w", w=128), kvbknT_b[h], s_hw)
        gv3 = G.dma(kvvb.ap().rearrange("p (k w) -> p k w", w=128), kvbvT_b[h], s_hw)
        chq0 = h // 2
        roff = (h % 2) * 64
        if h >= 2:
            Y.w(s_qk, head_qk_done[h - 2])
        sqr = s_qr[h % 2]
        qrv = sqr.inc(16)
        out_ap = qrTh[0:64, hp * S:(hp + 1) * S].rearrange("p (s w) -> p s w", w=W1)
        Y.w(s_cc, cc_c)

        def _fq(e, o=out_ap, c0=chq0, ro=roff, sh=sqr.h):
            src_ap = ag_out_c.ap()[:, bass.ds(_rank2c(e, c0), 1), ro:ro + 64, :]
            src_ap = src_ap.squeeze(1).transpose([1, 0, 2])
            return e.dma_start(out=o, in_=src_ap).then_inc(sh, 16)
        EM["sync"].append(_fq)
        T.w(s_hw, gv3)
        yield qrv
        for tw in range(NTW):
            mmv = 0
            for halfq in range(2):
                for kcx in range(6 * halfq, 6 * halfq + 6):
                    sl, dv = dma_lat(kcx, tw)
                    T.w(s_latq[sl], dv)
                    mmv = T(lambda e, k=kcx, sl=sl, st_=(kcx == 0), sp=(kcx == 11):
                            e.matmul(psB.ap(), qbTb[:, k * 128:(k + 1) * 128],
                                     latq[:, sl * TWW:(sl + 1) * TWW],
                                     start=st_, stop=sp), inc=s_tc)
                    lat_slot_consumed[sl] = mmv
                yield
            Sc.w(s_tc, mmv)
            ev = Sc(lambda e, hp=hp, tw=tw: e.copy(
                out=qnT[:, hp * S + tw * TWW: hp * S + (tw + 1) * TWW],
                in_=psB.ap()), inc=s_sc)
            T.w(s_sc, ev)
            kv_sl = []
            mmv = 0
            for kcx in range(4):
                sl, dv = dma_lat(20 + kcx, tw)
                kv_sl.append(sl)
                T.w(s_latq[sl], dv)
                mmv = T(lambda e, k=kcx, sl=sl, st_=(kcx == 0), sp=(kcx == 3):
                        e.matmul(psB.ap(), kvknb[:, k * 128:(k + 1) * 128],
                                 latq[:, sl * TWW:(sl + 1) * TWW],
                                 start=st_, stop=sp), inc=s_tc)
            yield
            Sc.w(s_tc, mmv)
            ev = Sc(lambda e, hp=hp, tw=tw: e.copy(
                out=knT[:, hp * S + tw * TWW: hp * S + (tw + 1) * TWW],
                in_=psB.ap()), inc=s_sc)
            T.w(s_sc, ev)
            mmv = 0
            for tsub in range(TWW // 128):
                for kcx in range(4):
                    sl = kv_sl[kcx]
                    mmv = T(lambda e, k=kcx, sl=sl, ts=tsub,
                            st_=(kcx == 0), sp=(kcx == 3): e.matmul(
                                psB[:, ts * 128:(ts + 1) * 128],
                                latq[:, sl * TWW + ts * 128: sl * TWW + (ts + 1) * 128],
                                kvvb[:, k * 128:(k + 1) * 128],
                                start=st_, stop=sp), inc=s_tc)
            for sl in kv_sl:
                lat_slot_consumed[sl] = mmv
            Sc.w(s_tc, mmv)
            ev = Sc(lambda e, hp=hp, tw=tw: e.copy(
                out=vTb[:, hp * S + tw * TWW: hp * S + (tw + 1) * TWW],
                in_=psB.ap()), inc=s_sc)
            T.w(s_sc, ev)
            proj_last_mm[0] = mmv
            yield

    psA_q_last_exp = [0, 0]
    pbuf_last_tp = [0, 0]
    ptb_last_pv = [0]
    qc_glob = [0]

    def attn_head(h, qr_ld, next_gen):
        hp = h % 2
        kn = knT[:, hp * S:(hp + 1) * S]
        vv_ = vTb[:, hp * S:(hp + 1) * S]
        qn = qnT[:, hp * S:(hp + 1) * S]
        qr = qrTh[0:64, hp * S:(hp + 1) * S]
        T.w(s_qr[h % 2], qr_ld)
        T.w(s_krg, krg_all)

        def emit_transp(info):
            g_, pb_, qoff_, pnv_ = info
            for gg in range((KC + 7) // 8):
                ntl = min(8, KC - gg * 8)
                T.w(s_pn, pnv_)
                if ptb_last_pv[0] > 0:
                    T.w(s_pv, ptb_last_pv[0])
                tiles = [pb_[:, (gg * 8 + i) * 128:(gg * 8 + i + 1) * 128]
                         for i in range(ntl)]

                def evp(hidx, gg=gg, qoff=qoff_, ntl=ntl):
                    V.w(s_tp, s_tp.n)
                    if ptb_last_pv[0] > 0:
                        V.w(s_pv, ptb_last_pv[0])
                    dst = ptb.ap().rearrange("p (k w) -> p k w", w=QB)[
                        :, gg * 8: gg * 8 + ntl, qoff:qoff + 128]
                    src = psC.ap().rearrange("p (h k w) -> p h k w", h=2, w=128)[
                        :, hidx, 0:ntl, :]
                    return V(lambda e, d=dst, s=src: e.tensor_copy(out=d, in_=s),
                             inc=s_pe)
                transpose_group(tiles, evp)
            pbuf_last_tp[g_ % 2] = s_tp.n

        def emit_pv(qb):
            T.w(s_pe, s_pe.n)
            T.w(s_pq, psD_last_evac[0])
            pvv = 0
            for kcx in range(KC):
                pvv = T(lambda e, k=kcx, v=vv_, st_=(kcx == 0), sp=(kcx == KC - 1):
                        e.matmul(psD.ap(), v[:, k * 128:(k + 1) * 128],
                                 ptb[:, k * QB:(k + 1) * QB],
                                 start=st_, stop=sp), inc=s_pv)
            ptb_last_pv[0] = pvv
            V.w(s_pv, pvv)
            if qb == 0:
                V.w(s_a2, attnT_last_store[0])
            pqv = V(lambda e, qb=qb: e.tensor_copy(
                out=attnT[:, qb * QB:(qb + 1) * QB], in_=psD.ap()), inc=s_pq)
            psD_last_evac[0] = pqv

        pending = None
        for qc in range(QC):
            g = qc_glob[0]
            qc_glob[0] += 1
            sb_ = (g % 2) * 32
            pb = pbuf[:, (g % 2) * S:(g % 2) * S + S]
            nmq = stats[:, sb_ + 0: sb_ + 1]
            nms = stats[:, sb_ + 1: sb_ + 2]
            for j in range(4):
                reg = j % 2
                T.w(s_ex, psA_q_last_exp[reg])
                T_mm(psA[:, reg * 1024: reg * 1024 + KQ],
                     qn[:, qc * 128:(qc + 1) * 128],
                     kn[:, j * KQ:(j + 1) * KQ], True, False)
                qkv = T_mm(psA[:, reg * 1024: reg * 1024 + KQ],
                           qr[:, qc * 128:(qc + 1) * 128],
                           krTg[0:64, j * KQ:(j + 1) * KQ], False, True, inc=s_qk)
                if j == 0:
                    V.w(s_qk, qkv)
                    V(lambda e, nmq=nmq: e.reduce_max(
                        nmq, psA[:, 0:128], axis=AX, negate=True))
                    mxv = V(lambda e, nms=nms, nmq=nmq: e.tensor_scalar_mul(
                        nms, nmq, ATT_SCALE), inc=s_mx)
                    Sc.w(s_mx, mxv)
                    if g >= 2:
                        Sc.w(s_tp, pbuf_last_tp[g % 2])
                Sc.w(s_qk, qkv)
                ssum = stats[:, sb_ + 20 + j: sb_ + 21 + j]
                exv = Sc(lambda e, j=j, reg=reg, pb=pb, nms=nms, ssum=ssum:
                         e.activation(pb[:, j * KQ:(j + 1) * KQ],
                                      psA[:, reg * 1024: reg * 1024 + KQ],
                                      AF.Exp, bias=nms, scale=ATT_SCALE,
                                      accum_out=ssum), inc=s_ex)
                psA_q_last_exp[reg] = exv
            V.w(s_ex, s_ex.n)
            tot = stats[:, sb_ + 24: sb_ + 25]
            rcp = stats[:, sb_ + 25: sb_ + 26]
            V(lambda e, tot=tot, sb_=sb_: e.reduce_sum(
                tot, stats[:, sb_ + 20: sb_ + 24], axis=AX))
            V(lambda e, rcp=rcp, tot=tot: e.reciprocal(rcp, tot))
            pnv = V(lambda e, pb=pb, rcp=rcp: e.tensor_scalar_mul(
                pb, pb, rcp), inc=s_pn)
            if pending is not None:
                emit_transp(pending)
                if qc % QCB == 0 and qc > 0:
                    emit_pv(qc // QCB - 1)
            pending = (g, pb, (qc % QCB) * 128, pnv)
            if next_gen is not None:
                next(next_gen, None)
            if q3_box[0] is not None:
                for _ in range(6):
                    next(q3_box[0], None)
        emit_transp(pending)
        emit_pv(QC // QCB - 1)
        G.w(s_pq, s_pq.n)
        av = 0
        for j in range(4):
            for tcl in range(TC):
                av = G.dma(a2a_in[h, j, tcl],
                           attnT[:, j * W1 + tcl * 128: j * W1 + (tcl + 1) * 128],
                           s_a2)
        attnT_last_store[0] = av
        G.w(s_a2, s_a2.n)
        cc_a2a_h[h] = G(lambda e, h=h: e.collective_compute(
            "AllGather", OP.bypass, replica_groups=[[0, 1, 2, 3], [4, 5, 6, 7]],
            ins=[a2a_in.ap()[h].opt()], outs=[a2a_out.ap()[h].opt()]), inc=s_cc)

    q3_done = [0]

    def quant3a_gen():
        Y.w(s_tc, max(kr_mm, qr_mm_last))
        Y.w(s_vc, partB_done)
        Y.w(s_sc, s_sc.n)
        Y.w(s_q2, s_q2.n)
        Y.w(s_ag, s_ag.n)
        x_next = Y.dma(p3xA.ap(), x_in[0:128, :], s_xs[0])
        qv = 0
        for t in range(TC):
            V.w(s_xs[0], x_next)
            V(lambda e: e.reduce_sum(negmu, p3xA.ap(), axis=AX, negate=True))
            V(lambda e: e.tensor_scalar_mul(negmu, negmu, 1.0 / D))
            vv = V(lambda e: e.tensor_scalar_add(q_c.ap(), p3xA.ap(), negmu),
                   inc=s_vc)
            yield
            Sc.w(s_vc, vv)
            Sc.w(s_q2, s_q2.n)
            sv = Sc(lambda e: e.activation(p3xA.ap(), q_c.ap(), AF.Square,
                                           accum_out=varl), inc=s_sc)
            V.w(s_sc, sv)
            vv = V(lambda e: e.tensor_scalar(varl, varl, 1.0 / D, float(EPS),
                                             OP.mult, OP.add), inc=s_vc)
            Sc.w(s_vc, vv)
            Sc(lambda e: e.activation(rstd1, varl, AF.Ln))
            sv = Sc(lambda e: e.activation(rstd1, rstd1, AF.Exp, scale=-0.5),
                    inc=s_sc)
            V.w(s_sc, sv)
            V(lambda e: e.tensor_scalar_mul(q_c.ap(), q_c.ap(), rstd1))
            V.w(s_xq, s_xq.n)
            yield
            for hf in range(2):
                for r in nvfp4_gen(q_c[:, hf * 1024:(hf + 1) * 1024], 1024,
                                   q_bs.ap(), q_sA.ap(), q_sB.ap(),
                                   cc=q_rB.ap(), rA=q_rA.ap(), rB=q_rC.ap(),
                                   rC=q_rD.ap(), rD=p3xA[:, 0:1024],
                                   rfac=wsc[:, 7:8], ts_lane=xtsb.ap(),
                                   dst=p3xdq[:, hf * 1024:(hf + 1) * 1024],
                                   post=wsc[:, 6:7]):
                    if r is not None:
                        qv = r
                    yield
            G.w(s_vc, qv)
            G.dma(xdq_scr[t], p3xdq.ap(), s_xq)
            if t + 1 < TC:
                Y.w(s_vc, qv)
                x_next = Y.dma(p3xA.ap(), x_in[(t + 1) * 128:(t + 2) * 128, :],
                               s_xs[0])
            yield
        q3_done[0] = qv

    cc_a2a_h = {}
    q3_box = [quant3a_gen()]
    cur_gen = proj_steps(0)
    cur_qr = next(cur_gen)
    for h in range(HG):
        for _ in cur_gen:
            pass
        if h + 1 < HG:
            nxt_gen = proj_steps(h + 1)
            nxt_qr = next(nxt_gen)
        else:
            nxt_gen, nxt_qr = None, 0
        attn_head(h, cur_qr, nxt_gen)
        head_qk_done[h] = s_qk.n
        cur_gen, cur_qr = nxt_gen, nxt_qr
    cc_a2a = cc_a2a_h[HG - 1]

    for _ in q3_box[0]:
        pass
    q3_box[0] = None
    quant_done = q3_done[0]
    T.w(s_ex, s_ex.n)

    Y.w(s_cc, cc_a2a)
    amv = 0
    for src_i in range(4):
        amv = s_am.inc(16)

        def _fa(e, si=src_i):
            sap = a2a_out.ap()[:, si][:, bass.ds(_rank_base(e), 1)].squeeze(1)
            return e.dma_start(out=a2a_mine[si * 4:(si + 1) * 4],
                               in_=sap).then_inc(s_am.h, 16)
        EM["sync"].append(_fa)
    a2a_mine_done = amv
    tile_mm = {}
    tile_cb = {}
    prev_mm = {0: None, 1: None}
    prev_cb = {0: None, 1: None}
    prev_ost = {0: 0, 1: 0}
    half_last_mm = 0
    for half in range(2):
        Y.w(s_cc, cc_wdq)
        Y.w(s_cc, cc_a2a)
        Y.w(s_vc, quant_done)
        G.w(s_cc, cc_a2a)
        if half == 1:
            Y.w(s_tc, half_last_mm)
            G.w(s_tc, half_last_mm)
        Y.w(s_am, s_am.n)
        Y.w(s_xq, s_xq.n)
        wv = 0
        for kcx in range(16):
            for sp in range(4):
                src = half * 4 + sp
                wv = Y.dma(p3wdq[:, kcx * 1024 + sp * 256: kcx * 1024 + (sp + 1) * 256],
                           wdq_out[src, kcx], s_wv)
        ov = 0
        for dvc in range(16):
            ov = G.dma(p3ow[:, dvc * 1024:(dvc + 1) * 1024], owT_b[half, dvc], s_p3g)
        for t in range(TC):
            tp = t % 2
            idx = half * TC + t
            if prev_mm[tp] is not None:
                Y.w(s_tc, prev_mm[tp])
                Y.w(s_vc, prev_cb[tp])
            av = 0
            for src in range(4):
                for hl in range(4):
                    av = Y.dma(p3at[:, tp * D + (src * 4 + hl) * 128:
                                    tp * D + (src * 4 + hl + 1) * 128],
                               a2a_mine[src * 4 + hl, t], s_p3t[tp])
            xv = Y.dma(p3xdqb[:, tp * D: tp * D + D], xdq_scr[t], s_p3t[tp])
            x3v = Y.dma(p3x[:, tp * D: tp * D + D], x_in[t * 128:(t + 1) * 128, :],
                        s_p3xl[tp])
            T.w(s_p3t[tp], xv)
            for gg in range(2):
                tiles = [p3xdqb[:, tp * D + (gg * 8 + i) * 128:
                                tp * D + (gg * 8 + i + 1) * 128] for i in range(8)]

                def evx(hidx, gg=gg, tp=tp):
                    Sc.w(s_tp, s_tp.n)
                    dst = p3xdqT[:, tp * D + gg * 1024: tp * D + (gg + 1) * 1024]
                    return Sc(lambda e, d=dst, h=hidx: e.copy(
                        out=d, in_=psC[:, h * 1024:(h + 1) * 1024]), inc=s_pe)
                transpose_group(tiles, evx)
            T.w(s_pe, s_pe.n)
            if t == 0:
                T.w(s_wv, wv)
                T.w(s_p3g, ov)
            side = t % 2
            for kcx in range(16):
                T_mm(psA[:, side * 1024:(side + 1) * 1024],
                     p3xdqT[:, tp * D + kcx * 128: tp * D + (kcx + 1) * 128],
                     p3wdq[:, kcx * 1024:(kcx + 1) * 1024], kcx == 0, False)
            mmv = 0
            for dvc in range(16):
                mmv = T_mm(psA[:, side * 1024:(side + 1) * 1024],
                           p3at[:, tp * D + dvc * 128: tp * D + (dvc + 1) * 128],
                           p3ow[:, dvc * 1024:(dvc + 1) * 1024],
                           False, dvc == 15, inc=s_tc)
            tile_mm[idx] = mmv
            if t == TC - 1:
                half_last_mm = mmv
            V.w(s_p3xl[tp], x3v)
            V.w(s_ost[tp], prev_ost[tp])
            V(lambda e, tp=tp, half=half: e.tensor_tensor(
                out=p3xb[:, tp * 1024:(tp + 1) * 1024],
                in0=p3x[:, tp * D + half * 1024: tp * D + half * 1024 + 1024],
                in1=biasb[:, half * 1024:(half + 1) * 1024], op=OP.add))
            V.w(s_tc, mmv)
            cbv = V(lambda e, tp=tp, side=side: e.tensor_tensor(
                out=p3xb[:, tp * 1024:(tp + 1) * 1024],
                in0=p3xb[:, tp * 1024:(tp + 1) * 1024],
                in1=psA[:, side * 1024:(side + 1) * 1024], op=OP.add), inc=s_vc)
            tile_cb[idx] = cbv
            T.w(s_vc, cbv)
            G.w(s_vc, cbv)
            prev_ost[tp] = G.dma(
                out_t[t * 128:(t + 1) * 128, half * 1024:half * 1024 + 1024],
                p3xb[:, tp * 1024:(tp + 1) * 1024], s_ost[tp])
            prev_mm[tp] = mmv
            prev_cb[tp] = cbv

    if dbg:
        G.dma(dbg_a2a.ap(), a2a_in.ap(), s_out)
        G.dma(dbg_agin.ap(), ag_in.ap(), s_out)
        G.dma(dbg_agout[0:4, 0:12], ag_out_a.ap(), s_out)
        G.dma(dbg_kn.ap(), knT.ap(), s_out)
        G.dma(dbg_qn.ap(), qnT.ap(), s_out)
        G.dma(dbg_wsc.ap(), wsc.ap(), s_out)
        G.dma(dbg_att.ap(), attnT.ap(), s_out)
        G.dma(dbg_p.ap(), pbuf[:, 0:S], s_out)
        G.dma(dbg_st.ap(), stats.ap(), s_out)
    G.w(s_out, s_out.n)
    G.w(s_ost[0], s_ost[0].n)
    G.w(s_ost[1], s_ost[1].n)
    G(lambda e: e.engine_nop(), inc=s_fin)
    G.w(s_fin, s_fin.n)

    with nc.Block() as block:
        @block.sync
        def _(eng):
            for f in EM["sync"]:
                f(eng)

        @block.tensor
        def _(eng):
            for f in EM["tensor"]:
                f(eng)

        @block.vector
        def _(eng):
            for f in EM["vector"]:
                f(eng)

        @block.scalar
        def _(eng):
            for f in EM["scalar"]:
                f(eng)

        @block.gpsimd
        def _(eng):
            for f in EM["gpsimd"]:
                f(eng)

    return nc


def _rope_tables(S):
    inv = 1.0 / 10000.0 ** (np.arange(0, DR, 2, dtype=np.float32) / DR)
    t = np.arange(S, dtype=np.float32)
    fr = np.outer(t, inv).astype(np.float32)
    emb = np.concatenate([fr, fr], -1)
    return np.cos(emb).astype(np.float32), np.sin(emb).astype(np.float32)


def stage_inputs(hs, q_a_w, q_a_ln_w, q_a_ln_b, q_b_w, q_rope_w, kv_a_w,
                 kv_a_ln_w, kv_a_ln_b, kv_b_w, o_w, ffn_w, ffn_b, x_ts, S):
    B = 2
    NT = B * S // NC
    xf = np.ascontiguousarray(hs.reshape(B * S, D))
    cos, sin = _rope_tables(S)
    sfold = np.concatenate([-sin[:, 0:32], sin[:, 32:64]], axis=1)

    def colblk(mat, nkc, nw=128):
        K, N = mat.shape
        return np.ascontiguousarray(
            mat.reshape(nkc, 128, N // nw, nw).transpose(2, 1, 0, 3))

    qaT = np.ascontiguousarray(q_a_w.T)
    kvaT = np.ascontiguousarray(kv_a_w.T)
    qrwT = np.ascontiguousarray(q_rope_w.T)
    owT = np.ascontiguousarray(o_w.T)

    qaT_b = colblk(qaT, 16)
    kvaT_b = colblk(kvaT[:, 0:KVL], 16)
    kvrT_b = np.ascontiguousarray(
        kvaT[:, KVL:KVL + DR].reshape(16, 128, DR).transpose(1, 0, 2))
    qrwT_b = colblk(qrwT, 12)
    owT_hb = np.ascontiguousarray(
        owT.reshape(16, 128, 2, 1024).transpose(2, 0, 1, 3))

    shared = dict(
        qaT_b=qaT_b, kvaT_b=kvaT_b, kvrT_b=kvrT_b, qrwT_b=qrwT_b, owT_b=owT_hb,
        qalnw=np.ascontiguousarray(q_a_ln_w.reshape(12, 128, 1)),
        qalnb=np.ascontiguousarray(q_a_ln_b.reshape(12, 128, 1)),
        kvalnw=np.ascontiguousarray(kv_a_ln_w.reshape(4, 128, 1)),
        kvalnb=np.ascontiguousarray(kv_a_ln_b.reshape(4, 128, 1)),
        bias_rep=np.ascontiguousarray(np.broadcast_to(ffn_b.reshape(1, D), (128, D))),
        xts_rep=np.full((128, 1), np.float32(x_ts), np.float32),
        ident=np.eye(128, dtype=ml_dtypes.bfloat16),
        identf=np.eye(128, dtype=np.float32),
    )
    ins = []
    for c in range(NC):
        hg0 = 4 * (c % 4)
        pos = (c % 4) * NT + np.arange(NT)
        qb_rows = np.concatenate(
            [q_b_w[128 * (hg0 + i):128 * (hg0 + i + 1)] for i in range(HG)], 0)
        kn_rows = np.concatenate(
            [kv_b_w[256 * (hg0 + i):256 * (hg0 + i) + 128] for i in range(HG)], 0)
        v_rows = np.concatenate(
            [kv_b_w[256 * (hg0 + i) + 128:256 * (hg0 + i + 1)] for i in range(HG)], 0)
        qbT = np.ascontiguousarray(qb_rows.T)
        knT_ = np.ascontiguousarray(kn_rows.T)
        vT_ = np.ascontiguousarray(v_rows.T)
        d = dict(shared)
        d.update(
            x=np.ascontiguousarray(xf[c * NT:(c + 1) * NT]),
            qbT_b=np.ascontiguousarray(
                qbT.reshape(12, 128, HG, 128).transpose(2, 1, 0, 3)),
            kvbknT_b=np.ascontiguousarray(
                knT_.reshape(4, 128, HG, 128).transpose(2, 1, 0, 3)),
            kvbvT_b=np.ascontiguousarray(
                vT_.reshape(4, 128, HG, 128).transpose(2, 1, 0, 3)),
            ffnw=np.ascontiguousarray(
                ffn_w[256 * c:256 * (c + 1)].reshape(2, 128, D)),
            ktcos=np.ascontiguousarray(cos[pos]),
            ktsf=np.ascontiguousarray(sfold[pos]),
            fcos2=np.ascontiguousarray(np.concatenate([cos[pos].T, cos[pos].T], 0)),
            fsf2=np.ascontiguousarray(np.concatenate([sfold[pos].T, sfold[pos].T], 0)),
        )
        ins.append(d)
    return ins


def kernel(L0_attn_hidden_states, L0_attn_q_a_w, L0_attn_q_a_ln_w,
           L0_attn_q_a_ln_b, L0_attn_q_b_w, L0_attn_q_rope_w, L0_attn_kv_a_w,
           L0_attn_kv_a_ln_w, L0_attn_kv_a_ln_b, L0_attn_kv_b_w, L0_attn_o_w,
           L0_ffn_weight, L0_ffn_bias, L0_ffn_x_tensor_scale):
    hs = np.asarray(L0_attn_hidden_states, dtype=np.float32)
    B, S, _ = hs.shape
    in_maps = stage_inputs(
        hs, np.asarray(L0_attn_q_a_w, np.float32),
        np.asarray(L0_attn_q_a_ln_w, np.float32),
        np.asarray(L0_attn_q_a_ln_b, np.float32),
        np.asarray(L0_attn_q_b_w, np.float32),
        np.asarray(L0_attn_q_rope_w, np.float32),
        np.asarray(L0_attn_kv_a_w, np.float32),
        np.asarray(L0_attn_kv_a_ln_w, np.float32),
        np.asarray(L0_attn_kv_a_ln_b, np.float32),
        np.asarray(L0_attn_kv_b_w, np.float32),
        np.asarray(L0_attn_o_w, np.float32),
        np.asarray(L0_ffn_weight, np.float32),
        np.asarray(L0_ffn_bias, np.float32),
        np.float32(L0_ffn_x_tensor_scale), S)
    nc = build_program(S)
    res = run_bass_kernel_spmd(nc, in_maps, core_ids=list(range(NC)))
    NT = B * S // NC
    out = np.concatenate([np.asarray(res.results[c]["out"]) for c in range(NC)],
                         axis=0)
    return out.reshape(B, S, D).astype(np.float32)


# version 13
# speedup vs baseline: 1.0556x; 1.0017x over previous
import os

import numpy as np
import ml_dtypes

import concourse.bass as bass
import concourse.mybir as mybir
from concourse.bass_utils import run_bass_kernel_spmd
from concourse.aot_env import allow_ici_all_to_all

F32 = mybir.dt.float32
BF16 = mybir.dt.bfloat16
OP = mybir.AluOpType
AF = mybir.ActivationFunctionType
AX = mybir.AxisListType.X

D = 2048
QL = 1536
KVL = 512
H = 16
DR = 64
DV = 128
EPS = 1e-5
ATT_SCALE = float(1.0 / np.sqrt(128 + 64))
NC = 8
HG = 4

C_RNE = float(np.float32(1.5 * 2 ** 23))
VELT = float(np.float32(2 ** 20 + 1))
M_SUB = float(np.float32(1.5 * 2 ** 14))
E4_MIN = float(np.float32(2 ** -6))


class Sem:
    __slots__ = ("h", "n", "name")

    def __init__(self, h, name):
        self.h, self.n, self.name = h, 0, name

    def inc(self, k=1):
        self.n += k
        return self.n


def build_program(S=4096, dbg=False):
    B = 2
    NT = B * S // NC
    TC = NT // 128
    W1 = NT
    KQ = S // 4
    QC = S // 128
    KC = S // 128
    QB = min(512, S)
    QCB = QB // 128
    NTW = max(1, S // 512)
    TWW = min(512, S)
    AGW = max(W1, 1024)
    assert KQ <= 1024 and NT % 128 == 0

    nc = bass.Bass("TRN2", target_bir_lowering=False, debug=False,
                   detect_race_conditions=bool(os.environ.get("KRACE")))
    st = nc._state

    def XI(name, shape, dtype=F32):
        return nc.dram_tensor(name, list(shape), dtype, kind="ExternalInput")

    x_in = XI("x", [NT, D])
    qaT_b = XI("qaT_b", [12, 128, 16, 128])
    kvaT_b = XI("kvaT_b", [4, 128, 16, 128])
    kvrT_b = XI("kvrT_b", [128, 16, DR])
    qrwT_b = XI("qrwT_b", [8, 128, 12, 128])
    qbT_b = XI("qbT_b", [HG, 128, 12, 128])
    kvbknT_b = XI("kvbknT_b", [HG, 128, 4, 128])
    kvbvT_b = XI("kvbvT_b", [HG, 128, 4, 128])
    owT_b = XI("owT_b", [2, 16, 128, 1024])
    ffnw_in = XI("ffnw", [2, 128, D])
    qalnw_in = XI("qalnw", [12, 128, 1])
    qalnb_in = XI("qalnb", [12, 128, 1])
    kvalnw_in = XI("kvalnw", [4, 128, 1])
    kvalnb_in = XI("kvalnb", [4, 128, 1])
    bias_in = XI("bias_rep", [128, D])
    xts_in = XI("xts_rep", [128, 1])
    ident_in = XI("ident", [128, 128], BF16)
    identf_in = XI("identf", [128, 128], F32)
    ktcos_in = XI("ktcos", [NT, DR])
    ktsf_in = XI("ktsf", [NT, DR])
    fcos_in = XI("fcos2", [128, W1])
    fsf_in = XI("fsf2", [128, W1])

    out_t = nc.dram_tensor("out", [NT, D], F32, kind="ExternalOutput")

    NAGC = 25
    ag_in = nc.dram_tensor("ag_in", [NAGC, 128, W1], BF16)
    ag_out_a = nc.dram_tensor("ag_out_a", [4, 12, 128, W1], BF16)
    ag_out_b = nc.dram_tensor("ag_out_b", [4, 5, 128, W1], BF16)
    ag_out_c = nc.dram_tensor("ag_out_c", [4, 8, 128, W1], BF16)
    a2a_in = nc.dram_tensor("a2a_in", [HG, 4, TC, 128, 128], BF16)
    a2a_out = nc.dram_tensor("a2a_out", [HG, 4, 4, TC, 128, 128], BF16)
    amax_in = nc.dram_tensor("amax_in", [1, 16], F32)
    amax_out = nc.dram_tensor("amax_out", [NC, 1, 16], F32,
                              addr_space="Shared")
    wdq_in = nc.dram_tensor("wdq_in", [16, 128, 256], BF16)
    a2a_mine = nc.dram_tensor("a2a_mine", [16, TC, 128, 128], BF16)
    wdq_out = nc.dram_tensor("wdq_out", [NC, 16, 128, 256], BF16,
                             addr_space="Shared")
    xdq_scr = nc.dram_tensor("xdq_scr", [TC, 128, D], BF16,
                             kind="ExternalOutput" if dbg else "Internal")
    dbg_wsc = nc.dram_tensor("dbg_wsc", [128, 16], F32, kind="ExternalOutput") \
        if dbg else None
    dbg_att = nc.dram_tensor("dbg_att", [128, S], BF16, kind="ExternalOutput") \
        if dbg else None
    dbg_p = nc.dram_tensor("dbg_p", [128, S], BF16, kind="ExternalOutput") \
        if dbg else None
    dbg_st = nc.dram_tensor("dbg_st", [128, 64], F32, kind="ExternalOutput") \
        if dbg else None
    dbg_agin = nc.dram_tensor("dbg_agin", [NAGC, 128, W1], BF16,
                              kind="ExternalOutput") if dbg else None
    dbg_a2a = nc.dram_tensor("dbg_a2a", [HG, 4, TC, 128, 128], BF16,
                             kind="ExternalOutput") if dbg else None
    dbg_ptb = nc.dram_tensor("dbg_ptb", [128, (S // 128) * QB], BF16,
                             kind="ExternalOutput") if dbg else None
    dbg_pb2 = nc.dram_tensor("dbg_pb2", [128, 2 * S], BF16,
                             kind="ExternalOutput") if dbg else None
    dbg_st2 = nc.dram_tensor("dbg_st2", [128, 64], F32,
                             kind="ExternalOutput") if dbg else None
    dbg_agout = nc.dram_tensor("dbg_agout", [4, NAGC, 128, W1], BF16,
                               kind="ExternalOutput") if dbg else None
    dbg_kn = nc.dram_tensor("dbg_kn", [128, 2 * S], BF16,
                            kind="ExternalOutput") if dbg else None
    dbg_xnT = nc.dram_tensor("dbg_xnT", [128, 16 * W1], BF16,
                             kind="ExternalOutput") if dbg else None
    dbg_qn = nc.dram_tensor("dbg_qn", [128, 2 * S], BF16,
                            kind="ExternalOutput") if dbg else None

    SB_BASE = 16512
    SB_TOP = 229344
    offs = {}

    def _alloc(name, shape, dtype, off):
        itb = 2 if dtype == BF16 else 4
        per = int(np.prod(shape[1:])) * itb
        assert off + per <= SB_TOP, (name, off, per)
        offs[name] = (off, per)
        return st.alloc_sbuf_tensor_at(name, list(shape), dtype, offset=off), off + per

    cur = SB_BASE

    def FX(name, shape, dtype):
        nonlocal cur
        off = (cur + 31) // 32 * 32
        t, cur = _alloc(name, shape, dtype, off)
        return t

    ident = FX("identW", [128, 128], BF16)
    identf = FX("identF", [128, 128], F32)
    onesb = FX("onesb", [128, 128], BF16)
    ones128f = FX("ones128f", [128, 128], F32)
    xtsb = FX("xtsb", [128, 1], F32)
    wsc = FX("wsc", [128, 16], F32)
    stats = FX("stats", [128, 64], F32)
    sconst = FX("sconst", [128, 8], F32)
    biasb = FX("biasb", [128, D], F32)
    fws = FX("fws", [128, 256], F32)
    lnwq = FX("lnwq", [128, 12], F32)
    lnbq = FX("lnbq", [128, 12], F32)
    lnwk = FX("lnwk", [128, 4], F32)
    lnbk = FX("lnbk", [128, 4], F32)
    warm = FX("warm", [128, 2], F32)
    AB = (cur + 255) // 256 * 256

    def mkarena():
        pos = [AB]

        def A(name, shape, dtype):
            off = (pos[0] + 31) // 32 * 32
            t, pos[0] = _alloc(name, shape, dtype, off)
            return t
        return A, pos

    A1, a1pos = mkarena()
    xtile = A1("xtile", [128, 2 * D], F32)
    xnbf = A1("xnbf", [128, D], BF16)
    xcs = A1("xcs", [128, D], F32)
    sqf = A1("sqf", [128, D], F32)
    xnT = A1("xnT", [128, 16 * W1], BF16)
    wcb = A1("wcb", [128, 4 * 2048], BF16)
    qlat = A1("qlat", [128, 12 * W1], BF16)
    sqb = A1("sqb", [128, 2 * W1], BF16)
    mub = A1("mub", [128, W1], F32)
    rstdb = A1("rstdb", [128, W1], F32)
    scr1 = A1("scr1", [128, max(W1, 128)], F32)
    qrev = A1("qrev", [128, W1], F32)
    qrrot = A1("qrrot", [128, W1], F32)
    kvlT = A1("kvlT", [128, 4 * W1], BF16)
    krT = A1("krT", [64, W1], BF16)
    krop = A1("krop", [128, DR], F32)
    agst = A1("agst", [128, AGW], BF16)
    kcosb = A1("kcosb", [128, TC * DR], F32)
    ksfb = A1("ksfb", [128, TC * DR], F32)
    fcosb = A1("fcosb", [128, W1], F32)
    fsfb = A1("fsfb", [128, W1], F32)
    fwn = A1("fwn", [128, D], F32)
    fwq = A1("fwq", [128, D], BF16)

    A2, a2pos = mkarena()
    knT = A2("knT", [128, 2 * S], BF16)
    vTb = A2("vTb", [128, 2 * S], BF16)
    qnT = A2("qnT", [128, 2 * S], BF16)
    qrTh = A2("qrTh", [64, 2 * S], BF16)
    krTg = A2("krTg", [64, S], BF16)
    latq = A2("latq", [128, 4 * TWW], BF16)
    pbuf = A2("pbuf", [128, 2 * S], BF16)
    ptb = A2("ptb", [128, KC * QB], BF16)
    attnT = A2("attnT", [128, S], BF16)
    qbTb = A2("qbTb", [128, 12 * 128], BF16)
    kvknb = A2("kvknb", [128, 4 * 128], BF16)
    kvvb = A2("kvvb", [128, 4 * 128], BF16)

    A3, a3pos = mkarena()
    p3x = A3("p3x", [128, 2 * D], F32)
    p3xdqb = A3("p3xdqb", [128, 2 * D], BF16)
    p3xdqT = A3("p3xdqT", [128, 2 * D], BF16)
    p3at = A3("p3at", [128, 2 * D], BF16)
    p3wdq = A3("p3wdq", [128, 16 * 1024], BF16)
    p3ow = A3("p3ow", [128, 16 * 1024], BF16)
    p3xb = A3("p3xb", [128, 2 * 1024], F32)

    a5pos = [(a2pos[0] + 255) // 256 * 256]

    def A5(name, shape, dtype):
        off = (a5pos[0] + 31) // 32 * 32
        t, a5pos[0] = _alloc(name, shape, dtype, off)
        return t

    p3xA = A5("p3xA", [128, D], F32)
    p3xdq = A5("p3xdq", [128, D], BF16)

    TAIL = (max(a1pos[0], a2pos[0], a3pos[0], a5pos[0]) + 255) // 256 * 256
    tl = [TAIL]

    def A4(name, shape, dtype):
        off = (tl[0] + 31) // 32 * 32
        t, tl[0] = _alloc(name, shape, dtype, off)
        return t

    q_c = A4("q_c", [128, D], F32)
    q_rA = A4("q_rA", [128, 1024], F32)
    q_rB = A4("q_rB", [128, 1024], F32)
    q_rC = A4("q_rC", [128, 1024], F32)
    q_rD = A4("q_rD", [128, 1024], F32)
    q_bs = A4("q_bs", [128, 64], F32)
    q_sA = A4("q_sA", [128, 64], F32)
    q_sB = A4("q_sB", [128, 64], F32)

    psA = st.alloc_psum_tensor("psA", [128, 2048], F32)
    psB = st.alloc_psum_tensor("psB", [128, 512], F32)
    psC = st.alloc_psum_tensor("psC", [128, 2048], BF16)
    psD = st.alloc_psum_tensor("psD", [128, 512], F32)

    EM = {k: [] for k in ("sync", "tensor", "vector", "scalar", "gpsimd")}

    class Q:
        def __init__(self, name):
            self.name = name

        def __call__(self, fn, inc=None, k=1):
            drain = self.name in ("vector", "scalar")
            if inc is None:
                EM[self.name].append(fn)
                if drain:
                    EM[self.name].append(lambda e: e.drain())
                return None
            v = inc.inc(k)
            if drain:
                EM[self.name].append(fn)
                EM[self.name].append(
                    lambda e, s=inc, kk=k: e.drain().then_inc(s.h, kk))
            else:
                EM[self.name].append(lambda e, f=fn, s=inc, kk=k: f(e).then_inc(s.h, kk))
            return v

        def w(self, sem, val):
            if val and val > 0:
                EM[self.name].append(lambda e, s=sem, v=val: e.wait_ge(s.h, v))

        def dma(self, out, in_, sem, slow=False, dyn=None):
            v = sem.inc(16)
            if dyn is None:
                EM[self.name].append(
                    lambda e, o=out, i=in_, s=sem, sl=slow: e.dma_start(
                        out=o, in_=i, allow_slow_non_contiguous=sl).then_inc(s.h, 16))
            else:
                ten, idx = dyn

                def _f(e, o=out, tn=ten, ix=idx, s=sem):
                    si, hl, tt = ix
                    src = tn.ap()[si][bass.ds(_rank_base(e), 1)].squeeze(0)[hl, tt]
                    return e.dma_start(out=o, in_=src).then_inc(s.h, 16)
                EM[self.name].append(_f)
            return v

    _rank_cache = {}

    def _rank_base(e):
        key = id(e)
        if key not in _rank_cache:
            _rank_cache[key] = e.compute_val(e.partition_id() % 4)
        return _rank_cache[key]

    _rank2_cache = {}

    def _rank2c(e, c0):
        key = (id(e), c0)
        if key not in _rank2_cache:
            _rank2_cache[key] = e.compute_val((e.partition_id() % 4) * 2 + c0)
        return _rank2_cache[key]

    T, V, Sc, G, Y = Q("tensor"), Q("vector"), Q("scalar"), Q("gpsimd"), Q("sync")

    def T_mm(ps_ap, lhs_ap, rhs_ap, start, stop, inc=None, nmax=512):
        n = rhs_ap.shape[-1]
        vals = None
        for o in range(0, n, nmax):
            w = min(nmax, n - o)
            last = o + w >= n
            vals = T(lambda e, p=ps_ap[:, o:o + w] if ps_ap.ndim == 2 else None,
                     l=lhs_ap, r=rhs_ap[:, o:o + w], st_=start, sp=stop:
                     e.matmul(p, l, r, start=st_, stop=sp),
                     inc=inc if last else None)
        return vals

    def mks(name):
        return Sem(st.alloc_semaphore(name), name)

    s_ld = mks("ld")
    s_x = mks("x")
    s_wg = mks("wg")
    s_tc = mks("tc")
    s_vc = mks("vc")
    s_sc = mks("sc")
    s_tp = mks("tp")
    s_pe = mks("pe")
    s_ag = mks("ag")
    s_cc = mks("cc")
    s_qk = mks("qk")
    s_mx = mks("mx")
    s_ex = mks("ex")
    s_ar = mks("ar")
    s_fe = mks("fe")
    s_pn = mks("pn")
    s_pv = mks("pv")
    s_pq = mks("pq")
    s_a2 = mks("a2")
    s_lat = mks("lat")
    s_latq = [mks(f"latq{i}") for i in range(4)]
    s_xs = [mks("xs0"), mks("xs1")]
    s_wgs = [mks("wgs0"), mks("wgs1"), mks("wgs2"), mks("wgs3")]
    s_fw = mks("fw")
    s_hw = mks("hw")
    s_qr = [mks("qr0"), mks("qr1")]
    s_krg = mks("krg")
    s_p3t = [mks("p3t0"), mks("p3t1")]
    s_p3xl = [mks("p3xl0"), mks("p3xl1")]
    s_ost = [mks("ost0"), mks("ost1")]
    s_wv = mks("wv")
    s_am = mks("am")
    s_rot = mks("rot")
    s_q2 = mks("q2")
    s_p3 = mks("p3")
    s_p3g = mks("p3g")
    s_xq = mks("xq")
    s_out = mks("out")
    s_fin = mks("fin")

    Y.dma(ident.ap(), ident_in.ap(), s_ld)
    Y.dma(identf.ap(), identf_in.ap(), s_ld)
    Y.dma(biasb.ap(), bias_in.ap(), s_ld)
    Y.dma(xtsb.ap(), xts_in.ap(), s_ld)
    for j in range(12):
        Y.dma(lnwq[:, j:j + 1], qalnw_in[j], s_ld)
        Y.dma(lnbq[:, j:j + 1], qalnb_in[j], s_ld)
    for j in range(4):
        Y.dma(lnwk[:, j:j + 1], kvalnw_in[j], s_ld)
        Y.dma(lnbk[:, j:j + 1], kvalnb_in[j], s_ld)
    for t in range(TC):
        Y.dma(kcosb[:, t * DR:(t + 1) * DR], ktcos_in[t * 128:(t + 1) * 128, :], s_ld)
        Y.dma(ksfb[:, t * DR:(t + 1) * DR], ktsf_in[t * 128:(t + 1) * 128, :], s_ld)
    Y.dma(fcosb.ap(), fcos_in.ap(), s_ld)
    Y.dma(fsfb.ap(), fsf_in.ap(), s_ld)
    ld_all = s_ld.n
    for t in range(min(2, TC)):
        Y.dma(xtile[:, (t % 2) * D:(t % 2) * D + D], x_in[t * 128:(t + 1) * 128, :],
              s_xs[t % 2])
    fw0_ld = Y.dma(fwn.ap(), ffnw_in[0], s_fw)

    V.w(s_ld, ld_all)
    Sc.w(s_ld, ld_all)
    Sc(lambda e: e.activation(warm[:, 0:1], xtsb.ap(), AF.Exp,
                              accum_out=warm[:, 1:2]))
    Sc(lambda e: e.activation(warm[:, 0:1], xtsb.ap(), AF.Square))
    Sc(lambda e: e.activation(warm[:, 0:1], warm[:, 0:1], AF.Ln))
    V(lambda e: e.memset(onesb.ap(), 1.0))
    V(lambda e: e.memset(ones128f.ap(), 1.0))
    V(lambda e: e.memset(sconst.ap(), 0.0))
    V(lambda e: e.memset(wsc.ap(), 0.0))
    V(lambda e: e.memset(stats.ap(), 0.0))

    psc_last_evac = [0, 0]
    psc_used = [0, 0]
    psc_next = [0]

    def transpose_group(srcs, dst_cb):
        hidx = psc_next[0] % 2
        psc_next[0] += 1
        if psc_used[hidx] > 0:
            T.w(s_pe, psc_last_evac[hidx])
        n = len(srcs)
        for i, ap_ in enumerate(srcs):
            wdt = ap_.shape[-1]
            dst = psC[0:wdt, hidx * 1024 + i * 128: hidx * 1024 + i * 128 + 128]
            T(lambda e, d=dst, s=ap_: e.transpose(d, s, ident.ap()),
              inc=s_tp if i == n - 1 else None)
        psc_used[hidx] += 1
        psc_last_evac[hidx] = dst_cb(hidx)
        return hidx

    negmu = sconst[:, 0:1]
    varl = sconst[:, 1:2]
    rstd1 = sconst[:, 2:3]

    for t in range(TC):
        xt = xtile[:, (t % 2) * D:(t % 2) * D + D]
        V.w(s_xs[t % 2], 16 * (t // 2 + 1))
        V.w(s_tp, s_tp.n)
        V(lambda e, xt=xt: e.reduce_sum(negmu, xt, axis=AX, negate=True))
        V(lambda e: e.tensor_scalar_mul(negmu, negmu, 1.0 / D))
        vv = V(lambda e, xt=xt: e.tensor_scalar_add(xcs.ap(), xt, negmu), inc=s_vc)
        xt_free = vv
        Sc.w(s_vc, vv)
        sv = Sc(lambda e: e.activation(sqf.ap(), xcs.ap(), AF.Square, accum_out=varl),
                inc=s_sc)
        V.w(s_sc, sv)
        vv = V(lambda e: e.tensor_scalar(varl, varl, 1.0 / D, float(EPS),
                                         OP.mult, OP.add), inc=s_vc)
        Sc.w(s_vc, vv)
        Sc(lambda e: e.activation(rstd1, varl, AF.Ln))
        sv = Sc(lambda e: e.activation(rstd1, rstd1, AF.Exp, scale=-0.5), inc=s_sc)
        V.w(s_sc, sv)
        v_xn = V(lambda e: e.tensor_scalar_mul(xnbf.ap(), xcs.ap(), rstd1), inc=s_vc)
        for g in range(2):
            T.w(s_vc, v_xn)
            if t == 0 and g == 0:
                T.w(s_ld, ld_all)
            tiles = [xnbf[:, (g * 8 + i) * 128:(g * 8 + i + 1) * 128] for i in range(8)]

            def ev(hidx, t=t, g=g):
                Sc.w(s_tp, s_tp.n)
                dst = xnT.ap().rearrange("p (k w) -> p k w", w=W1)[
                    :, g * 8:g * 8 + 8, t * 128:(t + 1) * 128]
                src = psC.ap().rearrange("p (h k w) -> p h k w", h=2, w=128)[:, hidx]
                return Sc(lambda e, d=dst, s=src: e.copy(out=d, in_=s), inc=s_pe)
            transpose_group(tiles, ev)
        if t + 2 < TC:
            Y.w(s_vc, xt_free)
            Y.dma(xtile[:, ((t + 2) % 2) * D:((t + 2) % 2) * D + D],
                  x_in[(t + 2) * 128:(t + 3) * 128, :], s_xs[t % 2])
    xnT_ready = s_pe.n

    wslot_seq = [0]
    wslot_consumed = {}

    def load_wblock(dram_ap_3d, kn, wn):
        sl = wslot_seq[0] % 4
        wslot_seq[0] += 1
        if sl in wslot_consumed:
            G.w(s_tc, wslot_consumed[sl])
        flat = wcb[:, sl * 2048: sl * 2048 + kn * wn]
        dst = flat.rearrange("p (k w) -> p k w", w=wn)
        v = G.dma(dst, dram_ap_3d, s_wgs[sl])
        return flat, v, sl

    def p1_proj(n_chunks, w_blk, out_cb):
        for ncx in range(n_chunks):
            blk, dv, sl = load_wblock(w_blk[ncx], 16, 128)
            T.w(s_wgs[sl], dv)
            mmv = 0
            for kcx in range(16):
                lhs = blk[:, kcx * 128:(kcx + 1) * 128]
                rhs = xnT.ap().rearrange("p (k w) -> p k w", w=W1)[:, kcx]
                mmv = T_mm(psA[:, 0:W1], lhs, rhs, kcx == 0, kcx == 15, inc=s_tc)
            wslot_consumed[sl] = mmv
            out_cb(ncx, mmv)

    T.w(s_pe, xnT_ready)
    if dbg:
        G.w(s_pe, xnT_ready)
        G.dma(dbg_xnT.ap(), xnT.ap(), s_out)

    def qa_out(ncx, mmv):
        Sc.w(s_tc, mmv)
        ev = Sc(lambda e, n=ncx: e.copy(out=qlat[:, n * W1:(n + 1) * W1],
                                        in_=psA[:, 0:W1]), inc=s_sc)
        T.w(s_sc, ev)

    p1_proj(12, qaT_b, qa_out)

    def ln_feature_major(nch, buf, lnw, lnb, dim):
        T.w(s_sc, s_sc.n)
        for ncx in range(nch):
            T_mm(psA[:, 1024:1024 + W1], onesb.ap(), buf[:, ncx * W1:(ncx + 1) * W1],
                 ncx == 0, ncx == nch - 1, inc=s_tc)
        V.w(s_tc, s_tc.n)
        vv = V(lambda e: e.tensor_scalar_mul(mub.ap(), psA[:, 1024:1024 + W1],
                                             1.0 / dim), inc=s_vc)
        T.w(s_vc, vv)
        sq_mm = {}
        for ncx in range(nch):
            if ncx >= 2:
                Sc.w(s_tc, sq_mm[ncx - 2])
            sv = Sc(lambda e, n=ncx: e.activation(
                sqb[:, (n % 2) * W1:(n % 2) * W1 + W1],
                buf[:, n * W1:(n + 1) * W1], AF.Square), inc=s_sc)
            T.w(s_sc, sv)
            sq_mm[ncx] = T_mm(psA[:, 1024:1024 + W1], onesb.ap(),
                              sqb[:, (ncx % 2) * W1:(ncx % 2) * W1 + W1],
                              ncx == 0, ncx == nch - 1, inc=s_tc)
        V.w(s_tc, s_tc.n)
        V(lambda e: e.tensor_scalar_mul(rstdb.ap(), psA[:, 1024:1024 + W1], 1.0 / dim))
        V(lambda e: e.tensor_tensor(out=mub.ap(), in0=mub.ap(), in1=mub.ap(),
                                    op=OP.mult) if False else
          e.tensor_tensor(out=scr1[:, 0:W1], in0=mub.ap(), in1=mub.ap(), op=OP.mult))
        V(lambda e: e.tensor_tensor(out=rstdb.ap(), in0=rstdb.ap(), in1=scr1[:, 0:W1],
                                    op=OP.subtract))
        vv = V(lambda e: e.tensor_scalar_add(rstdb.ap(), rstdb.ap(), float(EPS)),
               inc=s_vc)
        Sc.w(s_vc, vv)
        Sc(lambda e: e.activation(rstdb.ap(), rstdb.ap(), AF.Ln))
        sv = Sc(lambda e: e.activation(rstdb.ap(), rstdb.ap(), AF.Exp, scale=-0.5),
                inc=s_sc)
        V.w(s_sc, sv)
        out_vals = []
        for ncx in range(nch):
            V(lambda e, n=ncx: e.tensor_tensor(
                out=scr1[:, 0:W1], in0=buf[:, n * W1:(n + 1) * W1], in1=mub.ap(),
                op=OP.subtract))
            V(lambda e: e.tensor_tensor(out=scr1[:, 0:W1], in0=scr1[:, 0:W1],
                                        in1=rstdb.ap(), op=OP.mult))
            out_vals.append(V(lambda e, n=ncx: e.tensor_scalar(
                buf[:, n * W1:(n + 1) * W1], scr1[:, 0:W1],
                lnw[:, n:n + 1], lnb[:, n:n + 1], OP.mult, OP.add), inc=s_vc))
        return out_vals

    V.w(s_ld, ld_all)
    qa_vals = ln_feature_major(12, qlat, lnwq, lnbq, QL)
    for ncx in range(12):
        G.w(s_vc, qa_vals[ncx])
        G.dma(ag_in[ncx], qlat[:, ncx * W1:(ncx + 1) * W1], s_ag)

    G.w(s_ag, s_ag.n)
    cc_a = G(lambda e: e.collective_compute(
        "AllGather", OP.bypass, replica_groups=[[0, 1, 2, 3], [4, 5, 6, 7]],
        ins=[ag_in.ap()[0:12].opt()], outs=[ag_out_a.ap().opt()]), inc=s_cc)

    def kv_out(ncx, mmv):
        Sc.w(s_tc, mmv)
        ev = Sc(lambda e, n=ncx: e.copy(out=kvlT[:, n * W1:(n + 1) * W1],
                                        in_=psA[:, 0:W1]), inc=s_sc)
        T.w(s_sc, ev)

    p1_proj(4, kvaT_b, kv_out)
    kv_vals = ln_feature_major(4, kvlT, lnwk, lnbk, KVL)
    for ncx in range(4):
        G.w(s_vc, kv_vals[ncx])
        G.dma(ag_in[20 + ncx], kvlT[:, ncx * W1:(ncx + 1) * W1], s_ag)

    kr_blk, kr_dv, kr_sl = load_wblock(kvrT_b.ap(), 16, DR)
    T.w(s_wgs[kr_sl], kr_dv)
    kr_mm = 0
    for t in range(TC):
        for kcx in range(16):
            lhs = xnT.ap().rearrange("p (k w) -> p k w", w=W1)[
                :, kcx, t * 128:(t + 1) * 128]
            rhs = kr_blk[:, kcx * DR:(kcx + 1) * DR]
            kr_mm = T(lambda e, l=lhs, r=rhs, st_=(kcx == 0), sp=(kcx == 15):
                      e.matmul(psB[:, 0:DR], l, r, start=st_, stop=sp), inc=s_tc)
        V.w(s_tc, kr_mm)
        V.w(s_tp, s_tp.n)
        cosr = kcosb[:, t * DR:(t + 1) * DR]
        sfr = ksfb[:, t * DR:(t + 1) * DR]
        vv = V(lambda e: e.tensor_copy(out=krop.ap(), in_=psB[:, 0:DR]), inc=s_vc)
        T.w(s_vc, vv)

        def _rope_a(e, cosr=cosr, sfr=sfr):
            e.tensor_tensor(out=scr1[:, 0:DR], in0=krop.ap(), in1=cosr, op=OP.mult)
            e.tensor_tensor(out=scr1[:, DR:DR + 32], in0=krop[:, 32:64],
                            in1=sfr[:, 0:32], op=OP.mult)
            return e.tensor_tensor(out=scr1[:, DR + 32:DR + 64], in0=krop[:, 0:32],
                                   in1=sfr[:, 32:64], op=OP.mult)
        V(_rope_a)
        vv = V(lambda e: e.tensor_tensor(out=xnbf[:, 0:DR], in0=scr1[:, 0:DR],
                                         in1=scr1[:, DR:2 * DR], op=OP.add),
               inc=s_vc)
        T.w(s_vc, vv)

        def evk(hidx, t=t):
            Sc.w(s_tp, s_tp.n)
            return Sc(lambda e, t_=t, h=hidx: e.copy(
                out=krT[0:64, t_ * 128:(t_ + 1) * 128],
                in_=psC[0:64, h * 1024:h * 1024 + 128]), inc=s_pe)
        transpose_group([xnbf[:, 0:DR]], evk)
    wslot_consumed[kr_sl] = kr_mm
    G.w(s_pe, s_pe.n)
    G.dma(ag_in[24, 0:64, :], krT.ap(), s_ag)
    G.dma(ag_in[24, 64:128, :], krT.ap(), s_ag)
    G.w(s_ag, s_ag.n)
    cc_b = G(lambda e: e.collective_compute(
        "AllGather", OP.bypass, replica_groups=[[0, 1, 2, 3], [4, 5, 6, 7]],
        ins=[ag_in.ap()[20:25].opt()], outs=[ag_out_b.ap().opt()]), inc=s_cc)

    qr_mm_last = 0
    for ncx in range(8):
        blk, dv, sl = load_wblock(qrwT_b[ncx], 12, 128)
        T.w(s_wgs[sl], dv)
        mmv = 0
        for kcx in range(12):
            lhs = blk[:, kcx * 128:(kcx + 1) * 128]
            rhs = qlat.ap().rearrange("p (k w) -> p k w", w=W1)[:, kcx]
            mmv = T_mm(psA[:, 0:W1], lhs, rhs, kcx == 0, kcx == 11, inc=s_tc)
        wslot_consumed[sl] = mmv
        qr_mm_last = mmv
        Sc.w(s_tc, mmv)
        Sc.w(s_rot, s_rot.n)
        Sc.w(s_vc, s_vc.n)
        ev = Sc(lambda e: e.copy(out=qrev.ap(), in_=psA[:, 0:W1]), inc=s_sc)
        T.w(s_sc, ev)
        G.w(s_sc, ev)
        G.dma(qrrot[0:32, :], qrev[32:64, :], s_rot)
        G.dma(qrrot[32:64, :], qrev[0:32, :], s_rot)
        G.dma(qrrot[64:96, :], qrev[96:128, :], s_rot)
        rv = G.dma(qrrot[96:128, :], qrev[64:96, :], s_rot)
        V.w(s_rot, rv)
        V.w(s_ld, ld_all)
        V.w(s_ag, s_ag.n)
        V(lambda e: e.tensor_tensor(out=qrrot.ap(), in0=qrrot.ap(), in1=fsfb.ap(),
                                    op=OP.mult))
        V(lambda e: e.tensor_tensor(out=qrev.ap(), in0=qrev.ap(), in1=fcosb.ap(),
                                    op=OP.mult))
        vv = V(lambda e: e.tensor_tensor(out=agst[:, 0:W1], in0=qrev.ap(),
                                         in1=qrrot.ap(), op=OP.add), inc=s_vc)
        G.w(s_vc, vv)
        G.dma(ag_in[12 + ncx], agst[:, 0:W1], s_ag)

    V.w(s_fw, fw0_ld)
    fwA_vals = []
    for ch in range(2):
        if ch == 1:
            lv = Y.dma(fwn.ap(), ffnw_in[1], s_fw)
            V.w(s_fw, lv)
        wv_ = fwn.ap().rearrange("p (b k) -> p b k", k=16)
        wn_ = fwq.ap().bitcast(F32).rearrange("p (b k) -> p b k", k=8)
        V(lambda e, wv=wv_, wn=wn_: e.tensor_tensor(
            out=wn, in0=wv[:, :, 0:16:2], in1=wv[:, :, 1:16:2], op=OP.min))
        V(lambda e, wv=wv_: e.tensor_tensor(out=wv[:, :, 0:8], in0=wv[:, :, 0:16:2],
                                            in1=wv[:, :, 1:16:2], op=OP.max))
        V(lambda e, wv=wv_: e.tensor_tensor(out=wv[:, :, 0:4], in0=wv[:, :, 0:8:2],
                                            in1=wv[:, :, 1:8:2], op=OP.max))
        V(lambda e, wn=wn_: e.tensor_tensor(out=wn[:, :, 0:4], in0=wn[:, :, 0:8:2],
                                            in1=wn[:, :, 1:8:2], op=OP.min))
        V(lambda e, wv=wv_: e.tensor_tensor(out=wv[:, :, 0:2], in0=wv[:, :, 0:4:2],
                                            in1=wv[:, :, 1:4:2], op=OP.max))
        V(lambda e, wn=wn_: e.tensor_tensor(out=wn[:, :, 0:2], in0=wn[:, :, 0:4:2],
                                            in1=wn[:, :, 1:4:2], op=OP.min))
        V(lambda e, wv=wv_, c=ch: e.tensor_tensor(
            out=fws[:, c * 128:(c + 1) * 128], in0=wv[:, :, 0:1].squeeze(2),
            in1=wv[:, :, 1:2].squeeze(2), op=OP.max))
        V(lambda e, wn=wn_, c=ch: e.tensor_tensor(
            out=scr1[:, 0:128], in0=wn[:, :, 0:1].squeeze(2),
            in1=wn[:, :, 1:2].squeeze(2), op=OP.min))
        V(lambda e: e.tensor_scalar_mul(scr1[:, 0:128], scr1[:, 0:128], -1.0))
        fwA_vals.append(V(lambda e, c=ch: e.tensor_tensor(
            out=fws[:, c * 128:(c + 1) * 128], in0=fws[:, c * 128:(c + 1) * 128],
            in1=scr1[:, 0:128], op=OP.max), inc=s_vc))
        if ch == 0:
            Y.w(s_vc, fwA_vals[0])
    vv = V(lambda e: e.reduce_max(wsc[:, 0:1], fws.ap(), axis=AX), inc=s_vc)
    T.w(s_vc, vv)
    T.w(s_ld, ld_all)
    tv = T(lambda e: e.transpose(psB[0:1, 128:256], wsc[:, 0:1], identf.ap()), inc=s_tc)
    V.w(s_tc, tv)
    vv = V(lambda e: e.reduce_max(wsc[0:1, 4:5], psB[0:1, 128:256], axis=AX),
           inc=s_vc)
    T.w(s_vc, vv)
    G.w(s_vc, vv)
    qv = G.dma(amax_in.ap(), wsc[0:1, 0:16], s_q2)
    G.w(s_q2, qv)
    cc_amax = G(lambda e: e.collective_compute(
        "AllGather", OP.bypass, replica_groups=[list(range(NC))],
        ins=[amax_in.ap().opt()], outs=[amax_out.ap().opt()]), inc=s_cc)
    G.w(s_ag, s_ag.n)
    cc_c = G(lambda e: e.collective_compute(
        "AllGather", OP.bypass, replica_groups=[[0, 1, 2, 3], [4, 5, 6, 7]],
        ins=[ag_in.ap()[12:20].opt()], outs=[ag_out_c.ap().opt()]), inc=s_cc)

    G.w(s_cc, cc_amax)
    qv = G.dma(wsc[0:1, 8:16],
               amax_out.ap().rearrange("c a k -> a (c k)")[0:1, 4:125:16], s_q2,
               slow=True)
    V.w(s_q2, qv)
    V(lambda e: e.reduce_max(wsc[0:1, 1:2], wsc[0:1, 8:16], axis=AX))
    V(lambda e: e.tensor_scalar_mul(wsc[0:1, 2:3], wsc[0:1, 1:2], 1.0 / 2688.0))
    vv = V(lambda e: e.tensor_scalar_mul(wsc[0:1, 3:4], wsc[0:1, 2:3], 6.0), inc=s_vc)
    G.w(s_vc, vv)
    qv = G.dma(amax_in.ap(), wsc[0:1, 0:16], s_q2)
    G.w(s_q2, qv)
    qv = G.dma(wsc.ap(), amax_in.ap()[0:1, 0:16].broadcast_to([128, 16]), s_q2)
    V.w(s_q2, qv)
    V(lambda e: e.reciprocal(wsc[:, 5:6], wsc[:, 3:4]))
    V(lambda e: e.tensor_scalar_mul(wsc[:, 6:7], wsc[:, 2:3], xtsb.ap()))
    V(lambda e: e.tensor_scalar_mul(wsc[:, 7:8], xtsb.ap(), 6.0))
    rfx_done = V(lambda e: e.reciprocal(wsc[:, 7:8], wsc[:, 7:8]), inc=s_vc)

    def nvfp4(src, width, bs, sA, sB, cc, rA, rB, rC, rD, rfac, ts_lane, dst,
              post=None):
        last = [None]
        for r in nvfp4_gen(src, width, bs, sA, sB, cc, rA, rB, rC, rD, rfac,
                           ts_lane, dst, post):
            if r is not None:
                last[0] = r
        return last[0]

    def nvfp4_gen(src, width, bs, sA, sB, cc, rA, rB, rC, rD, rfac, ts_lane,
                  dst, post=None):
        nb = width // 16
        sv = src.rearrange("p (b k) -> p b k", k=16)
        t8 = rA[:, 0:width // 2].rearrange("p (b k) -> p b k", k=8)
        t8n = rB[:, 0:width // 2].rearrange("p (b k) -> p b k", k=8)
        V(lambda e: e.tensor_tensor(out=t8, in0=sv[:, :, 0:16:2],
                                    in1=sv[:, :, 1:16:2], op=OP.max))
        yield None
        V(lambda e: e.tensor_tensor(out=t8n, in0=sv[:, :, 0:16:2],
                                    in1=sv[:, :, 1:16:2], op=OP.min))
        yield None
        V(lambda e: e.tensor_tensor(out=t8[:, :, 0:4], in0=t8[:, :, 0:8:2],
                                    in1=t8[:, :, 1:8:2], op=OP.max))
        yield None
        V(lambda e: e.tensor_tensor(out=t8n[:, :, 0:4], in0=t8n[:, :, 0:8:2],
                                    in1=t8n[:, :, 1:8:2], op=OP.min))
        yield None
        V(lambda e: e.tensor_tensor(out=t8[:, :, 0:2], in0=t8[:, :, 0:4:2],
                                    in1=t8[:, :, 1:4:2], op=OP.max))
        yield None
        V(lambda e: e.tensor_tensor(out=t8n[:, :, 0:2], in0=t8n[:, :, 0:4:2],
                                    in1=t8n[:, :, 1:4:2], op=OP.min))
        yield None
        V(lambda e: e.tensor_tensor(out=bs[:, 0:nb], in0=t8[:, :, 0:1].squeeze(2),
                                    in1=t8[:, :, 1:2].squeeze(2), op=OP.max))
        yield None
        V(lambda e: e.tensor_tensor(out=sA[:, 0:nb], in0=t8n[:, :, 0:1].squeeze(2),
                                    in1=t8n[:, :, 1:2].squeeze(2), op=OP.min))
        yield None
        V(lambda e: e.tensor_scalar_mul(sA[:, 0:nb], sA[:, 0:nb], -1.0))
        yield None
        V(lambda e: e.tensor_tensor(out=bs[:, 0:nb], in0=bs[:, 0:nb],
                                    in1=sA[:, 0:nb], op=OP.max))
        yield None
        V(lambda e: e.tensor_scalar_mul(bs[:, 0:nb], bs[:, 0:nb], rfac))
        yield None
        V(lambda e: e.tensor_scalar(bs[:, 0:nb], bs[:, 0:nb], 0.0, 448.0,
                                    OP.max, OP.min))
        yield None
        V(lambda e: e.tensor_scalar_mul(sA[:, 0:nb], bs[:, 0:nb], VELT))
        yield None
        V(lambda e: e.tensor_tensor(out=sB[:, 0:nb], in0=sA[:, 0:nb],
                                    in1=bs[:, 0:nb], op=OP.subtract))
        yield None
        V(lambda e: e.tensor_tensor(out=sA[:, 0:nb], in0=sA[:, 0:nb],
                                    in1=sB[:, 0:nb], op=OP.subtract))
        yield None
        V(lambda e: e.tensor_scalar(sB[:, 0:nb], bs[:, 0:nb], M_SUB, M_SUB,
                                    OP.add, OP.subtract))
        yield None
        V(lambda e: e.tensor_scalar(cc[:, 0:nb], bs[:, 0:nb], E4_MIN, None, OP.is_lt))
        yield None
        V(lambda e: e.tensor_tensor(out=sB[:, 0:nb], in0=sB[:, 0:nb],
                                    in1=sA[:, 0:nb], op=OP.subtract))
        yield None
        V(lambda e: e.tensor_tensor(out=sB[:, 0:nb], in0=sB[:, 0:nb],
                                    in1=cc[:, 0:nb], op=OP.mult))
        yield None
        V(lambda e: e.tensor_tensor(out=bs[:, 0:nb], in0=sA[:, 0:nb],
                                    in1=sB[:, 0:nb], op=OP.add))
        yield None
        V(lambda e: e.tensor_scalar(sA[:, 0:nb], bs[:, 0:nb], 0.0, None, OP.is_equal))
        yield None
        V(lambda e: e.tensor_tensor(out=sB[:, 0:nb], in0=bs[:, 0:nb],
                                    in1=sA[:, 0:nb], op=OP.add))
        yield None
        V(lambda e: e.tensor_scalar_mul(sB[:, 0:nb], sB[:, 0:nb], ts_lane))
        yield None
        V(lambda e: e.reciprocal(sB[:, 0:nb], sB[:, 0:nb]))
        yield None
        V(lambda e: e.tensor_tensor(
            out=cc.rearrange("p (b k) -> p b k", k=16), in0=sv,
            in1=sB[:, 0:nb].unsqueeze(2).broadcast_to([128, nb, 16]), op=OP.mult))
        yield None
        V(lambda e: e.tensor_scalar(cc, cc, -6.0, 6.0, OP.max, OP.min))
        yield None
        V(lambda e: e.tensor_scalar(rB, cc, 2.0, C_RNE, OP.mult, OP.add))
        yield None
        V(lambda e: e.tensor_scalar(rB, rB, C_RNE, 0.5, OP.subtract, OP.mult))
        yield None
        V(lambda e: e.tensor_scalar(rC, cc, C_RNE, C_RNE, OP.add, OP.subtract))
        yield None
        V(lambda e: e.tensor_scalar(rD, cc, 0.5, C_RNE, OP.mult, OP.add))
        yield None
        V(lambda e: e.tensor_scalar(rD, rD, C_RNE, 2.0, OP.subtract, OP.mult))
        yield None
        V(lambda e: e.tensor_tensor(out=rA, in0=cc, in1=cc, op=OP.mult))
        yield None
        V(lambda e: e.tensor_scalar(cc, rA, 16.0, None, OP.is_lt))
        yield None
        V(lambda e: e.tensor_scalar(rA, rA, 4.0, None, OP.is_lt))
        yield None
        V(lambda e: e.tensor_tensor(out=rC, in0=rC, in1=rD, op=OP.subtract))
        yield None
        V(lambda e: e.tensor_tensor(out=rC, in0=rC, in1=cc, op=OP.mult))
        yield None
        V(lambda e: e.tensor_tensor(out=rD, in0=rD, in1=rC, op=OP.add))
        yield None
        V(lambda e: e.tensor_tensor(out=rB, in0=rB, in1=rD, op=OP.subtract))
        yield None
        V(lambda e: e.tensor_tensor(out=rB, in0=rB, in1=rA, op=OP.mult))
        yield None
        V(lambda e: e.tensor_tensor(out=rD, in0=rD, in1=rB, op=OP.add))
        yield None
        if post is not None:
            V(lambda e: e.tensor_scalar_mul(bs[:, 0:nb], bs[:, 0:nb], post))
            yield None
        yield V(lambda e: e.tensor_tensor(
            out=dst.rearrange("p (b k) -> p b k", k=16),
            in0=rD.rearrange("p (b k) -> p b k", k=16),
            in1=bs[:, 0:nb].unsqueeze(2).broadcast_to([128, nb, 16]), op=OP.mult),
            inc=s_vc)

    qv_prev = 0
    for ch in range(2):
        Y.w(s_tc, max(kr_mm, qr_mm_last))
        Y.w(s_vc, qv_prev)
        if ch == 0:
            Y.w(s_vc, fwA_vals[1])
        lv = Y.dma(q_c.ap(), ffnw_in[ch], s_fw)
        V.w(s_fw, lv)
        V.w(s_vc, rfx_done)
        V.w(s_tc, qr_mm_last)
        V.w(s_tp, s_tp.n)
        for hf in range(2):
            qv = nvfp4(q_c[:, hf * 1024:(hf + 1) * 1024], 1024,
                       q_bs.ap(), q_sA.ap(), q_sB.ap(),
                       cc=q_rB.ap(), rA=q_rA.ap(), rB=q_rC.ap(), rC=q_rD.ap(),
                       rD=fwn[:, 0:1024], rfac=wsc[:, 5:6], ts_lane=wsc[:, 2:3],
                       dst=fwq[:, hf * 1024:(hf + 1) * 1024])
        qv_prev = qv
        for g in range(2):
            T.w(s_vc, qv)
            tiles = [fwq[:, (g * 8 + i) * 128:(g * 8 + i + 1) * 128] for i in range(8)]

            def evw(hidx, ch=ch, g=g):
                Sc.w(s_tp, s_tp.n)
                Sc.w(s_q2, s_q2.n)
                return Sc(lambda e, h=hidx: e.copy(
                    out=agst[:, 0:1024], in_=psC[:, h * 1024:h * 1024 + 1024]),
                    inc=s_pe)
            hid = transpose_group(tiles, evw)
            G.w(s_pe, psc_last_evac[hid])
            for i in range(8):
                G.dma(wdq_in[g * 8 + i, :, ch * 128:(ch + 1) * 128],
                      agst[:, i * 128:(i + 1) * 128], s_q2)
    partB_done = qv_prev
    G.w(s_q2, s_q2.n)
    cc_wdq = G(lambda e: e.collective_compute(
        "AllGather", OP.bypass, replica_groups=[list(range(NC))],
        ins=[wdq_in.ap().opt()], outs=[wdq_out.ap().opt()]), inc=s_cc)

    Y.w(s_cc, cc_b)
    for src in range(4):
        Y.dma(krTg[0:64, src * W1:(src + 1) * W1], ag_out_b[src, 4, 0:64, :], s_krg)
    krg_all = s_krg.n

    lat_slot = [0]
    lat_slot_consumed = {}

    def dma_lat(chunk, tw):
        sl = lat_slot[0] % 4
        lat_slot[0] += 1
        if sl in lat_slot_consumed:
            Y.w(s_tc, lat_slot_consumed[sl])
        if chunk < 12:
            gat, gch = ag_out_a, chunk
            Y.w(s_cc, cc_a)
        else:
            gat, gch = ag_out_b, chunk - 20
            Y.w(s_cc, cc_b)
        dv = 0
        npc = max(1, TWW // W1)
        pw = min(TWW, W1)
        for p in range(npc):
            tok0 = tw * TWW + p * pw
            src = tok0 // W1
            coff = tok0 % W1
            dv = Y.dma(latq[:, sl * TWW + p * pw: sl * TWW + (p + 1) * pw],
                       gat[src, gch, :, coff:coff + pw], s_latq[sl])
        return sl, dv

    psD_last_evac = [0]
    attnT_last_store = [0]
    head_qk_done = {-2: 0, -1: 0}

    proj_last_mm = [0]

    def proj_steps(h):
        hp = h % 2
        G.w(s_tc, proj_last_mm[0])
        G.dma(qbTb.ap().rearrange("p (k w) -> p k w", w=128), qbT_b[h], s_hw)
        G.dma(kvknb.ap().rearrange("p (k w) -> p k w", w=128), kvbknT_b[h], s_hw)
        gv3 = G.dma(kvvb.ap().rearrange("p (k w) -> p k w", w=128), kvbvT_b[h], s_hw)
        chq0 = h // 2
        roff = (h % 2) * 64
        if h >= 2:
            Y.w(s_qk, head_qk_done[h - 2])
        sqr = s_qr[h % 2]
        qrv = sqr.inc(16)
        out_ap = qrTh[0:64, hp * S:(hp + 1) * S].rearrange("p (s w) -> p s w", w=W1)
        Y.w(s_cc, cc_c)

        def _fq(e, o=out_ap, c0=chq0, ro=roff, sh=sqr.h):
            src_ap = ag_out_c.ap()[:, bass.ds(_rank2c(e, c0), 1), ro:ro + 64, :]
            src_ap = src_ap.squeeze(1).transpose([1, 0, 2])
            return e.dma_start(out=o, in_=src_ap).then_inc(sh, 16)
        EM["sync"].append(_fq)
        T.w(s_hw, gv3)
        yield qrv
        for tw in range(NTW):
            mmv = 0
            for halfq in range(2):
                for kcx in range(6 * halfq, 6 * halfq + 6):
                    sl, dv = dma_lat(kcx, tw)
                    T.w(s_latq[sl], dv)
                    mmv = T(lambda e, k=kcx, sl=sl, st_=(kcx == 0), sp=(kcx == 11):
                            e.matmul(psB.ap(), qbTb[:, k * 128:(k + 1) * 128],
                                     latq[:, sl * TWW:(sl + 1) * TWW],
                                     start=st_, stop=sp), inc=s_tc)
                    lat_slot_consumed[sl] = mmv
                yield
            Sc.w(s_tc, mmv)
            ev = Sc(lambda e, hp=hp, tw=tw: e.copy(
                out=qnT[:, hp * S + tw * TWW: hp * S + (tw + 1) * TWW],
                in_=psB.ap()), inc=s_sc)
            T.w(s_sc, ev)
            kv_sl = []
            mmv = 0
            for kcx in range(4):
                sl, dv = dma_lat(20 + kcx, tw)
                kv_sl.append(sl)
                T.w(s_latq[sl], dv)
                mmv = T(lambda e, k=kcx, sl=sl, st_=(kcx == 0), sp=(kcx == 3):
                        e.matmul(psB.ap(), kvknb[:, k * 128:(k + 1) * 128],
                                 latq[:, sl * TWW:(sl + 1) * TWW],
                                 start=st_, stop=sp), inc=s_tc)
            yield
            Sc.w(s_tc, mmv)
            ev = Sc(lambda e, hp=hp, tw=tw: e.copy(
                out=knT[:, hp * S + tw * TWW: hp * S + (tw + 1) * TWW],
                in_=psB.ap()), inc=s_sc)
            T.w(s_sc, ev)
            mmv = 0
            for tsub in range(TWW // 128):
                for kcx in range(4):
                    sl = kv_sl[kcx]
                    mmv = T(lambda e, k=kcx, sl=sl, ts=tsub,
                            st_=(kcx == 0), sp=(kcx == 3): e.matmul(
                                psB[:, ts * 128:(ts + 1) * 128],
                                latq[:, sl * TWW + ts * 128: sl * TWW + (ts + 1) * 128],
                                kvvb[:, k * 128:(k + 1) * 128],
                                start=st_, stop=sp), inc=s_tc)
            for sl in kv_sl:
                lat_slot_consumed[sl] = mmv
            Sc.w(s_tc, mmv)
            ev = Sc(lambda e, hp=hp, tw=tw: e.copy(
                out=vTb[:, hp * S + tw * TWW: hp * S + (tw + 1) * TWW],
                in_=psB.ap()), inc=s_sc)
            T.w(s_sc, ev)
            proj_last_mm[0] = mmv
            yield

    psA_q_last_exp = [0, 0]
    pbuf_last_tp = [0, 0]
    ptb_last_pv = [0]
    qc_glob = [0]

    def attn_head(h, qr_ld, next_gen):
        hp = h % 2
        kn = knT[:, hp * S:(hp + 1) * S]
        vv_ = vTb[:, hp * S:(hp + 1) * S]
        qn = qnT[:, hp * S:(hp + 1) * S]
        qr = qrTh[0:64, hp * S:(hp + 1) * S]
        T.w(s_qr[h % 2], qr_ld)
        T.w(s_krg, krg_all)

        def emit_transp(info):
            g_, pb_, qoff_, pnv_ = info
            for gg in range((KC + 7) // 8):
                ntl = min(8, KC - gg * 8)
                T.w(s_pn, pnv_)
                if ptb_last_pv[0] > 0:
                    T.w(s_pv, ptb_last_pv[0])
                tiles = [pb_[:, (gg * 8 + i) * 128:(gg * 8 + i + 1) * 128]
                         for i in range(ntl)]

                def evp(hidx, gg=gg, qoff=qoff_, ntl=ntl):
                    V.w(s_tp, s_tp.n)
                    if ptb_last_pv[0] > 0:
                        V.w(s_pv, ptb_last_pv[0])
                    dst = ptb.ap().rearrange("p (k w) -> p k w", w=QB)[
                        :, gg * 8: gg * 8 + ntl, qoff:qoff + 128]
                    src = psC.ap().rearrange("p (h k w) -> p h k w", h=2, w=128)[
                        :, hidx, 0:ntl, :]
                    return V(lambda e, d=dst, s=src: e.tensor_copy(out=d, in_=s),
                             inc=s_pe)
                transpose_group(tiles, evp)
            pbuf_last_tp[g_ % 2] = s_tp.n

        pv_pend = [None]

        def emit_pv(qb):
            T.w(s_pe, s_pe.n)
            T.w(s_pq, psD_last_evac[0])
            pvv = 0
            for kcx in range(KC):
                pvv = T(lambda e, k=kcx, v=vv_, st_=(kcx == 0), sp=(kcx == KC - 1):
                        e.matmul(psD.ap(), v[:, k * 128:(k + 1) * 128],
                                 ptb[:, k * QB:(k + 1) * QB],
                                 start=st_, stop=sp), inc=s_pv)
            ptb_last_pv[0] = pvv
            pv_pend[0] = (qb, pvv)

        def flush_pv():
            if pv_pend[0] is None:
                return
            qb, pvv = pv_pend[0]
            pv_pend[0] = None
            V.w(s_pv, pvv)
            if qb == 0:
                V.w(s_a2, attnT_last_store[0])
            pqv = V(lambda e, qb=qb: e.tensor_copy(
                out=attnT[:, qb * QB:(qb + 1) * QB], in_=psD.ap()), inc=s_pq)
            psD_last_evac[0] = pqv

        pending = None
        for qc in range(QC):
            flush_pv()
            g = qc_glob[0]
            qc_glob[0] += 1
            sb_ = (g % 2) * 32
            pb = pbuf[:, (g % 2) * S:(g % 2) * S + S]
            nmq = stats[:, sb_ + 0: sb_ + 1]
            nms = stats[:, sb_ + 1: sb_ + 2]
            for j in range(4):
                reg = j % 2
                T.w(s_ex, psA_q_last_exp[reg])
                T_mm(psA[:, reg * 1024: reg * 1024 + KQ],
                     qn[:, qc * 128:(qc + 1) * 128],
                     kn[:, j * KQ:(j + 1) * KQ], True, False)
                qkv = T_mm(psA[:, reg * 1024: reg * 1024 + KQ],
                           qr[:, qc * 128:(qc + 1) * 128],
                           krTg[0:64, j * KQ:(j + 1) * KQ], False, True, inc=s_qk)
                if j == 0:
                    V.w(s_qk, qkv)
                    V(lambda e, nmq=nmq: e.reduce_max(
                        nmq, psA[:, 0:128], axis=AX, negate=True))
                    mxv = V(lambda e, nms=nms, nmq=nmq: e.tensor_scalar_mul(
                        nms, nmq, ATT_SCALE), inc=s_mx)
                    Sc.w(s_mx, mxv)
                    if g >= 2:
                        Sc.w(s_tp, pbuf_last_tp[g % 2])
                Sc.w(s_qk, qkv)
                ssum = stats[:, sb_ + 20 + j: sb_ + 21 + j]
                exv = Sc(lambda e, j=j, reg=reg, pb=pb, nms=nms, ssum=ssum:
                         e.activation(pb[:, j * KQ:(j + 1) * KQ],
                                      psA[:, reg * 1024: reg * 1024 + KQ],
                                      AF.Exp, bias=nms, scale=ATT_SCALE,
                                      accum_out=ssum), inc=s_ex)
                psA_q_last_exp[reg] = exv
            V.w(s_ex, s_ex.n)
            tot = stats[:, sb_ + 24: sb_ + 25]
            rcp = stats[:, sb_ + 25: sb_ + 26]
            V(lambda e, tot=tot, sb_=sb_: e.reduce_sum(
                tot, stats[:, sb_ + 20: sb_ + 24], axis=AX))
            V(lambda e, rcp=rcp, tot=tot: e.reciprocal(rcp, tot))
            pnv = V(lambda e, pb=pb, rcp=rcp: e.tensor_scalar_mul(
                pb, pb, rcp), inc=s_pn)
            if pending is not None:
                emit_transp(pending)
                if qc % QCB == 0 and qc > 0:
                    emit_pv(qc // QCB - 1)
            pending = (g, pb, (qc % QCB) * 128, pnv)
            if next_gen is not None:
                next(next_gen, None)
            if q3_box[0] is not None:
                for _ in range(6):
                    next(q3_box[0], None)
        emit_transp(pending)
        emit_pv(QC // QCB - 1)
        flush_pv()
        G.w(s_pq, s_pq.n)
        av = 0
        for j in range(4):
            for tcl in range(TC):
                av = G.dma(a2a_in[h, j, tcl],
                           attnT[:, j * W1 + tcl * 128: j * W1 + (tcl + 1) * 128],
                           s_a2)
        attnT_last_store[0] = av
        G.w(s_a2, s_a2.n)
        cc_a2a_h[h] = G(lambda e, h=h: e.collective_compute(
            "AllGather", OP.bypass, replica_groups=[[0, 1, 2, 3], [4, 5, 6, 7]],
            ins=[a2a_in.ap()[h].opt()], outs=[a2a_out.ap()[h].opt()]), inc=s_cc)
        Y.w(s_cc, cc_a2a_h[h])
        s_am.inc(16)

        def _fa(e, h=h):
            sap = a2a_out.ap()[h][:, bass.ds(_rank_base(e), 1)].squeeze(1)
            dst = a2a_mine.ap().rearrange("(s hl) t p w -> s hl t p w", s=4)[:, h]
            return e.dma_start(out=dst, in_=sap).then_inc(s_am.h, 16)
        EM["sync"].append(_fa)

    q3_done = [0]

    def quant3a_gen():
        Y.w(s_tc, max(kr_mm, qr_mm_last))
        Y.w(s_vc, partB_done)
        Y.w(s_sc, s_sc.n)
        Y.w(s_q2, s_q2.n)
        Y.w(s_ag, s_ag.n)
        x_next = Y.dma(p3xA.ap(), x_in[0:128, :], s_xs[0])
        qv = 0
        for t in range(TC):
            V.w(s_xs[0], x_next)
            V(lambda e: e.reduce_sum(negmu, p3xA.ap(), axis=AX, negate=True))
            V(lambda e: e.tensor_scalar_mul(negmu, negmu, 1.0 / D))
            vv = V(lambda e: e.tensor_scalar_add(q_c.ap(), p3xA.ap(), negmu),
                   inc=s_vc)
            yield
            Sc.w(s_vc, vv)
            Sc.w(s_q2, s_q2.n)
            sv = Sc(lambda e: e.activation(p3xA.ap(), q_c.ap(), AF.Square,
                                           accum_out=varl), inc=s_sc)
            V.w(s_sc, sv)
            vv = V(lambda e: e.tensor_scalar(varl, varl, 1.0 / D, float(EPS),
                                             OP.mult, OP.add), inc=s_vc)
            Sc.w(s_vc, vv)
            Sc(lambda e: e.activation(rstd1, varl, AF.Ln))
            sv = Sc(lambda e: e.activation(rstd1, rstd1, AF.Exp, scale=-0.5),
                    inc=s_sc)
            V.w(s_sc, sv)
            V(lambda e: e.tensor_scalar_mul(q_c.ap(), q_c.ap(), rstd1))
            V.w(s_xq, s_xq.n)
            yield
            for hf in range(2):
                for r in nvfp4_gen(q_c[:, hf * 1024:(hf + 1) * 1024], 1024,
                                   q_bs.ap(), q_sA.ap(), q_sB.ap(),
                                   cc=q_rB.ap(), rA=q_rA.ap(), rB=q_rC.ap(),
                                   rC=q_rD.ap(), rD=p3xA[:, 0:1024],
                                   rfac=wsc[:, 7:8], ts_lane=xtsb.ap(),
                                   dst=p3xdq[:, hf * 1024:(hf + 1) * 1024],
                                   post=wsc[:, 6:7]):
                    if r is not None:
                        qv = r
                    yield
            G.w(s_vc, qv)
            G.dma(xdq_scr[t], p3xdq.ap(), s_xq)
            if t + 1 < TC:
                Y.w(s_vc, qv)
                x_next = Y.dma(p3xA.ap(), x_in[(t + 1) * 128:(t + 2) * 128, :],
                               s_xs[0])
            yield
        q3_done[0] = qv

    cc_a2a_h = {}
    q3_box = [quant3a_gen()]
    cur_gen = proj_steps(0)
    cur_qr = next(cur_gen)
    for h in range(HG):
        for _ in cur_gen:
            pass
        if h + 1 < HG:
            nxt_gen = proj_steps(h + 1)
            nxt_qr = next(nxt_gen)
        else:
            nxt_gen, nxt_qr = None, 0
        attn_head(h, cur_qr, nxt_gen)
        head_qk_done[h] = s_qk.n
        cur_gen, cur_qr = nxt_gen, nxt_qr
    cc_a2a = cc_a2a_h[HG - 1]

    for _ in q3_box[0]:
        pass
    q3_box[0] = None
    quant_done = q3_done[0]
    T.w(s_ex, s_ex.n)

    tile_mm = {}
    tile_cb = {}
    prev_mm = {0: None, 1: None}
    prev_cb = {0: None, 1: None}
    prev_ost = {0: 0, 1: 0}
    half_last_mm = 0
    for half in range(2):
        Y.w(s_cc, cc_wdq)
        Y.w(s_cc, cc_a2a)
        Y.w(s_vc, quant_done)
        G.w(s_cc, cc_a2a)
        if half == 1:
            Y.w(s_tc, half_last_mm)
            G.w(s_tc, half_last_mm)
        Y.w(s_am, s_am.n)
        Y.w(s_xq, s_xq.n)
        wv = 0
        for kcx in range(16):
            for sp in range(4):
                src = half * 4 + sp
                wv = Y.dma(p3wdq[:, kcx * 1024 + sp * 256: kcx * 1024 + (sp + 1) * 256],
                           wdq_out[src, kcx], s_wv)
        ov = 0
        for dvc in range(16):
            ov = G.dma(p3ow[:, dvc * 1024:(dvc + 1) * 1024], owT_b[half, dvc], s_p3g)
        for t in range(TC):
            tp = t % 2
            idx = half * TC + t
            if prev_mm[tp] is not None:
                Y.w(s_tc, prev_mm[tp])
                Y.w(s_vc, prev_cb[tp])
            av = 0
            for src in range(4):
                for hl in range(4):
                    av = Y.dma(p3at[:, tp * D + (src * 4 + hl) * 128:
                                    tp * D + (src * 4 + hl + 1) * 128],
                               a2a_mine[src * 4 + hl, t], s_p3t[tp])
            xv = Y.dma(p3xdqb[:, tp * D: tp * D + D], xdq_scr[t], s_p3t[tp])
            x3v = Y.dma(p3x[:, tp * D: tp * D + D], x_in[t * 128:(t + 1) * 128, :],
                        s_p3xl[tp])
            T.w(s_p3t[tp], xv)
            for gg in range(2):
                tiles = [p3xdqb[:, tp * D + (gg * 8 + i) * 128:
                                tp * D + (gg * 8 + i + 1) * 128] for i in range(8)]

                def evx(hidx, gg=gg, tp=tp):
                    Sc.w(s_tp, s_tp.n)
                    dst = p3xdqT[:, tp * D + gg * 1024: tp * D + (gg + 1) * 1024]
                    return Sc(lambda e, d=dst, h=hidx: e.copy(
                        out=d, in_=psC[:, h * 1024:(h + 1) * 1024]), inc=s_pe)
                transpose_group(tiles, evx)
            T.w(s_pe, s_pe.n)
            if t == 0:
                T.w(s_wv, wv)
                T.w(s_p3g, ov)
            side = t % 2
            for kcx in range(16):
                T_mm(psA[:, side * 1024:(side + 1) * 1024],
                     p3xdqT[:, tp * D + kcx * 128: tp * D + (kcx + 1) * 128],
                     p3wdq[:, kcx * 1024:(kcx + 1) * 1024], kcx == 0, False)
            mmv = 0
            for dvc in range(16):
                mmv = T_mm(psA[:, side * 1024:(side + 1) * 1024],
                           p3at[:, tp * D + dvc * 128: tp * D + (dvc + 1) * 128],
                           p3ow[:, dvc * 1024:(dvc + 1) * 1024],
                           False, dvc == 15, inc=s_tc)
            tile_mm[idx] = mmv
            if t == TC - 1:
                half_last_mm = mmv
            V.w(s_p3xl[tp], x3v)
            V.w(s_ost[tp], prev_ost[tp])
            V(lambda e, tp=tp, half=half: e.tensor_tensor(
                out=p3xb[:, tp * 1024:(tp + 1) * 1024],
                in0=p3x[:, tp * D + half * 1024: tp * D + half * 1024 + 1024],
                in1=biasb[:, half * 1024:(half + 1) * 1024], op=OP.add))
            V.w(s_tc, mmv)
            cbv = V(lambda e, tp=tp, side=side: e.tensor_tensor(
                out=p3xb[:, tp * 1024:(tp + 1) * 1024],
                in0=p3xb[:, tp * 1024:(tp + 1) * 1024],
                in1=psA[:, side * 1024:(side + 1) * 1024], op=OP.add), inc=s_vc)
            tile_cb[idx] = cbv
            T.w(s_vc, cbv)
            G.w(s_vc, cbv)
            prev_ost[tp] = G.dma(
                out_t[t * 128:(t + 1) * 128, half * 1024:half * 1024 + 1024],
                p3xb[:, tp * 1024:(tp + 1) * 1024], s_ost[tp])
            prev_mm[tp] = mmv
            prev_cb[tp] = cbv

    if dbg:
        G.dma(dbg_a2a.ap(), a2a_in.ap(), s_out)
        G.dma(dbg_agin.ap(), ag_in.ap(), s_out)
        G.dma(dbg_agout[0:4, 0:12], ag_out_a.ap(), s_out)
        G.dma(dbg_kn.ap(), knT.ap(), s_out)
        G.dma(dbg_qn.ap(), qnT.ap(), s_out)
        G.dma(dbg_wsc.ap(), wsc.ap(), s_out)
        G.dma(dbg_att.ap(), attnT.ap(), s_out)
        G.dma(dbg_p.ap(), pbuf[:, 0:S], s_out)
        G.dma(dbg_st.ap(), stats.ap(), s_out)
    G.w(s_out, s_out.n)
    G.w(s_ost[0], s_ost[0].n)
    G.w(s_ost[1], s_ost[1].n)
    G(lambda e: e.engine_nop(), inc=s_fin)
    G.w(s_fin, s_fin.n)

    with nc.Block() as block:
        @block.sync
        def _(eng):
            for f in EM["sync"]:
                f(eng)

        @block.tensor
        def _(eng):
            for f in EM["tensor"]:
                f(eng)

        @block.vector
        def _(eng):
            for f in EM["vector"]:
                f(eng)

        @block.scalar
        def _(eng):
            for f in EM["scalar"]:
                f(eng)

        @block.gpsimd
        def _(eng):
            for f in EM["gpsimd"]:
                f(eng)

    return nc


def _rope_tables(S):
    inv = 1.0 / 10000.0 ** (np.arange(0, DR, 2, dtype=np.float32) / DR)
    t = np.arange(S, dtype=np.float32)
    fr = np.outer(t, inv).astype(np.float32)
    emb = np.concatenate([fr, fr], -1)
    return np.cos(emb).astype(np.float32), np.sin(emb).astype(np.float32)


def stage_inputs(hs, q_a_w, q_a_ln_w, q_a_ln_b, q_b_w, q_rope_w, kv_a_w,
                 kv_a_ln_w, kv_a_ln_b, kv_b_w, o_w, ffn_w, ffn_b, x_ts, S):
    B = 2
    NT = B * S // NC
    xf = np.ascontiguousarray(hs.reshape(B * S, D))
    cos, sin = _rope_tables(S)
    sfold = np.concatenate([-sin[:, 0:32], sin[:, 32:64]], axis=1)

    def colblk(mat, nkc, nw=128):
        K, N = mat.shape
        return np.ascontiguousarray(
            mat.reshape(nkc, 128, N // nw, nw).transpose(2, 1, 0, 3))

    qaT = np.ascontiguousarray(q_a_w.T)
    kvaT = np.ascontiguousarray(kv_a_w.T)
    qrwT = np.ascontiguousarray(q_rope_w.T)
    owT = np.ascontiguousarray(o_w.T)

    qaT_b = colblk(qaT, 16)
    kvaT_b = colblk(kvaT[:, 0:KVL], 16)
    kvrT_b = np.ascontiguousarray(
        kvaT[:, KVL:KVL + DR].reshape(16, 128, DR).transpose(1, 0, 2))
    qrwT_b = colblk(qrwT, 12)
    owT_hb = np.ascontiguousarray(
        owT.reshape(16, 128, 2, 1024).transpose(2, 0, 1, 3))

    shared = dict(
        qaT_b=qaT_b, kvaT_b=kvaT_b, kvrT_b=kvrT_b, qrwT_b=qrwT_b, owT_b=owT_hb,
        qalnw=np.ascontiguousarray(q_a_ln_w.reshape(12, 128, 1)),
        qalnb=np.ascontiguousarray(q_a_ln_b.reshape(12, 128, 1)),
        kvalnw=np.ascontiguousarray(kv_a_ln_w.reshape(4, 128, 1)),
        kvalnb=np.ascontiguousarray(kv_a_ln_b.reshape(4, 128, 1)),
        bias_rep=np.ascontiguousarray(np.broadcast_to(ffn_b.reshape(1, D), (128, D))),
        xts_rep=np.full((128, 1), np.float32(x_ts), np.float32),
        ident=np.eye(128, dtype=ml_dtypes.bfloat16),
        identf=np.eye(128, dtype=np.float32),
    )
    ins = []
    for c in range(NC):
        hg0 = 4 * (c % 4)
        pos = (c % 4) * NT + np.arange(NT)
        qb_rows = np.concatenate(
            [q_b_w[128 * (hg0 + i):128 * (hg0 + i + 1)] for i in range(HG)], 0)
        kn_rows = np.concatenate(
            [kv_b_w[256 * (hg0 + i):256 * (hg0 + i) + 128] for i in range(HG)], 0)
        v_rows = np.concatenate(
            [kv_b_w[256 * (hg0 + i) + 128:256 * (hg0 + i + 1)] for i in range(HG)], 0)
        qbT = np.ascontiguousarray(qb_rows.T)
        knT_ = np.ascontiguousarray(kn_rows.T)
        vT_ = np.ascontiguousarray(v_rows.T)
        d = dict(shared)
        d.update(
            x=np.ascontiguousarray(xf[c * NT:(c + 1) * NT]),
            qbT_b=np.ascontiguousarray(
                qbT.reshape(12, 128, HG, 128).transpose(2, 1, 0, 3)),
            kvbknT_b=np.ascontiguousarray(
                knT_.reshape(4, 128, HG, 128).transpose(2, 1, 0, 3)),
            kvbvT_b=np.ascontiguousarray(
                vT_.reshape(4, 128, HG, 128).transpose(2, 1, 0, 3)),
            ffnw=np.ascontiguousarray(
                ffn_w[256 * c:256 * (c + 1)].reshape(2, 128, D)),
            ktcos=np.ascontiguousarray(cos[pos]),
            ktsf=np.ascontiguousarray(sfold[pos]),
            fcos2=np.ascontiguousarray(np.concatenate([cos[pos].T, cos[pos].T], 0)),
            fsf2=np.ascontiguousarray(np.concatenate([sfold[pos].T, sfold[pos].T], 0)),
        )
        ins.append(d)
    return ins


def kernel(L0_attn_hidden_states, L0_attn_q_a_w, L0_attn_q_a_ln_w,
           L0_attn_q_a_ln_b, L0_attn_q_b_w, L0_attn_q_rope_w, L0_attn_kv_a_w,
           L0_attn_kv_a_ln_w, L0_attn_kv_a_ln_b, L0_attn_kv_b_w, L0_attn_o_w,
           L0_ffn_weight, L0_ffn_bias, L0_ffn_x_tensor_scale):
    hs = np.asarray(L0_attn_hidden_states, dtype=np.float32)
    B, S, _ = hs.shape
    in_maps = stage_inputs(
        hs, np.asarray(L0_attn_q_a_w, np.float32),
        np.asarray(L0_attn_q_a_ln_w, np.float32),
        np.asarray(L0_attn_q_a_ln_b, np.float32),
        np.asarray(L0_attn_q_b_w, np.float32),
        np.asarray(L0_attn_q_rope_w, np.float32),
        np.asarray(L0_attn_kv_a_w, np.float32),
        np.asarray(L0_attn_kv_a_ln_w, np.float32),
        np.asarray(L0_attn_kv_a_ln_b, np.float32),
        np.asarray(L0_attn_kv_b_w, np.float32),
        np.asarray(L0_attn_o_w, np.float32),
        np.asarray(L0_ffn_weight, np.float32),
        np.asarray(L0_ffn_bias, np.float32),
        np.float32(L0_ffn_x_tensor_scale), S)
    nc = build_program(S)
    res = run_bass_kernel_spmd(nc, in_maps, core_ids=list(range(NC)))
    NT = B * S // NC
    out = np.concatenate([np.asarray(res.results[c]["out"]) for c in range(NC)],
                         axis=0)
    return out.reshape(B, S, D).astype(np.float32)


# version 14
# speedup vs baseline: 1.0845x; 1.0274x over previous
import os

import numpy as np
import ml_dtypes

import concourse.bass as bass
import concourse.mybir as mybir
from concourse.bass_utils import run_bass_kernel_spmd
from concourse.aot_env import allow_ici_all_to_all

F32 = mybir.dt.float32
BF16 = mybir.dt.bfloat16
OP = mybir.AluOpType
AF = mybir.ActivationFunctionType
AX = mybir.AxisListType.X

D = 2048
QL = 1536
KVL = 512
H = 16
DR = 64
DV = 128
EPS = 1e-5
ATT_SCALE = float(1.0 / np.sqrt(128 + 64))
NC = 8
HG = 4

C_RNE = float(np.float32(1.5 * 2 ** 23))
VELT = float(np.float32(2 ** 20 + 1))
M_SUB = float(np.float32(1.5 * 2 ** 14))
E4_MIN = float(np.float32(2 ** -6))


class Sem:
    __slots__ = ("h", "n", "name")

    def __init__(self, h, name):
        self.h, self.n, self.name = h, 0, name

    def inc(self, k=1):
        self.n += k
        return self.n


def build_program(S=4096, dbg=False):
    B = 2
    NT = B * S // NC
    TC = NT // 128
    W1 = NT
    KQ = S // 4
    QC = S // 128
    KC = S // 128
    QB = min(512, S)
    QCB = QB // 128
    NTW = max(1, S // 512)
    TWW = min(512, S)
    AGW = max(W1, 1024)
    assert KQ <= 1024 and NT % 128 == 0

    nc = bass.Bass("TRN2", target_bir_lowering=False, debug=False,
                   detect_race_conditions=bool(os.environ.get("KRACE")))
    st = nc._state

    def XI(name, shape, dtype=F32):
        return nc.dram_tensor(name, list(shape), dtype, kind="ExternalInput")

    x_in = XI("x", [NT, D])
    qaT_b = XI("qaT_b", [12, 128, 16, 128])
    kvaT_b = XI("kvaT_b", [4, 128, 16, 128])
    kvrT_b = XI("kvrT_b", [128, 16, DR])
    qrwT_b = XI("qrwT_b", [8, 128, 12, 128])
    qbT_b = XI("qbT_b", [HG, 128, 12, 128])
    kvbknT_b = XI("kvbknT_b", [HG, 128, 4, 128])
    kvbvT_b = XI("kvbvT_b", [HG, 128, 4, 128])
    owT_b = XI("owT_b", [2, 16, 128, 1024])
    ffnw_in = XI("ffnw", [2, 128, D])
    qalnw_in = XI("qalnw", [12, 128, 1])
    qalnb_in = XI("qalnb", [12, 128, 1])
    kvalnw_in = XI("kvalnw", [4, 128, 1])
    kvalnb_in = XI("kvalnb", [4, 128, 1])
    bias_in = XI("bias_rep", [128, D])
    xts_in = XI("xts_rep", [128, 1])
    ident_in = XI("ident", [128, 128], BF16)
    identf_in = XI("identf", [128, 128], F32)
    ktcos_in = XI("ktcos", [NT, DR])
    ktsf_in = XI("ktsf", [NT, DR])
    fcos_in = XI("fcos2", [128, W1])
    fsf_in = XI("fsf2", [128, W1])

    out_t = nc.dram_tensor("out", [NT, D], F32, kind="ExternalOutput")

    NAGC = 25
    ag_in = nc.dram_tensor("ag_in", [NAGC, 128, W1], BF16)
    ag_out_a = nc.dram_tensor("ag_out_a", [4, 12, 128, W1], BF16)
    ag_out_b = nc.dram_tensor("ag_out_b", [4, 5, 128, W1], BF16)
    ag_out_c = nc.dram_tensor("ag_out_c", [4, 8, 128, W1], BF16)
    a2a_in = nc.dram_tensor("a2a_in", [HG, 4, TC, 128, 128], BF16)
    a2a_out = nc.dram_tensor("a2a_out", [HG, 4, 4, TC, 128, 128], BF16)
    amax_in = nc.dram_tensor("amax_in", [1, 16], F32)
    amax_out = nc.dram_tensor("amax_out", [NC, 1, 16], F32,
                              addr_space="Shared")
    wdq_in = nc.dram_tensor("wdq_in", [16, 128, 256], BF16)
    a2a_mine = nc.dram_tensor("a2a_mine", [16, TC, 128, 128], BF16)
    wdq_out = nc.dram_tensor("wdq_out", [NC, 16, 128, 256], BF16,
                             addr_space="Shared")
    xdq_scr = nc.dram_tensor("xdq_scr", [TC, 128, D], BF16,
                             kind="ExternalOutput" if dbg else "Internal")
    dbg_wsc = nc.dram_tensor("dbg_wsc", [128, 16], F32, kind="ExternalOutput") \
        if dbg else None
    dbg_att = nc.dram_tensor("dbg_att", [128, S], BF16, kind="ExternalOutput") \
        if dbg else None
    dbg_p = nc.dram_tensor("dbg_p", [128, S], BF16, kind="ExternalOutput") \
        if dbg else None
    dbg_st = nc.dram_tensor("dbg_st", [128, 64], F32, kind="ExternalOutput") \
        if dbg else None
    dbg_agin = nc.dram_tensor("dbg_agin", [NAGC, 128, W1], BF16,
                              kind="ExternalOutput") if dbg else None
    dbg_a2a = nc.dram_tensor("dbg_a2a", [HG, 4, TC, 128, 128], BF16,
                             kind="ExternalOutput") if dbg else None
    dbg_ptb = nc.dram_tensor("dbg_ptb", [128, (S // 128) * QB], BF16,
                             kind="ExternalOutput") if dbg else None
    dbg_pb2 = nc.dram_tensor("dbg_pb2", [128, 2 * S], BF16,
                             kind="ExternalOutput") if dbg else None
    dbg_st2 = nc.dram_tensor("dbg_st2", [128, 64], F32,
                             kind="ExternalOutput") if dbg else None
    dbg_agout = nc.dram_tensor("dbg_agout", [4, NAGC, 128, W1], BF16,
                               kind="ExternalOutput") if dbg else None
    dbg_kn = nc.dram_tensor("dbg_kn", [128, 2 * S], BF16,
                            kind="ExternalOutput") if dbg else None
    dbg_xnT = nc.dram_tensor("dbg_xnT", [128, 16 * W1], BF16,
                             kind="ExternalOutput") if dbg else None
    dbg_qn = nc.dram_tensor("dbg_qn", [128, 2 * S], BF16,
                            kind="ExternalOutput") if dbg else None

    SB_BASE = 16512
    SB_TOP = 229344
    offs = {}

    def _alloc(name, shape, dtype, off):
        itb = 2 if dtype == BF16 else 4
        per = int(np.prod(shape[1:])) * itb
        assert off + per <= SB_TOP, (name, off, per)
        offs[name] = (off, per)
        return st.alloc_sbuf_tensor_at(name, list(shape), dtype, offset=off), off + per

    cur = SB_BASE

    def FX(name, shape, dtype):
        nonlocal cur
        off = (cur + 31) // 32 * 32
        t, cur = _alloc(name, shape, dtype, off)
        return t

    ident = FX("identW", [128, 128], BF16)
    identf = FX("identF", [128, 128], F32)
    onesb = FX("onesb", [128, 128], BF16)
    ones128f = FX("ones128f", [128, 128], F32)
    xtsb = FX("xtsb", [128, 1], F32)
    wsc = FX("wsc", [128, 16], F32)
    stats = FX("stats", [128, 64], F32)
    sconst = FX("sconst", [128, 8], F32)
    biasb = FX("biasb", [128, D], F32)
    fws = FX("fws", [128, 256], F32)
    lnwq = FX("lnwq", [128, 12], F32)
    lnbq = FX("lnbq", [128, 12], F32)
    lnwk = FX("lnwk", [128, 4], F32)
    lnbk = FX("lnbk", [128, 4], F32)
    warm = FX("warm", [128, 2], F32)
    AB = (cur + 255) // 256 * 256

    def mkarena():
        pos = [AB]

        def A(name, shape, dtype):
            off = (pos[0] + 31) // 32 * 32
            t, pos[0] = _alloc(name, shape, dtype, off)
            return t
        return A, pos

    A1, a1pos = mkarena()
    xtile = A1("xtile", [128, 2 * D], F32)
    xnbf = A1("xnbf", [128, D], BF16)
    xcs = A1("xcs", [128, D], F32)
    sqf = A1("sqf", [128, D], F32)
    xnT = A1("xnT", [128, 16 * W1], BF16)
    wcb = A1("wcb", [128, 4 * 2048], BF16)
    qlat = A1("qlat", [128, 12 * W1], BF16)
    sqb = A1("sqb", [128, 2 * W1], BF16)
    mub = A1("mub", [128, W1], F32)
    rstdb = A1("rstdb", [128, W1], F32)
    scr1 = A1("scr1", [128, max(W1, 128)], F32)
    qrev = A1("qrev", [128, W1], F32)
    qrrot = A1("qrrot", [128, W1], F32)
    kvlT = A1("kvlT", [128, 4 * W1], BF16)
    krT = A1("krT", [64, W1], BF16)
    krop = A1("krop", [128, DR], F32)
    agst = A1("agst", [128, AGW], BF16)
    kcosb = A1("kcosb", [128, TC * DR], F32)
    ksfb = A1("ksfb", [128, TC * DR], F32)
    fcosb = A1("fcosb", [128, W1], F32)
    fsfb = A1("fsfb", [128, W1], F32)
    fwn = A1("fwn", [128, D], F32)
    fwq = A1("fwq", [128, D], BF16)

    A2, a2pos = mkarena()
    knT = A2("knT", [128, 2 * S], BF16)
    vTb = A2("vTb", [128, 2 * S], BF16)
    qnT = A2("qnT", [128, 2 * S], BF16)
    qrTh = A2("qrTh", [64, 2 * S], BF16)
    krTg = A2("krTg", [64, S], BF16)
    latq = A2("latq", [128, 4 * TWW], BF16)
    pbuf = A2("pbuf", [128, 2 * S], BF16)
    ptb = A2("ptb", [128, KC * QB], BF16)
    attnT = A2("attnT", [128, S], BF16)
    qbTb = A2("qbTb", [128, 12 * 128], BF16)
    kvknb = A2("kvknb", [128, 4 * 128], BF16)
    kvvb = A2("kvvb", [128, 4 * 128], BF16)

    A3, a3pos = mkarena()
    p3x = A3("p3x", [128, 2 * D], F32)
    p3xdqb = A3("p3xdqb", [128, 2 * D], BF16)
    p3xdqT = A3("p3xdqT", [128, 2 * D], BF16)
    p3at = A3("p3at", [128, 2 * D], BF16)
    p3wdq = A3("p3wdq", [128, 16 * 1024], BF16)
    p3ow = A3("p3ow", [128, 16 * 1024], BF16)
    p3xb = A3("p3xb", [128, 2 * 1024], F32)

    a5pos = [(a2pos[0] + 255) // 256 * 256]

    def A5(name, shape, dtype):
        off = (a5pos[0] + 31) // 32 * 32
        t, a5pos[0] = _alloc(name, shape, dtype, off)
        return t

    p3xA = A5("p3xA", [128, D], F32)
    p3xdq = A5("p3xdq", [128, D], BF16)

    TAIL = (max(a1pos[0], a2pos[0], a3pos[0], a5pos[0]) + 255) // 256 * 256
    tl = [TAIL]

    def A4(name, shape, dtype):
        off = (tl[0] + 31) // 32 * 32
        t, tl[0] = _alloc(name, shape, dtype, off)
        return t

    q_c = A4("q_c", [128, D], F32)
    q_rA = A4("q_rA", [128, 1024], F32)
    q_rB = A4("q_rB", [128, 1024], F32)
    q_rC = A4("q_rC", [128, 1024], F32)
    q_rD = A4("q_rD", [128, 1024], F32)
    q_bs = A4("q_bs", [128, 64], F32)
    q_sA = A4("q_sA", [128, 64], F32)
    q_sB = A4("q_sB", [128, 64], F32)

    psA = st.alloc_psum_tensor("psA", [128, 2048], F32)
    psB = st.alloc_psum_tensor("psB", [128, 512], F32)
    psC = st.alloc_psum_tensor("psC", [128, 2048], BF16)
    psD = st.alloc_psum_tensor("psD", [128, 512], F32)

    EM = {k: [] for k in ("sync", "tensor", "vector", "scalar", "gpsimd")}

    class Q:
        def __init__(self, name):
            self.name = name

        def __call__(self, fn, inc=None, k=1):
            drain = self.name in ("vector", "scalar")
            if inc is None:
                EM[self.name].append(fn)
                if drain:
                    EM[self.name].append(lambda e: e.drain())
                return None
            v = inc.inc(k)
            if drain:
                EM[self.name].append(fn)
                EM[self.name].append(
                    lambda e, s=inc, kk=k: e.drain().then_inc(s.h, kk))
            else:
                EM[self.name].append(lambda e, f=fn, s=inc, kk=k: f(e).then_inc(s.h, kk))
            return v

        def w(self, sem, val):
            if val and val > 0:
                EM[self.name].append(lambda e, s=sem, v=val: e.wait_ge(s.h, v))

        def dma(self, out, in_, sem, slow=False, dyn=None):
            v = sem.inc(16)
            if dyn is None:
                EM[self.name].append(
                    lambda e, o=out, i=in_, s=sem, sl=slow: e.dma_start(
                        out=o, in_=i, allow_slow_non_contiguous=sl).then_inc(s.h, 16))
            else:
                ten, idx = dyn

                def _f(e, o=out, tn=ten, ix=idx, s=sem):
                    si, hl, tt = ix
                    src = tn.ap()[si][bass.ds(_rank_base(e), 1)].squeeze(0)[hl, tt]
                    return e.dma_start(out=o, in_=src).then_inc(s.h, 16)
                EM[self.name].append(_f)
            return v

    _rank_cache = {}

    def _rank_base(e):
        key = id(e)
        if key not in _rank_cache:
            _rank_cache[key] = e.compute_val(e.partition_id() % 4)
        return _rank_cache[key]

    _rank2_cache = {}

    def _rank2c(e, c0):
        key = (id(e), c0)
        if key not in _rank2_cache:
            _rank2_cache[key] = e.compute_val((e.partition_id() % 4) * 2 + c0)
        return _rank2_cache[key]

    T, V, Sc, G, Y = Q("tensor"), Q("vector"), Q("scalar"), Q("gpsimd"), Q("sync")

    def T_mm(ps_ap, lhs_ap, rhs_ap, start, stop, inc=None, nmax=512):
        n = rhs_ap.shape[-1]
        vals = None
        for o in range(0, n, nmax):
            w = min(nmax, n - o)
            last = o + w >= n
            vals = T(lambda e, p=ps_ap[:, o:o + w] if ps_ap.ndim == 2 else None,
                     l=lhs_ap, r=rhs_ap[:, o:o + w], st_=start, sp=stop:
                     e.matmul(p, l, r, start=st_, stop=sp),
                     inc=inc if last else None)
        return vals

    def mks(name):
        return Sem(st.alloc_semaphore(name), name)

    s_ld = mks("ld")
    s_x = mks("x")
    s_wg = mks("wg")
    s_tc = mks("tc")
    s_vc = mks("vc")
    s_sc = mks("sc")
    s_tp = mks("tp")
    s_pe = mks("pe")
    s_ag = mks("ag")
    s_cc = mks("cc")
    s_qk = mks("qk")
    s_mx = mks("mx")
    s_ex = mks("ex")
    s_ar = mks("ar")
    s_fe = mks("fe")
    s_pn = mks("pn")
    s_pv = mks("pv")
    s_pq = mks("pq")
    s_a2 = mks("a2")
    s_lat = mks("lat")
    s_latq = [mks(f"latq{i}") for i in range(4)]
    s_xs = [mks("xs0"), mks("xs1")]
    s_wgs = [mks("wgs0"), mks("wgs1"), mks("wgs2"), mks("wgs3")]
    s_fw = mks("fw")
    s_hw = mks("hw")
    s_qr = [mks("qr0"), mks("qr1")]
    s_krg = mks("krg")
    s_p3t = [mks("p3t0"), mks("p3t1")]
    s_p3xl = [mks("p3xl0"), mks("p3xl1")]
    s_ost = [mks("ost0"), mks("ost1")]
    s_wv = mks("wv")
    s_am = mks("am")
    s_rot = mks("rot")
    s_q2 = mks("q2")
    s_p3 = mks("p3")
    s_p3g = mks("p3g")
    s_xq = mks("xq")
    s_out = mks("out")
    s_fin = mks("fin")

    Y.dma(ident.ap(), ident_in.ap(), s_ld)
    Y.dma(identf.ap(), identf_in.ap(), s_ld)
    Y.dma(biasb.ap(), bias_in.ap(), s_ld)
    Y.dma(xtsb.ap(), xts_in.ap(), s_ld)
    for j in range(12):
        Y.dma(lnwq[:, j:j + 1], qalnw_in[j], s_ld)
        Y.dma(lnbq[:, j:j + 1], qalnb_in[j], s_ld)
    for j in range(4):
        Y.dma(lnwk[:, j:j + 1], kvalnw_in[j], s_ld)
        Y.dma(lnbk[:, j:j + 1], kvalnb_in[j], s_ld)
    for t in range(TC):
        Y.dma(kcosb[:, t * DR:(t + 1) * DR], ktcos_in[t * 128:(t + 1) * 128, :], s_ld)
        Y.dma(ksfb[:, t * DR:(t + 1) * DR], ktsf_in[t * 128:(t + 1) * 128, :], s_ld)
    Y.dma(fcosb.ap(), fcos_in.ap(), s_ld)
    Y.dma(fsfb.ap(), fsf_in.ap(), s_ld)
    ld_all = s_ld.n
    for t in range(min(2, TC)):
        Y.dma(xtile[:, (t % 2) * D:(t % 2) * D + D], x_in[t * 128:(t + 1) * 128, :],
              s_xs[t % 2])
    fw0_ld = Y.dma(fwn.ap(), ffnw_in[0], s_fw)

    V.w(s_ld, ld_all)
    Sc.w(s_ld, ld_all)
    Sc(lambda e: e.activation(warm[:, 0:1], xtsb.ap(), AF.Exp,
                              accum_out=warm[:, 1:2]))
    Sc(lambda e: e.activation(warm[:, 0:1], xtsb.ap(), AF.Square))
    Sc(lambda e: e.activation(warm[:, 0:1], warm[:, 0:1], AF.Ln))
    V(lambda e: e.memset(onesb.ap(), 1.0))
    V(lambda e: e.memset(ones128f.ap(), 1.0))
    V(lambda e: e.memset(sconst.ap(), 0.0))
    V(lambda e: e.memset(wsc.ap(), 0.0))
    V(lambda e: e.memset(stats.ap(), 0.0))

    psc_last_evac = [0, 0]
    psc_used = [0, 0]
    psc_next = [0]

    def transpose_group(srcs, dst_cb):
        hidx = psc_next[0] % 2
        psc_next[0] += 1
        if psc_used[hidx] > 0:
            T.w(s_pe, psc_last_evac[hidx])
        n = len(srcs)
        for i, ap_ in enumerate(srcs):
            wdt = ap_.shape[-1]
            dst = psC[0:wdt, hidx * 1024 + i * 128: hidx * 1024 + i * 128 + 128]
            T(lambda e, d=dst, s=ap_: e.transpose(d, s, ident.ap()),
              inc=s_tp if i == n - 1 else None)
        psc_used[hidx] += 1
        psc_last_evac[hidx] = dst_cb(hidx)
        return hidx

    negmu = sconst[:, 0:1]
    varl = sconst[:, 1:2]
    rstd1 = sconst[:, 2:3]

    for t in range(TC):
        xt = xtile[:, (t % 2) * D:(t % 2) * D + D]
        V.w(s_xs[t % 2], 16 * (t // 2 + 1))
        V.w(s_tp, s_tp.n)
        V(lambda e, xt=xt: e.reduce_sum(negmu, xt, axis=AX, negate=True))
        V(lambda e: e.tensor_scalar_mul(negmu, negmu, 1.0 / D))
        vv = V(lambda e, xt=xt: e.tensor_scalar_add(xcs.ap(), xt, negmu), inc=s_vc)
        xt_free = vv
        Sc.w(s_vc, vv)
        sv = Sc(lambda e: e.activation(sqf.ap(), xcs.ap(), AF.Square, accum_out=varl),
                inc=s_sc)
        V.w(s_sc, sv)
        vv = V(lambda e: e.tensor_scalar(varl, varl, 1.0 / D, float(EPS),
                                         OP.mult, OP.add), inc=s_vc)
        Sc.w(s_vc, vv)
        Sc(lambda e: e.activation(rstd1, varl, AF.Ln))
        sv = Sc(lambda e: e.activation(rstd1, rstd1, AF.Exp, scale=-0.5), inc=s_sc)
        V.w(s_sc, sv)
        v_xn = V(lambda e: e.tensor_scalar_mul(xnbf.ap(), xcs.ap(), rstd1), inc=s_vc)
        for g in range(2):
            T.w(s_vc, v_xn)
            if t == 0 and g == 0:
                T.w(s_ld, ld_all)
            tiles = [xnbf[:, (g * 8 + i) * 128:(g * 8 + i + 1) * 128] for i in range(8)]

            def ev(hidx, t=t, g=g):
                Sc.w(s_tp, s_tp.n)
                dst = xnT.ap().rearrange("p (k w) -> p k w", w=W1)[
                    :, g * 8:g * 8 + 8, t * 128:(t + 1) * 128]
                src = psC.ap().rearrange("p (h k w) -> p h k w", h=2, w=128)[:, hidx]
                return Sc(lambda e, d=dst, s=src: e.copy(out=d, in_=s), inc=s_pe)
            transpose_group(tiles, ev)
        if t + 2 < TC:
            Y.w(s_vc, xt_free)
            Y.dma(xtile[:, ((t + 2) % 2) * D:((t + 2) % 2) * D + D],
                  x_in[(t + 2) * 128:(t + 3) * 128, :], s_xs[t % 2])
    xnT_ready = s_pe.n

    wslot_seq = [0]
    wslot_consumed = {}

    def load_wblock(dram_ap_3d, kn, wn):
        sl = wslot_seq[0] % 4
        wslot_seq[0] += 1
        if sl in wslot_consumed:
            G.w(s_tc, wslot_consumed[sl])
        flat = wcb[:, sl * 2048: sl * 2048 + kn * wn]
        dst = flat.rearrange("p (k w) -> p k w", w=wn)
        v = G.dma(dst, dram_ap_3d, s_wgs[sl])
        return flat, v, sl

    def p1_proj(n_chunks, w_blk, out_cb):
        for ncx in range(n_chunks):
            blk, dv, sl = load_wblock(w_blk[ncx], 16, 128)
            T.w(s_wgs[sl], dv)
            mmv = 0
            for kcx in range(16):
                lhs = blk[:, kcx * 128:(kcx + 1) * 128]
                rhs = xnT.ap().rearrange("p (k w) -> p k w", w=W1)[:, kcx]
                mmv = T_mm(psA[:, 0:W1], lhs, rhs, kcx == 0, kcx == 15, inc=s_tc)
            wslot_consumed[sl] = mmv
            out_cb(ncx, mmv)

    T.w(s_pe, xnT_ready)
    if dbg:
        G.w(s_pe, xnT_ready)
        G.dma(dbg_xnT.ap(), xnT.ap(), s_out)

    def qa_out(ncx, mmv):
        Sc.w(s_tc, mmv)
        ev = Sc(lambda e, n=ncx: e.copy(out=qlat[:, n * W1:(n + 1) * W1],
                                        in_=psA[:, 0:W1]), inc=s_sc)
        T.w(s_sc, ev)

    p1_proj(12, qaT_b, qa_out)

    def ln_feature_major(nch, buf, lnw, lnb, dim):
        T.w(s_sc, s_sc.n)
        for ncx in range(nch):
            T_mm(psA[:, 1024:1024 + W1], onesb.ap(), buf[:, ncx * W1:(ncx + 1) * W1],
                 ncx == 0, ncx == nch - 1, inc=s_tc)
        V.w(s_tc, s_tc.n)
        vv = V(lambda e: e.tensor_scalar_mul(mub.ap(), psA[:, 1024:1024 + W1],
                                             1.0 / dim), inc=s_vc)
        T.w(s_vc, vv)
        sq_mm = {}
        for ncx in range(nch):
            if ncx >= 2:
                Sc.w(s_tc, sq_mm[ncx - 2])
            sv = Sc(lambda e, n=ncx: e.activation(
                sqb[:, (n % 2) * W1:(n % 2) * W1 + W1],
                buf[:, n * W1:(n + 1) * W1], AF.Square), inc=s_sc)
            T.w(s_sc, sv)
            sq_mm[ncx] = T_mm(psA[:, 1024:1024 + W1], onesb.ap(),
                              sqb[:, (ncx % 2) * W1:(ncx % 2) * W1 + W1],
                              ncx == 0, ncx == nch - 1, inc=s_tc)
        V.w(s_tc, s_tc.n)
        V(lambda e: e.tensor_scalar_mul(rstdb.ap(), psA[:, 1024:1024 + W1], 1.0 / dim))
        V(lambda e: e.tensor_tensor(out=mub.ap(), in0=mub.ap(), in1=mub.ap(),
                                    op=OP.mult) if False else
          e.tensor_tensor(out=scr1[:, 0:W1], in0=mub.ap(), in1=mub.ap(), op=OP.mult))
        V(lambda e: e.tensor_tensor(out=rstdb.ap(), in0=rstdb.ap(), in1=scr1[:, 0:W1],
                                    op=OP.subtract))
        vv = V(lambda e: e.tensor_scalar_add(rstdb.ap(), rstdb.ap(), float(EPS)),
               inc=s_vc)
        Sc.w(s_vc, vv)
        Sc(lambda e: e.activation(rstdb.ap(), rstdb.ap(), AF.Ln))
        sv = Sc(lambda e: e.activation(rstdb.ap(), rstdb.ap(), AF.Exp, scale=-0.5),
                inc=s_sc)
        V.w(s_sc, sv)
        out_vals = []
        for ncx in range(nch):
            V(lambda e, n=ncx: e.tensor_tensor(
                out=scr1[:, 0:W1], in0=buf[:, n * W1:(n + 1) * W1], in1=mub.ap(),
                op=OP.subtract))
            V(lambda e: e.tensor_tensor(out=scr1[:, 0:W1], in0=scr1[:, 0:W1],
                                        in1=rstdb.ap(), op=OP.mult))
            out_vals.append(V(lambda e, n=ncx: e.tensor_scalar(
                buf[:, n * W1:(n + 1) * W1], scr1[:, 0:W1],
                lnw[:, n:n + 1], lnb[:, n:n + 1], OP.mult, OP.add), inc=s_vc))
        return out_vals

    V.w(s_ld, ld_all)
    qa_vals = ln_feature_major(12, qlat, lnwq, lnbq, QL)
    for ncx in range(12):
        G.w(s_vc, qa_vals[ncx])
        G.dma(ag_in[ncx], qlat[:, ncx * W1:(ncx + 1) * W1], s_ag)

    G.w(s_ag, s_ag.n)
    cc_a = G(lambda e: e.collective_compute(
        "AllGather", OP.bypass, replica_groups=[[0, 1, 2, 3], [4, 5, 6, 7]],
        ins=[ag_in.ap()[0:12].opt()], outs=[ag_out_a.ap().opt()]), inc=s_cc)

    def kv_out(ncx, mmv):
        Sc.w(s_tc, mmv)
        ev = Sc(lambda e, n=ncx: e.copy(out=kvlT[:, n * W1:(n + 1) * W1],
                                        in_=psA[:, 0:W1]), inc=s_sc)
        T.w(s_sc, ev)

    p1_proj(4, kvaT_b, kv_out)
    kv_vals = ln_feature_major(4, kvlT, lnwk, lnbk, KVL)
    for ncx in range(4):
        G.w(s_vc, kv_vals[ncx])
        G.dma(ag_in[20 + ncx], kvlT[:, ncx * W1:(ncx + 1) * W1], s_ag)

    kr_blk, kr_dv, kr_sl = load_wblock(kvrT_b.ap(), 16, DR)
    T.w(s_wgs[kr_sl], kr_dv)
    kr_mm = 0
    for t in range(TC):
        for kcx in range(16):
            lhs = xnT.ap().rearrange("p (k w) -> p k w", w=W1)[
                :, kcx, t * 128:(t + 1) * 128]
            rhs = kr_blk[:, kcx * DR:(kcx + 1) * DR]
            kr_mm = T(lambda e, l=lhs, r=rhs, st_=(kcx == 0), sp=(kcx == 15):
                      e.matmul(psB[:, 0:DR], l, r, start=st_, stop=sp), inc=s_tc)
        V.w(s_tc, kr_mm)
        V.w(s_tp, s_tp.n)
        cosr = kcosb[:, t * DR:(t + 1) * DR]
        sfr = ksfb[:, t * DR:(t + 1) * DR]
        vv = V(lambda e: e.tensor_copy(out=krop.ap(), in_=psB[:, 0:DR]), inc=s_vc)
        T.w(s_vc, vv)

        def _rope_a(e, cosr=cosr, sfr=sfr):
            e.tensor_tensor(out=scr1[:, 0:DR], in0=krop.ap(), in1=cosr, op=OP.mult)
            e.tensor_tensor(out=scr1[:, DR:DR + 32], in0=krop[:, 32:64],
                            in1=sfr[:, 0:32], op=OP.mult)
            return e.tensor_tensor(out=scr1[:, DR + 32:DR + 64], in0=krop[:, 0:32],
                                   in1=sfr[:, 32:64], op=OP.mult)
        V(_rope_a)
        vv = V(lambda e: e.tensor_tensor(out=xnbf[:, 0:DR], in0=scr1[:, 0:DR],
                                         in1=scr1[:, DR:2 * DR], op=OP.add),
               inc=s_vc)
        T.w(s_vc, vv)

        def evk(hidx, t=t):
            Sc.w(s_tp, s_tp.n)
            return Sc(lambda e, t_=t, h=hidx: e.copy(
                out=krT[0:64, t_ * 128:(t_ + 1) * 128],
                in_=psC[0:64, h * 1024:h * 1024 + 128]), inc=s_pe)
        transpose_group([xnbf[:, 0:DR]], evk)
    wslot_consumed[kr_sl] = kr_mm
    G.w(s_pe, s_pe.n)
    G.dma(ag_in[24, 0:64, :], krT.ap(), s_ag)
    G.dma(ag_in[24, 64:128, :], krT.ap(), s_ag)
    G.w(s_ag, s_ag.n)
    cc_b = G(lambda e: e.collective_compute(
        "AllGather", OP.bypass, replica_groups=[[0, 1, 2, 3], [4, 5, 6, 7]],
        ins=[ag_in.ap()[20:25].opt()], outs=[ag_out_b.ap().opt()]), inc=s_cc)

    qr_mm_last = 0
    for ncx in range(8):
        blk, dv, sl = load_wblock(qrwT_b[ncx], 12, 128)
        T.w(s_wgs[sl], dv)
        mmv = 0
        for kcx in range(12):
            lhs = blk[:, kcx * 128:(kcx + 1) * 128]
            rhs = qlat.ap().rearrange("p (k w) -> p k w", w=W1)[:, kcx]
            mmv = T_mm(psA[:, 0:W1], lhs, rhs, kcx == 0, kcx == 11, inc=s_tc)
        wslot_consumed[sl] = mmv
        qr_mm_last = mmv
        Sc.w(s_tc, mmv)
        Sc.w(s_rot, s_rot.n)
        Sc.w(s_vc, s_vc.n)
        ev = Sc(lambda e: e.copy(out=qrev.ap(), in_=psA[:, 0:W1]), inc=s_sc)
        T.w(s_sc, ev)
        G.w(s_sc, ev)
        G.dma(qrrot[0:32, :], qrev[32:64, :], s_rot)
        G.dma(qrrot[32:64, :], qrev[0:32, :], s_rot)
        G.dma(qrrot[64:96, :], qrev[96:128, :], s_rot)
        rv = G.dma(qrrot[96:128, :], qrev[64:96, :], s_rot)
        V.w(s_rot, rv)
        V.w(s_ld, ld_all)
        V.w(s_ag, s_ag.n)
        V(lambda e: e.tensor_tensor(out=qrrot.ap(), in0=qrrot.ap(), in1=fsfb.ap(),
                                    op=OP.mult))
        V(lambda e: e.tensor_tensor(out=qrev.ap(), in0=qrev.ap(), in1=fcosb.ap(),
                                    op=OP.mult))
        vv = V(lambda e: e.tensor_tensor(out=agst[:, 0:W1], in0=qrev.ap(),
                                         in1=qrrot.ap(), op=OP.add), inc=s_vc)
        G.w(s_vc, vv)
        G.dma(ag_in[12 + ncx], agst[:, 0:W1], s_ag)

    V.w(s_fw, fw0_ld)
    fwA_vals = []
    for ch in range(2):
        if ch == 1:
            lv = Y.dma(fwn.ap(), ffnw_in[1], s_fw)
            V.w(s_fw, lv)
        wv_ = fwn.ap().rearrange("p (b k) -> p b k", k=16)
        wn_ = fwq.ap().bitcast(F32).rearrange("p (b k) -> p b k", k=8)
        V(lambda e, wv=wv_, wn=wn_: e.tensor_tensor(
            out=wn, in0=wv[:, :, 0:16:2], in1=wv[:, :, 1:16:2], op=OP.min))
        V(lambda e, wv=wv_: e.tensor_tensor(out=wv[:, :, 0:8], in0=wv[:, :, 0:16:2],
                                            in1=wv[:, :, 1:16:2], op=OP.max))
        V(lambda e, wv=wv_: e.tensor_tensor(out=wv[:, :, 0:4], in0=wv[:, :, 0:8:2],
                                            in1=wv[:, :, 1:8:2], op=OP.max))
        V(lambda e, wn=wn_: e.tensor_tensor(out=wn[:, :, 0:4], in0=wn[:, :, 0:8:2],
                                            in1=wn[:, :, 1:8:2], op=OP.min))
        V(lambda e, wv=wv_: e.tensor_tensor(out=wv[:, :, 0:2], in0=wv[:, :, 0:4:2],
                                            in1=wv[:, :, 1:4:2], op=OP.max))
        V(lambda e, wn=wn_: e.tensor_tensor(out=wn[:, :, 0:2], in0=wn[:, :, 0:4:2],
                                            in1=wn[:, :, 1:4:2], op=OP.min))
        V(lambda e, wv=wv_, c=ch: e.tensor_tensor(
            out=fws[:, c * 128:(c + 1) * 128], in0=wv[:, :, 0:1].squeeze(2),
            in1=wv[:, :, 1:2].squeeze(2), op=OP.max))
        V(lambda e, wn=wn_, c=ch: e.tensor_tensor(
            out=scr1[:, 0:128], in0=wn[:, :, 0:1].squeeze(2),
            in1=wn[:, :, 1:2].squeeze(2), op=OP.min))
        V(lambda e: e.tensor_scalar_mul(scr1[:, 0:128], scr1[:, 0:128], -1.0))
        fwA_vals.append(V(lambda e, c=ch: e.tensor_tensor(
            out=fws[:, c * 128:(c + 1) * 128], in0=fws[:, c * 128:(c + 1) * 128],
            in1=scr1[:, 0:128], op=OP.max), inc=s_vc))
        if ch == 0:
            Y.w(s_vc, fwA_vals[0])
    vv = V(lambda e: e.reduce_max(wsc[:, 0:1], fws.ap(), axis=AX), inc=s_vc)
    T.w(s_vc, vv)
    T.w(s_ld, ld_all)
    tv = T(lambda e: e.transpose(psB[0:1, 128:256], wsc[:, 0:1], identf.ap()), inc=s_tc)
    V.w(s_tc, tv)
    vv = V(lambda e: e.reduce_max(wsc[0:1, 4:5], psB[0:1, 128:256], axis=AX),
           inc=s_vc)
    T.w(s_vc, vv)
    G.w(s_vc, vv)
    qv = G.dma(amax_in.ap(), wsc[0:1, 0:16], s_q2)
    G.w(s_q2, qv)
    cc_amax = G(lambda e: e.collective_compute(
        "AllGather", OP.bypass, replica_groups=[list(range(NC))],
        ins=[amax_in.ap().opt()], outs=[amax_out.ap().opt()]), inc=s_cc)
    G.w(s_ag, s_ag.n)
    cc_c = G(lambda e: e.collective_compute(
        "AllGather", OP.bypass, replica_groups=[[0, 1, 2, 3], [4, 5, 6, 7]],
        ins=[ag_in.ap()[12:20].opt()], outs=[ag_out_c.ap().opt()]), inc=s_cc)

    G.w(s_cc, cc_amax)
    qv = G.dma(wsc[0:1, 8:16],
               amax_out.ap().rearrange("c a k -> a (c k)")[0:1, 4:125:16], s_q2,
               slow=True)
    V.w(s_q2, qv)
    V(lambda e: e.reduce_max(wsc[0:1, 1:2], wsc[0:1, 8:16], axis=AX))
    V(lambda e: e.tensor_scalar_mul(wsc[0:1, 2:3], wsc[0:1, 1:2], 1.0 / 2688.0))
    vv = V(lambda e: e.tensor_scalar_mul(wsc[0:1, 3:4], wsc[0:1, 2:3], 6.0), inc=s_vc)
    G.w(s_vc, vv)
    qv = G.dma(amax_in.ap(), wsc[0:1, 0:16], s_q2)
    G.w(s_q2, qv)
    qv = G.dma(wsc.ap(), amax_in.ap()[0:1, 0:16].broadcast_to([128, 16]), s_q2)
    V.w(s_q2, qv)
    V(lambda e: e.reciprocal(wsc[:, 5:6], wsc[:, 3:4]))
    V(lambda e: e.tensor_scalar_mul(wsc[:, 6:7], wsc[:, 2:3], xtsb.ap()))
    V(lambda e: e.tensor_scalar_mul(wsc[:, 7:8], xtsb.ap(), 6.0))
    rfx_done = V(lambda e: e.reciprocal(wsc[:, 7:8], wsc[:, 7:8]), inc=s_vc)

    def nvfp4(src, width, bs, sA, sB, cc, rA, rB, rC, rD, rfac, ts_lane, dst,
              post=None):
        last = [None]
        for r in nvfp4_gen(src, width, bs, sA, sB, cc, rA, rB, rC, rD, rfac,
                           ts_lane, dst, post):
            if r is not None:
                last[0] = r
        return last[0]

    def nvfp4_gen(src, width, bs, sA, sB, cc, rA, rB, rC, rD, rfac, ts_lane,
                  dst, post=None):
        nb = width // 16
        sv = src.rearrange("p (b k) -> p b k", k=16)
        t8 = rA[:, 0:width // 2].rearrange("p (b k) -> p b k", k=8)
        t8n = rB[:, 0:width // 2].rearrange("p (b k) -> p b k", k=8)
        V(lambda e: e.tensor_tensor(out=t8, in0=sv[:, :, 0:16:2],
                                    in1=sv[:, :, 1:16:2], op=OP.max))
        yield None
        V(lambda e: e.tensor_tensor(out=t8n, in0=sv[:, :, 0:16:2],
                                    in1=sv[:, :, 1:16:2], op=OP.min))
        yield None
        V(lambda e: e.tensor_tensor(out=t8[:, :, 0:4], in0=t8[:, :, 0:8:2],
                                    in1=t8[:, :, 1:8:2], op=OP.max))
        yield None
        V(lambda e: e.tensor_tensor(out=t8n[:, :, 0:4], in0=t8n[:, :, 0:8:2],
                                    in1=t8n[:, :, 1:8:2], op=OP.min))
        yield None
        V(lambda e: e.tensor_tensor(out=t8[:, :, 0:2], in0=t8[:, :, 0:4:2],
                                    in1=t8[:, :, 1:4:2], op=OP.max))
        yield None
        V(lambda e: e.tensor_tensor(out=t8n[:, :, 0:2], in0=t8n[:, :, 0:4:2],
                                    in1=t8n[:, :, 1:4:2], op=OP.min))
        yield None
        V(lambda e: e.tensor_tensor(out=bs[:, 0:nb], in0=t8[:, :, 0:1].squeeze(2),
                                    in1=t8[:, :, 1:2].squeeze(2), op=OP.max))
        yield None
        V(lambda e: e.tensor_tensor(out=sA[:, 0:nb], in0=t8n[:, :, 0:1].squeeze(2),
                                    in1=t8n[:, :, 1:2].squeeze(2), op=OP.min))
        yield None
        V(lambda e: e.tensor_scalar_mul(sA[:, 0:nb], sA[:, 0:nb], -1.0))
        yield None
        V(lambda e: e.tensor_tensor(out=bs[:, 0:nb], in0=bs[:, 0:nb],
                                    in1=sA[:, 0:nb], op=OP.max))
        yield None
        V(lambda e: e.tensor_scalar_mul(bs[:, 0:nb], bs[:, 0:nb], rfac))
        yield None
        V(lambda e: e.tensor_scalar(bs[:, 0:nb], bs[:, 0:nb], 0.0, 448.0,
                                    OP.max, OP.min))
        yield None
        V(lambda e: e.tensor_scalar_mul(sA[:, 0:nb], bs[:, 0:nb], VELT))
        yield None
        V(lambda e: e.tensor_tensor(out=sB[:, 0:nb], in0=sA[:, 0:nb],
                                    in1=bs[:, 0:nb], op=OP.subtract))
        yield None
        V(lambda e: e.tensor_tensor(out=sA[:, 0:nb], in0=sA[:, 0:nb],
                                    in1=sB[:, 0:nb], op=OP.subtract))
        yield None
        V(lambda e: e.tensor_scalar(sB[:, 0:nb], bs[:, 0:nb], M_SUB, M_SUB,
                                    OP.add, OP.subtract))
        yield None
        V(lambda e: e.tensor_scalar(cc[:, 0:nb], bs[:, 0:nb], E4_MIN, None, OP.is_lt))
        yield None
        V(lambda e: e.tensor_tensor(out=sB[:, 0:nb], in0=sB[:, 0:nb],
                                    in1=sA[:, 0:nb], op=OP.subtract))
        yield None
        V(lambda e: e.tensor_tensor(out=sB[:, 0:nb], in0=sB[:, 0:nb],
                                    in1=cc[:, 0:nb], op=OP.mult))
        yield None
        V(lambda e: e.tensor_tensor(out=bs[:, 0:nb], in0=sA[:, 0:nb],
                                    in1=sB[:, 0:nb], op=OP.add))
        yield None
        V(lambda e: e.tensor_scalar(sA[:, 0:nb], bs[:, 0:nb], 0.0, None, OP.is_equal))
        yield None
        V(lambda e: e.tensor_tensor(out=sB[:, 0:nb], in0=bs[:, 0:nb],
                                    in1=sA[:, 0:nb], op=OP.add))
        yield None
        V(lambda e: e.tensor_scalar_mul(sB[:, 0:nb], sB[:, 0:nb], ts_lane))
        yield None
        V(lambda e: e.reciprocal(sB[:, 0:nb], sB[:, 0:nb]))
        yield None
        V(lambda e: e.tensor_tensor(
            out=cc.rearrange("p (b k) -> p b k", k=16), in0=sv,
            in1=sB[:, 0:nb].unsqueeze(2).broadcast_to([128, nb, 16]), op=OP.mult))
        yield None
        V(lambda e: e.tensor_scalar(cc, cc, -6.0, 6.0, OP.max, OP.min))
        yield None
        V(lambda e: e.tensor_scalar(rB, cc, 2.0, C_RNE, OP.mult, OP.add))
        yield None
        V(lambda e: e.tensor_scalar(rB, rB, C_RNE, 0.5, OP.subtract, OP.mult))
        yield None
        V(lambda e: e.tensor_scalar(rC, cc, C_RNE, C_RNE, OP.add, OP.subtract))
        yield None
        V(lambda e: e.tensor_scalar(rD, cc, 0.5, C_RNE, OP.mult, OP.add))
        yield None
        V(lambda e: e.tensor_scalar(rD, rD, C_RNE, 2.0, OP.subtract, OP.mult))
        yield None
        V(lambda e: e.tensor_tensor(out=rA, in0=cc, in1=cc, op=OP.mult))
        yield None
        V(lambda e: e.tensor_scalar(cc, rA, 16.0, None, OP.is_lt))
        yield None
        V(lambda e: e.tensor_scalar(rA, rA, 4.0, None, OP.is_lt))
        yield None
        V(lambda e: e.tensor_tensor(out=rC, in0=rC, in1=rD, op=OP.subtract))
        yield None
        V(lambda e: e.tensor_tensor(out=rC, in0=rC, in1=cc, op=OP.mult))
        yield None
        V(lambda e: e.tensor_tensor(out=rD, in0=rD, in1=rC, op=OP.add))
        yield None
        V(lambda e: e.tensor_tensor(out=rB, in0=rB, in1=rD, op=OP.subtract))
        yield None
        V(lambda e: e.tensor_tensor(out=rB, in0=rB, in1=rA, op=OP.mult))
        yield None
        V(lambda e: e.tensor_tensor(out=rD, in0=rD, in1=rB, op=OP.add))
        yield None
        if post is not None:
            V(lambda e: e.tensor_scalar_mul(bs[:, 0:nb], bs[:, 0:nb], post))
            yield None
        yield V(lambda e: e.tensor_tensor(
            out=dst.rearrange("p (b k) -> p b k", k=16),
            in0=rD.rearrange("p (b k) -> p b k", k=16),
            in1=bs[:, 0:nb].unsqueeze(2).broadcast_to([128, nb, 16]), op=OP.mult),
            inc=s_vc)

    qv_prev = 0
    for ch in range(2):
        Y.w(s_tc, max(kr_mm, qr_mm_last))
        Y.w(s_vc, qv_prev)
        if ch == 0:
            Y.w(s_vc, fwA_vals[1])
        lv = Y.dma(q_c.ap(), ffnw_in[ch], s_fw)
        V.w(s_fw, lv)
        V.w(s_vc, rfx_done)
        V.w(s_tc, qr_mm_last)
        V.w(s_tp, s_tp.n)
        for hf in range(2):
            qv = nvfp4(q_c[:, hf * 1024:(hf + 1) * 1024], 1024,
                       q_bs.ap(), q_sA.ap(), q_sB.ap(),
                       cc=q_rB.ap(), rA=q_rA.ap(), rB=q_rC.ap(), rC=q_rD.ap(),
                       rD=fwn[:, 0:1024], rfac=wsc[:, 5:6], ts_lane=wsc[:, 2:3],
                       dst=fwq[:, hf * 1024:(hf + 1) * 1024])
        qv_prev = qv
        for g in range(2):
            T.w(s_vc, qv)
            tiles = [fwq[:, (g * 8 + i) * 128:(g * 8 + i + 1) * 128] for i in range(8)]

            def evw(hidx, ch=ch, g=g):
                Sc.w(s_tp, s_tp.n)
                Sc.w(s_q2, s_q2.n)
                return Sc(lambda e, h=hidx: e.copy(
                    out=agst[:, 0:1024], in_=psC[:, h * 1024:h * 1024 + 1024]),
                    inc=s_pe)
            hid = transpose_group(tiles, evw)
            G.w(s_pe, psc_last_evac[hid])
            for i in range(8):
                G.dma(wdq_in[g * 8 + i, :, ch * 128:(ch + 1) * 128],
                      agst[:, i * 128:(i + 1) * 128], s_q2)
    partB_done = qv_prev
    G.w(s_q2, s_q2.n)
    cc_wdq = G(lambda e: e.collective_compute(
        "AllGather", OP.bypass, replica_groups=[list(range(NC))],
        ins=[wdq_in.ap().opt()], outs=[wdq_out.ap().opt()]), inc=s_cc)

    Y.w(s_cc, cc_b)
    for src in range(4):
        Y.dma(krTg[0:64, src * W1:(src + 1) * W1], ag_out_b[src, 4, 0:64, :], s_krg)
    krg_all = s_krg.n

    lat_slot = [0]
    lat_slot_consumed = {}

    def dma_lat(chunk, tw):
        sl = lat_slot[0] % 4
        lat_slot[0] += 1
        if sl in lat_slot_consumed:
            Y.w(s_tc, lat_slot_consumed[sl])
        if chunk < 12:
            gat, gch = ag_out_a, chunk
            Y.w(s_cc, cc_a)
        else:
            gat, gch = ag_out_b, chunk - 20
            Y.w(s_cc, cc_b)
        dv = 0
        npc = max(1, TWW // W1)
        pw = min(TWW, W1)
        for p in range(npc):
            tok0 = tw * TWW + p * pw
            src = tok0 // W1
            coff = tok0 % W1
            dv = Y.dma(latq[:, sl * TWW + p * pw: sl * TWW + (p + 1) * pw],
                       gat[src, gch, :, coff:coff + pw], s_latq[sl])
        return sl, dv

    psD_last_evac = [0]
    attnT_last_store = [0]
    head_qk_done = {-2: 0, -1: 0}

    proj_last_mm = [0]

    def proj_steps(h):
        hp = h % 2
        G.w(s_tc, proj_last_mm[0])
        G.dma(qbTb.ap().rearrange("p (k w) -> p k w", w=128), qbT_b[h], s_hw)
        G.dma(kvknb.ap().rearrange("p (k w) -> p k w", w=128), kvbknT_b[h], s_hw)
        gv3 = G.dma(kvvb.ap().rearrange("p (k w) -> p k w", w=128), kvbvT_b[h], s_hw)
        chq0 = h // 2
        roff = (h % 2) * 64
        if h >= 2:
            Y.w(s_qk, head_qk_done[h - 2])
        sqr = s_qr[h % 2]
        qrv = sqr.inc(16)
        out_ap = qrTh[0:64, hp * S:(hp + 1) * S].rearrange("p (s w) -> p s w", w=W1)
        Y.w(s_cc, cc_c)

        def _fq(e, o=out_ap, c0=chq0, ro=roff, sh=sqr.h):
            src_ap = ag_out_c.ap()[:, bass.ds(_rank2c(e, c0), 1), ro:ro + 64, :]
            src_ap = src_ap.squeeze(1).transpose([1, 0, 2])
            return e.dma_start(out=o, in_=src_ap).then_inc(sh, 16)
        EM["sync"].append(_fq)
        T.w(s_hw, gv3)
        yield qrv
        for tw in range(NTW):
            mmv = 0
            for halfq in range(2):
                for kcx in range(6 * halfq, 6 * halfq + 6):
                    sl, dv = dma_lat(kcx, tw)
                    T.w(s_latq[sl], dv)
                    mmv = T(lambda e, k=kcx, sl=sl, st_=(kcx == 0), sp=(kcx == 11):
                            e.matmul(psB.ap(), qbTb[:, k * 128:(k + 1) * 128],
                                     latq[:, sl * TWW:(sl + 1) * TWW],
                                     start=st_, stop=sp), inc=s_tc)
                    lat_slot_consumed[sl] = mmv
                yield
            Sc.w(s_tc, mmv)
            ev = Sc(lambda e, hp=hp, tw=tw: e.copy(
                out=qnT[:, hp * S + tw * TWW: hp * S + (tw + 1) * TWW],
                in_=psB.ap()), inc=s_sc)
            T.w(s_sc, ev)
            kv_sl = []
            mmv = 0
            for kcx in range(4):
                sl, dv = dma_lat(20 + kcx, tw)
                kv_sl.append(sl)
                T.w(s_latq[sl], dv)
                mmv = T(lambda e, k=kcx, sl=sl, st_=(kcx == 0), sp=(kcx == 3):
                        e.matmul(psB.ap(), kvknb[:, k * 128:(k + 1) * 128],
                                 latq[:, sl * TWW:(sl + 1) * TWW],
                                 start=st_, stop=sp), inc=s_tc)
            yield
            Sc.w(s_tc, mmv)
            ev = Sc(lambda e, hp=hp, tw=tw: e.copy(
                out=knT[:, hp * S + tw * TWW: hp * S + (tw + 1) * TWW],
                in_=psB.ap()), inc=s_sc)
            T.w(s_sc, ev)
            mmv = 0
            for tsub in range(TWW // 128):
                for kcx in range(4):
                    sl = kv_sl[kcx]
                    mmv = T(lambda e, k=kcx, sl=sl, ts=tsub,
                            st_=(kcx == 0), sp=(kcx == 3): e.matmul(
                                psB[:, ts * 128:(ts + 1) * 128],
                                latq[:, sl * TWW + ts * 128: sl * TWW + (ts + 1) * 128],
                                kvvb[:, k * 128:(k + 1) * 128],
                                start=st_, stop=sp), inc=s_tc)
            for sl in kv_sl:
                lat_slot_consumed[sl] = mmv
            Sc.w(s_tc, mmv)
            ev = Sc(lambda e, hp=hp, tw=tw: e.copy(
                out=vTb[:, hp * S + tw * TWW: hp * S + (tw + 1) * TWW],
                in_=psB.ap()), inc=s_sc)
            T.w(s_sc, ev)
            proj_last_mm[0] = mmv
            yield

    psA_q_last_exp = [0, 0]
    pbuf_last_tp = [0, 0]
    ptb_last_pv = [0]
    qc_glob = [0]

    def attn_head(h, qr_ld, next_gen):
        hp = h % 2
        kn = knT[:, hp * S:(hp + 1) * S]
        vv_ = vTb[:, hp * S:(hp + 1) * S]
        qn = qnT[:, hp * S:(hp + 1) * S]
        qr = qrTh[0:64, hp * S:(hp + 1) * S]
        T.w(s_qr[h % 2], qr_ld)
        T.w(s_krg, krg_all)

        def emit_transp(info):
            g_, pb_, qoff_, pnv_ = info
            for gg in range((KC + 7) // 8):
                ntl = min(8, KC - gg * 8)
                T.w(s_pn, pnv_)
                if ptb_last_pv[0] > 0:
                    T.w(s_pv, ptb_last_pv[0])
                tiles = [pb_[:, (gg * 8 + i) * 128:(gg * 8 + i + 1) * 128]
                         for i in range(ntl)]

                def evp(hidx, gg=gg, qoff=qoff_, ntl=ntl):
                    V.w(s_tp, s_tp.n)
                    if ptb_last_pv[0] > 0:
                        V.w(s_pv, ptb_last_pv[0])
                    dst = ptb.ap().rearrange("p (k w) -> p k w", w=QB)[
                        :, gg * 8: gg * 8 + ntl, qoff:qoff + 128]
                    src = psC.ap().rearrange("p (h k w) -> p h k w", h=2, w=128)[
                        :, hidx, 0:ntl, :]
                    return V(lambda e, d=dst, s=src: e.tensor_copy(out=d, in_=s),
                             inc=s_pe)
                transpose_group(tiles, evp)
            pbuf_last_tp[g_ % 2] = s_tp.n

        pv_pend = [None]

        def emit_pv(qb):
            T.w(s_pe, s_pe.n)
            T.w(s_pq, psD_last_evac[0])
            pvv = 0
            for kcx in range(KC):
                pvv = T(lambda e, k=kcx, v=vv_, st_=(kcx == 0), sp=(kcx == KC - 1):
                        e.matmul(psD.ap(), v[:, k * 128:(k + 1) * 128],
                                 ptb[:, k * QB:(k + 1) * QB],
                                 start=st_, stop=sp), inc=s_pv)
            ptb_last_pv[0] = pvv
            pv_pend[0] = (qb, pvv)

        def flush_pv():
            if pv_pend[0] is None:
                return
            qb, pvv = pv_pend[0]
            pv_pend[0] = None
            V.w(s_pv, pvv)
            if qb == 0:
                V.w(s_a2, attnT_last_store[0])
            pqv = V(lambda e, qb=qb: e.tensor_copy(
                out=attnT[:, qb * QB:(qb + 1) * QB], in_=psD.ap()), inc=s_pq)
            psD_last_evac[0] = pqv

        pending = None
        for qc in range(QC):
            flush_pv()
            g = qc_glob[0]
            qc_glob[0] += 1
            sb_ = (g % 2) * 32
            pb = pbuf[:, (g % 2) * S:(g % 2) * S + S]
            nmq = stats[:, sb_ + 0: sb_ + 1]
            nms = stats[:, sb_ + 1: sb_ + 2]
            for j in range(4):
                reg = j % 2
                T.w(s_ex, psA_q_last_exp[reg])
                T_mm(psA[:, reg * 1024: reg * 1024 + KQ],
                     qn[:, qc * 128:(qc + 1) * 128],
                     kn[:, j * KQ:(j + 1) * KQ], True, False)
                qkv = T_mm(psA[:, reg * 1024: reg * 1024 + KQ],
                           qr[:, qc * 128:(qc + 1) * 128],
                           krTg[0:64, j * KQ:(j + 1) * KQ], False, True, inc=s_qk)
                if j == 0:
                    V.w(s_qk, qkv)
                    V(lambda e, nmq=nmq: e.reduce_max(
                        nmq, psA[:, 0:128], axis=AX, negate=True))
                    mxv = V(lambda e, nms=nms, nmq=nmq: e.tensor_scalar_mul(
                        nms, nmq, ATT_SCALE), inc=s_mx)
                    Sc.w(s_mx, mxv)
                    if g >= 2:
                        Sc.w(s_tp, pbuf_last_tp[g % 2])
                Sc.w(s_qk, qkv)
                ssum = stats[:, sb_ + 20 + j: sb_ + 21 + j]
                exv = Sc(lambda e, j=j, reg=reg, pb=pb, nms=nms, ssum=ssum:
                         e.activation(pb[:, j * KQ:(j + 1) * KQ],
                                      psA[:, reg * 1024: reg * 1024 + KQ],
                                      AF.Exp, bias=nms, scale=ATT_SCALE,
                                      accum_out=ssum), inc=s_ex)
                psA_q_last_exp[reg] = exv
            if pending is not None:
                emit_transp(pending)
                if qc % QCB == 0 and qc > 0:
                    emit_pv(qc // QCB - 1)
            V.w(s_ex, s_ex.n)
            tot = stats[:, sb_ + 24: sb_ + 25]
            rcp = stats[:, sb_ + 25: sb_ + 26]
            V(lambda e, tot=tot, sb_=sb_: e.reduce_sum(
                tot, stats[:, sb_ + 20: sb_ + 24], axis=AX))
            V(lambda e, rcp=rcp, tot=tot: e.reciprocal(rcp, tot))
            pnv = V(lambda e, pb=pb, rcp=rcp: e.tensor_scalar_mul(
                pb, pb, rcp), inc=s_pn)
            pending = (g, pb, (qc % QCB) * 128, pnv)
            if next_gen is not None:
                next(next_gen, None)
            if q3_box[0] is not None:
                for _ in range(6):
                    next(q3_box[0], None)
        emit_transp(pending)
        emit_pv(QC // QCB - 1)
        flush_pv()
        G.w(s_pq, s_pq.n)
        av = 0
        for j in range(4):
            for tcl in range(TC):
                av = G.dma(a2a_in[h, j, tcl],
                           attnT[:, j * W1 + tcl * 128: j * W1 + (tcl + 1) * 128],
                           s_a2)
        attnT_last_store[0] = av
        G.w(s_a2, s_a2.n)
        cc_a2a_h[h] = G(lambda e, h=h: e.collective_compute(
            "AllGather", OP.bypass, replica_groups=[[0, 1, 2, 3], [4, 5, 6, 7]],
            ins=[a2a_in.ap()[h].opt()], outs=[a2a_out.ap()[h].opt()]), inc=s_cc)
        Y.w(s_cc, cc_a2a_h[h])
        s_am.inc(16)

        def _fa(e, h=h):
            sap = a2a_out.ap()[h][:, bass.ds(_rank_base(e), 1)].squeeze(1)
            dst = a2a_mine.ap().rearrange("(s hl) t p w -> s hl t p w", s=4)[:, h]
            return e.dma_start(out=dst, in_=sap).then_inc(s_am.h, 16)
        EM["sync"].append(_fa)

    q3_done = [0]

    def quant3a_gen():
        Y.w(s_tc, max(kr_mm, qr_mm_last))
        Y.w(s_vc, partB_done)
        Y.w(s_sc, s_sc.n)
        Y.w(s_q2, s_q2.n)
        Y.w(s_ag, s_ag.n)
        x_next = Y.dma(p3xA.ap(), x_in[0:128, :], s_xs[0])
        qv = 0
        for t in range(TC):
            V.w(s_xs[0], x_next)
            V(lambda e: e.reduce_sum(negmu, p3xA.ap(), axis=AX, negate=True))
            V(lambda e: e.tensor_scalar_mul(negmu, negmu, 1.0 / D))
            vv = V(lambda e: e.tensor_scalar_add(q_c.ap(), p3xA.ap(), negmu),
                   inc=s_vc)
            yield
            Sc.w(s_vc, vv)
            Sc.w(s_q2, s_q2.n)
            sv = Sc(lambda e: e.activation(p3xA.ap(), q_c.ap(), AF.Square,
                                           accum_out=varl), inc=s_sc)
            V.w(s_sc, sv)
            vv = V(lambda e: e.tensor_scalar(varl, varl, 1.0 / D, float(EPS),
                                             OP.mult, OP.add), inc=s_vc)
            Sc.w(s_vc, vv)
            Sc(lambda e: e.activation(rstd1, varl, AF.Ln))
            sv = Sc(lambda e: e.activation(rstd1, rstd1, AF.Exp, scale=-0.5),
                    inc=s_sc)
            V.w(s_sc, sv)
            V(lambda e: e.tensor_scalar_mul(q_c.ap(), q_c.ap(), rstd1))
            V.w(s_xq, s_xq.n)
            yield
            for hf in range(2):
                for r in nvfp4_gen(q_c[:, hf * 1024:(hf + 1) * 1024], 1024,
                                   q_bs.ap(), q_sA.ap(), q_sB.ap(),
                                   cc=q_rB.ap(), rA=q_rA.ap(), rB=q_rC.ap(),
                                   rC=q_rD.ap(), rD=p3xA[:, 0:1024],
                                   rfac=wsc[:, 7:8], ts_lane=xtsb.ap(),
                                   dst=p3xdq[:, hf * 1024:(hf + 1) * 1024],
                                   post=wsc[:, 6:7]):
                    if r is not None:
                        qv = r
                    yield
            G.w(s_vc, qv)
            G.dma(xdq_scr[t], p3xdq.ap(), s_xq)
            if t + 1 < TC:
                Y.w(s_vc, qv)
                x_next = Y.dma(p3xA.ap(), x_in[(t + 1) * 128:(t + 2) * 128, :],
                               s_xs[0])
            yield
        q3_done[0] = qv

    cc_a2a_h = {}
    q3_box = [quant3a_gen()]
    cur_gen = proj_steps(0)
    cur_qr = next(cur_gen)
    for h in range(HG):
        for _ in cur_gen:
            pass
        if h + 1 < HG:
            nxt_gen = proj_steps(h + 1)
            nxt_qr = next(nxt_gen)
        else:
            nxt_gen, nxt_qr = None, 0
        attn_head(h, cur_qr, nxt_gen)
        head_qk_done[h] = s_qk.n
        cur_gen, cur_qr = nxt_gen, nxt_qr
    cc_a2a = cc_a2a_h[HG - 1]

    for _ in q3_box[0]:
        pass
    q3_box[0] = None
    quant_done = q3_done[0]
    T.w(s_ex, s_ex.n)

    tile_mm = {}
    tile_cb = {}
    prev_mm = {0: None, 1: None}
    prev_cb = {0: None, 1: None}
    prev_ost = {0: 0, 1: 0}
    half_last_mm = 0
    for half in range(2):
        Y.w(s_cc, cc_wdq)
        Y.w(s_cc, cc_a2a)
        Y.w(s_vc, quant_done)
        G.w(s_cc, cc_a2a)
        if half == 1:
            Y.w(s_tc, half_last_mm)
            G.w(s_tc, half_last_mm)
        Y.w(s_am, s_am.n)
        Y.w(s_xq, s_xq.n)
        wv = 0
        for kcx in range(16):
            for sp in range(4):
                src = half * 4 + sp
                wv = Y.dma(p3wdq[:, kcx * 1024 + sp * 256: kcx * 1024 + (sp + 1) * 256],
                           wdq_out[src, kcx], s_wv)
        ov = 0
        for dvc in range(16):
            ov = G.dma(p3ow[:, dvc * 1024:(dvc + 1) * 1024], owT_b[half, dvc], s_p3g)
        for t in range(TC):
            tp = t % 2
            idx = half * TC + t
            if prev_mm[tp] is not None:
                Y.w(s_tc, prev_mm[tp])
                Y.w(s_vc, prev_cb[tp])
            av = 0
            for src in range(4):
                for hl in range(4):
                    av = Y.dma(p3at[:, tp * D + (src * 4 + hl) * 128:
                                    tp * D + (src * 4 + hl + 1) * 128],
                               a2a_mine[src * 4 + hl, t], s_p3t[tp])
            xv = Y.dma(p3xdqb[:, tp * D: tp * D + D], xdq_scr[t], s_p3t[tp])
            x3v = Y.dma(p3x[:, tp * D: tp * D + D], x_in[t * 128:(t + 1) * 128, :],
                        s_p3xl[tp])
            T.w(s_p3t[tp], xv)
            for gg in range(2):
                tiles = [p3xdqb[:, tp * D + (gg * 8 + i) * 128:
                                tp * D + (gg * 8 + i + 1) * 128] for i in range(8)]

                def evx(hidx, gg=gg, tp=tp):
                    Sc.w(s_tp, s_tp.n)
                    dst = p3xdqT[:, tp * D + gg * 1024: tp * D + (gg + 1) * 1024]
                    return Sc(lambda e, d=dst, h=hidx: e.copy(
                        out=d, in_=psC[:, h * 1024:(h + 1) * 1024]), inc=s_pe)
                transpose_group(tiles, evx)
            T.w(s_pe, s_pe.n)
            if t == 0:
                T.w(s_wv, wv)
                T.w(s_p3g, ov)
            side = t % 2
            for kcx in range(16):
                T_mm(psA[:, side * 1024:(side + 1) * 1024],
                     p3xdqT[:, tp * D + kcx * 128: tp * D + (kcx + 1) * 128],
                     p3wdq[:, kcx * 1024:(kcx + 1) * 1024], kcx == 0, False)
            mmv = 0
            for dvc in range(16):
                mmv = T_mm(psA[:, side * 1024:(side + 1) * 1024],
                           p3at[:, tp * D + dvc * 128: tp * D + (dvc + 1) * 128],
                           p3ow[:, dvc * 1024:(dvc + 1) * 1024],
                           False, dvc == 15, inc=s_tc)
            tile_mm[idx] = mmv
            if t == TC - 1:
                half_last_mm = mmv
            V.w(s_p3xl[tp], x3v)
            V.w(s_ost[tp], prev_ost[tp])
            V(lambda e, tp=tp, half=half: e.tensor_tensor(
                out=p3xb[:, tp * 1024:(tp + 1) * 1024],
                in0=p3x[:, tp * D + half * 1024: tp * D + half * 1024 + 1024],
                in1=biasb[:, half * 1024:(half + 1) * 1024], op=OP.add))
            V.w(s_tc, mmv)
            cbv = V(lambda e, tp=tp, side=side: e.tensor_tensor(
                out=p3xb[:, tp * 1024:(tp + 1) * 1024],
                in0=p3xb[:, tp * 1024:(tp + 1) * 1024],
                in1=psA[:, side * 1024:(side + 1) * 1024], op=OP.add), inc=s_vc)
            tile_cb[idx] = cbv
            T.w(s_vc, cbv)
            G.w(s_vc, cbv)
            prev_ost[tp] = G.dma(
                out_t[t * 128:(t + 1) * 128, half * 1024:half * 1024 + 1024],
                p3xb[:, tp * 1024:(tp + 1) * 1024], s_ost[tp])
            prev_mm[tp] = mmv
            prev_cb[tp] = cbv

    if dbg:
        G.dma(dbg_a2a.ap(), a2a_in.ap(), s_out)
        G.dma(dbg_agin.ap(), ag_in.ap(), s_out)
        G.dma(dbg_agout[0:4, 0:12], ag_out_a.ap(), s_out)
        G.dma(dbg_kn.ap(), knT.ap(), s_out)
        G.dma(dbg_qn.ap(), qnT.ap(), s_out)
        G.dma(dbg_wsc.ap(), wsc.ap(), s_out)
        G.dma(dbg_att.ap(), attnT.ap(), s_out)
        G.dma(dbg_p.ap(), pbuf[:, 0:S], s_out)
        G.dma(dbg_st.ap(), stats.ap(), s_out)
    G.w(s_out, s_out.n)
    G.w(s_ost[0], s_ost[0].n)
    G.w(s_ost[1], s_ost[1].n)
    G(lambda e: e.engine_nop(), inc=s_fin)
    G.w(s_fin, s_fin.n)

    with nc.Block() as block:
        @block.sync
        def _(eng):
            for f in EM["sync"]:
                f(eng)

        @block.tensor
        def _(eng):
            for f in EM["tensor"]:
                f(eng)

        @block.vector
        def _(eng):
            for f in EM["vector"]:
                f(eng)

        @block.scalar
        def _(eng):
            for f in EM["scalar"]:
                f(eng)

        @block.gpsimd
        def _(eng):
            for f in EM["gpsimd"]:
                f(eng)

    return nc


def _rope_tables(S):
    inv = 1.0 / 10000.0 ** (np.arange(0, DR, 2, dtype=np.float32) / DR)
    t = np.arange(S, dtype=np.float32)
    fr = np.outer(t, inv).astype(np.float32)
    emb = np.concatenate([fr, fr], -1)
    return np.cos(emb).astype(np.float32), np.sin(emb).astype(np.float32)


def stage_inputs(hs, q_a_w, q_a_ln_w, q_a_ln_b, q_b_w, q_rope_w, kv_a_w,
                 kv_a_ln_w, kv_a_ln_b, kv_b_w, o_w, ffn_w, ffn_b, x_ts, S):
    B = 2
    NT = B * S // NC
    xf = np.ascontiguousarray(hs.reshape(B * S, D))
    cos, sin = _rope_tables(S)
    sfold = np.concatenate([-sin[:, 0:32], sin[:, 32:64]], axis=1)

    def colblk(mat, nkc, nw=128):
        K, N = mat.shape
        return np.ascontiguousarray(
            mat.reshape(nkc, 128, N // nw, nw).transpose(2, 1, 0, 3))

    qaT = np.ascontiguousarray(q_a_w.T)
    kvaT = np.ascontiguousarray(kv_a_w.T)
    qrwT = np.ascontiguousarray(q_rope_w.T)
    owT = np.ascontiguousarray(o_w.T)

    qaT_b = colblk(qaT, 16)
    kvaT_b = colblk(kvaT[:, 0:KVL], 16)
    kvrT_b = np.ascontiguousarray(
        kvaT[:, KVL:KVL + DR].reshape(16, 128, DR).transpose(1, 0, 2))
    qrwT_b = colblk(qrwT, 12)
    owT_hb = np.ascontiguousarray(
        owT.reshape(16, 128, 2, 1024).transpose(2, 0, 1, 3))

    shared = dict(
        qaT_b=qaT_b, kvaT_b=kvaT_b, kvrT_b=kvrT_b, qrwT_b=qrwT_b, owT_b=owT_hb,
        qalnw=np.ascontiguousarray(q_a_ln_w.reshape(12, 128, 1)),
        qalnb=np.ascontiguousarray(q_a_ln_b.reshape(12, 128, 1)),
        kvalnw=np.ascontiguousarray(kv_a_ln_w.reshape(4, 128, 1)),
        kvalnb=np.ascontiguousarray(kv_a_ln_b.reshape(4, 128, 1)),
        bias_rep=np.ascontiguousarray(np.broadcast_to(ffn_b.reshape(1, D), (128, D))),
        xts_rep=np.full((128, 1), np.float32(x_ts), np.float32),
        ident=np.eye(128, dtype=ml_dtypes.bfloat16),
        identf=np.eye(128, dtype=np.float32),
    )
    ins = []
    for c in range(NC):
        hg0 = 4 * (c % 4)
        pos = (c % 4) * NT + np.arange(NT)
        qb_rows = np.concatenate(
            [q_b_w[128 * (hg0 + i):128 * (hg0 + i + 1)] for i in range(HG)], 0)
        kn_rows = np.concatenate(
            [kv_b_w[256 * (hg0 + i):256 * (hg0 + i) + 128] for i in range(HG)], 0)
        v_rows = np.concatenate(
            [kv_b_w[256 * (hg0 + i) + 128:256 * (hg0 + i + 1)] for i in range(HG)], 0)
        qbT = np.ascontiguousarray(qb_rows.T)
        knT_ = np.ascontiguousarray(kn_rows.T)
        vT_ = np.ascontiguousarray(v_rows.T)
        d = dict(shared)
        d.update(
            x=np.ascontiguousarray(xf[c * NT:(c + 1) * NT]),
            qbT_b=np.ascontiguousarray(
                qbT.reshape(12, 128, HG, 128).transpose(2, 1, 0, 3)),
            kvbknT_b=np.ascontiguousarray(
                knT_.reshape(4, 128, HG, 128).transpose(2, 1, 0, 3)),
            kvbvT_b=np.ascontiguousarray(
                vT_.reshape(4, 128, HG, 128).transpose(2, 1, 0, 3)),
            ffnw=np.ascontiguousarray(
                ffn_w[256 * c:256 * (c + 1)].reshape(2, 128, D)),
            ktcos=np.ascontiguousarray(cos[pos]),
            ktsf=np.ascontiguousarray(sfold[pos]),
            fcos2=np.ascontiguousarray(np.concatenate([cos[pos].T, cos[pos].T], 0)),
            fsf2=np.ascontiguousarray(np.concatenate([sfold[pos].T, sfold[pos].T], 0)),
        )
        ins.append(d)
    return ins


def kernel(L0_attn_hidden_states, L0_attn_q_a_w, L0_attn_q_a_ln_w,
           L0_attn_q_a_ln_b, L0_attn_q_b_w, L0_attn_q_rope_w, L0_attn_kv_a_w,
           L0_attn_kv_a_ln_w, L0_attn_kv_a_ln_b, L0_attn_kv_b_w, L0_attn_o_w,
           L0_ffn_weight, L0_ffn_bias, L0_ffn_x_tensor_scale):
    hs = np.asarray(L0_attn_hidden_states, dtype=np.float32)
    B, S, _ = hs.shape
    in_maps = stage_inputs(
        hs, np.asarray(L0_attn_q_a_w, np.float32),
        np.asarray(L0_attn_q_a_ln_w, np.float32),
        np.asarray(L0_attn_q_a_ln_b, np.float32),
        np.asarray(L0_attn_q_b_w, np.float32),
        np.asarray(L0_attn_q_rope_w, np.float32),
        np.asarray(L0_attn_kv_a_w, np.float32),
        np.asarray(L0_attn_kv_a_ln_w, np.float32),
        np.asarray(L0_attn_kv_a_ln_b, np.float32),
        np.asarray(L0_attn_kv_b_w, np.float32),
        np.asarray(L0_attn_o_w, np.float32),
        np.asarray(L0_ffn_weight, np.float32),
        np.asarray(L0_ffn_bias, np.float32),
        np.float32(L0_ffn_x_tensor_scale), S)
    nc = build_program(S)
    res = run_bass_kernel_spmd(nc, in_maps, core_ids=list(range(NC)))
    NT = B * S // NC
    out = np.concatenate([np.asarray(res.results[c]["out"]) for c in range(NC)],
                         axis=0)
    return out.reshape(B, S, D).astype(np.float32)


# version 15 (final)
# speedup vs baseline: 1.1299x; 1.0419x over previous
import os

import numpy as np
import ml_dtypes

import concourse.bass as bass
import concourse.mybir as mybir
from concourse.bass_utils import run_bass_kernel_spmd
from concourse.aot_env import allow_ici_all_to_all

F32 = mybir.dt.float32
BF16 = mybir.dt.bfloat16
OP = mybir.AluOpType
AF = mybir.ActivationFunctionType
AX = mybir.AxisListType.X

D = 2048
QL = 1536
KVL = 512
H = 16
DR = 64
DV = 128
EPS = 1e-5
ATT_SCALE = float(1.0 / np.sqrt(128 + 64))
NC = 8
HG = 4

C_RNE = float(np.float32(1.5 * 2 ** 23))
VELT = float(np.float32(2 ** 20 + 1))
M_SUB = float(np.float32(1.5 * 2 ** 14))
E4_MIN = float(np.float32(2 ** -6))


class Sem:
    __slots__ = ("h", "n", "name")

    def __init__(self, h, name):
        self.h, self.n, self.name = h, 0, name

    def inc(self, k=1):
        self.n += k
        return self.n


def build_program(S=4096, dbg=False):
    B = 2
    NT = B * S // NC
    TC = NT // 128
    W1 = NT
    KQ = S // 4
    QC = S // 128
    KC = S // 128
    QB = min(512, S)
    QCB = QB // 128
    NTW = max(1, S // 512)
    TWW = min(512, S)
    AGW = max(W1, 1024)
    assert KQ <= 1024 and NT % 128 == 0

    nc = bass.Bass("TRN2", target_bir_lowering=False, debug=False,
                   detect_race_conditions=bool(os.environ.get("KRACE")))
    st = nc._state

    def XI(name, shape, dtype=F32):
        return nc.dram_tensor(name, list(shape), dtype, kind="ExternalInput")

    x_in = XI("x", [NT, D])
    qaT_b = XI("qaT_b", [12, 128, 16, 128])
    kvaT_b = XI("kvaT_b", [4, 128, 16, 128])
    kvrT_b = XI("kvrT_b", [128, 16, DR])
    qrwT_b = XI("qrwT_b", [8, 128, 12, 128])
    qbT_b = XI("qbT_b", [HG, 128, 12, 128])
    kvbknT_b = XI("kvbknT_b", [HG, 128, 4, 128])
    kvbvT_b = XI("kvbvT_b", [HG, 128, 4, 128])
    owT_b = XI("owT_b", [2, 16, 128, 1024])
    ffnw_in = XI("ffnw", [2, 128, D])
    qalnw_in = XI("qalnw", [12, 128, 1])
    qalnb_in = XI("qalnb", [12, 128, 1])
    kvalnw_in = XI("kvalnw", [4, 128, 1])
    kvalnb_in = XI("kvalnb", [4, 128, 1])
    bias_in = XI("bias_rep", [128, D])
    xts_in = XI("xts_rep", [128, 1])
    ident_in = XI("ident", [128, 128], BF16)
    identf_in = XI("identf", [128, 128], F32)
    ktcos_in = XI("ktcos", [NT, DR])
    ktsf_in = XI("ktsf", [NT, DR])
    fcos_in = XI("fcos2", [128, W1])
    fsf_in = XI("fsf2", [128, W1])

    out_t = nc.dram_tensor("out", [NT, D], F32, kind="ExternalOutput")

    NAGC = 25
    ag_in = nc.dram_tensor("ag_in", [NAGC, 128, W1], BF16)
    ag_out_a = nc.dram_tensor("ag_out_a", [4, 12, 128, W1], BF16)
    ag_out_b = nc.dram_tensor("ag_out_b", [4, 5, 128, W1], BF16)
    ag_out_c = nc.dram_tensor("ag_out_c", [4, 8, 128, W1], BF16)
    a2a_in = nc.dram_tensor("a2a_in", [HG, 4, TC, 128, 128], BF16)
    a2a_out = nc.dram_tensor("a2a_out", [HG, 4, 4, TC, 128, 128], BF16)
    amax_in = nc.dram_tensor("amax_in", [1, 16], F32)
    amax_out = nc.dram_tensor("amax_out", [NC, 1, 16], F32,
                              addr_space="Shared")
    wdq_in = nc.dram_tensor("wdq_in", [16, 128, 256], BF16)
    a2a_mine = nc.dram_tensor("a2a_mine", [16, TC, 128, 128], BF16)
    wdq_out = nc.dram_tensor("wdq_out", [NC, 16, 128, 256], BF16,
                             addr_space="Shared")
    xdq_scr = nc.dram_tensor("xdq_scr", [TC, 128, D], BF16,
                             kind="ExternalOutput" if dbg else "Internal")
    dbg_wsc = nc.dram_tensor("dbg_wsc", [128, 16], F32, kind="ExternalOutput") \
        if dbg else None
    dbg_att = nc.dram_tensor("dbg_att", [128, S], BF16, kind="ExternalOutput") \
        if dbg else None
    dbg_p = nc.dram_tensor("dbg_p", [128, S], BF16, kind="ExternalOutput") \
        if dbg else None
    dbg_st = nc.dram_tensor("dbg_st", [128, 64], F32, kind="ExternalOutput") \
        if dbg else None
    dbg_agin = nc.dram_tensor("dbg_agin", [NAGC, 128, W1], BF16,
                              kind="ExternalOutput") if dbg else None
    dbg_a2a = nc.dram_tensor("dbg_a2a", [HG, 4, TC, 128, 128], BF16,
                             kind="ExternalOutput") if dbg else None
    dbg_ptb = nc.dram_tensor("dbg_ptb", [128, (S // 128) * QB], BF16,
                             kind="ExternalOutput") if dbg else None
    dbg_pb2 = nc.dram_tensor("dbg_pb2", [128, 2 * S], BF16,
                             kind="ExternalOutput") if dbg else None
    dbg_st2 = nc.dram_tensor("dbg_st2", [128, 64], F32,
                             kind="ExternalOutput") if dbg else None
    dbg_agout = nc.dram_tensor("dbg_agout", [4, NAGC, 128, W1], BF16,
                               kind="ExternalOutput") if dbg else None
    dbg_kn = nc.dram_tensor("dbg_kn", [128, 2 * S], BF16,
                            kind="ExternalOutput") if dbg else None
    dbg_xnT = nc.dram_tensor("dbg_xnT", [128, 16 * W1], BF16,
                             kind="ExternalOutput") if dbg else None
    dbg_qn = nc.dram_tensor("dbg_qn", [128, 2 * S], BF16,
                            kind="ExternalOutput") if dbg else None

    SB_BASE = 16512
    SB_TOP = 229344
    offs = {}

    def _alloc(name, shape, dtype, off):
        itb = 2 if dtype == BF16 else 4
        per = int(np.prod(shape[1:])) * itb
        assert off + per <= SB_TOP, (name, off, per)
        offs[name] = (off, per)
        return st.alloc_sbuf_tensor_at(name, list(shape), dtype, offset=off), off + per

    cur = SB_BASE

    def FX(name, shape, dtype):
        nonlocal cur
        off = (cur + 31) // 32 * 32
        t, cur = _alloc(name, shape, dtype, off)
        return t

    ident = FX("identW", [128, 128], BF16)
    identf = FX("identF", [128, 128], F32)
    onesb = FX("onesb", [128, 128], BF16)
    ones128f = FX("ones128f", [128, 128], F32)
    xtsb = FX("xtsb", [128, 1], F32)
    wsc = FX("wsc", [128, 16], F32)
    stats = FX("stats", [128, 64], F32)
    sconst = FX("sconst", [128, 8], F32)
    biasb = FX("biasb", [128, D], F32)
    fws = FX("fws", [128, 256], F32)
    lnwq = FX("lnwq", [128, 12], F32)
    lnbq = FX("lnbq", [128, 12], F32)
    lnwk = FX("lnwk", [128, 4], F32)
    lnbk = FX("lnbk", [128, 4], F32)
    warm = FX("warm", [128, 2], F32)
    AB = (cur + 255) // 256 * 256

    def mkarena():
        pos = [AB]

        def A(name, shape, dtype):
            off = (pos[0] + 31) // 32 * 32
            t, pos[0] = _alloc(name, shape, dtype, off)
            return t
        return A, pos

    A1, a1pos = mkarena()
    xtile = A1("xtile", [128, 2 * D], F32)
    xnbf = A1("xnbf", [128, D], BF16)
    xcs = A1("xcs", [128, D], F32)
    sqf = A1("sqf", [128, D], F32)
    xnT = A1("xnT", [128, 16 * W1], BF16)
    wcb = A1("wcb", [128, 4 * 2048], BF16)
    qlat = A1("qlat", [128, 12 * W1], BF16)
    sqb = A1("sqb", [128, 2 * W1], BF16)
    mub = A1("mub", [128, W1], F32)
    rstdb = A1("rstdb", [128, W1], F32)
    scr1 = A1("scr1", [128, max(W1, 128)], F32)
    qrev = A1("qrev", [128, W1], F32)
    qrrot = A1("qrrot", [128, W1], F32)
    kvlT = A1("kvlT", [128, 4 * W1], BF16)
    krT = A1("krT", [64, W1], BF16)
    krop = A1("krop", [128, DR], F32)
    agst = A1("agst", [128, AGW], BF16)
    kcosb = A1("kcosb", [128, TC * DR], F32)
    ksfb = A1("ksfb", [128, TC * DR], F32)
    fcosb = A1("fcosb", [128, W1], F32)
    fsfb = A1("fsfb", [128, W1], F32)
    fwn = A1("fwn", [128, D], F32)
    fwq = A1("fwq", [128, D], BF16)

    A2, a2pos = mkarena()
    knT = A2("knT", [128, 2 * S], BF16)
    vTb = A2("vTb", [128, 2 * S], BF16)
    qnT = A2("qnT", [128, 2 * S], BF16)
    qrTh = A2("qrTh", [64, 2 * S], BF16)
    krTg = A2("krTg", [64, S], BF16)
    latq = A2("latq", [128, 6 * TWW], BF16)
    pbuf = A2("pbuf", [128, 2 * S], BF16)
    ptb = A2("ptb", [128, KC * QB], BF16)
    attnT = A2("attnT", [128, S], BF16)
    qbTb = A2("qbTb", [128, 12 * 128], BF16)
    kvknb = A2("kvknb", [128, 4 * 128], BF16)
    kvvb = A2("kvvb", [128, 4 * 128], BF16)

    A3, a3pos = mkarena()
    p3x = A3("p3x", [128, 2 * D], F32)
    p3xdqb = A3("p3xdqb", [128, 2 * D], BF16)
    p3xdqT = A3("p3xdqT", [128, 2 * D], BF16)
    p3at = A3("p3at", [128, 2 * D], BF16)
    p3wdq = A3("p3wdq", [128, 16 * 1024], BF16)
    p3ow = A3("p3ow", [128, 16 * 1024], BF16)
    p3xb = A3("p3xb", [128, 2 * 1024], F32)

    a5pos = [(a2pos[0] + 255) // 256 * 256]

    def A5(name, shape, dtype):
        off = (a5pos[0] + 31) // 32 * 32
        t, a5pos[0] = _alloc(name, shape, dtype, off)
        return t

    p3xA = A5("p3xA", [128, D], F32)
    p3xdq = A5("p3xdq", [128, D], BF16)

    TAIL = (max(a1pos[0], a2pos[0], a3pos[0], a5pos[0]) + 255) // 256 * 256
    tl = [TAIL]

    def A4(name, shape, dtype):
        off = (tl[0] + 31) // 32 * 32
        t, tl[0] = _alloc(name, shape, dtype, off)
        return t

    q_c = A4("q_c", [128, D], F32)
    q_rA = A4("q_rA", [128, 1024], F32)
    q_rB = A4("q_rB", [128, 1024], F32)
    q_rC = A4("q_rC", [128, 1024], F32)
    q_rD = A4("q_rD", [128, 1024], F32)
    q_bs = A4("q_bs", [128, 64], F32)
    q_sA = A4("q_sA", [128, 64], F32)
    q_sB = A4("q_sB", [128, 64], F32)

    psA = st.alloc_psum_tensor("psA", [128, 2048], F32)
    psB = st.alloc_psum_tensor("psB", [128, 512], F32)
    psC = st.alloc_psum_tensor("psC", [128, 2048], BF16)
    psD = st.alloc_psum_tensor("psD", [128, 512], F32)

    EM = {k: [] for k in ("sync", "tensor", "vector", "scalar", "gpsimd")}

    class Q:
        def __init__(self, name):
            self.name = name

        def __call__(self, fn, inc=None, k=1):
            drain = self.name in ("vector", "scalar")
            if inc is None:
                EM[self.name].append(fn)
                if drain:
                    EM[self.name].append(lambda e: e.drain())
                return None
            v = inc.inc(k)
            if drain:
                EM[self.name].append(fn)
                EM[self.name].append(
                    lambda e, s=inc, kk=k: e.drain().then_inc(s.h, kk))
            else:
                EM[self.name].append(lambda e, f=fn, s=inc, kk=k: f(e).then_inc(s.h, kk))
            return v

        def w(self, sem, val):
            if val and val > 0:
                EM[self.name].append(lambda e, s=sem, v=val: e.wait_ge(s.h, v))

        def dma(self, out, in_, sem, slow=False, dyn=None):
            v = sem.inc(16)
            if dyn is None:
                EM[self.name].append(
                    lambda e, o=out, i=in_, s=sem, sl=slow: e.dma_start(
                        out=o, in_=i, allow_slow_non_contiguous=sl).then_inc(s.h, 16))
            else:
                ten, idx = dyn

                def _f(e, o=out, tn=ten, ix=idx, s=sem):
                    si, hl, tt = ix
                    src = tn.ap()[si][bass.ds(_rank_base(e), 1)].squeeze(0)[hl, tt]
                    return e.dma_start(out=o, in_=src).then_inc(s.h, 16)
                EM[self.name].append(_f)
            return v

    _rank_cache = {}

    def _rank_base(e):
        key = id(e)
        if key not in _rank_cache:
            _rank_cache[key] = e.compute_val(e.partition_id() % 4)
        return _rank_cache[key]

    _rank2_cache = {}

    def _rank2c(e, c0):
        key = (id(e), c0)
        if key not in _rank2_cache:
            _rank2_cache[key] = e.compute_val((e.partition_id() % 4) * 2 + c0)
        return _rank2_cache[key]

    T, V, Sc, G, Y = Q("tensor"), Q("vector"), Q("scalar"), Q("gpsimd"), Q("sync")

    def T_mm(ps_ap, lhs_ap, rhs_ap, start, stop, inc=None, nmax=512):
        n = rhs_ap.shape[-1]
        vals = None
        for o in range(0, n, nmax):
            w = min(nmax, n - o)
            last = o + w >= n
            vals = T(lambda e, p=ps_ap[:, o:o + w] if ps_ap.ndim == 2 else None,
                     l=lhs_ap, r=rhs_ap[:, o:o + w], st_=start, sp=stop:
                     e.matmul(p, l, r, start=st_, stop=sp),
                     inc=inc if last else None)
        return vals

    def mks(name):
        return Sem(st.alloc_semaphore(name), name)

    s_ld = mks("ld")
    s_x = mks("x")
    s_wg = mks("wg")
    s_tc = mks("tc")
    s_vc = mks("vc")
    s_sc = mks("sc")
    s_tp = mks("tp")
    s_pe = mks("pe")
    s_ag = mks("ag")
    s_cc = mks("cc")
    s_qk = mks("qk")
    s_mx = mks("mx")
    s_ex = mks("ex")
    s_ar = mks("ar")
    s_fe = mks("fe")
    s_pn = mks("pn")
    s_pv = mks("pv")
    s_pq = mks("pq")
    s_a2 = mks("a2")
    s_lat = mks("lat")
    s_latq = [mks(f"latq{i}") for i in range(6)]
    s_xs = [mks("xs0"), mks("xs1")]
    s_wgs = [mks("wgs0"), mks("wgs1"), mks("wgs2"), mks("wgs3")]
    s_fw = mks("fw")
    s_hw = mks("hw")
    s_qr = [mks("qr0"), mks("qr1")]
    s_krg = mks("krg")
    s_p3t = [mks("p3t0"), mks("p3t1")]
    s_p3xl = [mks("p3xl0"), mks("p3xl1")]
    s_ost = [mks("ost0"), mks("ost1")]
    s_wv = mks("wv")
    s_am = mks("am")
    s_rot = mks("rot")
    s_q2 = mks("q2")
    s_p3 = mks("p3")
    s_p3g = mks("p3g")
    s_xq = mks("xq")
    s_out = mks("out")
    s_fin = mks("fin")

    Y.dma(ident.ap(), ident_in.ap(), s_ld)
    Y.dma(identf.ap(), identf_in.ap(), s_ld)
    Y.dma(biasb.ap(), bias_in.ap(), s_ld)
    Y.dma(xtsb.ap(), xts_in.ap(), s_ld)
    for j in range(12):
        Y.dma(lnwq[:, j:j + 1], qalnw_in[j], s_ld)
        Y.dma(lnbq[:, j:j + 1], qalnb_in[j], s_ld)
    for j in range(4):
        Y.dma(lnwk[:, j:j + 1], kvalnw_in[j], s_ld)
        Y.dma(lnbk[:, j:j + 1], kvalnb_in[j], s_ld)
    for t in range(TC):
        Y.dma(kcosb[:, t * DR:(t + 1) * DR], ktcos_in[t * 128:(t + 1) * 128, :], s_ld)
        Y.dma(ksfb[:, t * DR:(t + 1) * DR], ktsf_in[t * 128:(t + 1) * 128, :], s_ld)
    Y.dma(fcosb.ap(), fcos_in.ap(), s_ld)
    Y.dma(fsfb.ap(), fsf_in.ap(), s_ld)
    ld_all = s_ld.n
    for t in range(min(2, TC)):
        Y.dma(xtile[:, (t % 2) * D:(t % 2) * D + D], x_in[t * 128:(t + 1) * 128, :],
              s_xs[t % 2])
    fw0_ld = Y.dma(fwn.ap(), ffnw_in[0], s_fw)

    V.w(s_ld, ld_all)
    Sc.w(s_ld, ld_all)
    Sc(lambda e: e.activation(warm[:, 0:1], xtsb.ap(), AF.Exp,
                              accum_out=warm[:, 1:2]))
    Sc(lambda e: e.activation(warm[:, 0:1], xtsb.ap(), AF.Square))
    Sc(lambda e: e.activation(warm[:, 0:1], warm[:, 0:1], AF.Ln))
    V(lambda e: e.memset(onesb.ap(), 1.0))
    V(lambda e: e.memset(ones128f.ap(), 1.0))
    V(lambda e: e.memset(sconst.ap(), 0.0))
    V(lambda e: e.memset(wsc.ap(), 0.0))
    V(lambda e: e.memset(stats.ap(), 0.0))

    psc_last_evac = [0, 0]
    psc_used = [0, 0]
    psc_next = [0]

    def transpose_group(srcs, dst_cb):
        hidx = psc_next[0] % 2
        psc_next[0] += 1
        if psc_used[hidx] > 0:
            T.w(s_pe, psc_last_evac[hidx])
        n = len(srcs)
        for i, ap_ in enumerate(srcs):
            wdt = ap_.shape[-1]
            dst = psC[0:wdt, hidx * 1024 + i * 128: hidx * 1024 + i * 128 + 128]
            T(lambda e, d=dst, s=ap_: e.transpose(d, s, ident.ap()),
              inc=s_tp if i == n - 1 else None)
        psc_used[hidx] += 1
        psc_last_evac[hidx] = dst_cb(hidx)
        return hidx

    negmu = sconst[:, 0:1]
    varl = sconst[:, 1:2]
    rstd1 = sconst[:, 2:3]

    for t in range(TC):
        xt = xtile[:, (t % 2) * D:(t % 2) * D + D]
        V.w(s_xs[t % 2], 16 * (t // 2 + 1))
        V.w(s_tp, s_tp.n)
        V(lambda e, xt=xt: e.reduce_sum(negmu, xt, axis=AX, negate=True))
        V(lambda e: e.tensor_scalar_mul(negmu, negmu, 1.0 / D))
        vv = V(lambda e, xt=xt: e.tensor_scalar_add(xcs.ap(), xt, negmu), inc=s_vc)
        xt_free = vv
        Sc.w(s_vc, vv)
        sv = Sc(lambda e: e.activation(sqf.ap(), xcs.ap(), AF.Square, accum_out=varl),
                inc=s_sc)
        V.w(s_sc, sv)
        vv = V(lambda e: e.tensor_scalar(varl, varl, 1.0 / D, float(EPS),
                                         OP.mult, OP.add), inc=s_vc)
        Sc.w(s_vc, vv)
        Sc(lambda e: e.activation(rstd1, varl, AF.Ln))
        sv = Sc(lambda e: e.activation(rstd1, rstd1, AF.Exp, scale=-0.5), inc=s_sc)
        V.w(s_sc, sv)
        v_xn = V(lambda e: e.tensor_scalar_mul(xnbf.ap(), xcs.ap(), rstd1), inc=s_vc)
        for g in range(2):
            T.w(s_vc, v_xn)
            if t == 0 and g == 0:
                T.w(s_ld, ld_all)
            tiles = [xnbf[:, (g * 8 + i) * 128:(g * 8 + i + 1) * 128] for i in range(8)]

            def ev(hidx, t=t, g=g):
                Sc.w(s_tp, s_tp.n)
                dst = xnT.ap().rearrange("p (k w) -> p k w", w=W1)[
                    :, g * 8:g * 8 + 8, t * 128:(t + 1) * 128]
                src = psC.ap().rearrange("p (h k w) -> p h k w", h=2, w=128)[:, hidx]
                return Sc(lambda e, d=dst, s=src: e.copy(out=d, in_=s), inc=s_pe)
            transpose_group(tiles, ev)
        if t + 2 < TC:
            Y.w(s_vc, xt_free)
            Y.dma(xtile[:, ((t + 2) % 2) * D:((t + 2) % 2) * D + D],
                  x_in[(t + 2) * 128:(t + 3) * 128, :], s_xs[t % 2])
    xnT_ready = s_pe.n

    wslot_seq = [0]
    wslot_consumed = {}

    def load_wblock(dram_ap_3d, kn, wn):
        sl = wslot_seq[0] % 4
        wslot_seq[0] += 1
        if sl in wslot_consumed:
            G.w(s_tc, wslot_consumed[sl])
        flat = wcb[:, sl * 2048: sl * 2048 + kn * wn]
        dst = flat.rearrange("p (k w) -> p k w", w=wn)
        v = G.dma(dst, dram_ap_3d, s_wgs[sl])
        return flat, v, sl

    def p1_proj(n_chunks, w_blk, out_cb):
        for ncx in range(n_chunks):
            blk, dv, sl = load_wblock(w_blk[ncx], 16, 128)
            T.w(s_wgs[sl], dv)
            mmv = 0
            for kcx in range(16):
                lhs = blk[:, kcx * 128:(kcx + 1) * 128]
                rhs = xnT.ap().rearrange("p (k w) -> p k w", w=W1)[:, kcx]
                mmv = T_mm(psA[:, 0:W1], lhs, rhs, kcx == 0, kcx == 15, inc=s_tc)
            wslot_consumed[sl] = mmv
            out_cb(ncx, mmv)

    T.w(s_pe, xnT_ready)
    if dbg:
        G.w(s_pe, xnT_ready)
        G.dma(dbg_xnT.ap(), xnT.ap(), s_out)

    def qa_out(ncx, mmv):
        Sc.w(s_tc, mmv)
        ev = Sc(lambda e, n=ncx: e.copy(out=qlat[:, n * W1:(n + 1) * W1],
                                        in_=psA[:, 0:W1]), inc=s_sc)
        T.w(s_sc, ev)

    p1_proj(12, qaT_b, qa_out)

    def ln_feature_major(nch, buf, lnw, lnb, dim):
        T.w(s_sc, s_sc.n)
        for ncx in range(nch):
            T_mm(psA[:, 1024:1024 + W1], onesb.ap(), buf[:, ncx * W1:(ncx + 1) * W1],
                 ncx == 0, ncx == nch - 1, inc=s_tc)
        V.w(s_tc, s_tc.n)
        vv = V(lambda e: e.tensor_scalar_mul(mub.ap(), psA[:, 1024:1024 + W1],
                                             1.0 / dim), inc=s_vc)
        T.w(s_vc, vv)
        sq_mm = {}
        for ncx in range(nch):
            if ncx >= 2:
                Sc.w(s_tc, sq_mm[ncx - 2])
            sv = Sc(lambda e, n=ncx: e.activation(
                sqb[:, (n % 2) * W1:(n % 2) * W1 + W1],
                buf[:, n * W1:(n + 1) * W1], AF.Square), inc=s_sc)
            T.w(s_sc, sv)
            sq_mm[ncx] = T_mm(psA[:, 1024:1024 + W1], onesb.ap(),
                              sqb[:, (ncx % 2) * W1:(ncx % 2) * W1 + W1],
                              ncx == 0, ncx == nch - 1, inc=s_tc)
        V.w(s_tc, s_tc.n)
        V(lambda e: e.tensor_scalar_mul(rstdb.ap(), psA[:, 1024:1024 + W1], 1.0 / dim))
        V(lambda e: e.tensor_tensor(out=mub.ap(), in0=mub.ap(), in1=mub.ap(),
                                    op=OP.mult) if False else
          e.tensor_tensor(out=scr1[:, 0:W1], in0=mub.ap(), in1=mub.ap(), op=OP.mult))
        V(lambda e: e.tensor_tensor(out=rstdb.ap(), in0=rstdb.ap(), in1=scr1[:, 0:W1],
                                    op=OP.subtract))
        vv = V(lambda e: e.tensor_scalar_add(rstdb.ap(), rstdb.ap(), float(EPS)),
               inc=s_vc)
        Sc.w(s_vc, vv)
        Sc(lambda e: e.activation(rstdb.ap(), rstdb.ap(), AF.Ln))
        sv = Sc(lambda e: e.activation(rstdb.ap(), rstdb.ap(), AF.Exp, scale=-0.5),
                inc=s_sc)
        V.w(s_sc, sv)
        out_vals = []
        for ncx in range(nch):
            V(lambda e, n=ncx: e.tensor_tensor(
                out=scr1[:, 0:W1], in0=buf[:, n * W1:(n + 1) * W1], in1=mub.ap(),
                op=OP.subtract))
            V(lambda e: e.tensor_tensor(out=scr1[:, 0:W1], in0=scr1[:, 0:W1],
                                        in1=rstdb.ap(), op=OP.mult))
            out_vals.append(V(lambda e, n=ncx: e.tensor_scalar(
                buf[:, n * W1:(n + 1) * W1], scr1[:, 0:W1],
                lnw[:, n:n + 1], lnb[:, n:n + 1], OP.mult, OP.add), inc=s_vc))
        return out_vals

    V.w(s_ld, ld_all)
    qa_vals = ln_feature_major(12, qlat, lnwq, lnbq, QL)
    for ncx in range(12):
        G.w(s_vc, qa_vals[ncx])
        G.dma(ag_in[ncx], qlat[:, ncx * W1:(ncx + 1) * W1], s_ag)

    G.w(s_ag, s_ag.n)
    cc_a = G(lambda e: e.collective_compute(
        "AllGather", OP.bypass, replica_groups=[[0, 1, 2, 3], [4, 5, 6, 7]],
        ins=[ag_in.ap()[0:12].opt()], outs=[ag_out_a.ap().opt()]), inc=s_cc)

    def kv_out(ncx, mmv):
        Sc.w(s_tc, mmv)
        ev = Sc(lambda e, n=ncx: e.copy(out=kvlT[:, n * W1:(n + 1) * W1],
                                        in_=psA[:, 0:W1]), inc=s_sc)
        T.w(s_sc, ev)

    p1_proj(4, kvaT_b, kv_out)
    kv_vals = ln_feature_major(4, kvlT, lnwk, lnbk, KVL)
    for ncx in range(4):
        G.w(s_vc, kv_vals[ncx])
        G.dma(ag_in[20 + ncx], kvlT[:, ncx * W1:(ncx + 1) * W1], s_ag)

    kr_blk, kr_dv, kr_sl = load_wblock(kvrT_b.ap(), 16, DR)
    T.w(s_wgs[kr_sl], kr_dv)
    kr_mm = 0
    for t in range(TC):
        for kcx in range(16):
            lhs = xnT.ap().rearrange("p (k w) -> p k w", w=W1)[
                :, kcx, t * 128:(t + 1) * 128]
            rhs = kr_blk[:, kcx * DR:(kcx + 1) * DR]
            kr_mm = T(lambda e, l=lhs, r=rhs, st_=(kcx == 0), sp=(kcx == 15):
                      e.matmul(psB[:, 0:DR], l, r, start=st_, stop=sp), inc=s_tc)
        V.w(s_tc, kr_mm)
        V.w(s_tp, s_tp.n)
        cosr = kcosb[:, t * DR:(t + 1) * DR]
        sfr = ksfb[:, t * DR:(t + 1) * DR]
        vv = V(lambda e: e.tensor_copy(out=krop.ap(), in_=psB[:, 0:DR]), inc=s_vc)
        T.w(s_vc, vv)

        def _rope_a(e, cosr=cosr, sfr=sfr):
            e.tensor_tensor(out=scr1[:, 0:DR], in0=krop.ap(), in1=cosr, op=OP.mult)
            e.tensor_tensor(out=scr1[:, DR:DR + 32], in0=krop[:, 32:64],
                            in1=sfr[:, 0:32], op=OP.mult)
            return e.tensor_tensor(out=scr1[:, DR + 32:DR + 64], in0=krop[:, 0:32],
                                   in1=sfr[:, 32:64], op=OP.mult)
        V(_rope_a)
        vv = V(lambda e: e.tensor_tensor(out=xnbf[:, 0:DR], in0=scr1[:, 0:DR],
                                         in1=scr1[:, DR:2 * DR], op=OP.add),
               inc=s_vc)
        T.w(s_vc, vv)

        def evk(hidx, t=t):
            Sc.w(s_tp, s_tp.n)
            return Sc(lambda e, t_=t, h=hidx: e.copy(
                out=krT[0:64, t_ * 128:(t_ + 1) * 128],
                in_=psC[0:64, h * 1024:h * 1024 + 128]), inc=s_pe)
        transpose_group([xnbf[:, 0:DR]], evk)
    wslot_consumed[kr_sl] = kr_mm
    G.w(s_pe, s_pe.n)
    G.dma(ag_in[24, 0:64, :], krT.ap(), s_ag)
    G.dma(ag_in[24, 64:128, :], krT.ap(), s_ag)
    G.w(s_ag, s_ag.n)
    cc_b = G(lambda e: e.collective_compute(
        "AllGather", OP.bypass, replica_groups=[[0, 1, 2, 3], [4, 5, 6, 7]],
        ins=[ag_in.ap()[20:25].opt()], outs=[ag_out_b.ap().opt()]), inc=s_cc)

    qr_mm_last = 0
    for ncx in range(8):
        blk, dv, sl = load_wblock(qrwT_b[ncx], 12, 128)
        T.w(s_wgs[sl], dv)
        mmv = 0
        for kcx in range(12):
            lhs = blk[:, kcx * 128:(kcx + 1) * 128]
            rhs = qlat.ap().rearrange("p (k w) -> p k w", w=W1)[:, kcx]
            mmv = T_mm(psA[:, 0:W1], lhs, rhs, kcx == 0, kcx == 11, inc=s_tc)
        wslot_consumed[sl] = mmv
        qr_mm_last = mmv
        Sc.w(s_tc, mmv)
        Sc.w(s_rot, s_rot.n)
        Sc.w(s_vc, s_vc.n)
        ev = Sc(lambda e: e.copy(out=qrev.ap(), in_=psA[:, 0:W1]), inc=s_sc)
        T.w(s_sc, ev)
        G.w(s_sc, ev)
        G.dma(qrrot[0:32, :], qrev[32:64, :], s_rot)
        G.dma(qrrot[32:64, :], qrev[0:32, :], s_rot)
        G.dma(qrrot[64:96, :], qrev[96:128, :], s_rot)
        rv = G.dma(qrrot[96:128, :], qrev[64:96, :], s_rot)
        V.w(s_rot, rv)
        V.w(s_ld, ld_all)
        V.w(s_ag, s_ag.n)
        V(lambda e: e.tensor_tensor(out=qrrot.ap(), in0=qrrot.ap(), in1=fsfb.ap(),
                                    op=OP.mult))
        V(lambda e: e.tensor_tensor(out=qrev.ap(), in0=qrev.ap(), in1=fcosb.ap(),
                                    op=OP.mult))
        vv = V(lambda e: e.tensor_tensor(out=agst[:, 0:W1], in0=qrev.ap(),
                                         in1=qrrot.ap(), op=OP.add), inc=s_vc)
        G.w(s_vc, vv)
        G.dma(ag_in[12 + ncx], agst[:, 0:W1], s_ag)

    V.w(s_fw, fw0_ld)
    fwA_vals = []
    for ch in range(2):
        if ch == 1:
            lv = Y.dma(fwn.ap(), ffnw_in[1], s_fw)
            V.w(s_fw, lv)
        wv_ = fwn.ap().rearrange("p (b k) -> p b k", k=16)
        wn_ = fwq.ap().bitcast(F32).rearrange("p (b k) -> p b k", k=8)
        V(lambda e, wv=wv_, wn=wn_: e.tensor_tensor(
            out=wn, in0=wv[:, :, 0:16:2], in1=wv[:, :, 1:16:2], op=OP.min))
        V(lambda e, wv=wv_: e.tensor_tensor(out=wv[:, :, 0:8], in0=wv[:, :, 0:16:2],
                                            in1=wv[:, :, 1:16:2], op=OP.max))
        V(lambda e, wv=wv_: e.tensor_tensor(out=wv[:, :, 0:4], in0=wv[:, :, 0:8:2],
                                            in1=wv[:, :, 1:8:2], op=OP.max))
        V(lambda e, wn=wn_: e.tensor_tensor(out=wn[:, :, 0:4], in0=wn[:, :, 0:8:2],
                                            in1=wn[:, :, 1:8:2], op=OP.min))
        V(lambda e, wv=wv_: e.tensor_tensor(out=wv[:, :, 0:2], in0=wv[:, :, 0:4:2],
                                            in1=wv[:, :, 1:4:2], op=OP.max))
        V(lambda e, wn=wn_: e.tensor_tensor(out=wn[:, :, 0:2], in0=wn[:, :, 0:4:2],
                                            in1=wn[:, :, 1:4:2], op=OP.min))
        V(lambda e, wv=wv_, c=ch: e.tensor_tensor(
            out=fws[:, c * 128:(c + 1) * 128], in0=wv[:, :, 0:1].squeeze(2),
            in1=wv[:, :, 1:2].squeeze(2), op=OP.max))
        V(lambda e, wn=wn_, c=ch: e.tensor_tensor(
            out=scr1[:, 0:128], in0=wn[:, :, 0:1].squeeze(2),
            in1=wn[:, :, 1:2].squeeze(2), op=OP.min))
        V(lambda e: e.tensor_scalar_mul(scr1[:, 0:128], scr1[:, 0:128], -1.0))
        fwA_vals.append(V(lambda e, c=ch: e.tensor_tensor(
            out=fws[:, c * 128:(c + 1) * 128], in0=fws[:, c * 128:(c + 1) * 128],
            in1=scr1[:, 0:128], op=OP.max), inc=s_vc))
        if ch == 0:
            Y.w(s_vc, fwA_vals[0])
    vv = V(lambda e: e.reduce_max(wsc[:, 0:1], fws.ap(), axis=AX), inc=s_vc)
    T.w(s_vc, vv)
    T.w(s_ld, ld_all)
    tv = T(lambda e: e.transpose(psB[0:1, 128:256], wsc[:, 0:1], identf.ap()), inc=s_tc)
    V.w(s_tc, tv)
    vv = V(lambda e: e.reduce_max(wsc[0:1, 4:5], psB[0:1, 128:256], axis=AX),
           inc=s_vc)
    T.w(s_vc, vv)
    G.w(s_vc, vv)
    qv = G.dma(amax_in.ap(), wsc[0:1, 0:16], s_q2)
    G.w(s_q2, qv)
    cc_amax = G(lambda e: e.collective_compute(
        "AllGather", OP.bypass, replica_groups=[list(range(NC))],
        ins=[amax_in.ap().opt()], outs=[amax_out.ap().opt()]), inc=s_cc)
    G.w(s_ag, s_ag.n)
    cc_c = G(lambda e: e.collective_compute(
        "AllGather", OP.bypass, replica_groups=[[0, 1, 2, 3], [4, 5, 6, 7]],
        ins=[ag_in.ap()[12:20].opt()], outs=[ag_out_c.ap().opt()]), inc=s_cc)

    G.w(s_cc, cc_amax)
    qv = G.dma(wsc[0:1, 8:16],
               amax_out.ap().rearrange("c a k -> a (c k)")[0:1, 4:125:16], s_q2,
               slow=True)
    V.w(s_q2, qv)
    V(lambda e: e.reduce_max(wsc[0:1, 1:2], wsc[0:1, 8:16], axis=AX))
    V(lambda e: e.tensor_scalar_mul(wsc[0:1, 2:3], wsc[0:1, 1:2], 1.0 / 2688.0))
    vv = V(lambda e: e.tensor_scalar_mul(wsc[0:1, 3:4], wsc[0:1, 2:3], 6.0), inc=s_vc)
    G.w(s_vc, vv)
    qv = G.dma(amax_in.ap(), wsc[0:1, 0:16], s_q2)
    G.w(s_q2, qv)
    qv = G.dma(wsc.ap(), amax_in.ap()[0:1, 0:16].broadcast_to([128, 16]), s_q2)
    V.w(s_q2, qv)
    V(lambda e: e.reciprocal(wsc[:, 5:6], wsc[:, 3:4]))
    V(lambda e: e.tensor_scalar_mul(wsc[:, 6:7], wsc[:, 2:3], xtsb.ap()))
    V(lambda e: e.tensor_scalar_mul(wsc[:, 7:8], xtsb.ap(), 6.0))
    rfx_done = V(lambda e: e.reciprocal(wsc[:, 7:8], wsc[:, 7:8]), inc=s_vc)

    def nvfp4(src, width, bs, sA, sB, cc, rA, rB, rC, rD, rfac, ts_lane, dst,
              post=None):
        last = [None]
        for r in nvfp4_gen(src, width, bs, sA, sB, cc, rA, rB, rC, rD, rfac,
                           ts_lane, dst, post):
            if r is not None:
                last[0] = r
        return last[0]

    def nvfp4_gen(src, width, bs, sA, sB, cc, rA, rB, rC, rD, rfac, ts_lane,
                  dst, post=None):
        nb = width // 16
        sv = src.rearrange("p (b k) -> p b k", k=16)
        t8 = rA[:, 0:width // 2].rearrange("p (b k) -> p b k", k=8)
        t8n = rB[:, 0:width // 2].rearrange("p (b k) -> p b k", k=8)
        V(lambda e: e.tensor_tensor(out=t8, in0=sv[:, :, 0:16:2],
                                    in1=sv[:, :, 1:16:2], op=OP.max))
        yield None
        V(lambda e: e.tensor_tensor(out=t8n, in0=sv[:, :, 0:16:2],
                                    in1=sv[:, :, 1:16:2], op=OP.min))
        yield None
        V(lambda e: e.tensor_tensor(out=t8[:, :, 0:4], in0=t8[:, :, 0:8:2],
                                    in1=t8[:, :, 1:8:2], op=OP.max))
        yield None
        V(lambda e: e.tensor_tensor(out=t8n[:, :, 0:4], in0=t8n[:, :, 0:8:2],
                                    in1=t8n[:, :, 1:8:2], op=OP.min))
        yield None
        V(lambda e: e.tensor_tensor(out=t8[:, :, 0:2], in0=t8[:, :, 0:4:2],
                                    in1=t8[:, :, 1:4:2], op=OP.max))
        yield None
        V(lambda e: e.tensor_tensor(out=t8n[:, :, 0:2], in0=t8n[:, :, 0:4:2],
                                    in1=t8n[:, :, 1:4:2], op=OP.min))
        yield None
        V(lambda e: e.tensor_tensor(out=bs[:, 0:nb], in0=t8[:, :, 0:1].squeeze(2),
                                    in1=t8[:, :, 1:2].squeeze(2), op=OP.max))
        yield None
        V(lambda e: e.tensor_tensor(out=sA[:, 0:nb], in0=t8n[:, :, 0:1].squeeze(2),
                                    in1=t8n[:, :, 1:2].squeeze(2), op=OP.min))
        yield None
        V(lambda e: e.tensor_scalar_mul(sA[:, 0:nb], sA[:, 0:nb], -1.0))
        yield None
        V(lambda e: e.tensor_tensor(out=bs[:, 0:nb], in0=bs[:, 0:nb],
                                    in1=sA[:, 0:nb], op=OP.max))
        yield None
        V(lambda e: e.tensor_scalar_mul(bs[:, 0:nb], bs[:, 0:nb], rfac))
        yield None
        V(lambda e: e.tensor_scalar(bs[:, 0:nb], bs[:, 0:nb], 0.0, 448.0,
                                    OP.max, OP.min))
        yield None
        V(lambda e: e.tensor_scalar_mul(sA[:, 0:nb], bs[:, 0:nb], VELT))
        yield None
        V(lambda e: e.tensor_tensor(out=sB[:, 0:nb], in0=sA[:, 0:nb],
                                    in1=bs[:, 0:nb], op=OP.subtract))
        yield None
        V(lambda e: e.tensor_tensor(out=sA[:, 0:nb], in0=sA[:, 0:nb],
                                    in1=sB[:, 0:nb], op=OP.subtract))
        yield None
        V(lambda e: e.tensor_scalar(sB[:, 0:nb], bs[:, 0:nb], M_SUB, M_SUB,
                                    OP.add, OP.subtract))
        yield None
        V(lambda e: e.tensor_scalar(cc[:, 0:nb], bs[:, 0:nb], E4_MIN, None, OP.is_lt))
        yield None
        V(lambda e: e.tensor_tensor(out=sB[:, 0:nb], in0=sB[:, 0:nb],
                                    in1=sA[:, 0:nb], op=OP.subtract))
        yield None
        V(lambda e: e.tensor_tensor(out=sB[:, 0:nb], in0=sB[:, 0:nb],
                                    in1=cc[:, 0:nb], op=OP.mult))
        yield None
        V(lambda e: e.tensor_tensor(out=bs[:, 0:nb], in0=sA[:, 0:nb],
                                    in1=sB[:, 0:nb], op=OP.add))
        yield None
        V(lambda e: e.tensor_scalar(sA[:, 0:nb], bs[:, 0:nb], 0.0, None, OP.is_equal))
        yield None
        V(lambda e: e.tensor_tensor(out=sB[:, 0:nb], in0=bs[:, 0:nb],
                                    in1=sA[:, 0:nb], op=OP.add))
        yield None
        V(lambda e: e.tensor_scalar_mul(sB[:, 0:nb], sB[:, 0:nb], ts_lane))
        yield None
        V(lambda e: e.reciprocal(sB[:, 0:nb], sB[:, 0:nb]))
        yield None
        V(lambda e: e.tensor_tensor(
            out=cc.rearrange("p (b k) -> p b k", k=16), in0=sv,
            in1=sB[:, 0:nb].unsqueeze(2).broadcast_to([128, nb, 16]), op=OP.mult))
        yield None
        V(lambda e: e.tensor_scalar(cc, cc, -6.0, 6.0, OP.max, OP.min))
        yield None
        V(lambda e: e.tensor_scalar(rB, cc, 2.0, C_RNE, OP.mult, OP.add))
        yield None
        V(lambda e: e.tensor_scalar(rB, rB, C_RNE, 0.5, OP.subtract, OP.mult))
        yield None
        V(lambda e: e.tensor_scalar(rC, cc, C_RNE, C_RNE, OP.add, OP.subtract))
        yield None
        V(lambda e: e.tensor_scalar(rD, cc, 0.5, C_RNE, OP.mult, OP.add))
        yield None
        V(lambda e: e.tensor_scalar(rD, rD, C_RNE, 2.0, OP.subtract, OP.mult))
        yield None
        V(lambda e: e.tensor_tensor(out=rA, in0=cc, in1=cc, op=OP.mult))
        yield None
        V(lambda e: e.tensor_scalar(cc, rA, 16.0, None, OP.is_lt))
        yield None
        V(lambda e: e.tensor_scalar(rA, rA, 4.0, None, OP.is_lt))
        yield None
        V(lambda e: e.tensor_tensor(out=rC, in0=rC, in1=rD, op=OP.subtract))
        yield None
        V(lambda e: e.tensor_tensor(out=rC, in0=rC, in1=cc, op=OP.mult))
        yield None
        V(lambda e: e.tensor_tensor(out=rD, in0=rD, in1=rC, op=OP.add))
        yield None
        V(lambda e: e.tensor_tensor(out=rB, in0=rB, in1=rD, op=OP.subtract))
        yield None
        V(lambda e: e.tensor_tensor(out=rB, in0=rB, in1=rA, op=OP.mult))
        yield None
        V(lambda e: e.tensor_tensor(out=rD, in0=rD, in1=rB, op=OP.add))
        yield None
        if post is not None:
            V(lambda e: e.tensor_scalar_mul(bs[:, 0:nb], bs[:, 0:nb], post))
            yield None
        yield V(lambda e: e.tensor_tensor(
            out=dst.rearrange("p (b k) -> p b k", k=16),
            in0=rD.rearrange("p (b k) -> p b k", k=16),
            in1=bs[:, 0:nb].unsqueeze(2).broadcast_to([128, nb, 16]), op=OP.mult),
            inc=s_vc)

    qv_prev = 0
    for ch in range(2):
        Y.w(s_tc, max(kr_mm, qr_mm_last))
        Y.w(s_vc, qv_prev)
        if ch == 0:
            Y.w(s_vc, fwA_vals[1])
        lv = Y.dma(q_c.ap(), ffnw_in[ch], s_fw)
        V.w(s_fw, lv)
        V.w(s_vc, rfx_done)
        V.w(s_tc, qr_mm_last)
        V.w(s_tp, s_tp.n)
        for hf in range(2):
            qv = nvfp4(q_c[:, hf * 1024:(hf + 1) * 1024], 1024,
                       q_bs.ap(), q_sA.ap(), q_sB.ap(),
                       cc=q_rB.ap(), rA=q_rA.ap(), rB=q_rC.ap(), rC=q_rD.ap(),
                       rD=fwn[:, 0:1024], rfac=wsc[:, 5:6], ts_lane=wsc[:, 2:3],
                       dst=fwq[:, hf * 1024:(hf + 1) * 1024])
        qv_prev = qv
        for g in range(2):
            T.w(s_vc, qv)
            tiles = [fwq[:, (g * 8 + i) * 128:(g * 8 + i + 1) * 128] for i in range(8)]

            def evw(hidx, ch=ch, g=g):
                Sc.w(s_tp, s_tp.n)
                Sc.w(s_q2, s_q2.n)
                return Sc(lambda e, h=hidx: e.copy(
                    out=agst[:, 0:1024], in_=psC[:, h * 1024:h * 1024 + 1024]),
                    inc=s_pe)
            hid = transpose_group(tiles, evw)
            G.w(s_pe, psc_last_evac[hid])
            for i in range(8):
                G.dma(wdq_in[g * 8 + i, :, ch * 128:(ch + 1) * 128],
                      agst[:, i * 128:(i + 1) * 128], s_q2)
    partB_done = qv_prev
    G.w(s_q2, s_q2.n)
    cc_wdq = G(lambda e: e.collective_compute(
        "AllGather", OP.bypass, replica_groups=[list(range(NC))],
        ins=[wdq_in.ap().opt()], outs=[wdq_out.ap().opt()]), inc=s_cc)

    Y.w(s_cc, cc_b)
    for src in range(4):
        Y.dma(krTg[0:64, src * W1:(src + 1) * W1], ag_out_b[src, 4, 0:64, :], s_krg)
    krg_all = s_krg.n

    lat_slot = [0]
    lat_slot_consumed = {}

    def dma_lat(chunk, tw):
        sl = lat_slot[0] % 6
        lat_slot[0] += 1
        if sl in lat_slot_consumed:
            Y.w(s_tc, lat_slot_consumed[sl])
        if chunk < 12:
            gat, gch = ag_out_a, chunk
            Y.w(s_cc, cc_a)
        else:
            gat, gch = ag_out_b, chunk - 20
            Y.w(s_cc, cc_b)
        dv = 0
        npc = max(1, TWW // W1)
        pw = min(TWW, W1)
        for p in range(npc):
            tok0 = tw * TWW + p * pw
            src = tok0 // W1
            coff = tok0 % W1
            dv = Y.dma(latq[:, sl * TWW + p * pw: sl * TWW + (p + 1) * pw],
                       gat[src, gch, :, coff:coff + pw], s_latq[sl])
        return sl, dv

    psD_last_evac = [0]
    attnT_last_store = [0]
    head_qk_done = {-2: 0, -1: 0}

    proj_last_mm = [0]

    def proj_steps(h):
        hp = h % 2
        G.w(s_tc, proj_last_mm[0])
        G.dma(qbTb.ap().rearrange("p (k w) -> p k w", w=128), qbT_b[h], s_hw)
        G.dma(kvknb.ap().rearrange("p (k w) -> p k w", w=128), kvbknT_b[h], s_hw)
        gv3 = G.dma(kvvb.ap().rearrange("p (k w) -> p k w", w=128), kvbvT_b[h], s_hw)
        chq0 = h // 2
        roff = (h % 2) * 64
        if h >= 2:
            Y.w(s_qk, head_qk_done[h - 2])
        sqr = s_qr[h % 2]
        qrv = sqr.inc(16)
        out_ap = qrTh[0:64, hp * S:(hp + 1) * S].rearrange("p (s w) -> p s w", w=W1)
        Y.w(s_cc, cc_c)

        def _fq(e, o=out_ap, c0=chq0, ro=roff, sh=sqr.h):
            src_ap = ag_out_c.ap()[:, bass.ds(_rank2c(e, c0), 1), ro:ro + 64, :]
            src_ap = src_ap.squeeze(1).transpose([1, 0, 2])
            return e.dma_start(out=o, in_=src_ap).then_inc(sh, 16)
        EM["sync"].append(_fq)
        T.w(s_hw, gv3)
        yield qrv
        for tw in range(NTW):
            mmv = 0
            for halfq in range(2):
                for kcx in range(6 * halfq, 6 * halfq + 6):
                    sl, dv = dma_lat(kcx, tw)
                    T.w(s_latq[sl], dv)
                    mmv = T(lambda e, k=kcx, sl=sl, st_=(kcx == 0), sp=(kcx == 11):
                            e.matmul(psB.ap(), qbTb[:, k * 128:(k + 1) * 128],
                                     latq[:, sl * TWW:(sl + 1) * TWW],
                                     start=st_, stop=sp), inc=s_tc)
                    lat_slot_consumed[sl] = mmv
                yield
            Sc.w(s_tc, mmv)
            ev = Sc(lambda e, hp=hp, tw=tw: e.copy(
                out=qnT[:, hp * S + tw * TWW: hp * S + (tw + 1) * TWW],
                in_=psB.ap()), inc=s_sc)
            T.w(s_sc, ev)
            kv_sl = []
            mmv = 0
            for kcx in range(4):
                sl, dv = dma_lat(20 + kcx, tw)
                kv_sl.append(sl)
                T.w(s_latq[sl], dv)
                mmv = T(lambda e, k=kcx, sl=sl, st_=(kcx == 0), sp=(kcx == 3):
                        e.matmul(psB.ap(), kvknb[:, k * 128:(k + 1) * 128],
                                 latq[:, sl * TWW:(sl + 1) * TWW],
                                 start=st_, stop=sp), inc=s_tc)
            yield
            Sc.w(s_tc, mmv)
            ev = Sc(lambda e, hp=hp, tw=tw: e.copy(
                out=knT[:, hp * S + tw * TWW: hp * S + (tw + 1) * TWW],
                in_=psB.ap()), inc=s_sc)
            T.w(s_sc, ev)
            mmv = 0
            for tsub in range(TWW // 128):
                for kcx in range(4):
                    sl = kv_sl[kcx]
                    mmv = T(lambda e, k=kcx, sl=sl, ts=tsub,
                            st_=(kcx == 0), sp=(kcx == 3): e.matmul(
                                psB[:, ts * 128:(ts + 1) * 128],
                                latq[:, sl * TWW + ts * 128: sl * TWW + (ts + 1) * 128],
                                kvvb[:, k * 128:(k + 1) * 128],
                                start=st_, stop=sp), inc=s_tc)
            for sl in kv_sl:
                lat_slot_consumed[sl] = mmv
            Sc.w(s_tc, mmv)
            ev = Sc(lambda e, hp=hp, tw=tw: e.copy(
                out=vTb[:, hp * S + tw * TWW: hp * S + (tw + 1) * TWW],
                in_=psB.ap()), inc=s_sc)
            T.w(s_sc, ev)
            proj_last_mm[0] = mmv
            yield

    psA_q_last_exp = [0, 0]
    pbuf_last_tp = [0, 0]
    ptb_last_pv = [0]
    qc_glob = [0]

    def attn_head(h, qr_ld, next_gen):
        hp = h % 2
        kn = knT[:, hp * S:(hp + 1) * S]
        vv_ = vTb[:, hp * S:(hp + 1) * S]
        qn = qnT[:, hp * S:(hp + 1) * S]
        qr = qrTh[0:64, hp * S:(hp + 1) * S]
        T.w(s_qr[h % 2], qr_ld)
        T.w(s_krg, krg_all)

        def emit_transp(info):
            g_, pb_, qoff_, pnv_ = info
            for gg in range((KC + 7) // 8):
                ntl = min(8, KC - gg * 8)
                T.w(s_pn, pnv_)
                if ptb_last_pv[0] > 0:
                    T.w(s_pv, ptb_last_pv[0])
                tiles = [pb_[:, (gg * 8 + i) * 128:(gg * 8 + i + 1) * 128]
                         for i in range(ntl)]

                def evp(hidx, gg=gg, qoff=qoff_, ntl=ntl):
                    V.w(s_tp, s_tp.n)
                    if ptb_last_pv[0] > 0:
                        V.w(s_pv, ptb_last_pv[0])
                    dst = ptb.ap().rearrange("p (k w) -> p k w", w=QB)[
                        :, gg * 8: gg * 8 + ntl, qoff:qoff + 128]
                    src = psC.ap().rearrange("p (h k w) -> p h k w", h=2, w=128)[
                        :, hidx, 0:ntl, :]
                    return V(lambda e, d=dst, s=src: e.tensor_copy(out=d, in_=s),
                             inc=s_pe)
                transpose_group(tiles, evp)
            pbuf_last_tp[g_ % 2] = s_tp.n

        pv_pend = [None]

        def emit_pv(qb):
            T.w(s_pe, s_pe.n)
            T.w(s_pq, psD_last_evac[0])
            pvv = 0
            for kcx in range(KC):
                pvv = T(lambda e, k=kcx, v=vv_, st_=(kcx == 0), sp=(kcx == KC - 1):
                        e.matmul(psD.ap(), v[:, k * 128:(k + 1) * 128],
                                 ptb[:, k * QB:(k + 1) * QB],
                                 start=st_, stop=sp), inc=s_pv)
            ptb_last_pv[0] = pvv
            pv_pend[0] = (qb, pvv)

        def flush_pv():
            if pv_pend[0] is None:
                return
            qb, pvv = pv_pend[0]
            pv_pend[0] = None
            V.w(s_pv, pvv)
            if qb == 0:
                V.w(s_a2, attnT_last_store[0])
            pqv = V(lambda e, qb=qb: e.tensor_copy(
                out=attnT[:, qb * QB:(qb + 1) * QB], in_=psD.ap()), inc=s_pq)
            psD_last_evac[0] = pqv

        pending = None
        for qc in range(QC):
            flush_pv()
            g = qc_glob[0]
            qc_glob[0] += 1
            sb_ = (g % 2) * 32
            pb = pbuf[:, (g % 2) * S:(g % 2) * S + S]
            nmq = stats[:, sb_ + 0: sb_ + 1]
            nms = stats[:, sb_ + 1: sb_ + 2]
            for j in range(4):
                reg = j % 2
                T.w(s_ex, psA_q_last_exp[reg])
                T_mm(psA[:, reg * 1024: reg * 1024 + KQ],
                     qn[:, qc * 128:(qc + 1) * 128],
                     kn[:, j * KQ:(j + 1) * KQ], True, False)
                qkv = T_mm(psA[:, reg * 1024: reg * 1024 + KQ],
                           qr[:, qc * 128:(qc + 1) * 128],
                           krTg[0:64, j * KQ:(j + 1) * KQ], False, True, inc=s_qk)
                if j == 0:
                    V.w(s_qk, qkv)
                    V(lambda e, nmq=nmq: e.reduce_max(
                        nmq, psA[:, 0:128], axis=AX, negate=True))
                    mxv = V(lambda e, nms=nms, nmq=nmq: e.tensor_scalar_mul(
                        nms, nmq, ATT_SCALE), inc=s_mx)
                    Sc.w(s_mx, mxv)
                    if g >= 2:
                        Sc.w(s_tp, pbuf_last_tp[g % 2])
                Sc.w(s_qk, qkv)
                ssum = stats[:, sb_ + 20 + j: sb_ + 21 + j]
                exv = Sc(lambda e, j=j, reg=reg, pb=pb, nms=nms, ssum=ssum:
                         e.activation(pb[:, j * KQ:(j + 1) * KQ],
                                      psA[:, reg * 1024: reg * 1024 + KQ],
                                      AF.Exp, bias=nms, scale=ATT_SCALE,
                                      accum_out=ssum), inc=s_ex)
                psA_q_last_exp[reg] = exv
            if pending is not None:
                emit_transp(pending)
                if qc % QCB == 0 and qc > 0:
                    emit_pv(qc // QCB - 1)
            V.w(s_ex, s_ex.n)
            tot = stats[:, sb_ + 24: sb_ + 25]
            rcp = stats[:, sb_ + 25: sb_ + 26]
            V(lambda e, tot=tot, sb_=sb_: e.reduce_sum(
                tot, stats[:, sb_ + 20: sb_ + 24], axis=AX))
            V(lambda e, rcp=rcp, tot=tot: e.reciprocal(rcp, tot))
            pnv = V(lambda e, pb=pb, rcp=rcp: e.tensor_scalar_mul(
                pb, pb, rcp), inc=s_pn)
            pending = (g, pb, (qc % QCB) * 128, pnv)
            if next_gen is not None:
                next(next_gen, None)
            if q3_box[0] is not None:
                for _ in range(8):
                    next(q3_box[0], None)
        emit_transp(pending)
        emit_pv(QC // QCB - 1)
        flush_pv()
        G.w(s_pq, s_pq.n)
        av = 0
        for j in range(4):
            for tcl in range(TC):
                av = G.dma(a2a_in[h, j, tcl],
                           attnT[:, j * W1 + tcl * 128: j * W1 + (tcl + 1) * 128],
                           s_a2)
        attnT_last_store[0] = av
        G.w(s_a2, s_a2.n)
        cc_a2a_h[h] = G(lambda e, h=h: e.collective_compute(
            "AllGather", OP.bypass, replica_groups=[[0, 1, 2, 3], [4, 5, 6, 7]],
            ins=[a2a_in.ap()[h].opt()], outs=[a2a_out.ap()[h].opt()]), inc=s_cc)
        Y.w(s_cc, cc_a2a_h[h])
        s_am.inc(16)

        def _fa(e, h=h):
            sap = a2a_out.ap()[h][:, bass.ds(_rank_base(e), 1)].squeeze(1)
            dst = a2a_mine.ap().rearrange("(s hl) t p w -> s hl t p w", s=4)[:, h]
            return e.dma_start(out=dst, in_=sap).then_inc(s_am.h, 16)
        EM["sync"].append(_fa)

    q3_done = [0]

    def quant3a_gen():
        Y.w(s_tc, max(kr_mm, qr_mm_last))
        Y.w(s_vc, partB_done)
        Y.w(s_sc, s_sc.n)
        Y.w(s_q2, s_q2.n)
        Y.w(s_ag, s_ag.n)
        x_next = Y.dma(p3xA.ap(), x_in[0:128, :], s_xs[0])
        qv = 0
        for t in range(TC):
            V.w(s_xs[0], x_next)
            V(lambda e: e.reduce_sum(negmu, p3xA.ap(), axis=AX, negate=True))
            V(lambda e: e.tensor_scalar_mul(negmu, negmu, 1.0 / D))
            vv = V(lambda e: e.tensor_scalar_add(q_c.ap(), p3xA.ap(), negmu),
                   inc=s_vc)
            yield
            Sc.w(s_vc, vv)
            Sc.w(s_q2, s_q2.n)
            sv = Sc(lambda e: e.activation(p3xA.ap(), q_c.ap(), AF.Square,
                                           accum_out=varl), inc=s_sc)
            V.w(s_sc, sv)
            vv = V(lambda e: e.tensor_scalar(varl, varl, 1.0 / D, float(EPS),
                                             OP.mult, OP.add), inc=s_vc)
            Sc.w(s_vc, vv)
            Sc(lambda e: e.activation(rstd1, varl, AF.Ln))
            sv = Sc(lambda e: e.activation(rstd1, rstd1, AF.Exp, scale=-0.5),
                    inc=s_sc)
            V.w(s_sc, sv)
            V(lambda e: e.tensor_scalar_mul(q_c.ap(), q_c.ap(), rstd1))
            V.w(s_xq, s_xq.n)
            yield
            for hf in range(2):
                for r in nvfp4_gen(q_c[:, hf * 1024:(hf + 1) * 1024], 1024,
                                   q_bs.ap(), q_sA.ap(), q_sB.ap(),
                                   cc=q_rB.ap(), rA=q_rA.ap(), rB=q_rC.ap(),
                                   rC=q_rD.ap(), rD=p3xA[:, 0:1024],
                                   rfac=wsc[:, 7:8], ts_lane=xtsb.ap(),
                                   dst=p3xdq[:, hf * 1024:(hf + 1) * 1024],
                                   post=wsc[:, 6:7]):
                    if r is not None:
                        qv = r
                    yield
            G.w(s_vc, qv)
            G.dma(xdq_scr[t], p3xdq.ap(), s_xq)
            if t + 1 < TC:
                Y.w(s_vc, qv)
                x_next = Y.dma(p3xA.ap(), x_in[(t + 1) * 128:(t + 2) * 128, :],
                               s_xs[0])
            yield
        q3_done[0] = qv

    cc_a2a_h = {}
    q3_box = [quant3a_gen()]
    cur_gen = proj_steps(0)
    cur_qr = next(cur_gen)
    for h in range(HG):
        for _ in cur_gen:
            pass
        if h + 1 < HG:
            nxt_gen = proj_steps(h + 1)
            nxt_qr = next(nxt_gen)
        else:
            nxt_gen, nxt_qr = None, 0
        attn_head(h, cur_qr, nxt_gen)
        head_qk_done[h] = s_qk.n
        cur_gen, cur_qr = nxt_gen, nxt_qr
    cc_a2a = cc_a2a_h[HG - 1]

    for _ in q3_box[0]:
        pass
    q3_box[0] = None
    quant_done = q3_done[0]
    T.w(s_ex, s_ex.n)

    tile_mm = {}
    tile_cb = {}
    prev_mm = {0: None, 1: None}
    prev_cb = {0: None, 1: None}
    prev_ost = {0: 0, 1: 0}
    half_last_mm = 0
    for half in range(2):
        Y.w(s_cc, cc_wdq)
        Y.w(s_cc, cc_a2a)
        Y.w(s_vc, quant_done)
        G.w(s_cc, cc_a2a)
        if half == 1:
            Y.w(s_tc, half_last_mm)
            G.w(s_tc, half_last_mm)
        Y.w(s_am, s_am.n)
        Y.w(s_xq, s_xq.n)
        wv = 0
        for kcx in range(16):
            for sp in range(4):
                src = half * 4 + sp
                wv = Y.dma(p3wdq[:, kcx * 1024 + sp * 256: kcx * 1024 + (sp + 1) * 256],
                           wdq_out[src, kcx], s_wv)
        ov = 0
        for dvc in range(16):
            ov = G.dma(p3ow[:, dvc * 1024:(dvc + 1) * 1024], owT_b[half, dvc], s_p3g)
        for t in range(TC):
            tp = t % 2
            idx = half * TC + t
            if prev_mm[tp] is not None:
                Y.w(s_tc, prev_mm[tp])
                Y.w(s_vc, prev_cb[tp])
            av = 0
            for src in range(4):
                for hl in range(4):
                    av = Y.dma(p3at[:, tp * D + (src * 4 + hl) * 128:
                                    tp * D + (src * 4 + hl + 1) * 128],
                               a2a_mine[src * 4 + hl, t], s_p3t[tp])
            xv = Y.dma(p3xdqb[:, tp * D: tp * D + D], xdq_scr[t], s_p3t[tp])
            x3v = Y.dma(p3x[:, tp * D: tp * D + D], x_in[t * 128:(t + 1) * 128, :],
                        s_p3xl[tp])
            T.w(s_p3t[tp], xv)
            for gg in range(2):
                tiles = [p3xdqb[:, tp * D + (gg * 8 + i) * 128:
                                tp * D + (gg * 8 + i + 1) * 128] for i in range(8)]

                def evx(hidx, gg=gg, tp=tp):
                    Sc.w(s_tp, s_tp.n)
                    dst = p3xdqT[:, tp * D + gg * 1024: tp * D + (gg + 1) * 1024]
                    return Sc(lambda e, d=dst, h=hidx: e.copy(
                        out=d, in_=psC[:, h * 1024:(h + 1) * 1024]), inc=s_pe)
                transpose_group(tiles, evx)
            T.w(s_pe, s_pe.n)
            if t == 0:
                T.w(s_wv, wv)
                T.w(s_p3g, ov)
            side = t % 2
            for kcx in range(16):
                T_mm(psA[:, side * 1024:(side + 1) * 1024],
                     p3xdqT[:, tp * D + kcx * 128: tp * D + (kcx + 1) * 128],
                     p3wdq[:, kcx * 1024:(kcx + 1) * 1024], kcx == 0, False)
            mmv = 0
            for dvc in range(16):
                mmv = T_mm(psA[:, side * 1024:(side + 1) * 1024],
                           p3at[:, tp * D + dvc * 128: tp * D + (dvc + 1) * 128],
                           p3ow[:, dvc * 1024:(dvc + 1) * 1024],
                           False, dvc == 15, inc=s_tc)
            tile_mm[idx] = mmv
            if t == TC - 1:
                half_last_mm = mmv
            V.w(s_p3xl[tp], x3v)
            V.w(s_ost[tp], prev_ost[tp])
            V(lambda e, tp=tp, half=half: e.tensor_tensor(
                out=p3xb[:, tp * 1024:(tp + 1) * 1024],
                in0=p3x[:, tp * D + half * 1024: tp * D + half * 1024 + 1024],
                in1=biasb[:, half * 1024:(half + 1) * 1024], op=OP.add))
            V.w(s_tc, mmv)
            cbv = V(lambda e, tp=tp, side=side: e.tensor_tensor(
                out=p3xb[:, tp * 1024:(tp + 1) * 1024],
                in0=p3xb[:, tp * 1024:(tp + 1) * 1024],
                in1=psA[:, side * 1024:(side + 1) * 1024], op=OP.add), inc=s_vc)
            tile_cb[idx] = cbv
            T.w(s_vc, cbv)
            G.w(s_vc, cbv)
            prev_ost[tp] = G.dma(
                out_t[t * 128:(t + 1) * 128, half * 1024:half * 1024 + 1024],
                p3xb[:, tp * 1024:(tp + 1) * 1024], s_ost[tp])
            prev_mm[tp] = mmv
            prev_cb[tp] = cbv

    if dbg:
        G.dma(dbg_a2a.ap(), a2a_in.ap(), s_out)
        G.dma(dbg_agin.ap(), ag_in.ap(), s_out)
        G.dma(dbg_agout[0:4, 0:12], ag_out_a.ap(), s_out)
        G.dma(dbg_kn.ap(), knT.ap(), s_out)
        G.dma(dbg_qn.ap(), qnT.ap(), s_out)
        G.dma(dbg_wsc.ap(), wsc.ap(), s_out)
        G.dma(dbg_att.ap(), attnT.ap(), s_out)
        G.dma(dbg_p.ap(), pbuf[:, 0:S], s_out)
        G.dma(dbg_st.ap(), stats.ap(), s_out)
    G.w(s_out, s_out.n)
    G.w(s_ost[0], s_ost[0].n)
    G.w(s_ost[1], s_ost[1].n)
    G(lambda e: e.engine_nop(), inc=s_fin)
    G.w(s_fin, s_fin.n)

    with nc.Block() as block:
        @block.sync
        def _(eng):
            for f in EM["sync"]:
                f(eng)

        @block.tensor
        def _(eng):
            for f in EM["tensor"]:
                f(eng)

        @block.vector
        def _(eng):
            for f in EM["vector"]:
                f(eng)

        @block.scalar
        def _(eng):
            for f in EM["scalar"]:
                f(eng)

        @block.gpsimd
        def _(eng):
            for f in EM["gpsimd"]:
                f(eng)

    return nc


def _rope_tables(S):
    inv = 1.0 / 10000.0 ** (np.arange(0, DR, 2, dtype=np.float32) / DR)
    t = np.arange(S, dtype=np.float32)
    fr = np.outer(t, inv).astype(np.float32)
    emb = np.concatenate([fr, fr], -1)
    return np.cos(emb).astype(np.float32), np.sin(emb).astype(np.float32)


def stage_inputs(hs, q_a_w, q_a_ln_w, q_a_ln_b, q_b_w, q_rope_w, kv_a_w,
                 kv_a_ln_w, kv_a_ln_b, kv_b_w, o_w, ffn_w, ffn_b, x_ts, S):
    B = 2
    NT = B * S // NC
    xf = np.ascontiguousarray(hs.reshape(B * S, D))
    cos, sin = _rope_tables(S)
    sfold = np.concatenate([-sin[:, 0:32], sin[:, 32:64]], axis=1)

    def colblk(mat, nkc, nw=128):
        K, N = mat.shape
        return np.ascontiguousarray(
            mat.reshape(nkc, 128, N // nw, nw).transpose(2, 1, 0, 3))

    qaT = np.ascontiguousarray(q_a_w.T)
    kvaT = np.ascontiguousarray(kv_a_w.T)
    qrwT = np.ascontiguousarray(q_rope_w.T)
    owT = np.ascontiguousarray(o_w.T)

    qaT_b = colblk(qaT, 16)
    kvaT_b = colblk(kvaT[:, 0:KVL], 16)
    kvrT_b = np.ascontiguousarray(
        kvaT[:, KVL:KVL + DR].reshape(16, 128, DR).transpose(1, 0, 2))
    qrwT_b = colblk(qrwT, 12)
    owT_hb = np.ascontiguousarray(
        owT.reshape(16, 128, 2, 1024).transpose(2, 0, 1, 3))

    shared = dict(
        qaT_b=qaT_b, kvaT_b=kvaT_b, kvrT_b=kvrT_b, qrwT_b=qrwT_b, owT_b=owT_hb,
        qalnw=np.ascontiguousarray(q_a_ln_w.reshape(12, 128, 1)),
        qalnb=np.ascontiguousarray(q_a_ln_b.reshape(12, 128, 1)),
        kvalnw=np.ascontiguousarray(kv_a_ln_w.reshape(4, 128, 1)),
        kvalnb=np.ascontiguousarray(kv_a_ln_b.reshape(4, 128, 1)),
        bias_rep=np.ascontiguousarray(np.broadcast_to(ffn_b.reshape(1, D), (128, D))),
        xts_rep=np.full((128, 1), np.float32(x_ts), np.float32),
        ident=np.eye(128, dtype=ml_dtypes.bfloat16),
        identf=np.eye(128, dtype=np.float32),
    )
    ins = []
    for c in range(NC):
        hg0 = 4 * (c % 4)
        pos = (c % 4) * NT + np.arange(NT)
        qb_rows = np.concatenate(
            [q_b_w[128 * (hg0 + i):128 * (hg0 + i + 1)] for i in range(HG)], 0)
        kn_rows = np.concatenate(
            [kv_b_w[256 * (hg0 + i):256 * (hg0 + i) + 128] for i in range(HG)], 0)
        v_rows = np.concatenate(
            [kv_b_w[256 * (hg0 + i) + 128:256 * (hg0 + i + 1)] for i in range(HG)], 0)
        qbT = np.ascontiguousarray(qb_rows.T)
        knT_ = np.ascontiguousarray(kn_rows.T)
        vT_ = np.ascontiguousarray(v_rows.T)
        d = dict(shared)
        d.update(
            x=np.ascontiguousarray(xf[c * NT:(c + 1) * NT]),
            qbT_b=np.ascontiguousarray(
                qbT.reshape(12, 128, HG, 128).transpose(2, 1, 0, 3)),
            kvbknT_b=np.ascontiguousarray(
                knT_.reshape(4, 128, HG, 128).transpose(2, 1, 0, 3)),
            kvbvT_b=np.ascontiguousarray(
                vT_.reshape(4, 128, HG, 128).transpose(2, 1, 0, 3)),
            ffnw=np.ascontiguousarray(
                ffn_w[256 * c:256 * (c + 1)].reshape(2, 128, D)),
            ktcos=np.ascontiguousarray(cos[pos]),
            ktsf=np.ascontiguousarray(sfold[pos]),
            fcos2=np.ascontiguousarray(np.concatenate([cos[pos].T, cos[pos].T], 0)),
            fsf2=np.ascontiguousarray(np.concatenate([sfold[pos].T, sfold[pos].T], 0)),
        )
        ins.append(d)
    return ins


def kernel(L0_attn_hidden_states, L0_attn_q_a_w, L0_attn_q_a_ln_w,
           L0_attn_q_a_ln_b, L0_attn_q_b_w, L0_attn_q_rope_w, L0_attn_kv_a_w,
           L0_attn_kv_a_ln_w, L0_attn_kv_a_ln_b, L0_attn_kv_b_w, L0_attn_o_w,
           L0_ffn_weight, L0_ffn_bias, L0_ffn_x_tensor_scale):
    hs = np.asarray(L0_attn_hidden_states, dtype=np.float32)
    B, S, _ = hs.shape
    in_maps = stage_inputs(
        hs, np.asarray(L0_attn_q_a_w, np.float32),
        np.asarray(L0_attn_q_a_ln_w, np.float32),
        np.asarray(L0_attn_q_a_ln_b, np.float32),
        np.asarray(L0_attn_q_b_w, np.float32),
        np.asarray(L0_attn_q_rope_w, np.float32),
        np.asarray(L0_attn_kv_a_w, np.float32),
        np.asarray(L0_attn_kv_a_ln_w, np.float32),
        np.asarray(L0_attn_kv_a_ln_b, np.float32),
        np.asarray(L0_attn_kv_b_w, np.float32),
        np.asarray(L0_attn_o_w, np.float32),
        np.asarray(L0_ffn_weight, np.float32),
        np.asarray(L0_ffn_bias, np.float32),
        np.float32(L0_ffn_x_tensor_scale), S)
    nc = build_program(S)
    res = run_bass_kernel_spmd(nc, in_maps, core_ids=list(range(NC)))
    NT = B * S // NC
    out = np.concatenate([np.asarray(res.results[c]["out"]) for c in range(NC)],
                         axis=0)
    return out.reshape(B, S, D).astype(np.float32)
